# version 01
# baseline (speedup 1.0000x reference)
import contextlib
import numpy as np
import ml_dtypes

import concourse.bass as bass
import concourse.mybir as mybir
from concourse.bass_utils import run_bass_kernel_spmd

F32 = mybir.dt.float32
BF16 = mybir.dt.bfloat16
AF = mybir.ActivationFunctionType
ALU = mybir.AluOpType
AX = mybir.AxisListType

B, S, D = 4, 4096, 1024
NCH = 8
TOWN = 2048
HALO = 640
TEXT = HALO + TOWN
M = 128
CC = 128
DECAY, LR = 0.99, 0.01
DEC_C = float(DECAY ** CC)
NEG = -1.0e30
FH, FDH, WIN = 8, 64, 512
HID, HIDP = 2730, 2816
EPS = 1e-5

_f32 = np.float32
_bf16 = ml_dtypes.bfloat16


def _host_consts(half):
    c = {}
    i = np.arange(CC)
    dmT = np.where(i[:, None] < i[None, :],
                   LR * DECAY ** (-1.0 - i[:, None]), 0.0)
    c["dmaskT"] = (dmT * np.ones((CC, CC))).astype(_f32)
    c["decq512"] = np.broadcast_to(
        DECAY ** (np.arange(512) % 128), (128, 512)).astype(_f32)
    c["deck"] = (LR * DECAY ** (127.0 - i)).reshape(128, 1).astype(_f32)
    c["identity"] = np.eye(128, dtype=_f32)
    R = np.zeros((128, 128), dtype=_f32)
    for h0 in (0, 64):
        for k in range(32):
            R[h0 + 32 + k, h0 + k] = -1.0
            R[h0 + k, h0 + 32 + k] = 1.0
    c["Rperm"] = R
    oc2 = np.zeros((128, 2), dtype=_f32)
    oc2[:64, 0] = 1.0
    oc2[64:, 1] = 1.0
    c["ones2col"] = oc2
    e2 = np.zeros((2, 128), dtype=_f32)
    e2[0, :64] = 1.0
    e2[1, 64:] = 1.0
    c["e2"] = e2
    c["flag"] = np.full((128, 1), float(half), dtype=_f32)
    c["flagn"] = np.full((128, 1), 1.0 - float(half), dtype=_f32)
    c["epsc"] = np.full((128, 1), EPS, dtype=_f32)
    c["ones512"] = np.ones((1, 512), dtype=_f32)

    t0 = 2048 * half
    pos = np.arange(t0 - HALO, t0 + TOWN, dtype=np.float64)
    half_idx = np.arange(0, FDH, 2, dtype=np.float64)
    inv_freq = 1.0 / 10000.0 ** (half_idx / FDH)
    freqs = np.concatenate([pos[:, None] * inv_freq[None, :]] * 2, axis=1)
    base = (half_idx + 0.4 * FDH) / (1.4 * FDH)
    power = (pos - S // 2) / float(WIN // 2)
    scale = np.concatenate([base[None, :] ** power[:, None]] * 2, axis=1)
    cos, sin = np.cos(freqs), np.sin(freqs)
    for nm, tb in (("rotAq", cos * scale * 8.0), ("rotBq", sin * scale * 8.0),
                   ("rotAk", cos / scale), ("rotBk", sin / scale)):
        c[nm] = np.tile(tb.T.astype(_f32), (2, 1))

    kl = np.arange(128)[:, None]
    cf = np.arange(1408)[None, :]
    df = kl - cf + 384
    c["fmaskM"] = np.where((df > 0) | (df < -512), NEG, 0.0).astype(_f32)
    km = np.zeros((128, 640), dtype=_f32)
    if half == 0:
        km[:, 128:] = NEG
    c["khmask0"] = km
    return c


def _stage_core(inputs, core):
    b, half = core // 2, core % 2
    t0 = 2048 * half
    x = inputs["L0_attn_a_x"][b]
    d = dict(_host_consts(half))
    d["x_own"] = x[t0:t0 + TOWN].T
    d["x_pre"] = x[0:2048].T if half else np.zeros((D, 2048), dtype=_f32)
    d["x_halo"] = x[t0 - HALO:t0].T if half else np.zeros((D, HALO), dtype=_f32)
    d["wkT"] = inputs["L0_attn_a_Wk"].T
    d["wvT"] = inputs["L0_attn_a_Wv"].T
    d["wqT"] = inputs["L0_attn_a_Wq"].T
    d["woT"] = inputs["L0_attn_a_Wo"].T
    d["bk"] = inputs["L0_attn_a_bk"].reshape(128, 1)
    d["bq"] = inputs["L0_attn_a_bq"].reshape(128, 1)
    d["bkrow"] = inputs["L0_attn_a_bk"].reshape(1, 128)
    d["bvrow"] = inputs["L0_attn_a_bv"].reshape(1, 128)
    d["bo"] = inputs["L0_attn_a_bo"].reshape(D, 1)
    for l in range(2):
        d[f"wqkv{l}"] = inputs["L0_ffn_wqkv"][l]
        d[f"fwo{l}"] = inputs["L0_ffn_wo"][l]
        w1 = inputs["L0_ffn_w1"][l]
        w1p = np.zeros((D, 2 * HIDP), dtype=_f32)
        w1p[:, :HID] = w1[:, :HID]
        w1p[:, HIDP:HIDP + HID] = w1[:, HID:]
        d[f"w1p{l}"] = w1p
        b1 = inputs["L0_ffn_b1"][l]
        b1p = np.zeros((2 * HIDP, 1), dtype=_f32)
        b1p[:HID, 0] = b1[:HID]
        b1p[HIDP:HIDP + HID, 0] = b1[HID:]
        d[f"b1p{l}"] = b1p
        w2p = np.zeros((HIDP, D), dtype=_f32)
        w2p[:HID] = inputs["L0_ffn_w2"][l]
        d[f"w2p{l}"] = w2p
        d[f"b2_{l}"] = inputs["L0_ffn_b2"][l].reshape(D, 1)
        d[f"lng{l}"] = inputs["L0_ffn_ln_g"][l].reshape(D, 1)
        d[f"lnb{l}"] = inputs["L0_ffn_ln_b"][l].reshape(D, 1)
        d[f"lngb{l}"] = np.stack([inputs["L0_ffn_ln_g"][l],
                                  inputs["L0_ffn_ln_b"][l]])
        d[f"qs{l}"] = np.tile(inputs["L0_ffn_q_scale"][l], 2).reshape(128, 1)
        d[f"ks{l}"] = np.tile(inputs["L0_ffn_k_scale"][l], 2).reshape(128, 1)
    Pi = inputs["L0_attn_b_P_i"]
    myh = range(8 * half, 8 * half + 8)
    d["PiQ"] = np.concatenate([Pi[:, 192 * h:192 * h + 64] for h in myh], axis=1)
    d["PiK"] = np.concatenate([Pi[:, 192 * h + 64:192 * h + 128] for h in myh], axis=1)
    d["PiV"] = np.concatenate([Pi[:, 192 * h + 128:192 * h + 192] for h in myh], axis=1)
    d["Po"] = inputs["L0_attn_b_P_o"]
    return {k: np.ascontiguousarray(v, dtype=_f32) for k, v in d.items()}


class Sem:
    def __init__(self, handle):
        self.h = handle
        self.v = 0

    def inc(self, inst, n=1):
        inst.then_inc(self.h, n)
        self.v += n
        return self.v


class Seg:

    def __init__(self, t, off, cols, dtype):
        self.t, self.off, self.cols, self.dtype = t, off, cols, dtype

    def ap(self, a=0, b=None, p0=0, p1=128):
        b = self.cols if b is None else b
        if self.dtype == F32:
            return self.t[p0:p1, self.off + a:self.off + b]
        assert self.off % 1 == 0
        base = self.t[p0:p1, self.off:self.off + (self.cols + 1) // 2]
        return base.bitcast(BF16)[:, a:b]


class Arena:
    def __init__(self, p, name, cols):
        self.t = p.sb(name, [128, cols], F32)
        self.cols = cols
        self.off = 0
        self.name = name

    def alloc(self, cols, dtype=F32):
        fcols = cols if dtype == F32 else (cols + 1) // 2
        assert self.off + fcols <= self.cols, \
            f"arena {self.name} overflow: {self.off}+{fcols} > {self.cols}"
        s = Seg(self.t, self.off, cols, dtype)
        self.off += fcols
        return s

    def reset(self, off=0):
        self.off = off


class Prog:
    def __init__(self):
        self.nc = bass.Bass("TRN2", target_bir_lowering=False, debug=False)
        self.ctx = contextlib.ExitStack()
        self._sems = {}
        self.di = {}
        self.dbg_outs = {}
        self.debug = ()

    def sem(self, name):
        if name not in self._sems:
            self._sems[name] = Sem(self.ctx.enter_context(self.nc.semaphore(name)))
        return self._sems[name]

    def sb(self, name, shape, dtype):
        return self.ctx.enter_context(self.nc.sbuf_tensor(name, shape, dtype))

    def psum(self, name, shape, dtype=F32):
        return self.ctx.enter_context(self.nc.psum_tensor(name, shape, dtype))

    def barrier(self, tag):
        s = self.sem("bar")
        nc = self.nc
        engs = [nc.sync, nc.tensor, nc.vector, nc.scalar, nc.gpsimd]
        base = s.v
        for e in engs:
            s.inc(e.nop(), 1)
        for e in engs:
            e.wait_ge(s.h, base + 5)


class PsBank:

    def __init__(self, t):
        self.t = t
        self.free_at = None

    def acquire(self, eng):
        if self.free_at is not None:
            sem, v = self.free_at
            eng.wait_ge(sem.h, v)
            self.free_at = None
        return self.t

    def consumed(self, sem, v):
        self.free_at = (sem, v)


def build(upto="full", debug=()):
    p = Prog()
    p.debug = tuple(debug)
    nc = p.nc
    di = p.di

    def dram_in(name, shape, dtype=F32):
        di[name] = nc.dram_tensor(name, list(shape), dtype,
                                  kind="ExternalInput").ap()

    for nm, sh in [
        ("x_own", (D, TOWN)), ("x_pre", (D, 2048)), ("x_halo", (D, HALO)),
        ("flag", (128, 1)), ("flagn", (128, 1)),
        ("epsc", (128, 1)), ("ones512", (1, 512)),
        ("wkT", (D, 128)), ("wvT", (D, 128)), ("wqT", (D, 128)), ("woT", (128, D)),
        ("bk", (128, 1)), ("bq", (128, 1)), ("bkrow", (1, 128)), ("bvrow", (1, 128)),
        ("bo", (D, 1)),
        ("dmaskT", (128, 128)), ("decq512", (128, 512)), ("deck", (128, 1)),
        ("identity", (128, 128)), ("Rperm", (128, 128)),
        ("ones2col", (128, 2)), ("e2", (2, 128)),
        ("rotAq", (128, TEXT)), ("rotBq", (128, TEXT)),
        ("rotAk", (128, TEXT)), ("rotBk", (128, TEXT)),
        ("khmask0", (128, 640)),
        ("PiQ", (D, 512)), ("PiK", (D, 512)), ("PiV", (D, 512)), ("Po", (D, D)),
    ]:
        dram_in(nm, sh)
    dram_in("fmaskM", (128, 1408))
    for l in range(2):
        for nm, sh in [
            (f"wqkv{l}", (D, 1536)), (f"fwo{l}", (512, D)),
            (f"w1p{l}", (D, 2 * HIDP)), (f"b1p{l}", (2 * HIDP, 1)),
            (f"w2p{l}", (HIDP, D)), (f"b2_{l}", (D, 1)),
            (f"lng{l}", (D, 1)), (f"lnb{l}", (D, 1)),
            (f"lngb{l}", (2, D)),
            (f"qs{l}", (128, 1)), (f"ks{l}", (128, 1)),
        ]:
            dram_in(nm, sh)

    out_x4 = nc.dram_tensor("o_x4", [D, TOWN], F32, kind="ExternalOutput").ap()

    def dbg(name, shape, dtype=F32):
        if name in p.debug and name not in p.dbg_outs:
            p.dbg_outs[name] = nc.dram_tensor(
                "dbg_" + name, list(shape), dtype, kind="ExternalOutput").ap()
        return p.dbg_outs.get(name)

    x_spill = nc.dram_tensor("x_spill", [D, TOWN], F32, kind="Internal").ap()
    ag1_in = nc.dram_tensor("ag1_in", [D, HALO], BF16, kind="Internal").ap()
    ag1_out = nc.dram_tensor("ag1_out", [2 * D, HALO], BF16, kind="Internal").ap()
    ag2z_in = nc.dram_tensor("ag2z_in", [D, TOWN], BF16, kind="Internal").ap()
    ag2z_out = nc.dram_tensor("ag2z_out", [2 * D, TOWN], BF16, kind="Internal").ap()
    ag2x_in = nc.dram_tensor("ag2x_in", [D, HALO], BF16, kind="Internal").ap()
    ag2x_out = nc.dram_tensor("ag2x_out", [2 * D, HALO], BF16, kind="Internal").ap()
    ag3_in = nc.dram_tensor("ag3_in", [512, S], BF16, kind="Internal").ap()
    ag3_out = nc.dram_tensor("ag3_out", [1024, S], BF16, kind="Internal").ap()
    ag4_in = nc.dram_tensor("ag4_in", [D, HALO], BF16, kind="Internal").ap()
    ag4_out = nc.dram_tensor("ag4_out", [2 * D, HALO], BF16, kind="Internal").ap()
    RG = [[0, 1], [2, 3], [4, 5], [6, 7]]

    a1 = Arena(p, "a1", 19544)
    a2 = Arena(p, "a2", NCH * (TEXT + 1))
    a3 = Arena(p, "a3", 6656)
    sc = Arena(p, "sc", 768)
    flag = p.sb("flag_s", [128, 1], F32)
    flagn = p.sb("flagn_s", [128, 1], F32)
    epsc = p.sb("epsc_s", [128, 1], F32)
    ones1 = p.sb("ones1", [1, 128], F32)
    ones1b = p.sb("ones1b", [1, 128], BF16)
    onescol = p.sb("onescol", [128, 1], BF16)
    identb = p.sb("identb", [128, 128], BF16)
    srow = p.sb("srow", [2, 2048], F32)
    masksF = p.sb("masksF", [128, 1408], BF16)

    pp = [PsBank(p.psum(f"pp{i}", [128, 512])) for i in range(8)]

    block = p.ctx.enter_context(nc.Block())
    SY, PE, VE, ACT, GP = nc.sync, nc.tensor, nc.vector, nc.scalar, nc.gpsimd
    sP = p.sem("P")
    sV = p.sem("V")
    sA = p.sem("A")

    def dma(eng, out, in_, stream, **kw):
        sm = p.sem(stream)
        sm.inc(eng.dma_start(out=out, in_=in_, **kw), 16)
        return sm.v

    def wait_stream(eng, stream):
        sm = p.sem(stream)
        eng.wait_ge(sm.h, sm.v)

    def vchain(inst):
        vv = sV.inc(inst)
        VE.wait_ge(sV.h, vv)
        return vv

    xo = a1.alloc(16384, F32)

    def XO(c, a, b):
        return xo.ap(TOWN * c + a, TOWN * c + b)

    dma(SY, flag[:], di["flag"], "L1")
    dma(SY, flagn[:], di["flagn"], "L1")
    dma(SY, epsc[:], di["epsc"], "L1")
    dma(GP, identb[:], di["identity"], "G1")
    dma(GP, ones1b[:], di["ones512"][0:1, 0:128], "G1")
    for c in range(NCH):
        dma(SY, XO(c, 0, TOWN), di["x_own"][128 * c:128 * (c + 1), :], "L1")
    dma(GP, masksF[:], di["fmaskM"], "G1")

    def fmask(kk):
        return masksF[:, 896 - 128 * kk:1408 - 128 * kk]

    def bmask(j):
        return masksF[:, 384 - 128 * j:896 - 128 * j]
    dma(SY, srow[1:2, 1536:2048], di["ones512"], "L1")
    dma(SY, ones1[:], di["ones512"][0:1, 0:128], "L1")
    VE.memset(onescol[:], 1.0)

    def ln_group(get_in, put_out, W, scratch, banks, gb=None, ready_v=None):
        stA, stB, bcA, bcB = banks
        xbs, sqs, t1s = scratch["xb"], scratch["sq"], scratch["t1"]
        if ready_v is not None:
            ACT.wait_ge(sV.h, ready_v)
        tA = stA.acquire(PE)
        tB = stB.acquire(PE)
        slot_pe = scratch.setdefault("slot_pe", [0, 0])
        for c in range(NCH):
            xin = get_in(c)
            xb = xbs[c % 2]
            sq = sqs[c % 2]
            VE.wait_ge(sP.h, slot_pe[c % 2])
            ACT.wait_ge(sP.h, slot_pe[c % 2])
            vv = sV.inc(VE.tensor_copy(xb.ap(0, W), xin))
            aa = sA.inc(ACT.activation(sq.ap(0, W), xin, AF.Square))
            PE.wait_ge(sV.h, vv)
            PE.wait_ge(sA.h, aa)
            PE.matmul(tA[0:1, 0:W], onescol[:], xb.ap(0, W),
                      start=(c == 0), stop=(c == NCH - 1))
            pv = sP.inc(PE.matmul(tB[0:1, 0:W], onescol[:], sq.ap(0, W),
                                  start=(c == 0), stop=(c == NCH - 1)))
            slot_pe[c % 2] = pv
        pe_done = sP.v
        mu = srow[0:1, 0:0 + W]
        msq = srow[0:1, 512:512 + W]
        var = srow[0:1, 1024:1024 + W]
        sd = srow[0:1, 512:512 + W]
        rst = srow[0:1, 1024:1024 + W]
        nmb = srow[0:1, 1536:1536 + W]
        VE.wait_ge(sP.h, pe_done)
        vv = sV.inc(VE.tensor_scalar(mu, tA[0:1, 0:W], 1.0 / D, None,
                                     op0=ALU.mult))
        ACT.wait_ge(sV.h, vv)
        av = sA.inc(ACT.activation(msq, mu, AF.Square))
        VE.wait_ge(sA.h, av)
        vv = sV.inc(VE.scalar_tensor_tensor(
            var, tB[0:1, 0:W], 1.0 / D, msq,
            op0=ALU.mult, op1=ALU.subtract))
        stA.consumed(sV, vv)
        stB.consumed(sV, vv)
        ACT.wait_ge(sV.h, vv)
        av = sA.inc(ACT.activation(sd, var, AF.Ln, bias=epsc[0:1, :]))
        ACT.wait_ge(sA.h, av)
        av = sA.inc(ACT.activation(rst, sd, AF.Exp, scale=-0.5))
        VE.wait_ge(sA.h, av)
        vv = vchain(VE.scalar_tensor_tensor(nmb, mu, -1.0, rst,
                                            op0=ALU.mult, op1=ALU.mult))
        tbcA = bcA.acquire(PE)
        PE.wait_ge(sV.h, vv)
        if gb is not None:
            vv = vchain(VE.tensor_copy(scratch["lnrb"][0:1, 0:W], nmb))
            PE.wait_ge(sV.h, vv)
        pva = sP.inc(PE.matmul(tbcA[:, 0:W], ones1[:], rst,
                               start=True, stop=True))
        if gb is None:
            tbcB = bcB.acquire(PE)
            pvb = sP.inc(PE.matmul(tbcB[:, 0:W], ones1[:], nmb,
                                   start=True, stop=True))
            VE.wait_ge(sP.h, pvb)
        VE.wait_ge(sP.h, pva)
        last = None
        for c in range(NCH):
            xin = get_in(c)
            t1 = t1s.ap(0, W)
            if gb is not None:
                tbcB = bcB.acquire(PE)
                pvb = sP.inc(PE.matmul(tbcB[:, 0:W], gb(c),
                                       scratch["lnrb"][0:2, 0:W],
                                       start=True, stop=True))
            vchain(VE.tensor_tensor(t1, xin, tbcA[:, 0:W], op=ALU.mult))
            dst, lgcol = put_out(c)
            if gb is None:
                last = vchain(VE.scalar_tensor_tensor(
                    dst, t1, 1.0, tbcB[:, 0:W], op0=ALU.mult, op1=ALU.add))
            else:
                VE.wait_ge(sP.h, pvb)
                last = vchain(VE.scalar_tensor_tensor(
                    dst, t1, lgcol, tbcB[:, 0:W], op0=ALU.mult, op1=ALU.add))
                bcB.consumed(sV, last)
        bcA.consumed(sV, last)
        bcB.consumed(sV, last)
        return last

    za = a2.alloc(4096, BF16)
    kfm = a2.alloc(4096, BF16)
    qd0 = a2.alloc(3072, BF16)
    ktm = a2.alloc(4096, BF16)
    vtm = a2.alloc(4096, BF16)
    obuf = a2.alloc(2688, BF16)
    wob = a2.alloc(1024, BF16)
    awk = a2.alloc(512, F32)
    xpA = a2.alloc(4096, F32)
    xpB = a2.alloc(4096, F32)

    class _QD:
        def ap(self, a, b):
            return qd0.ap(a - 1024, b - 1024)
    qd = _QD()
    xh = a3.alloc(5120, F32)
    wkb = a3.alloc(1024, BF16)
    wvb = a3.alloc(1024, BF16)
    wqb = a3.alloc(1024, BF16)
    dmk = a1.alloc(128, F32)
    stf = a1.alloc(128, F32)
    stb = a1.alloc(128, BF16)
    wts = [a1.alloc(128, BF16), a1.alloc(128, BF16)]
    ktds = [a1.alloc(128, BF16), a1.alloc(128, BF16)]
    csm = a1.alloc(16, F32)
    bo8 = a1.alloc(8, F32)
    xb_s = [a1.alloc(512, BF16), a1.alloc(512, BF16)]
    sq_s = [a1.alloc(512, BF16), a1.alloc(512, BF16)]
    t1_s = a1.alloc(512, F32)

    brow_k = p.sb("brow_k", [1, 128], BF16)
    brow_v = p.sb("brow_v", [1, 128], BF16)

    for (dst, src) in ((wkb, "wkT"), (wvb, "wvT"), (wqb, "wqT")):
        dma(GP, dst.ap().rearrange("p (c m) -> p c m", m=128),
            di[src].rearrange("(c p) m -> p c m", p=128), "G1")
    dma(GP, wob.ap(), di["woT"], "G1")
    dma(GP, brow_k[:], di["bkrow"], "G1")
    dma(GP, brow_v[:], di["bvrow"], "G1")
    dma(SY, awk.ap(), di["decq512"], "L1")
    dma(SY, dmk.ap(), di["dmaskT"], "L1")
    dma(SY, csm.ap(0, 1), di["bk"], "L1")
    dma(SY, csm.ap(1, 2), di["bq"], "L1")
    dma(SY, csm.ap(2, 3), di["deck"], "L1")
    dma(SY, bo8.ap().rearrange("p (c o) -> p c o", o=1),
        di["bo"].rearrange("(c p) o -> p c o", p=128),
        allow_slow_non_contiguous=True, stream="L1")
    bk_col = lambda: csm.ap(0, 1)
    bq_col = lambda: csm.ap(1, 2)
    deck_col = lambda: csm.ap(2, 3)

    def load_xpre(g, seg, slot):
        for c in range(NCH):
            dma(SY, seg.ap(512 * c, 512 * (c + 1)),
                di["x_pre"][128 * c:128 * (c + 1), 512 * g:512 * (g + 1)],
                f"Lx{slot}")
        return p.sem(f"Lx{slot}").v

    ln_scr = {"xb": xb_s, "sq": sq_s, "t1": t1_s}

    xp_ready = [None, None]
    xp_ready[0] = load_xpre(0, xpA, 0)
    xp_ready[1] = load_xpre(1, xpB, 1)

    for e in (VE, PE, ACT):
        wait_stream(e, "L1")
        wait_stream(e, "G1")

    for g in range(8):
        own = g >= 4
        if not own:
            seg = (xpA, xpB)[g % 2]
            sm = p.sem(f"Lx{g % 2}")
            VE.wait_ge(sm.h, xp_ready[g % 2])
            ACT.wait_ge(sm.h, xp_ready[g % 2])
            get_in = lambda c, seg=seg: seg.ap(512 * c, 512 * c + 512)
        else:
            go = 512 * (g - 4)
            get_in = lambda c, go=go: XO(c, go, go + 512)

        def put_out(c):
            return za.ap(512 * c, 512 * c + 512), None

        lnv = ln_group(get_in, put_out, 512, ln_scr,
                       (pp[0], pp[1], pp[2], pp[3]))

        if not own and g + 2 < 4:
            SY.wait_ge(sV.h, lnv)
            xp_ready[g % 2] = load_xpre(g + 2, (xpA, xpB)[g % 2], g % 2)

        PE.wait_ge(sV.h, lnv)
        gs = 512 * g
        t = pp[4].acquire(PE)
        pv = None
        for c in range(NCH):
            mm = PE.matmul(t[:, 0:512], wkb.ap(128 * c, 128 * (c + 1)),
                           za.ap(512 * c, 512 * (c + 1)),
                           start=(c == 0), stop=(c == NCH - 1))
            if c == NCH - 1:
                pv = sP.inc(mm)
        VE.wait_ge(sP.h, pv)
        vv = sV.inc(VE.tensor_scalar(kfm.ap(gs, gs + 512), t[:, 0:512],
                                     bk_col(), None, op0=ALU.add))
        pp[4].consumed(sV, vv)
        if g >= 2:
            t = pp[5].acquire(PE)
            pv = None
            for c in range(NCH):
                mm = PE.matmul(t[:, 0:512], wqb.ap(128 * c, 128 * (c + 1)),
                               za.ap(512 * c, 512 * (c + 1)),
                               start=(c == 0), stop=(c == NCH - 1))
                if c == NCH - 1:
                    pv = sP.inc(mm)
            VE.wait_ge(sP.h, pv)
            vchain(VE.tensor_scalar(t1_s.ap(), t[:, 0:512], bq_col(), None,
                                    op0=ALU.add))
            vv = sV.inc(VE.tensor_tensor(qd.ap(gs, gs + 512), t1_s.ap(),
                                         awk.ap(), op=ALU.mult))
            pp[5].consumed(sV, vv)
        for srowb, wrow, dstseg in ((brow_k, wkb, ktm), (brow_v, wvb, vtm)):
            for ts in range(4):
                t = pp[6 if dstseg is ktm else 7].acquire(PE)
                sub = gs + 128 * ts
                for c in range(NCH):
                    PE.matmul(t[:, 0:128],
                              za.ap(512 * c + 128 * ts, 512 * c + 128 * (ts + 1)),
                              wrow.ap(128 * c, 128 * (c + 1)),
                              start=(c == 0), stop=False)
                pv = sP.inc(PE.matmul(t[:, 0:128], ones1b[:], srowb[:],
                                      start=False, stop=True))
                VE.wait_ge(sP.h, pv)
                vv = sV.inc(VE.tensor_copy(dstseg.ap(sub, sub + 128), t[:, 0:128]))
                (pp[6] if dstseg is ktm else pp[7]).consumed(sV, vv)

    vchain(VE.memset(stf.ap(), 0.0))
    vv = vchain(VE.tensor_copy(stb.ap(), stf.ap()))
    st_ready = vv
    for cch in range(32):
        cs = 128 * cch
        do_out = cch >= 11
        if do_out:
            t_at = pp[0].acquire(PE)
            pv = sP.inc(PE.matmul(t_at[:, 0:128], kfm.ap(cs, cs + 128),
                                  qd.ap(cs, cs + 128), start=True, stop=True))
            wt = wts[cch % 2]
            VE.wait_ge(sP.h, pv)
            vv = sV.inc(VE.tensor_tensor(wt.ap(), t_at[:, 0:128], dmk.ap(),
                                         op=ALU.mult))
            pp[0].consumed(sV, vv)
            wt_v = vv
        ktd = ktds[cch % 2]
        vv = sV.inc(VE.tensor_scalar(ktd.ap(), ktm.ap(cs, cs + 128),
                                     deck_col(), None, op0=ALU.mult))
        ktd_v = vv
        if do_out:
            t_o = pp[1].acquire(PE)
            PE.wait_ge(sV.h, max(st_ready, wt_v))
            PE.matmul(t_o[:, 0:128], stb.ap(), qd.ap(cs, cs + 128),
                      start=True, stop=False)
            pv_o = sP.inc(PE.matmul(t_o[:, 0:128], vtm.ap(cs, cs + 128), wt.ap(),
                                    start=False, stop=True))
        t_u = pp[2].acquire(PE)
        PE.wait_ge(sV.h, ktd_v)
        pv_u = sP.inc(PE.matmul(t_u[:, 0:128], ktd.ap(), vtm.ap(cs, cs + 128),
                                start=True, stop=True))
        VE.wait_ge(sP.h, pv_u)
        vchain(VE.scalar_tensor_tensor(stf.ap(), stf.ap(), DEC_C, t_u[:, 0:128],
                                       op0=ALU.mult, op1=ALU.add))
        vv = vchain(VE.tensor_copy(stb.ap(), stf.ap()))
        pp[2].consumed(sV, vv)
        st_ready = vv
        if cch == 15:
            vchain(VE.tensor_scalar(stf.ap(), stf.ap(), flag[:],
                                    None, op0=ALU.mult))
            st_ready = vchain(VE.tensor_copy(stb.ap(), stf.ap()))
        if do_out:
            VE.wait_ge(sP.h, pv_o)
            vv = sV.inc(VE.tensor_copy(obuf.ap(cs - 1408, cs - 1408 + 128),
                                       t_o[:, 0:128]))
            pp[1].consumed(sV, vv)
    obuf_v = sV.v

    for c in range(NCH):
        dma(SY, xh.ap(640 * c, 640 * (c + 1)),
            di["x_halo"][128 * c:128 * (c + 1), :], "LH")
    PE.wait_ge(sV.h, obuf_v)
    wait_stream(VE, "LH")
    tiles = [(0, 512, "h"), (512, 640, "h"),
             (640, 1152, "o"), (1152, 1664, "o"), (1664, 2176, "o"), (2176, 2688, "o")]
    for (ta, tb_, kind) in tiles:
        W = tb_ - ta
        for c in range(NCH):
            t = pp[c % 4].acquire(PE)
            pv = sP.inc(PE.matmul(t[:, 0:W], wob.ap(128 * c, 128 * (c + 1)),
                                  obuf.ap(ta, tb_), start=True, stop=True))
            VE.wait_ge(sP.h, pv)
            if kind == "h":
                dst = xh.ap(640 * c + ta, 640 * c + tb_)
                vchain(VE.scalar_tensor_tensor(dst, t[:, 0:W], bo8.ap(c, c + 1),
                                               dst, op0=ALU.add, op1=ALU.add))
                vv = vchain(VE.tensor_scalar(dst, dst, flag[:], None,
                                             op0=ALU.mult))
            else:
                dst = XO(c, ta - 640, tb_ - 640)
                vv = sV.inc(VE.scalar_tensor_tensor(
                    dst, t[:, 0:W], bo8.ap(c, c + 1), dst,
                    op0=ALU.add, op1=ALU.add))
            pp[c % 4].consumed(sV, vv)
    x1_v = sV.v

    for nm_, seg_, a_, w_ in (("za", za, 0, 4096), ("kfm", kfm, 0, 4096), ("qd", qd, 1024, 4096),
                              ("ktm", ktm, 0, 4096), ("vtm", vtm, 0, 4096),
                              ("obuf", obuf, 0, 2688)):
        if dbg(nm_, (128, w_ - a_), BF16) is not None:
            GP.wait_ge(sV.h, obuf_v)
            dma(GP, p.dbg_outs[nm_], seg_.ap(a_, w_), "GD")
    if dbg("x1", (D, TOWN)) is not None:
        GP.wait_ge(sV.h, x1_v)
        for c in range(NCH):
            dma(GP, p.dbg_outs["x1"][128 * c:128 * (c + 1), :],
                XO(c, 0, TOWN), "GD")
    if dbg("x1h", (D, HALO)) is not None:
        GP.wait_ge(sV.h, x1_v)
        for c in range(NCH):
            dma(GP, p.dbg_outs["x1h"][128 * c:128 * (c + 1), :],
                xh.ap(640 * c, 640 * (c + 1)), "GD")

    if upto == "x1":
        finish(p, out_x4, XO, x1_v)
        return p

    p.barrier("ub")
    a2.reset()
    UBW = TEXT + 1
    ubt = a2.alloc(NCH * UBW, F32)

    def UB(c, a, b):
        return ubt.ap(UBW * c + 1 + a, UBW * c + 1 + b)

    for c in range(NCH):
        VE.memset(ubt.ap(UBW * c, UBW * c + 1), 0.0)

    ln_tiles = [(0, 512), (512, 640), (640, 1152), (1152, 1664),
                (1664, 2176), (2176, 2688)]
    z0_v = 0
    for (ta, tb_) in ln_tiles:
        W = tb_ - ta

        def get_in(c, ta=ta, tb_=tb_):
            if tb_ <= 640:
                return xh.ap(640 * c + ta, 640 * c + tb_)
            return XO(c, ta - 640, tb_ - 640)

        def put_out(c, ta=ta, tb_=tb_):
            return UB(c, ta, tb_), None

        z0_v = ln_group(get_in, put_out, W, ln_scr,
                        (pp[4], pp[5], pp[6], pp[7]), ready_v=x1_v)

    if dbg("z0", (D, TEXT)) is not None:
        GP.wait_ge(sV.h, z0_v)
        for c in range(NCH):
            dma(GP, p.dbg_outs["z0"][128 * c:128 * (c + 1), :],
                UB(c, 0, TEXT), "GD")

    if upto == "z0":
        finish(p, out_x4, XO, x1_v)
        return p

    GP.wait_ge(sV.h, x1_v)
    for c in range(NCH):
        dma(GP, x_spill[128 * c:128 * (c + 1), :], XO(c, 0, TOWN), "GS")

    G3 = [(0, 512), (512, 1024), (1024, 1536), (1536, 2048),
          (2048, 2560), (2560, 2688)]
    G4 = [(512, 1024), (1024, 1536), (1536, 2048), (2048, 2560), (2560, 2688)]

    sc.reset()
    sqscr = sc.alloc(512, BF16)
    gelscr = sc.alloc(512, BF16)
    fsm = sc.alloc(64, F32)
    b1c = [sc.alloc(44, F32), sc.alloc(44, F32)]

    qs_col = lambda l: fsm.ap(2 * l, 2 * l + 1)
    ks_col = lambda l: fsm.ap(2 * l + 1, 2 * l + 2)
    b2_col = lambda l: fsm.ap(4 + l, 5 + l)

    for l in range(2):
        dma(SY, fsm.ap(2 * l, 2 * l + 1), di[f"qs{l}"], "L1")
        dma(SY, fsm.ap(2 * l + 1, 2 * l + 2), di[f"ks{l}"], "L1")
        dma(SY, fsm.ap(4 + l, 5 + l), di[f"b2_{l}"][0:128, :], "L1")
    b2c8 = [sc.alloc(8, F32), sc.alloc(8, F32)]
    lgc8 = [sc.alloc(8, F32), sc.alloc(8, F32)]
    assert sc.off <= sc.cols
    for l in range(2):
        dma(SY, b2c8[l].ap().rearrange("p (c o) -> p c o", o=1),
            di[f"b2_{l}"].rearrange("(c p) o -> p c o", p=128),
            allow_slow_non_contiguous=True, stream="L1")
        dma(SY, lgc8[l].ap().rearrange("p (c o) -> p c o", o=1),
            di[f"lng{l}"].rearrange("(c p) o -> p c o", p=128),
            allow_slow_non_contiguous=True, stream="L1")
        dma(SY, b1c[l].ap().rearrange("p (c o) -> p c o", o=1),
            di[f"b1p{l}"].rearrange("(c p) o -> p c o", p=128),
            allow_slow_non_contiguous=True, stream="L1")
    e2b0 = p.sb("e2b0", [1, 128], F32)
    e2b1 = p.sb("e2b1", [1, 128], F32)
    dma(SY, e2b0[:], di["e2"][0:1, :], "L1")
    dma(SY, e2b1[:], di["e2"][1:2, :], "L1")
    e2f = p.sb("e2f", [2, 128], F32)
    dma(SY, e2f[:], di["e2"], "L1")
    oc2b = p.sb("oc2b", [128, 2], BF16)
    dma(GP, oc2b[:], di["ones2col"], "G1")
    rpb = p.sb("rpb", [128, 128], BF16)
    dma(GP, rpb[:], di["Rperm"], "G1")
    khm0b = p.sb("khm0b", [128, 640], BF16)
    dma(GP, khm0b[:], di["khmask0"], "G1")

    for e in (GP, SY, PE, VE, ACT):
        wait_stream(e, "GS")
        wait_stream(e, "G1")
        wait_stream(e, "L1")
    p.barrier("ffn_setup")
    a1.reset()
    zcast = a1.alloc(4096, BF16)
    kbuf = a1.alloc(4 * TEXT, BF16)
    qbuf = a1.alloc(4 * (TEXT - 512), BF16)
    vtbuf = a1.alloc(21 * 520, BF16)
    e0buf = a1.alloc(512, F32)
    ptbuf = a1.alloc(1024, BF16)
    lrbuf = a1.alloc(512, BF16)
    obq = a1.alloc(4 * 512, BF16)
    A1_ATTN = a1.off
    a1.reset()
    abuf = a1.alloc(22 * 512, BF16)
    w2blk = a1.alloc(22 * 1024, BF16)
    a1.off = max(a1.off, A1_ATTN)
    LN2_BASE = 16896
    assert LN2_BASE >= a1.off - 2648
    xb_s2 = [Seg(a1.t, LN2_BASE + 0, 512, BF16),
             Seg(a1.t, LN2_BASE + 256, 512, BF16)]
    sq_s2 = [Seg(a1.t, LN2_BASE + 512, 512, BF16),
             Seg(a1.t, LN2_BASE + 768, 512, BF16)]
    t1_s2 = Seg(a1.t, LN2_BASE + 1024, 512, F32)
    ln_scr2 = {"xb": xb_s2, "sq": sq_s2, "t1": t1_s2}

    def KB(hp, a, b):
        return kbuf.ap(TEXT * hp + a, TEXT * hp + b)

    def QB(hp, a, b):
        return qbuf.ap((TEXT - 512) * hp + a - 512, (TEXT - 512) * hp + b - 512)

    def VT(t):
        return vtbuf.ap(520 * t, 520 * (t + 1))

    a3.reset()
    wqkvb = a3.alloc(8 * 1536, BF16)
    rots = a3.alloc(2 * 512, BF16)
    assert a3.off <= a3.cols
    fwob = Seg(a3.t, 0, 4 * 1024, BF16)
    z2roll = [Seg(a3.t, 0, 8 * 512, BF16), Seg(a3.t, 2048, 8 * 512, BF16)]
    w1blk = [Seg(a3.t, 4096, 8 * 128, BF16), Seg(a3.t, 4608, 8 * 128, BF16)]
    k1scr = Seg(e0buf.t, e0buf.off + 0, 512, BF16)
    k2scr = Seg(e0buf.t, e0buf.off + 256, 512, BF16)
    tascr = Seg(ptbuf.t, ptbuf.off + 0, 512, BF16)
    tbscr = Seg(ptbuf.t, ptbuf.off + 256, 512, BF16)

    lnrb = p.sb("lnrb", [2, 512], BF16)
    dma(GP, lnrb[1:2, :], di["ones512"], "G1")
    ln_scr2["lnrb"] = lnrb
    for e_ in (PE, VE, ACT, SY, GP):
        wait_stream(e_, "G1")
    lngbs = p.sb("lngbs", [2, D], BF16)

    G3 = [(0, 512), (512, 1024), (1024, 1536), (1536, 2048),
          (2048, 2560), (2560, 2688)]
    G4 = [(512, 1024), (1024, 1536), (1536, 2048), (2048, 2560), (2560, 2688)]

    def emit_ffn_layer(lidx, stage_tag):
        tag = f"{stage_tag}l{lidx}"
        dma(GP, wqkvb.ap().rearrange("p (c m) -> p c m", m=1536),
            di[f"wqkv{lidx}"].rearrange("(c p) m -> p c m", p=128),
            f"Lw_{tag}")
        dma(GP, lngbs[:], di[f"lngb{lidx}"], f"Lw_{tag}")
        wait_stream(PE, f"Lw_{tag}")

        rot_used = 0
        zc_pe = 0
        for g, (ta, tb_) in enumerate(G3):
            W = tb_ - ta
            st_r = f"Lr_{tag}"
            if rot_used:
                GP.wait_ge(sV.h, rot_used)
            wait_stream(GP, st_r)
            dma(GP, rots.ap(0, W), di["rotAk"][:, ta:tb_], st_r)
            dma(GP, rots.ap(512, 512 + W), di["rotBk"][:, ta:tb_], st_r)
            VE.wait_ge(sP.h, zc_pe)
            zcv = 0
            for c in range(NCH):
                off = 1 if c >= 4 else 0
                zcv = sV.inc(VE.tensor_copy(zcast.ap(512 * c, 512 * c + W),
                                            UB(c, ta - off, tb_ - off)))
            PE.wait_ge(sV.h, zcv)

            for isq in (0, 1):
                if isq:
                    GP.wait_ge(sV.h, sV.v)
                    wait_stream(GP, st_r)
                    dma(GP, rots.ap(0, W), di["rotAq"][:, ta:tb_], st_r)
                    dma(GP, rots.ap(512, 512 + W), di["rotBq"][:, ta:tb_], st_r)
                    if g == 0:
                        continue
                for hp in range(4):
                    bank = pp[hp % 4]
                    t = bank.acquire(PE)
                    cols = (0 if isq else 512) + 128 * hp
                    pv = None
                    for c in range(NCH):
                        mm = PE.matmul(t[:, 0:W],
                                       wqkvb.ap(1536 * c + cols,
                                                1536 * c + cols + 128),
                                       zcast.ap(512 * c, 512 * c + W),
                                       start=(c == 0), stop=(c == NCH - 1))
                        if c == NCH - 1:
                            pv = sP.inc(mm)
                    ACT.wait_ge(sP.h, pv)
                    aa = sA.inc(ACT.activation(sqscr.ap(0, W), t[:, 0:W],
                                               AF.Square))
                    PE.wait_ge(sA.h, aa)
                    tn = pp[4].acquire(PE)
                    pn = sP.inc(PE.matmul(tn[0:2, 0:W], oc2b[:], sqscr.ap(0, W),
                                          start=True, stop=True))
                    VE.wait_ge(sP.h, pn)
                    nv = vchain(VE.tensor_scalar(srow[0:2, 0:W], tn[0:2, 0:W],
                                                 1e-24, None, op0=ALU.max))
                    pp[4].consumed(sV, nv)
                    ACT.wait_ge(sV.h, nv)
                    av = sA.inc(ACT.activation(srow[0:2, 512:512 + W],
                                               srow[0:2, 0:W], AF.Ln))
                    ACT.wait_ge(sA.h, av)
                    av = sA.inc(ACT.activation(srow[0:2, 1024:1024 + W],
                                               srow[0:2, 512:512 + W],
                                               AF.Exp, scale=-0.5))
                    PE.wait_ge(sA.h, av)
                    tb2 = pp[5].acquire(PE)
                    pn2 = sP.inc(PE.matmul(tb2[:, 0:W], e2f[:],
                                           srow[0:2, 1024:1024 + W],
                                           start=True, stop=True))
                    VE.wait_ge(sP.h, pn2)
                    scol = qs_col(lidx) if isq else ks_col(lidx)
                    vchain(VE.tensor_scalar(k1scr.ap(0, W), t[:, 0:W], scol,
                                            None, op0=ALU.mult))
                    bank.consumed(sV, sV.v)
                    k2v = vchain(VE.tensor_tensor(k2scr.ap(0, W),
                                                  k1scr.ap(0, W),
                                                  tb2[:, 0:W], op=ALU.mult))
                    pp[5].consumed(sV, k2v)
                    PE.wait_ge(sV.h, k2v)
                    tr = pp[6].acquire(PE)
                    pr = sP.inc(PE.matmul(tr[:, 0:W], rpb[:], k2scr.ap(0, W),
                                          start=True, stop=True))
                    ra = rots.ap(0, W)
                    rb = rots.ap(512, 512 + W)
                    wait_stream(VE, st_r)
                    vchain(VE.tensor_tensor(tascr.ap(0, W), k2scr.ap(0, W), ra,
                                            op=ALU.mult))
                    VE.wait_ge(sP.h, pr)
                    vchain(VE.tensor_tensor(tbscr.ap(0, W), tr[:, 0:W], rb,
                                            op=ALU.mult))
                    pp[6].consumed(sV, sV.v)
                    dst = QB(hp, ta, tb_) if isq else KB(hp, ta, tb_)
                    vchain(VE.tensor_tensor(dst, tascr.ap(0, W),
                                            tbscr.ap(0, W), op=ALU.add))
            rot_used = sV.v
            for ts_ in range(W // 128):
                tvi = (ta + 128 * ts_) // 128
                t = pp[7].acquire(PE)
                pv = None
                for c in range(NCH):
                    mm = PE.matmul(t[:, 0:512],
                                   zcast.ap(512 * c + 128 * ts_,
                                            512 * c + 128 * (ts_ + 1)),
                                   wqkvb.ap(1536 * c + 1024, 1536 * c + 1536),
                                   start=(c == 0), stop=(c == NCH - 1))
                    if c == NCH - 1:
                        pv = sP.inc(mm)
                VE.wait_ge(sP.h, pv)
                vv = sV.inc(VE.tensor_copy(
                    VT(tvi).rearrange("p (h j) -> p h j", j=65)[:, :, 0:64],
                    t[:, 0:512].rearrange("p (h j) -> p h j", j=64)))
                pp[7].consumed(sV, vv)
            zc_pe = sP.v
        vt1 = vchain(VE.memset(
            vtbuf.ap().rearrange("p (t h j) -> p t h j", h=8, j=65)[:, :, :, 64:65],
            1.0))

        GP.wait_ge(sP.h, sP.v)
        dma(GP, fwob.ap().rearrange("p (c m) -> p c m", m=1024),
            di[f"fwo{lidx}"].rearrange("(c p) m -> p c m", p=128),
            f"Lwo_{tag}")
        wait_stream(PE, f"Lwo_{tag}")

        PE.wait_ge(sV.h, vt1)
        pt_cons = [0, 0]
        e0_cons = 0
        for qg, (qa, qb_) in enumerate(G4):
            W = qb_ - qa
            nk = (W + 512) // 128
            for hp in range(4):
                toA = pp[3].acquire(PE)
                toB = pp[4].acquire(PE)
                avA = avB = 0
                for kk in range(nk):
                    ka = qa - 512 + 128 * kk
                    banks = (pp[(2 * kk) % 3], pp[(2 * kk + 1) % 3])
                    for hh in range(2):
                        bank = banks[hh]
                        tdst = bank.acquire(PE)
                        PE.matmul(tdst[:, 0:W], identb[:], fmask(kk)[:, 0:W],
                                  start=True, stop=False)
                        if (qg == 0 and kk <= 3) or (qg == 1 and kk == 0):
                            PE.matmul(tdst[:, 0:W], identb[:],
                                      khm0b[:, 128:128 + W],
                                      start=False, stop=False)
                        if qg == 0 and kk == 4:
                            PE.matmul(tdst[:, 0:W], identb[:], khm0b[:, 0:W],
                                      start=False, stop=False)
                        pv = sP.inc(PE.matmul(
                            tdst[:, 0:W],
                            KB(hp, ka, ka + 128)[64 * hh:64 * (hh + 1), :],
                            QB(hp, qa, qb_)[64 * hh:64 * (hh + 1), :],
                            start=False, stop=True))
                        ACT.wait_ge(sP.h, pv)
                        ACT.wait_ge(sV.h, e0_cons)
                        aa = sA.inc(ACT.activation(e0buf.ap(0, W),
                                                   tdst[:, 0:W],
                                                   AF.Exp, scale=0.25))
                        ACT.wait_ge(sA.h, aa)
                        aa = sA.inc(ACT.activation(e0buf.ap(0, W),
                                                   e0buf.ap(0, W), AF.Square))
                        bank.consumed(sA, aa)
                        VE.wait_ge(sA.h, aa)
                        VE.wait_ge(sP.h, pt_cons[hh])
                        ptv = sV.inc(VE.tensor_tensor(
                            ptbuf.ap(512 * hh, 512 * hh + W),
                            e0buf.ap(0, W), e0buf.ap(0, W), op=ALU.mult))
                        e0_cons = ptv
                        PE.wait_ge(sV.h, ptv)
                        to = toA if hh == 0 else toB
                        h = 2 * hp + hh
                        av = sP.inc(PE.matmul(
                            to[0:65, 0:W],
                            VT((qa - 512 + 128 * kk) // 128)[:, 65 * h:65 * (h + 1)],
                            ptbuf.ap(512 * hh, 512 * hh + W),
                            start=(kk == 0), stop=(kk == nk - 1)))
                        pt_cons[hh] = av
                        if hh == 0:
                            avA = av
                        else:
                            avB = av
                VE.wait_ge(sP.h, max(avA, avB))
                vchain(VE.tensor_copy(srow[0:1, 0:512][:, 0:W], toA[64:65, 0:W]))
                lv = vchain(VE.tensor_copy(srow[0:1, 512:512 + W],
                                           toB[64:65, 0:W]))
                ACT.wait_ge(sV.h, lv)
                for seg0 in (0, 512):
                    av = sA.inc(ACT.activation(
                        srow[0:1, 1024 + seg0:1024 + seg0 + W],
                        srow[0:1, seg0:seg0 + W], AF.Ln))
                    ACT.wait_ge(sA.h, av)
                    av = sA.inc(ACT.activation(
                        srow[0:1, 1024 + seg0:1024 + seg0 + W],
                        srow[0:1, 1024 + seg0:1024 + seg0 + W],
                        AF.Exp, scale=-1.0))
                PE.wait_ge(sA.h, av)
                tl = pp[5].acquire(PE)
                PE.matmul(tl[:, 0:W], e2b0[:], srow[0:1, 1024:1024 + W],
                          start=True, stop=False)
                pl = sP.inc(PE.matmul(tl[:, 0:W], e2b1[:],
                                      srow[0:1, 1536:1536 + W],
                                      start=False, stop=True))
                VE.wait_ge(sP.h, pl)
                lrv = vchain(VE.tensor_copy(lrbuf.ap(0, W), tl[:, 0:W]))
                pp[5].consumed(sV, lrv)
                v1 = vchain(VE.tensor_tensor(
                    obq.ap(512 * hp, 512 * hp + W, p0=0, p1=64),
                    toA[0:64, 0:W], lrbuf.ap(0, W, p0=0, p1=64), op=ALU.mult))
                v2 = vchain(VE.tensor_tensor(
                    obq.ap(512 * hp, 512 * hp + W, p0=64, p1=128),
                    toB[0:64, 0:W], lrbuf.ap(0, W, p0=64, p1=128),
                    op=ALU.mult))
                pp[3].consumed(sV, v1)
                pp[4].consumed(sV, v2)
            ob_v = sV.v
            PE.wait_ge(sV.h, ob_v)
            for dc in range(NCH):
                t = pp[6 + (dc % 2)].acquire(PE)
                pv = None
                for hp in range(4):
                    mm = PE.matmul(t[:, 0:W],
                                   fwob.ap(1024 * hp + 128 * dc,
                                           1024 * hp + 128 * dc + 128),
                                   obq.ap(512 * hp, 512 * hp + W),
                                   start=(hp == 0), stop=(hp == 3))
                    if hp == 3:
                        pv = sP.inc(mm)
                VE.wait_ge(sP.h, pv)
                dst = UB(dc, qa, qb_)
                vv = vchain(VE.tensor_tensor(dst, t[:, 0:W], dst, op=ALU.add))
                if qg == 0:
                    vv = vchain(VE.tensor_scalar(UB(dc, 512, 640),
                                                 UB(dc, 512, 640), flag[:],
                                                 None, op0=ALU.mult))
                pp[6 + (dc % 2)].consumed(sV, vv)

        p.barrier(f"mlp_{tag}")
        dma(GP, w2blk.ap().rearrange("p (c m) -> p c m", m=1024),
            di[f"w2p{lidx}"].rearrange("(c p) m -> p c m", p=128),
            f"Lw2_{tag}")

        def ln2(g, slot):
            ta = 640 + 512 * g

            def get_in(c):
                off = 1 if c >= 4 else 0
                return UB(c, ta - off, ta + 512 - off)

            def put_out(c):
                return (z2roll[slot].ap(512 * c, 512 * (c + 1)),
                        lgc8[lidx].ap(c, c + 1))

            def lgrow(c):
                return lngbs[0:2, 128 * c:128 * (c + 1)]

            return ln_group(get_in, put_out, 512, ln_scr2,
                            (pp[0], pp[1], pp[6], pp[7]), gb=lgrow,
                            ready_v=sV.v)

        wblk_used = [0, 0]
        gel_cons = [0]

        def mlp(g, slot):
            ta = 640 + 512 * g
            z2 = z2roll[slot]
            for ob in range(44):
                st_w = f"Lw1{ob % 2}_{tag}"
                if wblk_used[ob % 2]:
                    GP.wait_ge(sP.h, wblk_used[ob % 2])
                dma(GP, w1blk[ob % 2].ap().rearrange("p (c o) -> p c o", o=128),
                    di[f"w1p{lidx}"].rearrange("(c p) o -> p c o", p=128)
                    [:, :, 128 * ob:128 * (ob + 1)], st_w)
                wait_stream(PE, st_w)
                t = pp[2 + (ob % 2)].acquire(PE)
                pv = None
                for c in range(NCH):
                    mm = PE.matmul(t[:, 0:512],
                                   w1blk[ob % 2].ap(128 * c, 128 * (c + 1)),
                                   z2.ap(512 * c, 512 * (c + 1)),
                                   start=(c == 0), stop=(c == NCH - 1))
                    if c == NCH - 1:
                        pv = sP.inc(mm)
                wblk_used[ob % 2] = pv
                if ob < 22:
                    VE.wait_ge(sP.h, pv)
                    vv = sV.inc(VE.tensor_scalar(
                        abuf.ap(512 * ob, 512 * (ob + 1)), t[:, 0:512],
                        b1c[lidx].ap(ob, ob + 1), None, op0=ALU.add))
                    pp[2 + (ob % 2)].consumed(sV, vv)
                else:
                    hb = ob - 22
                    ACT.wait_ge(sP.h, pv)
                    ACT.wait_ge(sV.h, gel_cons[0])
                    aa = sA.inc(ACT.activation(gelscr.ap(0, 512), t[:, 0:512],
                                               AF.Gelu,
                                               bias=b1c[lidx].ap(ob, ob + 1)))
                    pp[2 + (ob % 2)].consumed(sA, aa)
                    VE.wait_ge(sA.h, aa)
                    gv = vchain(VE.tensor_tensor(
                        abuf.ap(512 * hb, 512 * (hb + 1)),
                        abuf.ap(512 * hb, 512 * (hb + 1)),
                        gelscr.ap(0, 512), op=ALU.mult))
                    gel_cons[0] = gv
            h_v = sV.v
            PE.wait_ge(sV.h, h_v)
            wait_stream(PE, f"Lw2_{tag}")
            for half in range(2):
                ts2 = [None] * 4
                pvs = [None] * 4
                for hc in range(22):
                    for di_ in range(4):
                        dc = 4 * half + di_
                        if hc == 0:
                            ts2[di_] = pp[2 + di_].acquire(PE)
                        mm = PE.matmul(ts2[di_][:, 0:512],
                                       w2blk.ap(1024 * hc + 128 * dc,
                                                1024 * hc + 128 * dc + 128),
                                       abuf.ap(512 * hc, 512 * (hc + 1)),
                                       start=(hc == 0), stop=(hc == 21))
                        if hc == 21:
                            pvs[di_] = sP.inc(mm)
                for di_ in range(4):
                    dc = 4 * half + di_
                    VE.wait_ge(sP.h, pvs[di_])
                    dst = UB(dc, ta, ta + 512)
                    vv = vchain(VE.scalar_tensor_tensor(
                        dst, ts2[di_][:, 0:512], b2c8[lidx].ap(dc, dc + 1),
                        dst, op0=ALU.add, op1=ALU.add))
                    pp[2 + di_].consumed(sV, vv)

        ln2(0, 0)
        ln2(1, 1)
        mlp(0, 0)
        ln2(2, 0)
        mlp(1, 1)
        ln2(3, 1)
        mlp(2, 0)
        mlp(3, 1)
        p.barrier(f"eol_{tag}")
        return sV.v

    def ag_exchange(ag_in, ag_out, tag):
        GP.wait_ge(sV.h, sV.v)
        for c in range(NCH):
            dma(GP, ag_in[128 * c:128 * (c + 1), :], UB(c, 2048, TEXT),
                f"GA_{tag}")
        wait_stream(GP, f"GA_{tag}")
        sc_ = p.sem(f"CC_{tag}")
        sc_.inc(GP.collective_compute(
            "AllGather", ALU.bypass, replica_groups=RG,
            ins=[ag_in.opt()], outs=[ag_out.opt()]), 1)
        GP.wait_ge(sc_.h, sc_.v)
        for c in range(NCH):
            dma(GP, UB(c, 0, HALO), ag_out[128 * c:128 * (c + 1), :],
                f"GH_{tag}")
        wait_stream(VE, f"GH_{tag}")
        hv = 0
        for c in range(NCH):
            hv = vchain(VE.tensor_scalar(UB(c, 0, HALO), UB(c, 0, HALO),
                                         flag[:], None, op0=ALU.mult))
        return hv

    def emit_ffn_stage(stage_tag, agi, ago):
        emit_ffn_layer(0, stage_tag)
        if stage_tag == "s2" and dbg("u1", (D, TOWN)) is not None:
            GP.wait_ge(sV.h, sV.v)
            for c in range(NCH):
                dma(GP, p.dbg_outs["u1"][128 * c:128 * (c + 1), :],
                    UB(c, HALO, TEXT), "GD")
            for e_ in (VE, GP, PE, ACT):
                wait_stream(e_, "GD")
        hv = ag_exchange(agi, ago, stage_tag)
        PE.wait_ge(sV.h, hv)
        ACT.wait_ge(sV.h, hv)
        emit_ffn_layer(1, stage_tag)

    emit_ffn_stage("s2", ag1_in, ag1_out)

    p.barrier("x2")
    a1.reset()
    xo2 = a1.alloc(16384, F32)

    def XO2(c, a, b):
        return xo2.ap(TOWN * c + a, TOWN * c + b)

    wait_stream(SY, "GS")
    for c in range(NCH):
        dma(SY, XO2(c, 0, TOWN), x_spill[128 * c:128 * (c + 1), :], "LS2")
    wait_stream(VE, "LS2")
    x2_v = 0
    for c in range(NCH):
        dst = xo2.ap(TOWN * c, TOWN * (c + 1))
        x2_v = vchain(VE.tensor_tensor(dst, dst, UB(c, HALO, TEXT),
                                       op=ALU.add))
    if dbg("x2", (D, TOWN)) is not None:
        GP.wait_ge(sV.h, x2_v)
        for c in range(NCH):
            dma(GP, p.dbg_outs["x2"][128 * c:128 * (c + 1), :],
                XO2(c, 0, TOWN), "GD")
        for e_ in (VE, GP, PE, ACT):
            wait_stream(e_, "GD")
    if upto == "x2":
        finish(p, out_x4, XO2, x2_v)
        return p

    a2.reset()
    kbb = a2.alloc(4 * S, BF16)
    vtb = a2.alloc(32 * 520, BF16)
    zrd = a2.alloc(8 * 512, BF16)
    qwn = a2.alloc(4 * 512, BF16)
    ost = a2.alloc(512, BF16)
    zown = zrd

    def KBB(hp, a, b):
        return kbb.ap(S * hp + a, S * hp + b)

    def VTB(t):
        return vtb.ap(520 * t, 520 * (t + 1))

    a3.reset()
    pib = [Seg(a3.t, 2048 * i, 8 * 512, BF16) for i in range(3)]
    pos = Seg(a3.t, 5120, 2 * 1024, BF16)

    for i, nm in enumerate(("PiQ", "PiK", "PiV")):
        dma(GP, pib[i].ap().rearrange("p (c m) -> p c m", m=512),
            di[nm].rearrange("(c p) m -> p c m", p=128), "LPi")

    for g in range(4):
        wait_stream(VE, "GZ")

        def get_in(c, g=g):
            return XO2(c, 512 * g, 512 * (g + 1))

        def put_out(c):
            return zown.ap(512 * c, 512 * (c + 1)), None

        zv = ln_group(get_in, put_out, 512, ln_scr2,
                      (pp[0], pp[1], pp[6], pp[7]), ready_v=x2_v)
        SY.wait_ge(sV.h, zv)
        for c in range(NCH):
            dma(SY, ag2z_in[128 * c:128 * (c + 1), 512 * g:512 * (g + 1)],
                zown.ap(512 * c, 512 * (c + 1)), "GZ")
    GP.wait_ge(sV.h, x2_v)
    for c in range(NCH):
        dma(GP, ag2x_in[128 * c:128 * (c + 1), :],
            XO2(c, TOWN - HALO, TOWN), "GX")
    wait_stream(GP, "GZ")
    wait_stream(GP, "GX")
    ccz = p.sem("CCz")
    ccz.inc(GP.collective_compute("AllGather", ALU.bypass, replica_groups=RG,
                                  ins=[ag2z_in.opt()], outs=[ag2z_out.opt()]), 1)
    ccz.inc(GP.collective_compute("AllGather", ALU.bypass, replica_groups=RG,
                                  ins=[ag2x_in.opt()], outs=[ag2x_out.opt()]), 1)
    GP.wait_ge(ccz.h, ccz.v)
    wait_stream(PE, "LPi")

    def zread_group(g, stream):
        sl = g // 4
        co = 512 * (g % 4)
        for c in range(NCH):
            dma(SY, zrd.ap(512 * c, 512 * (c + 1)),
                ag2z_out[1024 * sl + 128 * c:1024 * sl + 128 * (c + 1),
                         co:co + 512], stream)

    zr_pe = [0]
    for g in range(8):
        SY.wait_ge(sP.h, zr_pe[0])
        SY.wait_ge(ccz.h, 1)
        zread_group(g, "LZ")
        wait_stream(PE, "LZ")
        for hp in range(4):
            t = pp[hp % 2].acquire(PE)
            pv = None
            for c in range(NCH):
                mm = PE.matmul(t[:, 0:512],
                               pib[1].ap(512 * c + 128 * hp, 512 * c + 128 * (hp + 1)),
                               zrd.ap(512 * c, 512 * (c + 1)),
                               start=(c == 0), stop=(c == NCH - 1))
                if c == NCH - 1:
                    pv = sP.inc(mm)
            VE.wait_ge(sP.h, pv)
            vv = sV.inc(VE.tensor_copy(KBB(hp, 512 * g, 512 * (g + 1)),
                                       t[:, 0:512]))
            pp[hp % 2].consumed(sV, vv)
        for ts_ in range(4):
            tvi = 4 * g + ts_
            t = pp[2 + ts_ % 2].acquire(PE)
            pv = None
            for c in range(NCH):
                mm = PE.matmul(t[:, 0:512],
                               zrd.ap(512 * c + 128 * ts_, 512 * c + 128 * (ts_ + 1)),
                               pib[2].ap(512 * c, 512 * (c + 1)),
                               start=(c == 0), stop=(c == NCH - 1))
                if c == NCH - 1:
                    pv = sP.inc(mm)
            VE.wait_ge(sP.h, pv)
            vv = sV.inc(VE.tensor_copy(
                VTB(tvi).rearrange("p (h j) -> p h j", j=65)[:, :, 0:64],
                t[:, 0:512].rearrange("p (h j) -> p h j", j=64)))
            pp[2 + ts_ % 2].consumed(sV, vv)
        zr_pe[0] = sP.v
    vt1 = vchain(VE.memset(
        vtb.ap().rearrange("p (t h j) -> p t h j", h=8, j=65)[:, :, :, 64:65],
        1.0))
    PE.wait_ge(sV.h, vt1)

    pt_cons = [0, 0]
    e0_cons = [0]
    qw_pe = [0]
    wait_stream(SY, "LZ")
    SY.wait_ge(sP.h, zr_pe[0])
    for qw in range(8):
        SY.wait_ge(sP.h, qw_pe[0])
        zread_group(qw, "LZq")
        wait_stream(PE, "LZq")
        for hp in range(4):
            t = pp[5].acquire(PE)
            pv = None
            for c in range(NCH):
                mm = PE.matmul(t[:, 0:512],
                               pib[0].ap(512 * c + 128 * hp, 512 * c + 128 * (hp + 1)),
                               zrd.ap(512 * c, 512 * (c + 1)),
                               start=(c == 0), stop=(c == NCH - 1))
                if c == NCH - 1:
                    pv = sP.inc(mm)
            VE.wait_ge(sP.h, pv)
            vv = sV.inc(VE.tensor_copy(qwn.ap(512 * hp, 512 * (hp + 1)),
                                       t[:, 0:512]))
            pp[5].consumed(sV, vv)
        qw_pe[0] = sP.v
        PE.wait_ge(sV.h, sV.v)
        nk = 4 * (qw + 1)
        for hp in range(4):
            toA = pp[3].acquire(PE)
            toB = pp[4].acquire(PE)
            avA = avB = 0
            for kk in range(nk):
                ka = 128 * kk
                banks = (pp[(2 * kk) % 3], pp[(2 * kk + 1) % 3])
                for hh in range(2):
                    bank = banks[hh]
                    tdst = bank.acquire(PE)
                    first = True
                    if kk >= nk - 4:
                        PE.matmul(tdst[:, 0:512], identb[:],
                                  bmask(kk - (nk - 4))[:, 0:512],
                                  start=True, stop=False)
                        first = False
                    pv = sP.inc(PE.matmul(
                        tdst[:, 0:512],
                        KBB(hp, ka, ka + 128)[64 * hh:64 * (hh + 1), :],
                        qwn.ap(512 * hp, 512 * (hp + 1))[64 * hh:64 * (hh + 1), :],
                        start=first, stop=True))
                    ACT.wait_ge(sP.h, pv)
                    ACT.wait_ge(sV.h, e0_cons[0])
                    aa = sA.inc(ACT.activation(e0buf.ap(0, 512), tdst[:, 0:512],
                                               AF.Exp, scale=1.0 / 32.0))
                    ACT.wait_ge(sA.h, aa)
                    aa = sA.inc(ACT.activation(e0buf.ap(0, 512), e0buf.ap(0, 512),
                                               AF.Square))
                    bank.consumed(sA, aa)
                    VE.wait_ge(sA.h, aa)
                    VE.wait_ge(sP.h, pt_cons[hh])
                    ptv = sV.inc(VE.tensor_tensor(
                        ptbuf.ap(512 * hh, 512 * hh + 512),
                        e0buf.ap(0, 512), e0buf.ap(0, 512), op=ALU.mult))
                    e0_cons[0] = ptv
                    PE.wait_ge(sV.h, ptv)
                    to = toA if hh == 0 else toB
                    h = 2 * hp + hh
                    av = sP.inc(PE.matmul(
                        to[0:65, 0:512],
                        VTB(kk)[:, 65 * h:65 * (h + 1)],
                        ptbuf.ap(512 * hh, 512 * hh + 512),
                        start=(kk == 0), stop=(kk == nk - 1)))
                    pt_cons[hh] = av
                    if hh == 0:
                        avA = av
                    else:
                        avB = av
            VE.wait_ge(sP.h, max(avA, avB))
            vchain(VE.tensor_copy(srow[0:1, 0:512], toA[64:65, 0:512]))
            lv = vchain(VE.tensor_copy(srow[0:1, 512:1024], toB[64:65, 0:512]))
            ACT.wait_ge(sV.h, lv)
            for seg0 in (0, 512):
                av = sA.inc(ACT.activation(srow[0:1, 1024 + seg0:1536 + seg0],
                                           srow[0:1, seg0:512 + seg0], AF.Ln))
                ACT.wait_ge(sA.h, av)
                av = sA.inc(ACT.activation(srow[0:1, 1024 + seg0:1536 + seg0],
                                           srow[0:1, 1024 + seg0:1536 + seg0],
                                           AF.Exp, scale=-1.0))
            PE.wait_ge(sA.h, av)
            tl = pp[5].acquire(PE)
            PE.matmul(tl[:, 0:512], e2b0[:], srow[0:1, 1024:1536],
                      start=True, stop=False)
            pl = sP.inc(PE.matmul(tl[:, 0:512], e2b1[:], srow[0:1, 1536:2048],
                                  start=False, stop=True))
            VE.wait_ge(sP.h, pl)
            lrv = vchain(VE.tensor_copy(lrbuf.ap(0, 512), tl[:, 0:512]))
            pp[5].consumed(sV, lrv)
            wait_stream(VE, "GO")
            vchain(VE.tensor_tensor(ost.ap(0, 512, p0=0, p1=64),
                                    toA[0:64, 0:512],
                                    lrbuf.ap(0, 512, p0=0, p1=64), op=ALU.mult))
            ov = vchain(VE.tensor_tensor(ost.ap(0, 512, p0=64, p1=128),
                                         toB[0:64, 0:512],
                                         lrbuf.ap(0, 512, p0=64, p1=128),
                                         op=ALU.mult))
            pp[3].consumed(sV, ov)
            pp[4].consumed(sV, ov)
            SY.wait_ge(sV.h, ov)
            dma(SY, ag3_in[128 * hp:128 * (hp + 1), 512 * qw:512 * (qw + 1)],
                ost.ap(0, 512), "GO")

    for nm_, seg_, w_ in (("Kbb", kbb, 4 * S), ("Vtb", vtb, 32 * 520)):
        if dbg(nm_, (128, w_), BF16) is not None:
            GP.wait_ge(sV.h, sV.v)
            dma(GP, p.dbg_outs[nm_], seg_.ap(0, w_), "GD")
            wait_stream(GP, "GD")
    if dbg("Ob", (512, S), BF16) is not None:
        wait_stream(GP, "GO")
        dma(GP, p.dbg_outs["Ob"], ag3_in, "GD")
        wait_stream(GP, "GD")
    wait_stream(GP, "GO")
    cc3 = p.sem("CC3")
    cc3.inc(GP.collective_compute("AllGather", ALU.bypass, replica_groups=RG,
                                  ins=[ag3_in.opt()], outs=[ag3_out.opt()]), 1)
    GP.wait_ge(cc3.h, cc3.v)

    GP.wait_ge(sV.h, sV.v)
    for c in range(NCH):
        dma(GP, xh.ap(640 * c, 640 * (c + 1)),
            ag2x_out[128 * c:128 * (c + 1), :], "LXH")
    wait_stream(VE, "LXH")
    for c in range(NCH):
        vchain(VE.tensor_scalar(xh.ap(640 * c, 640 * (c + 1)),
                                xh.ap(640 * c, 640 * (c + 1)),
                                flag[:], None, op0=ALU.mult))
    p.barrier("ob_comb")
    a2.reset()
    ocb = a2.alloc(8 * 2688, BF16)
    ourd = a2.alloc(2 * 512, BF16)
    ot1 = a2.alloc(512, BF16)

    def OCB(ic, a, b):
        return ocb.ap(2688 * ic + a, 2688 * ic + b)

    ocv = 0
    for ic in range(NCH):
        for (ta, tb_) in ((1408, 1920), (1920, 2048), (2048, 2560),
                          (2560, 3072), (3072, 3584), (3584, 4096)):
            W = tb_ - ta
            SY.wait_ge(sV.h, ocv)
            for rot in range(2):
                ca = (ta + 2048 * rot) % 4096
                dma(SY, ourd.ap(512 * rot, 512 * rot + W),
                    ag3_out[128 * ic:128 * (ic + 1), ca:ca + W], "LOr")
            wait_stream(VE, "LOr")
            vchain(VE.tensor_scalar(ot1.ap(0, W), ourd.ap(0, W),
                                    flag[:], None, op0=ALU.mult))
            ocv = vchain(VE.scalar_tensor_tensor(
                OCB(ic, ta - 1408, tb_ - 1408), ourd.ap(512, 512 + W),
                flagn[:], ot1.ap(0, W), op0=ALU.mult, op1=ALU.add))
    pob = Seg(a3.t, 5120, 2 * 1024, BF16)
    ytiles = [(0, 512, "h"), (512, 640, "h"), (640, 1152, "o"),
              (1152, 1664, "o"), (1664, 2176, "o"), (2176, 2688, "o")]
    x3_v = 0
    po_used = [0, 0]
    for dc in range(NCH):
        st_p = f"LPo{dc % 2}"
        if po_used[dc % 2]:
            GP.wait_ge(sP.h, po_used[dc % 2])
        dma(GP, pob.ap(1024 * (dc % 2), 1024 * (dc % 2) + 1024)
            .rearrange("p (c m) -> p c m", m=128),
            di["Po"].rearrange("(c p) m -> p c m", p=128)
            [:, :, 128 * dc:128 * (dc + 1)], st_p)
        wait_stream(PE, st_p)
        PE.wait_ge(sV.h, ocv)
        for ti, (ta, tb_, kind) in enumerate(ytiles):
            W = tb_ - ta
            t = pp[ti % 4].acquire(PE)
            pv = None
            for ic in range(NCH):
                mm = PE.matmul(t[:, 0:W],
                               pob.ap(1024 * (dc % 2) + 128 * ic,
                                      1024 * (dc % 2) + 128 * (ic + 1)),
                               OCB(ic, ta, tb_),
                               start=(ic == 0), stop=(ic == NCH - 1))
                if ic == NCH - 1:
                    pv = sP.inc(mm)
            po_used[dc % 2] = pv
            VE.wait_ge(sP.h, pv)
            if kind == "h":
                dst = xh.ap(640 * dc + ta, 640 * dc + tb_)
                vchain(VE.tensor_tensor(dst, t[:, 0:W], dst, op=ALU.add))
                x3_v = vchain(VE.tensor_scalar(dst, dst, flag[:], None,
                                               op0=ALU.mult))
            else:
                dst = XO2(dc, ta - 640, tb_ - 640)
                x3_v = vchain(VE.tensor_tensor(dst, t[:, 0:W], dst, op=ALU.add))
            pp[ti % 4].consumed(sV, x3_v)

    if dbg("x3", (D, TOWN)) is not None:
        GP.wait_ge(sV.h, x3_v)
        for c in range(NCH):
            dma(GP, p.dbg_outs["x3"][128 * c:128 * (c + 1), :],
                XO2(c, 0, TOWN), "GD")
        for e_ in (VE, GP, PE, ACT):
            wait_stream(e_, "GD")
    if upto == "x3":
        finish(p, out_x4, XO2, x3_v)
        return p

    GP.wait_ge(sV.h, x3_v)
    for c in range(NCH):
        dma(GP, x_spill[128 * c:128 * (c + 1), :], XO2(c, 0, TOWN), "GS2")
    for e_ in (GP, SY, PE, VE, ACT):
        wait_stream(e_, "GS2")
    p.barrier("z0p")
    a2.reset()
    a2.off = NCH * UBW
    for c in range(NCH):
        vchain(VE.memset(ubt.ap(UBW * c, UBW * c + 1), 0.0))

    ln_tiles2 = [(0, 512), (512, 640), (640, 1152), (1152, 1664),
                 (1664, 2176), (2176, 2688)]
    z0p_v = 0
    for (ta, tb_) in ln_tiles2:
        W = tb_ - ta

        def get_in(c, ta=ta, tb_=tb_):
            if tb_ <= 640:
                return xh.ap(640 * c + ta, 640 * c + tb_)
            return XO2(c, ta - 640, tb_ - 640)

        def put_out(c, ta=ta, tb_=tb_):
            return UB(c, ta, tb_), None

        z0p_v = ln_group(get_in, put_out, W, ln_scr2,
                         (pp[4], pp[5], pp[6], pp[7]), ready_v=x3_v)
    p.barrier("s4")

    emit_ffn_stage("s4", ag4_in, ag4_out)

    p.barrier("x4")
    for c in range(NCH):
        dma(SY, XO2(c, 0, TOWN), x_spill[128 * c:128 * (c + 1), :], "LS4")
    wait_stream(VE, "LS4")
    x4_v = 0
    for c in range(NCH):
        dst = xo2.ap(TOWN * c, TOWN * (c + 1))
        x4_v = vchain(VE.tensor_tensor(dst, dst, UB(c, HALO, TEXT),
                                       op=ALU.add))
    finish(p, out_x4, XO2, x4_v)
    return p


def finish(p, out_x4, XO, ready_v):
    nc = p.nc
    sO = p.sem("O")
    nc.gpsimd.wait_ge(p.sem("V").h, ready_v)
    for c in range(NCH):
        sO.inc(nc.gpsimd.dma_start(out=out_x4[128 * c:128 * (c + 1), :],
                                   in_=XO(c, 0, TOWN)), 16)
    for e in (nc.gpsimd, nc.sync, nc.tensor, nc.vector, nc.scalar):
        e.wait_ge(sO.h, sO.v)
        if "GD" in p._sems:
            e.wait_ge(p.sem("GD").h, p.sem("GD").v)


def kernel(**inputs):
    in_maps = [_stage_core(inputs, c) for c in range(8)]
    p = build(upto=BUILD_UPTO, debug=DEBUG_OUT)
    res = run_bass_kernel_spmd(p.nc, in_maps, core_ids=list(range(8)))
    out = np.empty((B, S, D), dtype=np.float32)
    for c in range(8):
        b, half = c // 2, c % 2
        out[b, 2048 * half:2048 * half + 2048] = res.results[c]["o_x4"].T
    return out


BUILD_UPTO = "full"
DEBUG_OUT = ()


# version 02
# speedup vs baseline: 1.0096x; 1.0096x over previous
import contextlib
import numpy as np
import ml_dtypes

import concourse.bass as bass
import concourse.mybir as mybir
from concourse.bass_utils import run_bass_kernel_spmd

F32 = mybir.dt.float32
BF16 = mybir.dt.bfloat16
AF = mybir.ActivationFunctionType
ALU = mybir.AluOpType
AX = mybir.AxisListType

B, S, D = 4, 4096, 1024
NCH = 8
TOWN = 2048
HALO = 640
TEXT = HALO + TOWN
M = 128
CC = 128
DECAY, LR = 0.99, 0.01
DEC_C = float(DECAY ** CC)
NEG = -1.0e30
FH, FDH, WIN = 8, 64, 512
HID, HIDP = 2730, 2816
EPS = 1e-5

_f32 = np.float32
_bf16 = ml_dtypes.bfloat16


def _host_consts(half):
    c = {}
    i = np.arange(CC)
    dmT = np.where(i[:, None] < i[None, :],
                   LR * DECAY ** (-1.0 - i[:, None]), 0.0)
    c["dmaskT"] = (dmT * np.ones((CC, CC))).astype(_f32)
    c["decq512"] = np.broadcast_to(
        DECAY ** (np.arange(512) % 128), (128, 512)).astype(_f32)
    c["deck"] = (LR * DECAY ** (127.0 - i)).reshape(128, 1).astype(_f32)
    c["identity"] = np.eye(128, dtype=_f32)
    R = np.zeros((128, 128), dtype=_f32)
    for h0 in (0, 64):
        for k in range(32):
            R[h0 + 32 + k, h0 + k] = -1.0
            R[h0 + k, h0 + 32 + k] = 1.0
    c["Rperm"] = R
    oc2 = np.zeros((128, 2), dtype=_f32)
    oc2[:64, 0] = 1.0
    oc2[64:, 1] = 1.0
    c["ones2col"] = oc2
    e2 = np.zeros((2, 128), dtype=_f32)
    e2[0, :64] = 1.0
    e2[1, 64:] = 1.0
    c["e2"] = e2
    c["flag"] = np.full((128, 1), float(half), dtype=_f32)
    c["flagn"] = np.full((128, 1), 1.0 - float(half), dtype=_f32)
    c["epsc"] = np.full((128, 1), EPS, dtype=_f32)
    c["ones512"] = np.ones((1, 512), dtype=_f32)

    t0 = 2048 * half
    pos = np.arange(t0 - HALO, t0 + TOWN, dtype=np.float64)
    half_idx = np.arange(0, FDH, 2, dtype=np.float64)
    inv_freq = 1.0 / 10000.0 ** (half_idx / FDH)
    freqs = np.concatenate([pos[:, None] * inv_freq[None, :]] * 2, axis=1)
    base = (half_idx + 0.4 * FDH) / (1.4 * FDH)
    power = (pos - S // 2) / float(WIN // 2)
    scale = np.concatenate([base[None, :] ** power[:, None]] * 2, axis=1)
    cos, sin = np.cos(freqs), np.sin(freqs)
    for nm, tb in (("rotAq", cos * scale * 8.0), ("rotBq", sin * scale * 8.0),
                   ("rotAk", cos / scale), ("rotBk", sin / scale)):
        c[nm] = np.tile(tb.T.astype(_f32), (2, 1))

    kl = np.arange(128)[:, None]
    cf = np.arange(1408)[None, :]
    df = kl - cf + 384
    c["fmaskM"] = np.where((df > 0) | (df < -512), NEG, 0.0).astype(_f32)
    km = np.zeros((128, 640), dtype=_f32)
    if half == 0:
        km[:, 128:] = NEG
    c["khmask0"] = km
    return c


def _stage_core(inputs, core):
    b, half = core // 2, core % 2
    t0 = 2048 * half
    x = inputs["L0_attn_a_x"][b]
    d = dict(_host_consts(half))
    d["x_own"] = x[t0:t0 + TOWN].T
    d["x_pre"] = x[0:2048].T if half else np.zeros((D, 2048), dtype=_f32)
    d["x_halo"] = x[t0 - HALO:t0].T if half else np.zeros((D, HALO), dtype=_f32)
    d["wkT"] = inputs["L0_attn_a_Wk"].T
    d["wvT"] = inputs["L0_attn_a_Wv"].T
    d["wqT"] = inputs["L0_attn_a_Wq"].T
    d["woT"] = inputs["L0_attn_a_Wo"].T
    d["bk"] = inputs["L0_attn_a_bk"].reshape(128, 1)
    d["bq"] = inputs["L0_attn_a_bq"].reshape(128, 1)
    d["bkrow"] = inputs["L0_attn_a_bk"].reshape(1, 128)
    d["bvrow"] = inputs["L0_attn_a_bv"].reshape(1, 128)
    d["bo"] = inputs["L0_attn_a_bo"].reshape(D, 1)
    for l in range(2):
        d[f"wqkv{l}"] = inputs["L0_ffn_wqkv"][l]
        d[f"fwo{l}"] = inputs["L0_ffn_wo"][l]
        w1 = inputs["L0_ffn_w1"][l]
        w1p = np.zeros((D, 2 * HIDP), dtype=_f32)
        w1p[:, :HID] = w1[:, :HID]
        w1p[:, HIDP:HIDP + HID] = w1[:, HID:]
        d[f"w1p{l}"] = w1p
        b1 = inputs["L0_ffn_b1"][l]
        b1p = np.zeros((2 * HIDP, 1), dtype=_f32)
        b1p[:HID, 0] = b1[:HID]
        b1p[HIDP:HIDP + HID, 0] = b1[HID:]
        d[f"b1p{l}"] = b1p
        w2p = np.zeros((HIDP, D), dtype=_f32)
        w2p[:HID] = inputs["L0_ffn_w2"][l]
        d[f"w2p{l}"] = w2p
        d[f"b2_{l}"] = inputs["L0_ffn_b2"][l].reshape(D, 1)
        d[f"lng{l}"] = inputs["L0_ffn_ln_g"][l].reshape(D, 1)
        d[f"lnb{l}"] = inputs["L0_ffn_ln_b"][l].reshape(D, 1)
        d[f"lngb{l}"] = np.stack([inputs["L0_ffn_ln_g"][l],
                                  inputs["L0_ffn_ln_b"][l]])
        d[f"qs{l}"] = np.tile(inputs["L0_ffn_q_scale"][l], 2).reshape(128, 1)
        d[f"ks{l}"] = np.tile(inputs["L0_ffn_k_scale"][l], 2).reshape(128, 1)
    Pi = inputs["L0_attn_b_P_i"]
    myh = range(8 * half, 8 * half + 8)
    d["PiQ"] = np.concatenate([Pi[:, 192 * h:192 * h + 64] for h in myh], axis=1)
    d["PiK"] = np.concatenate([Pi[:, 192 * h + 64:192 * h + 128] for h in myh], axis=1)
    d["PiV"] = np.concatenate([Pi[:, 192 * h + 128:192 * h + 192] for h in myh], axis=1)
    d["Po"] = inputs["L0_attn_b_P_o"]
    return {k: np.ascontiguousarray(v, dtype=_f32) for k, v in d.items()}


class Sem:
    def __init__(self, handle):
        self.h = handle
        self.v = 0

    def inc(self, inst, n=1):
        inst.then_inc(self.h, n)
        self.v += n
        return self.v


class Seg:

    def __init__(self, t, off, cols, dtype):
        self.t, self.off, self.cols, self.dtype = t, off, cols, dtype

    def ap(self, a=0, b=None, p0=0, p1=128):
        b = self.cols if b is None else b
        if self.dtype == F32:
            return self.t[p0:p1, self.off + a:self.off + b]
        assert self.off % 1 == 0
        base = self.t[p0:p1, self.off:self.off + (self.cols + 1) // 2]
        return base.bitcast(BF16)[:, a:b]


class Arena:
    def __init__(self, p, name, cols):
        self.t = p.sb(name, [128, cols], F32)
        self.cols = cols
        self.off = 0
        self.name = name

    def alloc(self, cols, dtype=F32):
        fcols = cols if dtype == F32 else (cols + 1) // 2
        assert self.off + fcols <= self.cols, \
            f"arena {self.name} overflow: {self.off}+{fcols} > {self.cols}"
        s = Seg(self.t, self.off, cols, dtype)
        self.off += fcols
        return s

    def reset(self, off=0):
        self.off = off


class Prog:
    def __init__(self):
        self.nc = bass.Bass("TRN2", target_bir_lowering=False, debug=False)
        self.ctx = contextlib.ExitStack()
        self._sems = {}
        self.di = {}
        self.dbg_outs = {}
        self.debug = ()

    def sem(self, name):
        if name not in self._sems:
            self._sems[name] = Sem(self.ctx.enter_context(self.nc.semaphore(name)))
        return self._sems[name]

    def sb(self, name, shape, dtype):
        return self.ctx.enter_context(self.nc.sbuf_tensor(name, shape, dtype))

    def psum(self, name, shape, dtype=F32):
        return self.ctx.enter_context(self.nc.psum_tensor(name, shape, dtype))

    def barrier(self, tag):
        s = self.sem("bar")
        nc = self.nc
        engs = [nc.sync, nc.tensor, nc.vector, nc.scalar, nc.gpsimd]
        base = s.v
        for e in engs:
            s.inc(e.nop(), 1)
        for e in engs:
            e.wait_ge(s.h, base + 5)


class PsBank:

    def __init__(self, t):
        self.t = t
        self.free_at = None

    def acquire(self, eng):
        if self.free_at is not None:
            sem, v = self.free_at
            eng.wait_ge(sem.h, v)
            self.free_at = None
        return self.t

    def consumed(self, sem, v):
        self.free_at = (sem, v)


def build(upto="full", debug=()):
    p = Prog()
    p.debug = tuple(debug)
    nc = p.nc
    di = p.di

    def dram_in(name, shape, dtype=F32):
        di[name] = nc.dram_tensor(name, list(shape), dtype,
                                  kind="ExternalInput").ap()

    for nm, sh in [
        ("x_own", (D, TOWN)), ("x_pre", (D, 2048)), ("x_halo", (D, HALO)),
        ("flag", (128, 1)), ("flagn", (128, 1)),
        ("epsc", (128, 1)), ("ones512", (1, 512)),
        ("wkT", (D, 128)), ("wvT", (D, 128)), ("wqT", (D, 128)), ("woT", (128, D)),
        ("bk", (128, 1)), ("bq", (128, 1)), ("bkrow", (1, 128)), ("bvrow", (1, 128)),
        ("bo", (D, 1)),
        ("dmaskT", (128, 128)), ("decq512", (128, 512)), ("deck", (128, 1)),
        ("identity", (128, 128)), ("Rperm", (128, 128)),
        ("ones2col", (128, 2)), ("e2", (2, 128)),
        ("rotAq", (128, TEXT)), ("rotBq", (128, TEXT)),
        ("rotAk", (128, TEXT)), ("rotBk", (128, TEXT)),
        ("khmask0", (128, 640)),
        ("PiQ", (D, 512)), ("PiK", (D, 512)), ("PiV", (D, 512)), ("Po", (D, D)),
    ]:
        dram_in(nm, sh)
    dram_in("fmaskM", (128, 1408))
    for l in range(2):
        for nm, sh in [
            (f"wqkv{l}", (D, 1536)), (f"fwo{l}", (512, D)),
            (f"w1p{l}", (D, 2 * HIDP)), (f"b1p{l}", (2 * HIDP, 1)),
            (f"w2p{l}", (HIDP, D)), (f"b2_{l}", (D, 1)),
            (f"lng{l}", (D, 1)), (f"lnb{l}", (D, 1)),
            (f"lngb{l}", (2, D)),
            (f"qs{l}", (128, 1)), (f"ks{l}", (128, 1)),
        ]:
            dram_in(nm, sh)

    out_x4 = nc.dram_tensor("o_x4", [D, TOWN], F32, kind="ExternalOutput").ap()

    def dbg(name, shape, dtype=F32):
        if name in p.debug and name not in p.dbg_outs:
            p.dbg_outs[name] = nc.dram_tensor(
                "dbg_" + name, list(shape), dtype, kind="ExternalOutput").ap()
        return p.dbg_outs.get(name)

    x_spill = nc.dram_tensor("x_spill", [D, TOWN], F32, kind="Internal").ap()
    ag1_in = nc.dram_tensor("ag1_in", [D, HALO], BF16, kind="Internal").ap()
    ag1_out = nc.dram_tensor("ag1_out", [2 * D, HALO], BF16, kind="Internal").ap()
    ag2z_in = nc.dram_tensor("ag2z_in", [D, TOWN], BF16, kind="Internal").ap()
    ag2z_out = nc.dram_tensor("ag2z_out", [2 * D, TOWN], BF16, kind="Internal").ap()
    ag2x_in = nc.dram_tensor("ag2x_in", [D, HALO], BF16, kind="Internal").ap()
    ag2x_out = nc.dram_tensor("ag2x_out", [2 * D, HALO], BF16, kind="Internal").ap()
    ag3_in = nc.dram_tensor("ag3_in", [512, S], BF16, kind="Internal").ap()
    ag3_out = nc.dram_tensor("ag3_out", [1024, S], BF16, kind="Internal").ap()
    ag4_in = nc.dram_tensor("ag4_in", [D, HALO], BF16, kind="Internal").ap()
    ag4_out = nc.dram_tensor("ag4_out", [2 * D, HALO], BF16, kind="Internal").ap()
    RG = [[0, 1], [2, 3], [4, 5], [6, 7]]

    a1 = Arena(p, "a1", 19544)
    a2 = Arena(p, "a2", NCH * (TEXT + 1))
    a3 = Arena(p, "a3", 6656)
    sc = Arena(p, "sc", 768)
    flag = p.sb("flag_s", [128, 1], F32)
    flagn = p.sb("flagn_s", [128, 1], F32)
    epsc = p.sb("epsc_s", [128, 1], F32)
    ones1 = p.sb("ones1", [1, 128], F32)
    ones1b = p.sb("ones1b", [1, 128], BF16)
    onescol = p.sb("onescol", [128, 1], BF16)
    identb = p.sb("identb", [128, 128], BF16)
    srow = p.sb("srow", [2, 2048], F32)
    masksF = p.sb("masksF", [128, 1408], BF16)

    pp = [PsBank(p.psum(f"pp{i}", [128, 512])) for i in range(8)]

    block = p.ctx.enter_context(nc.Block())
    SY, PE, VE, ACT, GP = nc.sync, nc.tensor, nc.vector, nc.scalar, nc.gpsimd
    sP = p.sem("P")
    sV = p.sem("V")
    sA = p.sem("A")

    def dma(eng, out, in_, stream, **kw):
        sm = p.sem(stream)
        sm.inc(eng.dma_start(out=out, in_=in_, **kw), 16)
        return sm.v

    def wait_stream(eng, stream):
        sm = p.sem(stream)
        eng.wait_ge(sm.h, sm.v)

    def vchain(inst):
        vv = sV.inc(inst)
        VE.wait_ge(sV.h, vv)
        return vv

    xo = a1.alloc(16384, F32)

    def XO(c, a, b):
        return xo.ap(TOWN * c + a, TOWN * c + b)

    dma(SY, flag[:], di["flag"], "L1")
    dma(SY, flagn[:], di["flagn"], "L1")
    dma(SY, epsc[:], di["epsc"], "L1")
    dma(GP, identb[:], di["identity"], "G1")
    dma(GP, ones1b[:], di["ones512"][0:1, 0:128], "G1")
    for c in range(NCH):
        dma(SY, XO(c, 0, TOWN), di["x_own"][128 * c:128 * (c + 1), :], "L1")
    dma(GP, masksF[:], di["fmaskM"], "G1")

    def fmask(kk):
        return masksF[:, 896 - 128 * kk:1408 - 128 * kk]

    def bmask(j):
        return masksF[:, 384 - 128 * j:896 - 128 * j]
    dma(SY, srow[1:2, 1536:2048], di["ones512"], "L1")
    dma(SY, ones1[:], di["ones512"][0:1, 0:128], "L1")
    VE.memset(onescol[:], 1.0)

    def ln_group(get_in, put_out, W, scratch, banks, gb=None, ready_v=None):
        stA, stB, bcA, bcB = banks
        xbs, sqs, t1s = scratch["xb"], scratch["sq"], scratch["t1"]
        if ready_v is not None:
            ACT.wait_ge(sV.h, ready_v)
        tA = stA.acquire(PE)
        tB = stB.acquire(PE)
        slot_pe = scratch.setdefault("slot_pe", [0, 0])
        for c in range(NCH):
            xin = get_in(c)
            xb = xbs[c % 2]
            sq = sqs[c % 2]
            VE.wait_ge(sP.h, slot_pe[c % 2])
            ACT.wait_ge(sP.h, slot_pe[c % 2])
            vv = sV.inc(VE.tensor_copy(xb.ap(0, W), xin))
            aa = sA.inc(ACT.activation(sq.ap(0, W), xin, AF.Square))
            PE.wait_ge(sV.h, vv)
            PE.wait_ge(sA.h, aa)
            PE.matmul(tA[0:1, 0:W], onescol[:], xb.ap(0, W),
                      start=(c == 0), stop=(c == NCH - 1))
            pv = sP.inc(PE.matmul(tB[0:1, 0:W], onescol[:], sq.ap(0, W),
                                  start=(c == 0), stop=(c == NCH - 1)))
            slot_pe[c % 2] = pv
        pe_done = sP.v
        mu = srow[0:1, 0:0 + W]
        msq = srow[0:1, 512:512 + W]
        var = srow[0:1, 1024:1024 + W]
        sd = srow[0:1, 512:512 + W]
        rst = srow[0:1, 1024:1024 + W]
        nmb = srow[0:1, 1536:1536 + W]
        VE.wait_ge(sP.h, pe_done)
        vv = sV.inc(VE.tensor_scalar(mu, tA[0:1, 0:W], 1.0 / D, None,
                                     op0=ALU.mult))
        ACT.wait_ge(sV.h, vv)
        av = sA.inc(ACT.activation(msq, mu, AF.Square))
        VE.wait_ge(sA.h, av)
        vv = sV.inc(VE.scalar_tensor_tensor(
            var, tB[0:1, 0:W], 1.0 / D, msq,
            op0=ALU.mult, op1=ALU.subtract))
        stA.consumed(sV, vv)
        stB.consumed(sV, vv)
        ACT.wait_ge(sV.h, vv)
        av = sA.inc(ACT.activation(sd, var, AF.Ln, bias=epsc[0:1, :]))
        ACT.wait_ge(sA.h, av)
        av = sA.inc(ACT.activation(rst, sd, AF.Exp, scale=-0.5))
        VE.wait_ge(sA.h, av)
        vv = vchain(VE.scalar_tensor_tensor(nmb, mu, -1.0, rst,
                                            op0=ALU.mult, op1=ALU.mult))
        tbcA = bcA.acquire(PE)
        PE.wait_ge(sV.h, vv)
        if gb is not None:
            vv = vchain(VE.tensor_copy(scratch["lnrb"][0:1, 0:W], nmb))
            PE.wait_ge(sV.h, vv)
        pva = sP.inc(PE.matmul(tbcA[:, 0:W], ones1[:], rst,
                               start=True, stop=True))
        if gb is None:
            tbcB = bcB.acquire(PE)
            pvb = sP.inc(PE.matmul(tbcB[:, 0:W], ones1[:], nmb,
                                   start=True, stop=True))
            VE.wait_ge(sP.h, pvb)
        VE.wait_ge(sP.h, pva)
        last = None
        for c in range(NCH):
            xin = get_in(c)
            t1 = t1s.ap(0, W)
            if gb is not None:
                tbcB = bcB.acquire(PE)
                pvb = sP.inc(PE.matmul(tbcB[:, 0:W], gb(c),
                                       scratch["lnrb"][0:2, 0:W],
                                       start=True, stop=True))
            vchain(VE.tensor_tensor(t1, xin, tbcA[:, 0:W], op=ALU.mult))
            dst, lgcol = put_out(c)
            if gb is None:
                last = vchain(VE.scalar_tensor_tensor(
                    dst, t1, 1.0, tbcB[:, 0:W], op0=ALU.mult, op1=ALU.add))
            else:
                VE.wait_ge(sP.h, pvb)
                last = vchain(VE.scalar_tensor_tensor(
                    dst, t1, lgcol, tbcB[:, 0:W], op0=ALU.mult, op1=ALU.add))
                bcB.consumed(sV, last)
        bcA.consumed(sV, last)
        bcB.consumed(sV, last)
        return last

    za = a2.alloc(4096, BF16)
    kfm = a2.alloc(4096, BF16)
    qd0 = a2.alloc(3072, BF16)
    ktm = a2.alloc(4096, BF16)
    vtm = a2.alloc(4096, BF16)
    obuf = a2.alloc(2688, BF16)
    wob = a2.alloc(1024, BF16)
    awk = a2.alloc(512, F32)
    xpA = a2.alloc(4096, F32)
    xpB = a2.alloc(4096, F32)

    class _QD:
        def ap(self, a, b):
            return qd0.ap(a - 1024, b - 1024)
    qd = _QD()
    xh = a3.alloc(5120, F32)
    wkb = a3.alloc(1024, BF16)
    wvb = a3.alloc(1024, BF16)
    wqb = a3.alloc(1024, BF16)
    dmk = a1.alloc(128, F32)
    stf = a1.alloc(128, F32)
    stb = a1.alloc(128, BF16)
    wts = [a1.alloc(128, BF16), a1.alloc(128, BF16)]
    ktds = [a1.alloc(128, BF16), a1.alloc(128, BF16)]
    csm = a1.alloc(16, F32)
    bo8 = a1.alloc(8, F32)
    xb_s = [a1.alloc(512, BF16), a1.alloc(512, BF16)]
    sq_s = [a1.alloc(512, BF16), a1.alloc(512, BF16)]
    t1_s = a1.alloc(512, F32)

    brow_k = p.sb("brow_k", [1, 128], BF16)
    brow_v = p.sb("brow_v", [1, 128], BF16)

    for (dst, src) in ((wkb, "wkT"), (wvb, "wvT"), (wqb, "wqT")):
        dma(GP, dst.ap().rearrange("p (c m) -> p c m", m=128),
            di[src].rearrange("(c p) m -> p c m", p=128), "G1")
    dma(GP, wob.ap(), di["woT"], "G1")
    dma(GP, brow_k[:], di["bkrow"], "G1")
    dma(GP, brow_v[:], di["bvrow"], "G1")
    dma(SY, awk.ap(), di["decq512"], "L1")
    dma(SY, dmk.ap(), di["dmaskT"], "L1")
    dma(SY, csm.ap(0, 1), di["bk"], "L1")
    dma(SY, csm.ap(1, 2), di["bq"], "L1")
    dma(SY, csm.ap(2, 3), di["deck"], "L1")
    dma(SY, bo8.ap().rearrange("p (c o) -> p c o", o=1),
        di["bo"].rearrange("(c p) o -> p c o", p=128),
        allow_slow_non_contiguous=True, stream="L1")
    bk_col = lambda: csm.ap(0, 1)
    bq_col = lambda: csm.ap(1, 2)
    deck_col = lambda: csm.ap(2, 3)

    def load_xpre(g, seg, slot):
        for c in range(NCH):
            dma(SY, seg.ap(512 * c, 512 * (c + 1)),
                di["x_pre"][128 * c:128 * (c + 1), 512 * g:512 * (g + 1)],
                f"Lx{slot}")
        return p.sem(f"Lx{slot}").v

    ln_scr = {"xb": xb_s, "sq": sq_s, "t1": t1_s}

    xp_ready = [None, None]
    xp_ready[0] = load_xpre(0, xpA, 0)
    xp_ready[1] = load_xpre(1, xpB, 1)

    for e in (VE, PE, ACT):
        wait_stream(e, "L1")
        wait_stream(e, "G1")

    for g in range(8):
        own = g >= 4
        if not own:
            seg = (xpA, xpB)[g % 2]
            sm = p.sem(f"Lx{g % 2}")
            VE.wait_ge(sm.h, xp_ready[g % 2])
            ACT.wait_ge(sm.h, xp_ready[g % 2])
            get_in = lambda c, seg=seg: seg.ap(512 * c, 512 * c + 512)
        else:
            go = 512 * (g - 4)
            get_in = lambda c, go=go: XO(c, go, go + 512)

        def put_out(c):
            return za.ap(512 * c, 512 * c + 512), None

        lnv = ln_group(get_in, put_out, 512, ln_scr,
                       (pp[0], pp[1], pp[2], pp[3]))

        if not own and g + 2 < 4:
            SY.wait_ge(sV.h, lnv)
            xp_ready[g % 2] = load_xpre(g + 2, (xpA, xpB)[g % 2], g % 2)

        PE.wait_ge(sV.h, lnv)
        gs = 512 * g
        t = pp[4].acquire(PE)
        pv = None
        for c in range(NCH):
            mm = PE.matmul(t[:, 0:512], wkb.ap(128 * c, 128 * (c + 1)),
                           za.ap(512 * c, 512 * (c + 1)),
                           start=(c == 0), stop=(c == NCH - 1))
            if c == NCH - 1:
                pv = sP.inc(mm)
        VE.wait_ge(sP.h, pv)
        vv = sV.inc(VE.tensor_scalar(kfm.ap(gs, gs + 512), t[:, 0:512],
                                     bk_col(), None, op0=ALU.add))
        pp[4].consumed(sV, vv)
        if g >= 2:
            t = pp[5].acquire(PE)
            pv = None
            for c in range(NCH):
                mm = PE.matmul(t[:, 0:512], wqb.ap(128 * c, 128 * (c + 1)),
                               za.ap(512 * c, 512 * (c + 1)),
                               start=(c == 0), stop=(c == NCH - 1))
                if c == NCH - 1:
                    pv = sP.inc(mm)
            VE.wait_ge(sP.h, pv)
            vchain(VE.tensor_scalar(t1_s.ap(), t[:, 0:512], bq_col(), None,
                                    op0=ALU.add))
            vv = sV.inc(VE.tensor_tensor(qd.ap(gs, gs + 512), t1_s.ap(),
                                         awk.ap(), op=ALU.mult))
            pp[5].consumed(sV, vv)
        for srowb, wrow, dstseg in ((brow_k, wkb, ktm), (brow_v, wvb, vtm)):
            for ts in range(4):
                t = pp[6 if dstseg is ktm else 7].acquire(PE)
                sub = gs + 128 * ts
                for c in range(NCH):
                    PE.matmul(t[:, 0:128],
                              za.ap(512 * c + 128 * ts, 512 * c + 128 * (ts + 1)),
                              wrow.ap(128 * c, 128 * (c + 1)),
                              start=(c == 0), stop=False)
                pv = sP.inc(PE.matmul(t[:, 0:128], ones1b[:], srowb[:],
                                      start=False, stop=True))
                VE.wait_ge(sP.h, pv)
                vv = sV.inc(VE.tensor_copy(dstseg.ap(sub, sub + 128), t[:, 0:128]))
                (pp[6] if dstseg is ktm else pp[7]).consumed(sV, vv)

    vchain(VE.memset(stf.ap(), 0.0))
    vv = vchain(VE.tensor_copy(stb.ap(), stf.ap()))
    st_ready = vv
    for cch in range(32):
        cs = 128 * cch
        do_out = cch >= 11
        if do_out:
            t_at = pp[0].acquire(PE)
            pv = sP.inc(PE.matmul(t_at[:, 0:128], kfm.ap(cs, cs + 128),
                                  qd.ap(cs, cs + 128), start=True, stop=True))
            wt = wts[cch % 2]
            VE.wait_ge(sP.h, pv)
            vv = sV.inc(VE.tensor_tensor(wt.ap(), t_at[:, 0:128], dmk.ap(),
                                         op=ALU.mult))
            pp[0].consumed(sV, vv)
            wt_v = vv
        ktd = ktds[cch % 2]
        vv = sV.inc(VE.tensor_scalar(ktd.ap(), ktm.ap(cs, cs + 128),
                                     deck_col(), None, op0=ALU.mult))
        ktd_v = vv
        if do_out:
            t_o = pp[1].acquire(PE)
            PE.wait_ge(sV.h, max(st_ready, wt_v))
            PE.matmul(t_o[:, 0:128], stb.ap(), qd.ap(cs, cs + 128),
                      start=True, stop=False)
            pv_o = sP.inc(PE.matmul(t_o[:, 0:128], vtm.ap(cs, cs + 128), wt.ap(),
                                    start=False, stop=True))
        t_u = pp[2].acquire(PE)
        PE.wait_ge(sV.h, ktd_v)
        pv_u = sP.inc(PE.matmul(t_u[:, 0:128], ktd.ap(), vtm.ap(cs, cs + 128),
                                start=True, stop=True))
        VE.wait_ge(sP.h, pv_u)
        vchain(VE.scalar_tensor_tensor(stf.ap(), stf.ap(), DEC_C, t_u[:, 0:128],
                                       op0=ALU.mult, op1=ALU.add))
        vv = vchain(VE.tensor_copy(stb.ap(), stf.ap()))
        pp[2].consumed(sV, vv)
        st_ready = vv
        if cch == 15:
            vchain(VE.tensor_scalar(stf.ap(), stf.ap(), flag[:],
                                    None, op0=ALU.mult))
            st_ready = vchain(VE.tensor_copy(stb.ap(), stf.ap()))
        if do_out:
            VE.wait_ge(sP.h, pv_o)
            vv = sV.inc(VE.tensor_copy(obuf.ap(cs - 1408, cs - 1408 + 128),
                                       t_o[:, 0:128]))
            pp[1].consumed(sV, vv)
    obuf_v = sV.v

    for c in range(NCH):
        dma(SY, xh.ap(640 * c, 640 * (c + 1)),
            di["x_halo"][128 * c:128 * (c + 1), :], "LH")
    PE.wait_ge(sV.h, obuf_v)
    wait_stream(VE, "LH")
    tiles = [(0, 512, "h"), (512, 640, "h"),
             (640, 1152, "o"), (1152, 1664, "o"), (1664, 2176, "o"), (2176, 2688, "o")]
    for (ta, tb_, kind) in tiles:
        W = tb_ - ta
        for c in range(NCH):
            t = pp[c % 4].acquire(PE)
            pv = sP.inc(PE.matmul(t[:, 0:W], wob.ap(128 * c, 128 * (c + 1)),
                                  obuf.ap(ta, tb_), start=True, stop=True))
            VE.wait_ge(sP.h, pv)
            if kind == "h":
                dst = xh.ap(640 * c + ta, 640 * c + tb_)
                vchain(VE.scalar_tensor_tensor(dst, t[:, 0:W], bo8.ap(c, c + 1),
                                               dst, op0=ALU.add, op1=ALU.add))
                vv = vchain(VE.tensor_scalar(dst, dst, flag[:], None,
                                             op0=ALU.mult))
            else:
                dst = XO(c, ta - 640, tb_ - 640)
                vv = sV.inc(VE.scalar_tensor_tensor(
                    dst, t[:, 0:W], bo8.ap(c, c + 1), dst,
                    op0=ALU.add, op1=ALU.add))
            pp[c % 4].consumed(sV, vv)
    x1_v = sV.v

    for nm_, seg_, a_, w_ in (("za", za, 0, 4096), ("kfm", kfm, 0, 4096), ("qd", qd, 1024, 4096),
                              ("ktm", ktm, 0, 4096), ("vtm", vtm, 0, 4096),
                              ("obuf", obuf, 0, 2688)):
        if dbg(nm_, (128, w_ - a_), BF16) is not None:
            GP.wait_ge(sV.h, obuf_v)
            dma(GP, p.dbg_outs[nm_], seg_.ap(a_, w_), "GD")
    if dbg("x1", (D, TOWN)) is not None:
        GP.wait_ge(sV.h, x1_v)
        for c in range(NCH):
            dma(GP, p.dbg_outs["x1"][128 * c:128 * (c + 1), :],
                XO(c, 0, TOWN), "GD")
    if dbg("x1h", (D, HALO)) is not None:
        GP.wait_ge(sV.h, x1_v)
        for c in range(NCH):
            dma(GP, p.dbg_outs["x1h"][128 * c:128 * (c + 1), :],
                xh.ap(640 * c, 640 * (c + 1)), "GD")

    if upto == "x1":
        finish(p, out_x4, XO, x1_v)
        return p

    p.barrier("ub")
    a2.reset()
    UBW = TEXT + 1
    ubt = a2.alloc(NCH * UBW, F32)

    def UB(c, a, b):
        return ubt.ap(UBW * c + 1 + a, UBW * c + 1 + b)

    for c in range(NCH):
        VE.memset(ubt.ap(UBW * c, UBW * c + 1), 0.0)

    ln_tiles = [(0, 512), (512, 640), (640, 1152), (1152, 1664),
                (1664, 2176), (2176, 2688)]
    z0_v = 0
    for (ta, tb_) in ln_tiles:
        W = tb_ - ta

        def get_in(c, ta=ta, tb_=tb_):
            if tb_ <= 640:
                return xh.ap(640 * c + ta, 640 * c + tb_)
            return XO(c, ta - 640, tb_ - 640)

        def put_out(c, ta=ta, tb_=tb_):
            return UB(c, ta, tb_), None

        z0_v = ln_group(get_in, put_out, W, ln_scr,
                        (pp[4], pp[5], pp[6], pp[7]), ready_v=x1_v)

    if dbg("z0", (D, TEXT)) is not None:
        GP.wait_ge(sV.h, z0_v)
        for c in range(NCH):
            dma(GP, p.dbg_outs["z0"][128 * c:128 * (c + 1), :],
                UB(c, 0, TEXT), "GD")

    if upto == "z0":
        finish(p, out_x4, XO, x1_v)
        return p

    GP.wait_ge(sV.h, x1_v)
    for c in range(NCH):
        dma(GP, x_spill[128 * c:128 * (c + 1), :], XO(c, 0, TOWN), "GS")

    G3 = [(0, 512), (512, 1024), (1024, 1536), (1536, 2048),
          (2048, 2560), (2560, 2688)]
    G4 = [(512, 1024), (1024, 1536), (1536, 2048), (2048, 2560), (2560, 2688)]

    sc.reset()
    sqscr = sc.alloc(512, BF16)
    gelscr = sc.alloc(512, BF16)
    fsm = sc.alloc(64, F32)
    b1c = [sc.alloc(44, F32), sc.alloc(44, F32)]

    qs_col = lambda l: fsm.ap(2 * l, 2 * l + 1)
    ks_col = lambda l: fsm.ap(2 * l + 1, 2 * l + 2)
    b2_col = lambda l: fsm.ap(4 + l, 5 + l)

    for l in range(2):
        dma(SY, fsm.ap(2 * l, 2 * l + 1), di[f"qs{l}"], "L1")
        dma(SY, fsm.ap(2 * l + 1, 2 * l + 2), di[f"ks{l}"], "L1")
        dma(SY, fsm.ap(4 + l, 5 + l), di[f"b2_{l}"][0:128, :], "L1")
    b2c8 = [sc.alloc(8, F32), sc.alloc(8, F32)]
    lgc8 = [sc.alloc(8, F32), sc.alloc(8, F32)]
    assert sc.off <= sc.cols
    for l in range(2):
        dma(SY, b2c8[l].ap().rearrange("p (c o) -> p c o", o=1),
            di[f"b2_{l}"].rearrange("(c p) o -> p c o", p=128),
            allow_slow_non_contiguous=True, stream="L1")
        dma(SY, lgc8[l].ap().rearrange("p (c o) -> p c o", o=1),
            di[f"lng{l}"].rearrange("(c p) o -> p c o", p=128),
            allow_slow_non_contiguous=True, stream="L1")
        dma(SY, b1c[l].ap().rearrange("p (c o) -> p c o", o=1),
            di[f"b1p{l}"].rearrange("(c p) o -> p c o", p=128),
            allow_slow_non_contiguous=True, stream="L1")
    e2b0 = p.sb("e2b0", [1, 128], F32)
    e2b1 = p.sb("e2b1", [1, 128], F32)
    dma(SY, e2b0[:], di["e2"][0:1, :], "L1")
    dma(SY, e2b1[:], di["e2"][1:2, :], "L1")
    e2f = p.sb("e2f", [2, 128], F32)
    dma(SY, e2f[:], di["e2"], "L1")
    oc2b = p.sb("oc2b", [128, 2], BF16)
    dma(GP, oc2b[:], di["ones2col"], "G1")
    rpb = p.sb("rpb", [128, 128], BF16)
    dma(GP, rpb[:], di["Rperm"], "G1")
    khm0b = p.sb("khm0b", [128, 640], BF16)
    dma(GP, khm0b[:], di["khmask0"], "G1")

    for e in (GP, SY, PE, VE, ACT):
        wait_stream(e, "GS")
        wait_stream(e, "G1")
        wait_stream(e, "L1")
    p.barrier("ffn_setup")
    a1.reset()
    zcast = a1.alloc(4096, BF16)
    kbuf = a1.alloc(4 * TEXT, BF16)
    qbuf = a1.alloc(4 * (TEXT - 512), BF16)
    vtbuf = a1.alloc(21 * 520, BF16)
    e0buf = a1.alloc(512, F32)
    ptbuf = a1.alloc(1024, BF16)
    lrbuf = a1.alloc(512, BF16)
    obq = a1.alloc(4 * 512, BF16)
    A1_ATTN = a1.off
    a1.reset()
    abuf = a1.alloc(22 * 512, BF16)
    w2blk = a1.alloc(22 * 1024, BF16)
    a1.off = max(a1.off, A1_ATTN)
    LN2_BASE = 16896
    assert LN2_BASE >= a1.off - 2648
    xb_s2 = [Seg(a1.t, LN2_BASE + 0, 512, BF16),
             Seg(a1.t, LN2_BASE + 256, 512, BF16)]
    sq_s2 = [Seg(a1.t, LN2_BASE + 512, 512, BF16),
             Seg(a1.t, LN2_BASE + 768, 512, BF16)]
    t1_s2 = Seg(a1.t, LN2_BASE + 1024, 512, F32)
    ln_scr2 = {"xb": xb_s2, "sq": sq_s2, "t1": t1_s2}

    def KB(hp, a, b):
        return kbuf.ap(TEXT * hp + a, TEXT * hp + b)

    def QB(hp, a, b):
        return qbuf.ap((TEXT - 512) * hp + a - 512, (TEXT - 512) * hp + b - 512)

    def VT(t):
        return vtbuf.ap(520 * t, 520 * (t + 1))

    a3.reset()
    wqkvb = a3.alloc(8 * 1536, BF16)
    rots = a3.alloc(2 * 512, BF16)
    assert a3.off <= a3.cols
    fwob = Seg(a3.t, 0, 4 * 1024, BF16)
    z2roll = [Seg(a3.t, 0, 8 * 512, BF16), Seg(a3.t, 2048, 8 * 512, BF16)]
    w1blk = [Seg(a3.t, 4096, 8 * 128, BF16), Seg(a3.t, 4608, 8 * 128, BF16)]
    k1scr = Seg(e0buf.t, e0buf.off + 0, 512, BF16)
    k2scr = Seg(e0buf.t, e0buf.off + 256, 512, BF16)
    tascr = Seg(ptbuf.t, ptbuf.off + 0, 512, BF16)
    tbscr = Seg(ptbuf.t, ptbuf.off + 256, 512, BF16)

    lnrb = p.sb("lnrb", [2, 512], BF16)
    dma(GP, lnrb[1:2, :], di["ones512"], "G1")
    ln_scr2["lnrb"] = lnrb
    for e_ in (PE, VE, ACT, SY, GP):
        wait_stream(e_, "G1")
    lngbs = p.sb("lngbs", [2, D], BF16)

    G3 = [(0, 512), (512, 1024), (1024, 1536), (1536, 2048),
          (2048, 2560), (2560, 2688)]
    G4 = [(512, 1024), (1024, 1536), (1536, 2048), (2048, 2560), (2560, 2688)]

    def emit_ffn_layer(lidx, stage_tag):
        tag = f"{stage_tag}l{lidx}"
        dma(GP, wqkvb.ap().rearrange("p (c m) -> p c m", m=1536),
            di[f"wqkv{lidx}"].rearrange("(c p) m -> p c m", p=128),
            f"Lw_{tag}")
        dma(GP, lngbs[:], di[f"lngb{lidx}"], f"Lw_{tag}")
        wait_stream(PE, f"Lw_{tag}")

        rot_used = 0
        zc_pe = 0
        for g, (ta, tb_) in enumerate(G3):
            W = tb_ - ta
            st_r = f"Lr_{tag}"
            if rot_used:
                GP.wait_ge(sV.h, rot_used)
            wait_stream(GP, st_r)
            dma(GP, rots.ap(0, W), di["rotAk"][:, ta:tb_], st_r)
            dma(GP, rots.ap(512, 512 + W), di["rotBk"][:, ta:tb_], st_r)
            VE.wait_ge(sP.h, zc_pe)
            zcv = 0
            for c in range(NCH):
                off = 1 if c >= 4 else 0
                zcv = sV.inc(VE.tensor_copy(zcast.ap(512 * c, 512 * c + W),
                                            UB(c, ta - off, tb_ - off)))
            PE.wait_ge(sV.h, zcv)

            for isq in (0, 1):
                if isq:
                    GP.wait_ge(sV.h, sV.v)
                    wait_stream(GP, st_r)
                    dma(GP, rots.ap(0, W), di["rotAq"][:, ta:tb_], st_r)
                    dma(GP, rots.ap(512, 512 + W), di["rotBq"][:, ta:tb_], st_r)
                    if g == 0:
                        continue
                for hp in range(4):
                    bank = pp[hp % 4]
                    t = bank.acquire(PE)
                    cols = (0 if isq else 512) + 128 * hp
                    pv = None
                    for c in range(NCH):
                        mm = PE.matmul(t[:, 0:W],
                                       wqkvb.ap(1536 * c + cols,
                                                1536 * c + cols + 128),
                                       zcast.ap(512 * c, 512 * c + W),
                                       start=(c == 0), stop=(c == NCH - 1))
                        if c == NCH - 1:
                            pv = sP.inc(mm)
                    ACT.wait_ge(sP.h, pv)
                    aa = sA.inc(ACT.activation(sqscr.ap(0, W), t[:, 0:W],
                                               AF.Square))
                    PE.wait_ge(sA.h, aa)
                    tn = pp[4].acquire(PE)
                    pn = sP.inc(PE.matmul(tn[0:2, 0:W], oc2b[:], sqscr.ap(0, W),
                                          start=True, stop=True))
                    VE.wait_ge(sP.h, pn)
                    nv = vchain(VE.tensor_scalar(srow[0:2, 0:W], tn[0:2, 0:W],
                                                 1e-24, None, op0=ALU.max))
                    pp[4].consumed(sV, nv)
                    ACT.wait_ge(sV.h, nv)
                    av = sA.inc(ACT.activation(srow[0:2, 512:512 + W],
                                               srow[0:2, 0:W], AF.Ln))
                    ACT.wait_ge(sA.h, av)
                    av = sA.inc(ACT.activation(srow[0:2, 1024:1024 + W],
                                               srow[0:2, 512:512 + W],
                                               AF.Exp, scale=-0.5))
                    PE.wait_ge(sA.h, av)
                    tb2 = pp[5].acquire(PE)
                    pn2 = sP.inc(PE.matmul(tb2[:, 0:W], e2f[:],
                                           srow[0:2, 1024:1024 + W],
                                           start=True, stop=True))
                    VE.wait_ge(sP.h, pn2)
                    scol = qs_col(lidx) if isq else ks_col(lidx)
                    vchain(VE.tensor_scalar(k1scr.ap(0, W), t[:, 0:W], scol,
                                            None, op0=ALU.mult))
                    bank.consumed(sV, sV.v)
                    k2v = vchain(VE.tensor_tensor(k2scr.ap(0, W),
                                                  k1scr.ap(0, W),
                                                  tb2[:, 0:W], op=ALU.mult))
                    pp[5].consumed(sV, k2v)
                    PE.wait_ge(sV.h, k2v)
                    tr = pp[6].acquire(PE)
                    pr = sP.inc(PE.matmul(tr[:, 0:W], rpb[:], k2scr.ap(0, W),
                                          start=True, stop=True))
                    ra = rots.ap(0, W)
                    rb = rots.ap(512, 512 + W)
                    wait_stream(VE, st_r)
                    vchain(VE.tensor_tensor(tascr.ap(0, W), k2scr.ap(0, W), ra,
                                            op=ALU.mult))
                    VE.wait_ge(sP.h, pr)
                    vchain(VE.tensor_tensor(tbscr.ap(0, W), tr[:, 0:W], rb,
                                            op=ALU.mult))
                    pp[6].consumed(sV, sV.v)
                    dst = QB(hp, ta, tb_) if isq else KB(hp, ta, tb_)
                    vchain(VE.tensor_tensor(dst, tascr.ap(0, W),
                                            tbscr.ap(0, W), op=ALU.add))
            rot_used = sV.v
            for ts_ in range(W // 128):
                tvi = (ta + 128 * ts_) // 128
                t = pp[7].acquire(PE)
                pv = None
                for c in range(NCH):
                    mm = PE.matmul(t[:, 0:512],
                                   zcast.ap(512 * c + 128 * ts_,
                                            512 * c + 128 * (ts_ + 1)),
                                   wqkvb.ap(1536 * c + 1024, 1536 * c + 1536),
                                   start=(c == 0), stop=(c == NCH - 1))
                    if c == NCH - 1:
                        pv = sP.inc(mm)
                VE.wait_ge(sP.h, pv)
                vv = sV.inc(VE.tensor_copy(
                    VT(tvi).rearrange("p (h j) -> p h j", j=65)[:, :, 0:64],
                    t[:, 0:512].rearrange("p (h j) -> p h j", j=64)))
                pp[7].consumed(sV, vv)
            zc_pe = sP.v
        vt1 = vchain(VE.memset(
            vtbuf.ap().rearrange("p (t h j) -> p t h j", h=8, j=65)[:, :, :, 64:65],
            1.0))

        GP.wait_ge(sP.h, sP.v)
        dma(GP, fwob.ap().rearrange("p (c m) -> p c m", m=1024),
            di[f"fwo{lidx}"].rearrange("(c p) m -> p c m", p=128),
            f"Lwo_{tag}")
        wait_stream(PE, f"Lwo_{tag}")

        PE.wait_ge(sV.h, vt1)
        pt_cons = [0, 0]
        e0_cons = 0
        for qg, (qa, qb_) in enumerate(G4):
            W = qb_ - qa
            nk = (W + 512) // 128
            for hp in range(4):
                toA = pp[3].acquire(PE)
                toB = pp[4].acquire(PE)
                avA = avB = 0
                for kk in range(nk):
                    ka = qa - 512 + 128 * kk
                    banks = (pp[(2 * kk) % 3], pp[(2 * kk + 1) % 3])
                    for hh in range(2):
                        bank = banks[hh]
                        tdst = bank.acquire(PE)
                        PE.matmul(tdst[:, 0:W], identb[:], fmask(kk)[:, 0:W],
                                  start=True, stop=False)
                        if (qg == 0 and kk <= 3) or (qg == 1 and kk == 0):
                            PE.matmul(tdst[:, 0:W], identb[:],
                                      khm0b[:, 128:128 + W],
                                      start=False, stop=False)
                        if qg == 0 and kk == 4:
                            PE.matmul(tdst[:, 0:W], identb[:], khm0b[:, 0:W],
                                      start=False, stop=False)
                        pv = sP.inc(PE.matmul(
                            tdst[:, 0:W],
                            KB(hp, ka, ka + 128)[64 * hh:64 * (hh + 1), :],
                            QB(hp, qa, qb_)[64 * hh:64 * (hh + 1), :],
                            start=False, stop=True))
                        ACT.wait_ge(sP.h, pv)
                        ACT.wait_ge(sV.h, e0_cons)
                        aa = sA.inc(ACT.activation(e0buf.ap(0, W),
                                                   tdst[:, 0:W],
                                                   AF.Exp, scale=0.25))
                        ACT.wait_ge(sA.h, aa)
                        aa = sA.inc(ACT.activation(e0buf.ap(0, W),
                                                   e0buf.ap(0, W), AF.Square))
                        bank.consumed(sA, aa)
                        VE.wait_ge(sA.h, aa)
                        VE.wait_ge(sP.h, pt_cons[hh])
                        ptv = sV.inc(VE.tensor_tensor(
                            ptbuf.ap(512 * hh, 512 * hh + W),
                            e0buf.ap(0, W), e0buf.ap(0, W), op=ALU.mult))
                        e0_cons = ptv
                        PE.wait_ge(sV.h, ptv)
                        to = toA if hh == 0 else toB
                        h = 2 * hp + hh
                        av = sP.inc(PE.matmul(
                            to[0:65, 0:W],
                            VT((qa - 512 + 128 * kk) // 128)[:, 65 * h:65 * (h + 1)],
                            ptbuf.ap(512 * hh, 512 * hh + W),
                            start=(kk == 0), stop=(kk == nk - 1)))
                        pt_cons[hh] = av
                        if hh == 0:
                            avA = av
                        else:
                            avB = av
                VE.wait_ge(sP.h, max(avA, avB))
                vchain(VE.tensor_copy(srow[0:1, 0:512][:, 0:W], toA[64:65, 0:W]))
                lv = vchain(VE.tensor_copy(srow[0:1, 512:512 + W],
                                           toB[64:65, 0:W]))
                ACT.wait_ge(sV.h, lv)
                for seg0 in (0, 512):
                    av = sA.inc(ACT.activation(
                        srow[0:1, 1024 + seg0:1024 + seg0 + W],
                        srow[0:1, seg0:seg0 + W], AF.Ln))
                    ACT.wait_ge(sA.h, av)
                    av = sA.inc(ACT.activation(
                        srow[0:1, 1024 + seg0:1024 + seg0 + W],
                        srow[0:1, 1024 + seg0:1024 + seg0 + W],
                        AF.Exp, scale=-1.0))
                PE.wait_ge(sA.h, av)
                tl = pp[5].acquire(PE)
                PE.matmul(tl[:, 0:W], e2b0[:], srow[0:1, 1024:1024 + W],
                          start=True, stop=False)
                pl = sP.inc(PE.matmul(tl[:, 0:W], e2b1[:],
                                      srow[0:1, 1536:1536 + W],
                                      start=False, stop=True))
                VE.wait_ge(sP.h, pl)
                lrv = vchain(VE.tensor_copy(lrbuf.ap(0, W), tl[:, 0:W]))
                pp[5].consumed(sV, lrv)
                v1 = vchain(VE.tensor_tensor(
                    obq.ap(512 * hp, 512 * hp + W, p0=0, p1=64),
                    toA[0:64, 0:W], lrbuf.ap(0, W, p0=0, p1=64), op=ALU.mult))
                v2 = vchain(VE.tensor_tensor(
                    obq.ap(512 * hp, 512 * hp + W, p0=64, p1=128),
                    toB[0:64, 0:W], lrbuf.ap(0, W, p0=64, p1=128),
                    op=ALU.mult))
                pp[3].consumed(sV, v1)
                pp[4].consumed(sV, v2)
            ob_v = sV.v
            PE.wait_ge(sV.h, ob_v)
            for dc in range(NCH):
                t = pp[6 + (dc % 2)].acquire(PE)
                pv = None
                for hp in range(4):
                    mm = PE.matmul(t[:, 0:W],
                                   fwob.ap(1024 * hp + 128 * dc,
                                           1024 * hp + 128 * dc + 128),
                                   obq.ap(512 * hp, 512 * hp + W),
                                   start=(hp == 0), stop=(hp == 3))
                    if hp == 3:
                        pv = sP.inc(mm)
                VE.wait_ge(sP.h, pv)
                dst = UB(dc, qa, qb_)
                vv = vchain(VE.tensor_tensor(dst, t[:, 0:W], dst, op=ALU.add))
                if qg == 0:
                    vv = vchain(VE.tensor_scalar(UB(dc, 512, 640),
                                                 UB(dc, 512, 640), flag[:],
                                                 None, op0=ALU.mult))
                pp[6 + (dc % 2)].consumed(sV, vv)

        p.barrier(f"mlp_{tag}")
        dma(GP, w2blk.ap().rearrange("p (c m) -> p c m", m=1024),
            di[f"w2p{lidx}"].rearrange("(c p) m -> p c m", p=128),
            f"Lw2_{tag}")

        def ln2(g, slot):
            ta = 640 + 512 * g

            def get_in(c):
                off = 1 if c >= 4 else 0
                return UB(c, ta - off, ta + 512 - off)

            def put_out(c):
                return (z2roll[slot].ap(512 * c, 512 * (c + 1)),
                        lgc8[lidx].ap(c, c + 1))

            def lgrow(c):
                return lngbs[0:2, 128 * c:128 * (c + 1)]

            return ln_group(get_in, put_out, 512, ln_scr2,
                            (pp[0], pp[1], pp[6], pp[7]), gb=lgrow,
                            ready_v=sV.v)

        wblk_used = [0, 0]
        gel_cons = [0]

        def mlp(g, slot):
            ta = 640 + 512 * g
            z2 = z2roll[slot]

            def w1_load(ob):
                st_w = f"Lw1{ob % 2}_{tag}"
                if wblk_used[ob % 2]:
                    GP.wait_ge(sP.h, wblk_used[ob % 2])
                dma(GP, w1blk[ob % 2].ap().rearrange("p (c o) -> p c o", o=128),
                    di[f"w1p{lidx}"].rearrange("(c p) o -> p c o", p=128)
                    [:, :, 128 * ob:128 * (ob + 1)], st_w)
                return p.sem(st_w).v

            w1_ld = {0: w1_load(0), 1: w1_load(1)}
            for ob in range(44):
                st_w = f"Lw1{ob % 2}_{tag}"
                PE.wait_ge(p.sem(st_w).h, w1_ld[ob % 2])
                t = pp[2 + (ob % 2)].acquire(PE)
                pv = None
                for c in range(NCH):
                    mm = PE.matmul(t[:, 0:512],
                                   w1blk[ob % 2].ap(128 * c, 128 * (c + 1)),
                                   z2.ap(512 * c, 512 * (c + 1)),
                                   start=(c == 0), stop=(c == NCH - 1))
                    if c == NCH - 1:
                        pv = sP.inc(mm)
                wblk_used[ob % 2] = pv
                if ob + 2 < 44:
                    w1_ld[ob % 2] = w1_load(ob + 2)
                if ob < 22:
                    VE.wait_ge(sP.h, pv)
                    vv = sV.inc(VE.tensor_scalar(
                        abuf.ap(512 * ob, 512 * (ob + 1)), t[:, 0:512],
                        b1c[lidx].ap(ob, ob + 1), None, op0=ALU.add))
                    pp[2 + (ob % 2)].consumed(sV, vv)
                else:
                    hb = ob - 22
                    ACT.wait_ge(sP.h, pv)
                    ACT.wait_ge(sV.h, gel_cons[0])
                    aa = sA.inc(ACT.activation(gelscr.ap(0, 512), t[:, 0:512],
                                               AF.Gelu,
                                               bias=b1c[lidx].ap(ob, ob + 1)))
                    pp[2 + (ob % 2)].consumed(sA, aa)
                    VE.wait_ge(sA.h, aa)
                    gv = vchain(VE.tensor_tensor(
                        abuf.ap(512 * hb, 512 * (hb + 1)),
                        abuf.ap(512 * hb, 512 * (hb + 1)),
                        gelscr.ap(0, 512), op=ALU.mult))
                    gel_cons[0] = gv
            h_v = sV.v
            PE.wait_ge(sV.h, h_v)
            wait_stream(PE, f"Lw2_{tag}")
            for half in range(2):
                ts2 = [None] * 4
                pvs = [None] * 4
                for hc in range(22):
                    for di_ in range(4):
                        dc = 4 * half + di_
                        if hc == 0:
                            ts2[di_] = pp[2 + di_].acquire(PE)
                        mm = PE.matmul(ts2[di_][:, 0:512],
                                       w2blk.ap(1024 * hc + 128 * dc,
                                                1024 * hc + 128 * dc + 128),
                                       abuf.ap(512 * hc, 512 * (hc + 1)),
                                       start=(hc == 0), stop=(hc == 21))
                        if hc == 21:
                            pvs[di_] = sP.inc(mm)
                for di_ in range(4):
                    dc = 4 * half + di_
                    VE.wait_ge(sP.h, pvs[di_])
                    dst = UB(dc, ta, ta + 512)
                    vv = vchain(VE.scalar_tensor_tensor(
                        dst, ts2[di_][:, 0:512], b2c8[lidx].ap(dc, dc + 1),
                        dst, op0=ALU.add, op1=ALU.add))
                    pp[2 + di_].consumed(sV, vv)

        ln2(0, 0)
        ln2(1, 1)
        mlp(0, 0)
        ln2(2, 0)
        mlp(1, 1)
        ln2(3, 1)
        mlp(2, 0)
        mlp(3, 1)
        p.barrier(f"eol_{tag}")
        return sV.v

    def ag_exchange(ag_in, ag_out, tag):
        GP.wait_ge(sV.h, sV.v)
        for c in range(NCH):
            dma(GP, ag_in[128 * c:128 * (c + 1), :], UB(c, 2048, TEXT),
                f"GA_{tag}")
        wait_stream(GP, f"GA_{tag}")
        sc_ = p.sem(f"CC_{tag}")
        sc_.inc(GP.collective_compute(
            "AllGather", ALU.bypass, replica_groups=RG,
            ins=[ag_in.opt()], outs=[ag_out.opt()]), 1)
        GP.wait_ge(sc_.h, sc_.v)
        for c in range(NCH):
            dma(GP, UB(c, 0, HALO), ag_out[128 * c:128 * (c + 1), :],
                f"GH_{tag}")
        wait_stream(VE, f"GH_{tag}")
        hv = 0
        for c in range(NCH):
            hv = vchain(VE.tensor_scalar(UB(c, 0, HALO), UB(c, 0, HALO),
                                         flag[:], None, op0=ALU.mult))
        return hv

    def emit_ffn_stage(stage_tag, agi, ago):
        emit_ffn_layer(0, stage_tag)
        if stage_tag == "s2" and dbg("u1", (D, TOWN)) is not None:
            GP.wait_ge(sV.h, sV.v)
            for c in range(NCH):
                dma(GP, p.dbg_outs["u1"][128 * c:128 * (c + 1), :],
                    UB(c, HALO, TEXT), "GD")
            for e_ in (VE, GP, PE, ACT):
                wait_stream(e_, "GD")
        hv = ag_exchange(agi, ago, stage_tag)
        PE.wait_ge(sV.h, hv)
        ACT.wait_ge(sV.h, hv)
        emit_ffn_layer(1, stage_tag)

    emit_ffn_stage("s2", ag1_in, ag1_out)

    p.barrier("x2")
    a1.reset()
    xo2 = a1.alloc(16384, F32)

    def XO2(c, a, b):
        return xo2.ap(TOWN * c + a, TOWN * c + b)

    wait_stream(SY, "GS")
    for c in range(NCH):
        dma(SY, XO2(c, 0, TOWN), x_spill[128 * c:128 * (c + 1), :], "LS2")
    wait_stream(VE, "LS2")
    x2_v = 0
    for c in range(NCH):
        dst = xo2.ap(TOWN * c, TOWN * (c + 1))
        x2_v = vchain(VE.tensor_tensor(dst, dst, UB(c, HALO, TEXT),
                                       op=ALU.add))
    if dbg("x2", (D, TOWN)) is not None:
        GP.wait_ge(sV.h, x2_v)
        for c in range(NCH):
            dma(GP, p.dbg_outs["x2"][128 * c:128 * (c + 1), :],
                XO2(c, 0, TOWN), "GD")
        for e_ in (VE, GP, PE, ACT):
            wait_stream(e_, "GD")
    if upto == "x2":
        finish(p, out_x4, XO2, x2_v)
        return p

    a2.reset()
    kbb = a2.alloc(4 * S, BF16)
    vtb = a2.alloc(32 * 520, BF16)
    zrd = a2.alloc(8 * 512, BF16)
    qwn = a2.alloc(4 * 512, BF16)
    ost = a2.alloc(512, BF16)
    zown = zrd

    def KBB(hp, a, b):
        return kbb.ap(S * hp + a, S * hp + b)

    def VTB(t):
        return vtb.ap(520 * t, 520 * (t + 1))

    a3.reset()
    pib = [Seg(a3.t, 2048 * i, 8 * 512, BF16) for i in range(3)]
    pos = Seg(a3.t, 5120, 2 * 1024, BF16)

    for i, nm in enumerate(("PiQ", "PiK", "PiV")):
        dma(GP, pib[i].ap().rearrange("p (c m) -> p c m", m=512),
            di[nm].rearrange("(c p) m -> p c m", p=128), "LPi")

    for g in range(4):
        wait_stream(VE, "GZ")

        def get_in(c, g=g):
            return XO2(c, 512 * g, 512 * (g + 1))

        def put_out(c):
            return zown.ap(512 * c, 512 * (c + 1)), None

        zv = ln_group(get_in, put_out, 512, ln_scr2,
                      (pp[0], pp[1], pp[6], pp[7]), ready_v=x2_v)
        SY.wait_ge(sV.h, zv)
        for c in range(NCH):
            dma(SY, ag2z_in[128 * c:128 * (c + 1), 512 * g:512 * (g + 1)],
                zown.ap(512 * c, 512 * (c + 1)), "GZ")
    GP.wait_ge(sV.h, x2_v)
    for c in range(NCH):
        dma(GP, ag2x_in[128 * c:128 * (c + 1), :],
            XO2(c, TOWN - HALO, TOWN), "GX")
    wait_stream(GP, "GZ")
    wait_stream(GP, "GX")
    ccz = p.sem("CCz")
    ccz.inc(GP.collective_compute("AllGather", ALU.bypass, replica_groups=RG,
                                  ins=[ag2z_in.opt()], outs=[ag2z_out.opt()]), 1)
    ccz.inc(GP.collective_compute("AllGather", ALU.bypass, replica_groups=RG,
                                  ins=[ag2x_in.opt()], outs=[ag2x_out.opt()]), 1)
    GP.wait_ge(ccz.h, ccz.v)
    wait_stream(PE, "LPi")

    def zread_group(g, stream):
        sl = g // 4
        co = 512 * (g % 4)
        for c in range(NCH):
            dma(SY, zrd.ap(512 * c, 512 * (c + 1)),
                ag2z_out[1024 * sl + 128 * c:1024 * sl + 128 * (c + 1),
                         co:co + 512], stream)

    zr_pe = [0]
    for g in range(8):
        SY.wait_ge(sP.h, zr_pe[0])
        SY.wait_ge(ccz.h, 1)
        zread_group(g, "LZ")
        wait_stream(PE, "LZ")
        for hp in range(4):
            t = pp[hp % 2].acquire(PE)
            pv = None
            for c in range(NCH):
                mm = PE.matmul(t[:, 0:512],
                               pib[1].ap(512 * c + 128 * hp, 512 * c + 128 * (hp + 1)),
                               zrd.ap(512 * c, 512 * (c + 1)),
                               start=(c == 0), stop=(c == NCH - 1))
                if c == NCH - 1:
                    pv = sP.inc(mm)
            VE.wait_ge(sP.h, pv)
            vv = sV.inc(VE.tensor_copy(KBB(hp, 512 * g, 512 * (g + 1)),
                                       t[:, 0:512]))
            pp[hp % 2].consumed(sV, vv)
        for ts_ in range(4):
            tvi = 4 * g + ts_
            t = pp[2 + ts_ % 2].acquire(PE)
            pv = None
            for c in range(NCH):
                mm = PE.matmul(t[:, 0:512],
                               zrd.ap(512 * c + 128 * ts_, 512 * c + 128 * (ts_ + 1)),
                               pib[2].ap(512 * c, 512 * (c + 1)),
                               start=(c == 0), stop=(c == NCH - 1))
                if c == NCH - 1:
                    pv = sP.inc(mm)
            VE.wait_ge(sP.h, pv)
            vv = sV.inc(VE.tensor_copy(
                VTB(tvi).rearrange("p (h j) -> p h j", j=65)[:, :, 0:64],
                t[:, 0:512].rearrange("p (h j) -> p h j", j=64)))
            pp[2 + ts_ % 2].consumed(sV, vv)
        zr_pe[0] = sP.v
    vt1 = vchain(VE.memset(
        vtb.ap().rearrange("p (t h j) -> p t h j", h=8, j=65)[:, :, :, 64:65],
        1.0))
    PE.wait_ge(sV.h, vt1)

    pt_cons = [0, 0]
    e0_cons = [0]
    qw_pe = [0]
    wait_stream(SY, "LZ")
    SY.wait_ge(sP.h, zr_pe[0])
    for qw in range(8):
        SY.wait_ge(sP.h, qw_pe[0])
        zread_group(qw, "LZq")
        wait_stream(PE, "LZq")
        for hp in range(4):
            t = pp[5].acquire(PE)
            pv = None
            for c in range(NCH):
                mm = PE.matmul(t[:, 0:512],
                               pib[0].ap(512 * c + 128 * hp, 512 * c + 128 * (hp + 1)),
                               zrd.ap(512 * c, 512 * (c + 1)),
                               start=(c == 0), stop=(c == NCH - 1))
                if c == NCH - 1:
                    pv = sP.inc(mm)
            VE.wait_ge(sP.h, pv)
            vv = sV.inc(VE.tensor_copy(qwn.ap(512 * hp, 512 * (hp + 1)),
                                       t[:, 0:512]))
            pp[5].consumed(sV, vv)
        qw_pe[0] = sP.v
        PE.wait_ge(sV.h, sV.v)
        nk = 4 * (qw + 1)
        for hp in range(4):
            toA = pp[3].acquire(PE)
            toB = pp[4].acquire(PE)
            avA = avB = 0
            for kk in range(nk):
                ka = 128 * kk
                banks = (pp[(2 * kk) % 3], pp[(2 * kk + 1) % 3])
                for hh in range(2):
                    bank = banks[hh]
                    tdst = bank.acquire(PE)
                    first = True
                    if kk >= nk - 4:
                        PE.matmul(tdst[:, 0:512], identb[:],
                                  bmask(kk - (nk - 4))[:, 0:512],
                                  start=True, stop=False)
                        first = False
                    pv = sP.inc(PE.matmul(
                        tdst[:, 0:512],
                        KBB(hp, ka, ka + 128)[64 * hh:64 * (hh + 1), :],
                        qwn.ap(512 * hp, 512 * (hp + 1))[64 * hh:64 * (hh + 1), :],
                        start=first, stop=True))
                    ACT.wait_ge(sP.h, pv)
                    ACT.wait_ge(sV.h, e0_cons[0])
                    aa = sA.inc(ACT.activation(e0buf.ap(0, 512), tdst[:, 0:512],
                                               AF.Exp, scale=1.0 / 32.0))
                    ACT.wait_ge(sA.h, aa)
                    aa = sA.inc(ACT.activation(e0buf.ap(0, 512), e0buf.ap(0, 512),
                                               AF.Square))
                    bank.consumed(sA, aa)
                    VE.wait_ge(sA.h, aa)
                    VE.wait_ge(sP.h, pt_cons[hh])
                    ptv = sV.inc(VE.tensor_tensor(
                        ptbuf.ap(512 * hh, 512 * hh + 512),
                        e0buf.ap(0, 512), e0buf.ap(0, 512), op=ALU.mult))
                    e0_cons[0] = ptv
                    PE.wait_ge(sV.h, ptv)
                    to = toA if hh == 0 else toB
                    h = 2 * hp + hh
                    av = sP.inc(PE.matmul(
                        to[0:65, 0:512],
                        VTB(kk)[:, 65 * h:65 * (h + 1)],
                        ptbuf.ap(512 * hh, 512 * hh + 512),
                        start=(kk == 0), stop=(kk == nk - 1)))
                    pt_cons[hh] = av
                    if hh == 0:
                        avA = av
                    else:
                        avB = av
            VE.wait_ge(sP.h, max(avA, avB))
            vchain(VE.tensor_copy(srow[0:1, 0:512], toA[64:65, 0:512]))
            lv = vchain(VE.tensor_copy(srow[0:1, 512:1024], toB[64:65, 0:512]))
            ACT.wait_ge(sV.h, lv)
            for seg0 in (0, 512):
                av = sA.inc(ACT.activation(srow[0:1, 1024 + seg0:1536 + seg0],
                                           srow[0:1, seg0:512 + seg0], AF.Ln))
                ACT.wait_ge(sA.h, av)
                av = sA.inc(ACT.activation(srow[0:1, 1024 + seg0:1536 + seg0],
                                           srow[0:1, 1024 + seg0:1536 + seg0],
                                           AF.Exp, scale=-1.0))
            PE.wait_ge(sA.h, av)
            tl = pp[5].acquire(PE)
            PE.matmul(tl[:, 0:512], e2b0[:], srow[0:1, 1024:1536],
                      start=True, stop=False)
            pl = sP.inc(PE.matmul(tl[:, 0:512], e2b1[:], srow[0:1, 1536:2048],
                                  start=False, stop=True))
            VE.wait_ge(sP.h, pl)
            lrv = vchain(VE.tensor_copy(lrbuf.ap(0, 512), tl[:, 0:512]))
            pp[5].consumed(sV, lrv)
            wait_stream(VE, "GO")
            vchain(VE.tensor_tensor(ost.ap(0, 512, p0=0, p1=64),
                                    toA[0:64, 0:512],
                                    lrbuf.ap(0, 512, p0=0, p1=64), op=ALU.mult))
            ov = vchain(VE.tensor_tensor(ost.ap(0, 512, p0=64, p1=128),
                                         toB[0:64, 0:512],
                                         lrbuf.ap(0, 512, p0=64, p1=128),
                                         op=ALU.mult))
            pp[3].consumed(sV, ov)
            pp[4].consumed(sV, ov)
            SY.wait_ge(sV.h, ov)
            dma(SY, ag3_in[128 * hp:128 * (hp + 1), 512 * qw:512 * (qw + 1)],
                ost.ap(0, 512), "GO")

    for nm_, seg_, w_ in (("Kbb", kbb, 4 * S), ("Vtb", vtb, 32 * 520)):
        if dbg(nm_, (128, w_), BF16) is not None:
            GP.wait_ge(sV.h, sV.v)
            dma(GP, p.dbg_outs[nm_], seg_.ap(0, w_), "GD")
            wait_stream(GP, "GD")
    if dbg("Ob", (512, S), BF16) is not None:
        wait_stream(GP, "GO")
        dma(GP, p.dbg_outs["Ob"], ag3_in, "GD")
        wait_stream(GP, "GD")
    wait_stream(GP, "GO")
    cc3 = p.sem("CC3")
    cc3.inc(GP.collective_compute("AllGather", ALU.bypass, replica_groups=RG,
                                  ins=[ag3_in.opt()], outs=[ag3_out.opt()]), 1)
    GP.wait_ge(cc3.h, cc3.v)

    GP.wait_ge(sV.h, sV.v)
    for c in range(NCH):
        dma(GP, xh.ap(640 * c, 640 * (c + 1)),
            ag2x_out[128 * c:128 * (c + 1), :], "LXH")
    wait_stream(VE, "LXH")
    for c in range(NCH):
        vchain(VE.tensor_scalar(xh.ap(640 * c, 640 * (c + 1)),
                                xh.ap(640 * c, 640 * (c + 1)),
                                flag[:], None, op0=ALU.mult))
    p.barrier("ob_comb")
    a2.reset()
    ocb = a2.alloc(8 * 2688, BF16)
    ourd = [a2.alloc(2 * 512, BF16), a2.alloc(2 * 512, BF16)]
    ot1 = a2.alloc(512, BF16)

    def OCB(ic, a, b):
        return ocb.ap(2688 * ic + a, 2688 * ic + b)

    oc_tiles = [(ic, ta, tb_) for ic in range(NCH)
                for (ta, tb_) in ((1408, 1920), (1920, 2048), (2048, 2560),
                                  (2560, 3072), (3072, 3584), (3584, 4096))]
    oc_used = [0, 0]
    oc_ld = [0, 0]

    def oc_load(i):
        ic, ta, tb_ = oc_tiles[i]
        sl = i % 2
        st = f"LOr{sl}"
        if oc_used[sl]:
            SY.wait_ge(sV.h, oc_used[sl])
        for rot in range(2):
            ca = (ta + 2048 * rot) % 4096
            dma(SY, ourd[sl].ap(512 * rot, 512 * rot + (tb_ - ta)),
                ag3_out[128 * ic:128 * (ic + 1), ca:ca + (tb_ - ta)], st)
        return p.sem(st).v

    oc_ld[0] = oc_load(0)
    oc_ld[1] = oc_load(1)
    ocv = 0
    for i, (ic, ta, tb_) in enumerate(oc_tiles):
        W = tb_ - ta
        sl = i % 2
        VE.wait_ge(p.sem(f"LOr{sl}").h, oc_ld[sl])
        vchain(VE.tensor_scalar(ot1.ap(0, W), ourd[sl].ap(0, W),
                                flag[:], None, op0=ALU.mult))
        ocv = vchain(VE.scalar_tensor_tensor(
            OCB(ic, ta - 1408, tb_ - 1408), ourd[sl].ap(512, 512 + W),
            flagn[:], ot1.ap(0, W), op0=ALU.mult, op1=ALU.add))
        oc_used[sl] = ocv
        if i + 2 < len(oc_tiles):
            oc_ld[sl] = oc_load(i + 2)
    pob = Seg(a3.t, 5120, 2 * 1024, BF16)
    ytiles = [(0, 512, "h"), (512, 640, "h"), (640, 1152, "o"),
              (1152, 1664, "o"), (1664, 2176, "o"), (2176, 2688, "o")]
    x3_v = 0
    po_used = [0, 0]
    for dc in range(NCH):
        st_p = f"LPo{dc % 2}"
        if po_used[dc % 2]:
            GP.wait_ge(sP.h, po_used[dc % 2])
        dma(GP, pob.ap(1024 * (dc % 2), 1024 * (dc % 2) + 1024)
            .rearrange("p (c m) -> p c m", m=128),
            di["Po"].rearrange("(c p) m -> p c m", p=128)
            [:, :, 128 * dc:128 * (dc + 1)], st_p)
        wait_stream(PE, st_p)
        PE.wait_ge(sV.h, ocv)
        for ti, (ta, tb_, kind) in enumerate(ytiles):
            W = tb_ - ta
            t = pp[ti % 4].acquire(PE)
            pv = None
            for ic in range(NCH):
                mm = PE.matmul(t[:, 0:W],
                               pob.ap(1024 * (dc % 2) + 128 * ic,
                                      1024 * (dc % 2) + 128 * (ic + 1)),
                               OCB(ic, ta, tb_),
                               start=(ic == 0), stop=(ic == NCH - 1))
                if ic == NCH - 1:
                    pv = sP.inc(mm)
            po_used[dc % 2] = pv
            VE.wait_ge(sP.h, pv)
            if kind == "h":
                dst = xh.ap(640 * dc + ta, 640 * dc + tb_)
                vchain(VE.tensor_tensor(dst, t[:, 0:W], dst, op=ALU.add))
                x3_v = vchain(VE.tensor_scalar(dst, dst, flag[:], None,
                                               op0=ALU.mult))
            else:
                dst = XO2(dc, ta - 640, tb_ - 640)
                x3_v = vchain(VE.tensor_tensor(dst, t[:, 0:W], dst, op=ALU.add))
            pp[ti % 4].consumed(sV, x3_v)

    if dbg("x3", (D, TOWN)) is not None:
        GP.wait_ge(sV.h, x3_v)
        for c in range(NCH):
            dma(GP, p.dbg_outs["x3"][128 * c:128 * (c + 1), :],
                XO2(c, 0, TOWN), "GD")
        for e_ in (VE, GP, PE, ACT):
            wait_stream(e_, "GD")
    if upto == "x3":
        finish(p, out_x4, XO2, x3_v)
        return p

    GP.wait_ge(sV.h, x3_v)
    for c in range(NCH):
        dma(GP, x_spill[128 * c:128 * (c + 1), :], XO2(c, 0, TOWN), "GS2")
    for e_ in (GP, SY, PE, VE, ACT):
        wait_stream(e_, "GS2")
    p.barrier("z0p")
    a2.reset()
    a2.off = NCH * UBW
    for c in range(NCH):
        vchain(VE.memset(ubt.ap(UBW * c, UBW * c + 1), 0.0))

    ln_tiles2 = [(0, 512), (512, 640), (640, 1152), (1152, 1664),
                 (1664, 2176), (2176, 2688)]
    z0p_v = 0
    for (ta, tb_) in ln_tiles2:
        W = tb_ - ta

        def get_in(c, ta=ta, tb_=tb_):
            if tb_ <= 640:
                return xh.ap(640 * c + ta, 640 * c + tb_)
            return XO2(c, ta - 640, tb_ - 640)

        def put_out(c, ta=ta, tb_=tb_):
            return UB(c, ta, tb_), None

        z0p_v = ln_group(get_in, put_out, W, ln_scr2,
                         (pp[4], pp[5], pp[6], pp[7]), ready_v=x3_v)
    p.barrier("s4")

    emit_ffn_stage("s4", ag4_in, ag4_out)

    p.barrier("x4")
    for c in range(NCH):
        dma(SY, XO2(c, 0, TOWN), x_spill[128 * c:128 * (c + 1), :], "LS4")
    wait_stream(VE, "LS4")
    x4_v = 0
    for c in range(NCH):
        dst = xo2.ap(TOWN * c, TOWN * (c + 1))
        x4_v = vchain(VE.tensor_tensor(dst, dst, UB(c, HALO, TEXT),
                                       op=ALU.add))
    finish(p, out_x4, XO2, x4_v)
    return p


def finish(p, out_x4, XO, ready_v):
    nc = p.nc
    sO = p.sem("O")
    nc.gpsimd.wait_ge(p.sem("V").h, ready_v)
    for c in range(NCH):
        sO.inc(nc.gpsimd.dma_start(out=out_x4[128 * c:128 * (c + 1), :],
                                   in_=XO(c, 0, TOWN)), 16)
    for e in (nc.gpsimd, nc.sync, nc.tensor, nc.vector, nc.scalar):
        e.wait_ge(sO.h, sO.v)
        if "GD" in p._sems:
            e.wait_ge(p.sem("GD").h, p.sem("GD").v)


def kernel(**inputs):
    in_maps = [_stage_core(inputs, c) for c in range(8)]
    p = build(upto=BUILD_UPTO, debug=DEBUG_OUT)
    res = run_bass_kernel_spmd(p.nc, in_maps, core_ids=list(range(8)))
    out = np.empty((B, S, D), dtype=np.float32)
    for c in range(8):
        b, half = c // 2, c % 2
        out[b, 2048 * half:2048 * half + 2048] = res.results[c]["o_x4"].T
    return out


BUILD_UPTO = "full"
DEBUG_OUT = ()


# version 04
# speedup vs baseline: 1.3473x; 1.3345x over previous
import contextlib
import numpy as np
import ml_dtypes

import concourse.bass as bass
import concourse.mybir as mybir
from concourse.bass_utils import run_bass_kernel_spmd

F32 = mybir.dt.float32
BF16 = mybir.dt.bfloat16
AF = mybir.ActivationFunctionType
ALU = mybir.AluOpType
AX = mybir.AxisListType

B, S, D = 4, 4096, 1024
NCH = 8
TOWN = 2048
HALO = 640
TEXT = HALO + TOWN
M = 128
CC = 128
DECAY, LR = 0.99, 0.01
DEC_C = float(DECAY ** CC)
NEG = -1.0e30
FH, FDH, WIN = 8, 64, 512
HID, HIDP = 2730, 2816
EPS = 1e-5

_f32 = np.float32
_bf16 = ml_dtypes.bfloat16


def _host_consts(half):
    c = {}
    i = np.arange(CC)
    dmT = np.where(i[:, None] < i[None, :],
                   LR * DECAY ** (-1.0 - i[:, None]), 0.0)
    c["dmaskT"] = (dmT * np.ones((CC, CC))).astype(_f32)
    c["decq512"] = np.broadcast_to(
        DECAY ** (np.arange(512) % 128), (128, 512)).astype(_f32)
    c["deck"] = (LR * DECAY ** (127.0 - i)).reshape(128, 1).astype(_f32)
    c["identity"] = np.eye(128, dtype=_f32)
    R = np.zeros((128, 128), dtype=_f32)
    for h0 in (0, 64):
        for k in range(32):
            R[h0 + 32 + k, h0 + k] = -1.0
            R[h0 + k, h0 + 32 + k] = 1.0
    c["Rperm"] = R
    oc2 = np.zeros((128, 2), dtype=_f32)
    oc2[:64, 0] = 1.0
    oc2[64:, 1] = 1.0
    c["ones2col"] = oc2
    e2 = np.zeros((2, 128), dtype=_f32)
    e2[0, :64] = 1.0
    e2[1, 64:] = 1.0
    c["e2"] = e2
    c["flag"] = np.full((128, 1), float(half), dtype=_f32)
    c["flagn"] = np.full((128, 1), 1.0 - float(half), dtype=_f32)
    c["epsc"] = np.full((128, 1), EPS, dtype=_f32)
    c["ones512"] = np.ones((1, 512), dtype=_f32)

    t0 = 2048 * half
    pos = np.arange(t0 - HALO, t0 + TOWN, dtype=np.float64)
    half_idx = np.arange(0, FDH, 2, dtype=np.float64)
    inv_freq = 1.0 / 10000.0 ** (half_idx / FDH)
    freqs = np.concatenate([pos[:, None] * inv_freq[None, :]] * 2, axis=1)
    base = (half_idx + 0.4 * FDH) / (1.4 * FDH)
    power = (pos - S // 2) / float(WIN // 2)
    scale = np.concatenate([base[None, :] ** power[:, None]] * 2, axis=1)
    cos, sin = np.cos(freqs), np.sin(freqs)
    for nm, tb in (("rotAq", cos * scale * 8.0), ("rotBq", sin * scale * 8.0),
                   ("rotAk", cos / scale), ("rotBk", sin / scale)):
        c[nm] = np.tile(tb.T.astype(_f32), (2, 1))

    kl = np.arange(128)[:, None]
    cf = np.arange(1408)[None, :]
    df = kl - cf + 384
    c["fmaskM"] = np.where((df > 0) | (df < -512), NEG, 0.0).astype(_f32)
    km = np.zeros((128, 640), dtype=_f32)
    if half == 0:
        km[:, 128:] = NEG
    c["khmask0"] = km
    return c


def _stage_core(inputs, core):
    b, half = core // 2, core % 2
    t0 = 2048 * half
    x = inputs["L0_attn_a_x"][b]
    d = dict(_host_consts(half))
    d["x_own"] = x[t0:t0 + TOWN].T
    d["x_pre"] = x[0:2048].T if half else np.zeros((D, 2048), dtype=_f32)
    d["x_halo"] = x[t0 - HALO:t0].T if half else np.zeros((D, HALO), dtype=_f32)
    d["wkT"] = inputs["L0_attn_a_Wk"].T
    d["wvT"] = inputs["L0_attn_a_Wv"].T
    d["wqT"] = inputs["L0_attn_a_Wq"].T
    d["woT"] = inputs["L0_attn_a_Wo"].T
    d["bk"] = inputs["L0_attn_a_bk"].reshape(128, 1)
    d["bq"] = inputs["L0_attn_a_bq"].reshape(128, 1)
    d["bkrow"] = inputs["L0_attn_a_bk"].reshape(1, 128)
    d["bvrow"] = inputs["L0_attn_a_bv"].reshape(1, 128)
    d["bo"] = inputs["L0_attn_a_bo"].reshape(D, 1)
    for l in range(2):
        d[f"wqkv{l}"] = inputs["L0_ffn_wqkv"][l]
        d[f"fwo{l}"] = inputs["L0_ffn_wo"][l]
        w1 = inputs["L0_ffn_w1"][l]
        w1p = np.zeros((D, 2 * HIDP), dtype=_f32)
        w1p[:, :HID] = w1[:, :HID]
        w1p[:, HIDP:HIDP + HID] = w1[:, HID:]
        d[f"w1p{l}"] = w1p
        b1 = inputs["L0_ffn_b1"][l]
        b1p = np.zeros((2 * HIDP, 1), dtype=_f32)
        b1p[:HID, 0] = b1[:HID]
        b1p[HIDP:HIDP + HID, 0] = b1[HID:]
        d[f"b1p{l}"] = b1p
        w2p = np.zeros((HIDP, D), dtype=_f32)
        w2p[:HID] = inputs["L0_ffn_w2"][l]
        d[f"w2p{l}"] = w2p
        d[f"b2_{l}"] = inputs["L0_ffn_b2"][l].reshape(D, 1)
        d[f"lng{l}"] = inputs["L0_ffn_ln_g"][l].reshape(D, 1)
        d[f"lnb{l}"] = inputs["L0_ffn_ln_b"][l].reshape(D, 1)
        d[f"lngb{l}"] = np.stack([inputs["L0_ffn_ln_g"][l],
                                  inputs["L0_ffn_ln_b"][l]])
        d[f"qs{l}"] = np.tile(inputs["L0_ffn_q_scale"][l], 2).reshape(128, 1)
        d[f"ks{l}"] = np.tile(inputs["L0_ffn_k_scale"][l], 2).reshape(128, 1)
    Pi = inputs["L0_attn_b_P_i"]
    myh = range(8 * half, 8 * half + 8)
    d["PiQ"] = np.concatenate([Pi[:, 192 * h:192 * h + 64] for h in myh], axis=1)
    d["PiK"] = np.concatenate([Pi[:, 192 * h + 64:192 * h + 128] for h in myh], axis=1)
    d["PiV"] = np.concatenate([Pi[:, 192 * h + 128:192 * h + 192] for h in myh], axis=1)
    d["Po"] = inputs["L0_attn_b_P_o"]
    return {k: np.ascontiguousarray(v, dtype=_f32) for k, v in d.items()}


class Sem:
    def __init__(self, handle):
        self.h = handle
        self.v = 0

    def inc(self, inst, n=1):
        inst.then_inc(self.h, n)
        self.v += n
        return self.v


class Seg:

    def __init__(self, t, off, cols, dtype):
        self.t, self.off, self.cols, self.dtype = t, off, cols, dtype

    def ap(self, a=0, b=None, p0=0, p1=128):
        b = self.cols if b is None else b
        if self.dtype == F32:
            return self.t[p0:p1, self.off + a:self.off + b]
        assert self.off % 1 == 0
        base = self.t[p0:p1, self.off:self.off + (self.cols + 1) // 2]
        return base.bitcast(BF16)[:, a:b]


class Arena:
    def __init__(self, p, name, cols):
        self.t = p.sb(name, [128, cols], F32)
        self.cols = cols
        self.off = 0
        self.name = name

    def alloc(self, cols, dtype=F32):
        fcols = cols if dtype == F32 else (cols + 1) // 2
        assert self.off + fcols <= self.cols, \
            f"arena {self.name} overflow: {self.off}+{fcols} > {self.cols}"
        s = Seg(self.t, self.off, cols, dtype)
        self.off += fcols
        return s

    def reset(self, off=0):
        self.off = off


class Prog:
    def __init__(self):
        self.nc = bass.Bass("TRN2", target_bir_lowering=False, debug=False)
        self.ctx = contextlib.ExitStack()
        self._sems = {}
        self.di = {}
        self.dbg_outs = {}
        self.debug = ()

    def sem(self, name):
        if name not in self._sems:
            self._sems[name] = Sem(self.ctx.enter_context(self.nc.semaphore(name)))
        return self._sems[name]

    def sb(self, name, shape, dtype):
        return self.ctx.enter_context(self.nc.sbuf_tensor(name, shape, dtype))

    def psum(self, name, shape, dtype=F32):
        return self.ctx.enter_context(self.nc.psum_tensor(name, shape, dtype))

    def barrier(self, tag):
        s = self.sem("bar")
        nc = self.nc
        engs = [nc.sync, nc.tensor, nc.vector, nc.scalar, nc.gpsimd]
        base = s.v
        for e in engs:
            s.inc(e.nop(), 1)
        for e in engs:
            e.wait_ge(s.h, base + 5)


class PsBank:

    def __init__(self, t):
        self.t = t
        self.free_at = None

    def acquire(self, eng):
        if self.free_at is not None:
            sem, v = self.free_at
            eng.wait_ge(sem.h, v)
            self.free_at = None
        return self.t

    def consumed(self, sem, v):
        self.free_at = (sem, v)


def build(upto="full", debug=()):
    p = Prog()
    p.debug = tuple(debug)
    nc = p.nc
    di = p.di

    def dram_in(name, shape, dtype=F32):
        di[name] = nc.dram_tensor(name, list(shape), dtype,
                                  kind="ExternalInput").ap()

    for nm, sh in [
        ("x_own", (D, TOWN)), ("x_pre", (D, 2048)), ("x_halo", (D, HALO)),
        ("flag", (128, 1)), ("flagn", (128, 1)),
        ("epsc", (128, 1)), ("ones512", (1, 512)),
        ("wkT", (D, 128)), ("wvT", (D, 128)), ("wqT", (D, 128)), ("woT", (128, D)),
        ("bk", (128, 1)), ("bq", (128, 1)), ("bkrow", (1, 128)), ("bvrow", (1, 128)),
        ("bo", (D, 1)),
        ("dmaskT", (128, 128)), ("decq512", (128, 512)), ("deck", (128, 1)),
        ("identity", (128, 128)), ("Rperm", (128, 128)),
        ("ones2col", (128, 2)), ("e2", (2, 128)),
        ("rotAq", (128, TEXT)), ("rotBq", (128, TEXT)),
        ("rotAk", (128, TEXT)), ("rotBk", (128, TEXT)),
        ("khmask0", (128, 640)),
        ("PiQ", (D, 512)), ("PiK", (D, 512)), ("PiV", (D, 512)), ("Po", (D, D)),
    ]:
        dram_in(nm, sh)
    dram_in("fmaskM", (128, 1408))
    for l in range(2):
        for nm, sh in [
            (f"wqkv{l}", (D, 1536)), (f"fwo{l}", (512, D)),
            (f"w1p{l}", (D, 2 * HIDP)), (f"b1p{l}", (2 * HIDP, 1)),
            (f"w2p{l}", (HIDP, D)), (f"b2_{l}", (D, 1)),
            (f"lng{l}", (D, 1)), (f"lnb{l}", (D, 1)),
            (f"lngb{l}", (2, D)),
            (f"qs{l}", (128, 1)), (f"ks{l}", (128, 1)),
        ]:
            dram_in(nm, sh)

    out_x4 = nc.dram_tensor("o_x4", [D, TOWN], F32, kind="ExternalOutput").ap()

    def dbg(name, shape, dtype=F32):
        if name in p.debug and name not in p.dbg_outs:
            p.dbg_outs[name] = nc.dram_tensor(
                "dbg_" + name, list(shape), dtype, kind="ExternalOutput").ap()
        return p.dbg_outs.get(name)

    x_spill = nc.dram_tensor("x_spill", [D, TOWN], F32, kind="Internal").ap()
    ag1_in = nc.dram_tensor("ag1_in", [D, HALO], BF16, kind="Internal").ap()
    ag1_out = nc.dram_tensor("ag1_out", [2 * D, HALO], BF16, kind="Internal").ap()
    ag2z_in = nc.dram_tensor("ag2z_in", [D, TOWN], BF16, kind="Internal").ap()
    ag2z_out = nc.dram_tensor("ag2z_out", [2 * D, TOWN], BF16, kind="Internal").ap()
    ag2x_in = nc.dram_tensor("ag2x_in", [D, HALO], BF16, kind="Internal").ap()
    ag2x_out = nc.dram_tensor("ag2x_out", [2 * D, HALO], BF16, kind="Internal").ap()
    ag3_in = nc.dram_tensor("ag3_in", [512, S], BF16, kind="Internal").ap()
    ag3_out = nc.dram_tensor("ag3_out", [1024, S], BF16, kind="Internal").ap()
    ag4_in = nc.dram_tensor("ag4_in", [D, HALO], BF16, kind="Internal").ap()
    ag4_out = nc.dram_tensor("ag4_out", [2 * D, HALO], BF16, kind="Internal").ap()
    RG = [[0, 1], [2, 3], [4, 5], [6, 7]]

    a1 = Arena(p, "a1", 19544)
    a2 = Arena(p, "a2", NCH * (TEXT + 1))
    a3 = Arena(p, "a3", 6656)
    sc = Arena(p, "sc", 768)
    flag = p.sb("flag_s", [128, 1], F32)
    flagn = p.sb("flagn_s", [128, 1], F32)
    epsc = p.sb("epsc_s", [128, 1], F32)
    ones1 = p.sb("ones1", [1, 128], F32)
    ones1b = p.sb("ones1b", [1, 128], BF16)
    onescol = p.sb("onescol", [128, 1], BF16)
    identb = p.sb("identb", [128, 128], BF16)
    srow = p.sb("srow", [2, 2048], F32)
    masksF = p.sb("masksF", [128, 1408], BF16)

    pp = [PsBank(p.psum(f"pp{i}", [128, 512])) for i in range(8)]

    block = p.ctx.enter_context(nc.Block())
    SY, PE, VE, ACT, GP = nc.sync, nc.tensor, nc.vector, nc.scalar, nc.gpsimd
    sP = p.sem("P")
    sV = p.sem("V")
    sA = p.sem("A")

    def dma(eng, out, in_, stream, **kw):
        sm = p.sem(stream)
        sm.inc(eng.dma_start(out=out, in_=in_, **kw), 16)
        return sm.v

    def wait_stream(eng, stream):
        sm = p.sem(stream)
        eng.wait_ge(sm.h, sm.v)

    def vchain(inst):
        vv = sV.inc(inst)
        VE.wait_ge(sV.h, vv)
        return vv

    xo = a1.alloc(16384, F32)

    def XO(c, a, b):
        return xo.ap(TOWN * c + a, TOWN * c + b)

    dma(SY, flag[:], di["flag"], "L1")
    dma(SY, flagn[:], di["flagn"], "L1")
    dma(SY, epsc[:], di["epsc"], "L1")
    dma(GP, identb[:], di["identity"], "G1")
    dma(GP, ones1b[:], di["ones512"][0:1, 0:128], "G1")
    for c in range(NCH):
        dma(SY, XO(c, 0, TOWN), di["x_own"][128 * c:128 * (c + 1), :], "L1")
    dma(GP, masksF[:], di["fmaskM"], "G1")

    def fmask(kk):
        return masksF[:, 896 - 128 * kk:1408 - 128 * kk]

    def bmask(j):
        return masksF[:, 384 - 128 * j:896 - 128 * j]
    dma(SY, srow[1:2, 1536:2048], di["ones512"], "L1")
    dma(SY, ones1[:], di["ones512"][0:1, 0:128], "L1")
    VE.memset(onescol[:], 1.0)

    def ln_group(get_in, put_out, W, scratch, banks, gb=None, ready_v=None):
        stA, stB, bcA, bcB = banks
        xbs, sqs, t1s = scratch["xb"], scratch["sq"], scratch["t1"]
        if ready_v is not None:
            ACT.wait_ge(sV.h, ready_v)
        tA = stA.acquire(PE)
        tB = stB.acquire(PE)
        slot_pe = scratch.setdefault("slot_pe", [0, 0])
        for c in range(NCH):
            xin = get_in(c)
            xb = xbs[c % 2]
            sq = sqs[c % 2]
            VE.wait_ge(sP.h, slot_pe[c % 2])
            ACT.wait_ge(sP.h, slot_pe[c % 2])
            vv = sV.inc(VE.tensor_copy(xb.ap(0, W), xin))
            aa = sA.inc(ACT.activation(sq.ap(0, W), xin, AF.Square))
            PE.wait_ge(sV.h, vv)
            PE.wait_ge(sA.h, aa)
            PE.matmul(tA[0:1, 0:W], onescol[:], xb.ap(0, W),
                      start=(c == 0), stop=(c == NCH - 1))
            pv = sP.inc(PE.matmul(tB[0:1, 0:W], onescol[:], sq.ap(0, W),
                                  start=(c == 0), stop=(c == NCH - 1)))
            slot_pe[c % 2] = pv
        pe_done = sP.v
        mu = srow[0:1, 0:0 + W]
        msq = srow[0:1, 512:512 + W]
        var = srow[0:1, 1024:1024 + W]
        sd = srow[0:1, 512:512 + W]
        rst = srow[0:1, 1024:1024 + W]
        nmb = srow[0:1, 1536:1536 + W]
        VE.wait_ge(sP.h, pe_done)
        vv = sV.inc(VE.tensor_scalar(mu, tA[0:1, 0:W], 1.0 / D, None,
                                     op0=ALU.mult))
        ACT.wait_ge(sV.h, vv)
        av = sA.inc(ACT.activation(msq, mu, AF.Square))
        VE.wait_ge(sA.h, av)
        vv = sV.inc(VE.scalar_tensor_tensor(
            var, tB[0:1, 0:W], 1.0 / D, msq,
            op0=ALU.mult, op1=ALU.subtract))
        stA.consumed(sV, vv)
        stB.consumed(sV, vv)
        ACT.wait_ge(sV.h, vv)
        av = sA.inc(ACT.activation(sd, var, AF.Ln, bias=epsc[0:1, :]))
        ACT.wait_ge(sA.h, av)
        av = sA.inc(ACT.activation(rst, sd, AF.Exp, scale=-0.5))
        VE.wait_ge(sA.h, av)
        vv = vchain(VE.scalar_tensor_tensor(nmb, mu, -1.0, rst,
                                            op0=ALU.mult, op1=ALU.mult))
        tbcA = bcA.acquire(PE)
        PE.wait_ge(sV.h, vv)
        if gb is not None:
            vv = vchain(VE.tensor_copy(scratch["lnrb"][0:1, 0:W], nmb))
            PE.wait_ge(sV.h, vv)
        pva = sP.inc(PE.matmul(tbcA[:, 0:W], ones1[:], rst,
                               start=True, stop=True))
        if gb is None:
            tbcB = bcB.acquire(PE)
            pvb = sP.inc(PE.matmul(tbcB[:, 0:W], ones1[:], nmb,
                                   start=True, stop=True))
            VE.wait_ge(sP.h, pvb)
        VE.wait_ge(sP.h, pva)
        last = None
        for c in range(NCH):
            xin = get_in(c)
            t1 = t1s.ap(0, W)
            if gb is not None:
                tbcB = bcB.acquire(PE)
                pvb = sP.inc(PE.matmul(tbcB[:, 0:W], gb(c),
                                       scratch["lnrb"][0:2, 0:W],
                                       start=True, stop=True))
            vchain(VE.tensor_tensor(t1, xin, tbcA[:, 0:W], op=ALU.mult))
            dst, lgcol = put_out(c)
            if gb is None:
                last = vchain(VE.scalar_tensor_tensor(
                    dst, t1, 1.0, tbcB[:, 0:W], op0=ALU.mult, op1=ALU.add))
            else:
                VE.wait_ge(sP.h, pvb)
                last = vchain(VE.scalar_tensor_tensor(
                    dst, t1, lgcol, tbcB[:, 0:W], op0=ALU.mult, op1=ALU.add))
                bcB.consumed(sV, last)
        bcA.consumed(sV, last)
        bcB.consumed(sV, last)
        return last

    za = a2.alloc(4096, BF16)
    kfm = a2.alloc(4096, BF16)
    qd0 = a2.alloc(3072, BF16)
    ktm = a2.alloc(4096, BF16)
    vtm = a2.alloc(4096, BF16)
    obuf = a2.alloc(2688, BF16)
    wob = a2.alloc(1024, BF16)
    awk = a2.alloc(512, F32)
    xpA = a2.alloc(4096, F32)
    xpB = a2.alloc(4096, F32)

    class _QD:
        def ap(self, a, b):
            return qd0.ap(a - 1024, b - 1024)
    qd = _QD()
    xh = a3.alloc(5120, F32)
    wkb = a3.alloc(1024, BF16)
    wvb = a3.alloc(1024, BF16)
    wqb = a3.alloc(1024, BF16)
    dmk = a1.alloc(128, F32)
    stf = a1.alloc(128, F32)
    stb = a1.alloc(128, BF16)
    wts = [a1.alloc(128, BF16), a1.alloc(128, BF16)]
    ktds = [a1.alloc(128, BF16), a1.alloc(128, BF16)]
    csm = a1.alloc(16, F32)
    bo8 = a1.alloc(8, F32)
    xb_s = [a1.alloc(512, BF16), a1.alloc(512, BF16)]
    sq_s = [a1.alloc(512, BF16), a1.alloc(512, BF16)]
    t1_s = a1.alloc(512, F32)

    brow_k = p.sb("brow_k", [1, 128], BF16)
    brow_v = p.sb("brow_v", [1, 128], BF16)

    for (dst, src) in ((wkb, "wkT"), (wvb, "wvT"), (wqb, "wqT")):
        dma(GP, dst.ap().rearrange("p (c m) -> p c m", m=128),
            di[src].rearrange("(c p) m -> p c m", p=128), "G1")
    dma(GP, wob.ap(), di["woT"], "G1")
    dma(GP, brow_k[:], di["bkrow"], "G1")
    dma(GP, brow_v[:], di["bvrow"], "G1")
    dma(SY, awk.ap(), di["decq512"], "L1")
    dma(SY, dmk.ap(), di["dmaskT"], "L1")
    dma(SY, csm.ap(0, 1), di["bk"], "L1")
    dma(SY, csm.ap(1, 2), di["bq"], "L1")
    dma(SY, csm.ap(2, 3), di["deck"], "L1")
    dma(SY, bo8.ap().rearrange("p (c o) -> p c o", o=1),
        di["bo"].rearrange("(c p) o -> p c o", p=128),
        allow_slow_non_contiguous=True, stream="L1")
    bk_col = lambda: csm.ap(0, 1)
    bq_col = lambda: csm.ap(1, 2)
    deck_col = lambda: csm.ap(2, 3)

    def load_xpre(g, seg, slot):
        for c in range(NCH):
            dma(SY, seg.ap(512 * c, 512 * (c + 1)),
                di["x_pre"][128 * c:128 * (c + 1), 512 * g:512 * (g + 1)],
                f"Lx{slot}")
        return p.sem(f"Lx{slot}").v

    ln_scr = {"xb": xb_s, "sq": sq_s, "t1": t1_s}

    xp_ready = [None, None]
    xp_ready[0] = load_xpre(0, xpA, 0)
    xp_ready[1] = load_xpre(1, xpB, 1)

    for e in (VE, PE, ACT):
        wait_stream(e, "L1")
        wait_stream(e, "G1")

    for g in range(8):
        own = g >= 4
        if not own:
            seg = (xpA, xpB)[g % 2]
            sm = p.sem(f"Lx{g % 2}")
            VE.wait_ge(sm.h, xp_ready[g % 2])
            ACT.wait_ge(sm.h, xp_ready[g % 2])
            get_in = lambda c, seg=seg: seg.ap(512 * c, 512 * c + 512)
        else:
            go = 512 * (g - 4)
            get_in = lambda c, go=go: XO(c, go, go + 512)

        def put_out(c):
            return za.ap(512 * c, 512 * c + 512), None

        lnv = ln_group(get_in, put_out, 512, ln_scr,
                       (pp[0], pp[1], pp[2], pp[3]))

        if not own and g + 2 < 4:
            SY.wait_ge(sV.h, lnv)
            xp_ready[g % 2] = load_xpre(g + 2, (xpA, xpB)[g % 2], g % 2)

        PE.wait_ge(sV.h, lnv)
        gs = 512 * g
        t = pp[4].acquire(PE)
        pv = None
        for c in range(NCH):
            mm = PE.matmul(t[:, 0:512], wkb.ap(128 * c, 128 * (c + 1)),
                           za.ap(512 * c, 512 * (c + 1)),
                           start=(c == 0), stop=(c == NCH - 1))
            if c == NCH - 1:
                pv = sP.inc(mm)
        VE.wait_ge(sP.h, pv)
        vv = sV.inc(VE.tensor_scalar(kfm.ap(gs, gs + 512), t[:, 0:512],
                                     bk_col(), None, op0=ALU.add))
        pp[4].consumed(sV, vv)
        if g >= 2:
            t = pp[5].acquire(PE)
            pv = None
            for c in range(NCH):
                mm = PE.matmul(t[:, 0:512], wqb.ap(128 * c, 128 * (c + 1)),
                               za.ap(512 * c, 512 * (c + 1)),
                               start=(c == 0), stop=(c == NCH - 1))
                if c == NCH - 1:
                    pv = sP.inc(mm)
            VE.wait_ge(sP.h, pv)
            vchain(VE.tensor_scalar(t1_s.ap(), t[:, 0:512], bq_col(), None,
                                    op0=ALU.add))
            vv = sV.inc(VE.tensor_tensor(qd.ap(gs, gs + 512), t1_s.ap(),
                                         awk.ap(), op=ALU.mult))
            pp[5].consumed(sV, vv)
        for srowb, wrow, dstseg in ((brow_k, wkb, ktm), (brow_v, wvb, vtm)):
            for ts in range(4):
                t = pp[6 if dstseg is ktm else 7].acquire(PE)
                sub = gs + 128 * ts
                for c in range(NCH):
                    PE.matmul(t[:, 0:128],
                              za.ap(512 * c + 128 * ts, 512 * c + 128 * (ts + 1)),
                              wrow.ap(128 * c, 128 * (c + 1)),
                              start=(c == 0), stop=False)
                pv = sP.inc(PE.matmul(t[:, 0:128], ones1b[:], srowb[:],
                                      start=False, stop=True))
                VE.wait_ge(sP.h, pv)
                vv = sV.inc(VE.tensor_copy(dstseg.ap(sub, sub + 128), t[:, 0:128]))
                (pp[6] if dstseg is ktm else pp[7]).consumed(sV, vv)

    vchain(VE.memset(stf.ap(), 0.0))
    vv = vchain(VE.tensor_copy(stb.ap(), stf.ap()))
    st_ready = vv
    for cch in range(32):
        cs = 128 * cch
        do_out = cch >= 11
        if do_out:
            t_at = pp[0].acquire(PE)
            pv = sP.inc(PE.matmul(t_at[:, 0:128], kfm.ap(cs, cs + 128),
                                  qd.ap(cs, cs + 128), start=True, stop=True))
            wt = wts[cch % 2]
            VE.wait_ge(sP.h, pv)
            vv = sV.inc(VE.tensor_tensor(wt.ap(), t_at[:, 0:128], dmk.ap(),
                                         op=ALU.mult))
            pp[0].consumed(sV, vv)
            wt_v = vv
        ktd = ktds[cch % 2]
        vv = sV.inc(VE.tensor_scalar(ktd.ap(), ktm.ap(cs, cs + 128),
                                     deck_col(), None, op0=ALU.mult))
        ktd_v = vv
        if do_out:
            t_o = pp[1].acquire(PE)
            PE.wait_ge(sV.h, max(st_ready, wt_v))
            PE.matmul(t_o[:, 0:128], stb.ap(), qd.ap(cs, cs + 128),
                      start=True, stop=False)
            pv_o = sP.inc(PE.matmul(t_o[:, 0:128], vtm.ap(cs, cs + 128), wt.ap(),
                                    start=False, stop=True))
        t_u = pp[2].acquire(PE)
        PE.wait_ge(sV.h, ktd_v)
        pv_u = sP.inc(PE.matmul(t_u[:, 0:128], ktd.ap(), vtm.ap(cs, cs + 128),
                                start=True, stop=True))
        VE.wait_ge(sP.h, pv_u)
        vchain(VE.scalar_tensor_tensor(stf.ap(), stf.ap(), DEC_C, t_u[:, 0:128],
                                       op0=ALU.mult, op1=ALU.add))
        vv = vchain(VE.tensor_copy(stb.ap(), stf.ap()))
        pp[2].consumed(sV, vv)
        st_ready = vv
        if cch == 15:
            vchain(VE.tensor_scalar(stf.ap(), stf.ap(), flag[:],
                                    None, op0=ALU.mult))
            st_ready = vchain(VE.tensor_copy(stb.ap(), stf.ap()))
        if do_out:
            VE.wait_ge(sP.h, pv_o)
            vv = sV.inc(VE.tensor_copy(obuf.ap(cs - 1408, cs - 1408 + 128),
                                       t_o[:, 0:128]))
            pp[1].consumed(sV, vv)
    obuf_v = sV.v

    for c in range(NCH):
        dma(SY, xh.ap(640 * c, 640 * (c + 1)),
            di["x_halo"][128 * c:128 * (c + 1), :], "LH")
    PE.wait_ge(sV.h, obuf_v)
    wait_stream(VE, "LH")
    tiles = [(0, 512, "h"), (512, 640, "h"),
             (640, 1152, "o"), (1152, 1664, "o"), (1664, 2176, "o"), (2176, 2688, "o")]
    for (ta, tb_, kind) in tiles:
        W = tb_ - ta
        for c in range(NCH):
            t = pp[c % 4].acquire(PE)
            pv = sP.inc(PE.matmul(t[:, 0:W], wob.ap(128 * c, 128 * (c + 1)),
                                  obuf.ap(ta, tb_), start=True, stop=True))
            VE.wait_ge(sP.h, pv)
            if kind == "h":
                dst = xh.ap(640 * c + ta, 640 * c + tb_)
                vchain(VE.scalar_tensor_tensor(dst, t[:, 0:W], bo8.ap(c, c + 1),
                                               dst, op0=ALU.add, op1=ALU.add))
                vv = vchain(VE.tensor_scalar(dst, dst, flag[:], None,
                                             op0=ALU.mult))
            else:
                dst = XO(c, ta - 640, tb_ - 640)
                vv = sV.inc(VE.scalar_tensor_tensor(
                    dst, t[:, 0:W], bo8.ap(c, c + 1), dst,
                    op0=ALU.add, op1=ALU.add))
            pp[c % 4].consumed(sV, vv)
    x1_v = sV.v

    for nm_, seg_, a_, w_ in (("za", za, 0, 4096), ("kfm", kfm, 0, 4096), ("qd", qd, 1024, 4096),
                              ("ktm", ktm, 0, 4096), ("vtm", vtm, 0, 4096),
                              ("obuf", obuf, 0, 2688)):
        if dbg(nm_, (128, w_ - a_), BF16) is not None:
            GP.wait_ge(sV.h, obuf_v)
            dma(GP, p.dbg_outs[nm_], seg_.ap(a_, w_), "GD")
    if dbg("x1", (D, TOWN)) is not None:
        GP.wait_ge(sV.h, x1_v)
        for c in range(NCH):
            dma(GP, p.dbg_outs["x1"][128 * c:128 * (c + 1), :],
                XO(c, 0, TOWN), "GD")
    if dbg("x1h", (D, HALO)) is not None:
        GP.wait_ge(sV.h, x1_v)
        for c in range(NCH):
            dma(GP, p.dbg_outs["x1h"][128 * c:128 * (c + 1), :],
                xh.ap(640 * c, 640 * (c + 1)), "GD")

    if upto == "x1":
        finish(p, out_x4, XO, x1_v)
        return p

    p.barrier("ub")
    a2.reset()
    UBW = TEXT + 1
    ubt = a2.alloc(NCH * UBW, F32)

    def UB(c, a, b):
        return ubt.ap(UBW * c + 1 + a, UBW * c + 1 + b)

    for c in range(NCH):
        VE.memset(ubt.ap(UBW * c, UBW * c + 1), 0.0)

    ln_tiles = [(0, 512), (512, 640), (640, 1152), (1152, 1664),
                (1664, 2176), (2176, 2688)]
    z0_v = 0
    for (ta, tb_) in ln_tiles:
        W = tb_ - ta

        def get_in(c, ta=ta, tb_=tb_):
            if tb_ <= 640:
                return xh.ap(640 * c + ta, 640 * c + tb_)
            return XO(c, ta - 640, tb_ - 640)

        def put_out(c, ta=ta, tb_=tb_):
            return UB(c, ta, tb_), None

        z0_v = ln_group(get_in, put_out, W, ln_scr,
                        (pp[4], pp[5], pp[6], pp[7]), ready_v=x1_v)

    if dbg("z0", (D, TEXT)) is not None:
        GP.wait_ge(sV.h, z0_v)
        for c in range(NCH):
            dma(GP, p.dbg_outs["z0"][128 * c:128 * (c + 1), :],
                UB(c, 0, TEXT), "GD")

    if upto == "z0":
        finish(p, out_x4, XO, x1_v)
        return p

    GP.wait_ge(sV.h, x1_v)
    for c in range(NCH):
        dma(GP, x_spill[128 * c:128 * (c + 1), :], XO(c, 0, TOWN), "GS")

    G3 = [(0, 512), (512, 1024), (1024, 1536), (1536, 2048),
          (2048, 2560), (2560, 2688)]
    G4 = [(512, 1024), (1024, 1536), (1536, 2048), (2048, 2560), (2560, 2688)]

    sc.reset()
    sqscr = sc.alloc(512, BF16)
    gelscr = sc.alloc(512, BF16)
    fsm = sc.alloc(64, F32)
    b1c = [sc.alloc(44, F32), sc.alloc(44, F32)]

    qs_col = lambda l: fsm.ap(2 * l, 2 * l + 1)
    ks_col = lambda l: fsm.ap(2 * l + 1, 2 * l + 2)
    b2_col = lambda l: fsm.ap(4 + l, 5 + l)

    for l in range(2):
        dma(SY, fsm.ap(2 * l, 2 * l + 1), di[f"qs{l}"], "L1")
        dma(SY, fsm.ap(2 * l + 1, 2 * l + 2), di[f"ks{l}"], "L1")
        dma(SY, fsm.ap(4 + l, 5 + l), di[f"b2_{l}"][0:128, :], "L1")
    b2c8 = [sc.alloc(8, F32), sc.alloc(8, F32)]
    lgc8 = [sc.alloc(8, F32), sc.alloc(8, F32)]
    assert sc.off <= sc.cols
    for l in range(2):
        dma(SY, b2c8[l].ap().rearrange("p (c o) -> p c o", o=1),
            di[f"b2_{l}"].rearrange("(c p) o -> p c o", p=128),
            allow_slow_non_contiguous=True, stream="L1")
        dma(SY, lgc8[l].ap().rearrange("p (c o) -> p c o", o=1),
            di[f"lng{l}"].rearrange("(c p) o -> p c o", p=128),
            allow_slow_non_contiguous=True, stream="L1")
        dma(SY, b1c[l].ap().rearrange("p (c o) -> p c o", o=1),
            di[f"b1p{l}"].rearrange("(c p) o -> p c o", p=128),
            allow_slow_non_contiguous=True, stream="L1")
    e2b0 = p.sb("e2b0", [1, 128], F32)
    e2b1 = p.sb("e2b1", [1, 128], F32)
    dma(SY, e2b0[:], di["e2"][0:1, :], "L1")
    dma(SY, e2b1[:], di["e2"][1:2, :], "L1")
    e2f = p.sb("e2f", [2, 128], F32)
    dma(SY, e2f[:], di["e2"], "L1")
    oc2b = p.sb("oc2b", [128, 2], BF16)
    dma(GP, oc2b[:], di["ones2col"], "G1")
    rpb = p.sb("rpb", [128, 128], BF16)
    dma(GP, rpb[:], di["Rperm"], "G1")
    khm0b = p.sb("khm0b", [128, 640], BF16)
    dma(GP, khm0b[:], di["khmask0"], "G1")

    for e in (GP, SY, PE, VE, ACT):
        wait_stream(e, "GS")
        wait_stream(e, "G1")
        wait_stream(e, "L1")
    p.barrier("ffn_setup")
    a1.reset()
    zcast = a1.alloc(4096, BF16)
    kbuf = a1.alloc(4 * TEXT, BF16)
    qbuf = a1.alloc(4 * (TEXT - 512), BF16)
    vtbuf = a1.alloc(21 * 520, BF16)
    e0buf = a1.alloc(512, F32)
    ptbuf = a1.alloc(1024, BF16)
    lrbuf = a1.alloc(512, BF16)
    obq = a1.alloc(4 * 512, BF16)
    A1_ATTN = a1.off
    a1.reset()
    abuf = a1.alloc(22 * 512, BF16)
    w2blk = a1.alloc(22 * 1024, BF16)
    a1.off = max(a1.off, A1_ATTN)
    LN2_BASE = 16896
    assert LN2_BASE >= a1.off - 2648
    xb_s2 = [Seg(a1.t, LN2_BASE + 0, 512, BF16),
             Seg(a1.t, LN2_BASE + 256, 512, BF16)]
    sq_s2 = [Seg(a1.t, LN2_BASE + 512, 512, BF16),
             Seg(a1.t, LN2_BASE + 768, 512, BF16)]
    t1_s2 = Seg(a1.t, LN2_BASE + 1024, 512, F32)
    ln_scr2 = {"xb": xb_s2, "sq": sq_s2, "t1": t1_s2}

    def KB(hp, a, b):
        return kbuf.ap(TEXT * hp + a, TEXT * hp + b)

    def QB(hp, a, b):
        return qbuf.ap((TEXT - 512) * hp + a - 512, (TEXT - 512) * hp + b - 512)

    def VT(t):
        return vtbuf.ap(520 * t, 520 * (t + 1))

    a3.reset()
    wqkvb = a3.alloc(8 * 1536, BF16)
    rots = a3.alloc(2 * 512, BF16)
    assert a3.off <= a3.cols
    fwob = Seg(a3.t, 0, 4 * 1024, BF16)
    z2roll = [Seg(a3.t, 0, 8 * 512, BF16), Seg(a3.t, 2048, 8 * 512, BF16)]
    w1blk = [Seg(a3.t, 4096, 8 * 128, BF16), Seg(a3.t, 4608, 8 * 128, BF16)]
    e0b2 = Seg(zcast.t, zcast.off + 1536, 512, F32)
    k1scr = Seg(e0buf.t, e0buf.off + 0, 512, BF16)
    k2scr = Seg(e0buf.t, e0buf.off + 256, 512, BF16)
    tascr = Seg(ptbuf.t, ptbuf.off + 0, 512, BF16)
    tbscr = Seg(ptbuf.t, ptbuf.off + 256, 512, BF16)

    lnrb = p.sb("lnrb", [2, 512], BF16)
    dma(GP, lnrb[1:2, :], di["ones512"], "G1")
    ln_scr2["lnrb"] = lnrb
    for e_ in (PE, VE, ACT, SY, GP):
        wait_stream(e_, "G1")
    lngbs = p.sb("lngbs", [2, D], BF16)

    G3 = [(0, 512), (512, 1024), (1024, 1536), (1536, 2048),
          (2048, 2560), (2560, 2688)]
    G4 = [(512, 1024), (1024, 1536), (1536, 2048), (2048, 2560), (2560, 2688)]

    def emit_ffn_layer(lidx, stage_tag):
        tag = f"{stage_tag}l{lidx}"
        dma(GP, wqkvb.ap().rearrange("p (c m) -> p c m", m=1536),
            di[f"wqkv{lidx}"].rearrange("(c p) m -> p c m", p=128),
            f"Lw_{tag}")
        dma(GP, lngbs[:], di[f"lngb{lidx}"], f"Lw_{tag}")
        wait_stream(PE, f"Lw_{tag}")

        rot_used = 0
        zc_pe = 0
        for g, (ta, tb_) in enumerate(G3):
            W = tb_ - ta
            st_r = f"Lr_{tag}"
            if rot_used:
                GP.wait_ge(sV.h, rot_used)
            wait_stream(GP, st_r)
            dma(GP, rots.ap(0, W), di["rotAk"][:, ta:tb_], st_r)
            dma(GP, rots.ap(512, 512 + W), di["rotBk"][:, ta:tb_], st_r)
            VE.wait_ge(sP.h, zc_pe)
            zcv = 0
            for c in range(NCH):
                off = 1 if c >= 4 else 0
                zcv = sV.inc(VE.tensor_copy(zcast.ap(512 * c, 512 * c + W),
                                            UB(c, ta - off, tb_ - off)))
            PE.wait_ge(sV.h, zcv)

            for isq in (0, 1):
                if isq:
                    GP.wait_ge(sV.h, sV.v)
                    wait_stream(GP, st_r)
                    dma(GP, rots.ap(0, W), di["rotAq"][:, ta:tb_], st_r)
                    dma(GP, rots.ap(512, 512 + W), di["rotBq"][:, ta:tb_], st_r)
                    if g == 0:
                        continue
                for hp in range(4):
                    bank = pp[hp % 4]
                    t = bank.acquire(PE)
                    cols = (0 if isq else 512) + 128 * hp
                    pv = None
                    for c in range(NCH):
                        mm = PE.matmul(t[:, 0:W],
                                       wqkvb.ap(1536 * c + cols,
                                                1536 * c + cols + 128),
                                       zcast.ap(512 * c, 512 * c + W),
                                       start=(c == 0), stop=(c == NCH - 1))
                        if c == NCH - 1:
                            pv = sP.inc(mm)
                    ACT.wait_ge(sP.h, pv)
                    aa = sA.inc(ACT.activation(sqscr.ap(0, W), t[:, 0:W],
                                               AF.Square))
                    PE.wait_ge(sA.h, aa)
                    tn = pp[4].acquire(PE)
                    pn = sP.inc(PE.matmul(tn[0:2, 0:W], oc2b[:], sqscr.ap(0, W),
                                          start=True, stop=True))
                    VE.wait_ge(sP.h, pn)
                    nv = vchain(VE.tensor_scalar(srow[0:2, 0:W], tn[0:2, 0:W],
                                                 1e-24, None, op0=ALU.max))
                    pp[4].consumed(sV, nv)
                    ACT.wait_ge(sV.h, nv)
                    av = sA.inc(ACT.activation(srow[0:2, 512:512 + W],
                                               srow[0:2, 0:W], AF.Ln))
                    ACT.wait_ge(sA.h, av)
                    av = sA.inc(ACT.activation(srow[0:2, 1024:1024 + W],
                                               srow[0:2, 512:512 + W],
                                               AF.Exp, scale=-0.5))
                    PE.wait_ge(sA.h, av)
                    tb2 = pp[5].acquire(PE)
                    pn2 = sP.inc(PE.matmul(tb2[:, 0:W], e2f[:],
                                           srow[0:2, 1024:1024 + W],
                                           start=True, stop=True))
                    VE.wait_ge(sP.h, pn2)
                    scol = qs_col(lidx) if isq else ks_col(lidx)
                    vchain(VE.tensor_scalar(k1scr.ap(0, W), t[:, 0:W], scol,
                                            None, op0=ALU.mult))
                    bank.consumed(sV, sV.v)
                    k2v = vchain(VE.tensor_tensor(k2scr.ap(0, W),
                                                  k1scr.ap(0, W),
                                                  tb2[:, 0:W], op=ALU.mult))
                    pp[5].consumed(sV, k2v)
                    PE.wait_ge(sV.h, k2v)
                    tr = pp[6].acquire(PE)
                    pr = sP.inc(PE.matmul(tr[:, 0:W], rpb[:], k2scr.ap(0, W),
                                          start=True, stop=True))
                    ra = rots.ap(0, W)
                    rb = rots.ap(512, 512 + W)
                    wait_stream(VE, st_r)
                    vchain(VE.tensor_tensor(tascr.ap(0, W), k2scr.ap(0, W), ra,
                                            op=ALU.mult))
                    VE.wait_ge(sP.h, pr)
                    vchain(VE.tensor_tensor(tbscr.ap(0, W), tr[:, 0:W], rb,
                                            op=ALU.mult))
                    pp[6].consumed(sV, sV.v)
                    dst = QB(hp, ta, tb_) if isq else KB(hp, ta, tb_)
                    vchain(VE.tensor_tensor(dst, tascr.ap(0, W),
                                            tbscr.ap(0, W), op=ALU.add))
            rot_used = sV.v
            for ts_ in range(W // 128):
                tvi = (ta + 128 * ts_) // 128
                t = pp[7].acquire(PE)
                pv = None
                for c in range(NCH):
                    mm = PE.matmul(t[:, 0:512],
                                   zcast.ap(512 * c + 128 * ts_,
                                            512 * c + 128 * (ts_ + 1)),
                                   wqkvb.ap(1536 * c + 1024, 1536 * c + 1536),
                                   start=(c == 0), stop=(c == NCH - 1))
                    if c == NCH - 1:
                        pv = sP.inc(mm)
                VE.wait_ge(sP.h, pv)
                vv = sV.inc(VE.tensor_copy(
                    VT(tvi).rearrange("p (h j) -> p h j", j=65)[:, :, 0:64],
                    t[:, 0:512].rearrange("p (h j) -> p h j", j=64)))
                pp[7].consumed(sV, vv)
            zc_pe = sP.v
        vt1 = vchain(VE.memset(
            vtbuf.ap().rearrange("p (t h j) -> p t h j", h=8, j=65)[:, :, :, 64:65],
            1.0))

        GP.wait_ge(sP.h, sP.v)
        dma(GP, fwob.ap().rearrange("p (c m) -> p c m", m=1024),
            di[f"fwo{lidx}"].rearrange("(c p) m -> p c m", p=128),
            f"Lwo_{tag}")
        wait_stream(PE, f"Lwo_{tag}")

        PE.wait_ge(sV.h, vt1)
        pt_cons = [0, 0]
        e0_cons = 0
        for qg, (qa, qb_) in enumerate(G4):
            W = qb_ - qa
            nk = (W + 512) // 128
            for hp in range(4):
                toA = pp[4].acquire(PE)
                toB = pp[5].acquire(PE)
                avs = [0, 0]
                pvk = {}
                sbk = {}

                def emit_qk(kk):
                    ka = qa - 512 + 128 * kk
                    banks = (pp[(2 * kk) % 4], pp[(2 * kk + 1) % 4])
                    sbk[kk] = banks
                    pvs_ = []
                    for hh in range(2):
                        tdst = banks[hh].acquire(PE)
                        PE.matmul(tdst[:, 0:W], identb[:], fmask(kk)[:, 0:W],
                                  start=True, stop=False)
                        if (qg == 0 and kk <= 3) or (qg == 1 and kk == 0):
                            PE.matmul(tdst[:, 0:W], identb[:],
                                      khm0b[:, 128:128 + W],
                                      start=False, stop=False)
                        if qg == 0 and kk == 4:
                            PE.matmul(tdst[:, 0:W], identb[:], khm0b[:, 0:W],
                                      start=False, stop=False)
                        pvs_.append(sP.inc(PE.matmul(
                            tdst[:, 0:W],
                            KB(hp, ka, ka + 128)[64 * hh:64 * (hh + 1), :],
                            QB(hp, qa, qb_)[64 * hh:64 * (hh + 1), :],
                            start=False, stop=True)))
                    pvk[kk] = pvs_

                def emit_tail(kk):
                    nonlocal e0_cons
                    banks = sbk[kk]
                    for hh in range(2):
                        tdst = banks[hh].t
                        e0 = e0buf.ap(0, W) if hh == 0 else e0b2.ap(0, W)
                        ACT.wait_ge(sP.h, pvk[kk][hh])
                        ACT.wait_ge(sV.h, e0_cons)
                        aa = sA.inc(ACT.activation(e0, tdst[:, 0:W],
                                                   AF.Exp, scale=0.25))
                        ACT.wait_ge(sA.h, aa)
                        aa = sA.inc(ACT.activation(e0, e0, AF.Square))
                        banks[hh].consumed(sA, aa)
                        VE.wait_ge(sA.h, aa)
                        VE.wait_ge(sP.h, pt_cons[hh])
                        ptv = sV.inc(VE.tensor_tensor(
                            ptbuf.ap(512 * hh, 512 * hh + W), e0, e0,
                            op=ALU.mult))
                        if hh == 1:
                            e0_cons = ptv
                        PE.wait_ge(sV.h, ptv)
                        to = toA if hh == 0 else toB
                        h = 2 * hp + hh
                        av = sP.inc(PE.matmul(
                            to[0:65, 0:W],
                            VT((qa - 512 + 128 * kk) // 128)[:, 65 * h:65 * (h + 1)],
                            ptbuf.ap(512 * hh, 512 * hh + W),
                            start=(kk == 0), stop=(kk == nk - 1)))
                        pt_cons[hh] = av
                        avs[hh] = av

                emit_qk(0)
                for kk in range(1, nk):
                    emit_qk(kk)
                    emit_tail(kk - 1)
                emit_tail(nk - 1)
                avA, avB = avs
                VE.wait_ge(sP.h, max(avA, avB))
                vchain(VE.tensor_copy(srow[0:1, 0:512][:, 0:W], toA[64:65, 0:W]))
                lv = vchain(VE.tensor_copy(srow[0:1, 512:512 + W],
                                           toB[64:65, 0:W]))
                ACT.wait_ge(sV.h, lv)
                for seg0 in (0, 512):
                    av = sA.inc(ACT.activation(
                        srow[0:1, 1024 + seg0:1024 + seg0 + W],
                        srow[0:1, seg0:seg0 + W], AF.Ln))
                    ACT.wait_ge(sA.h, av)
                    av = sA.inc(ACT.activation(
                        srow[0:1, 1024 + seg0:1024 + seg0 + W],
                        srow[0:1, 1024 + seg0:1024 + seg0 + W],
                        AF.Exp, scale=-1.0))
                PE.wait_ge(sA.h, av)
                tl = pp[6].acquire(PE)
                PE.matmul(tl[:, 0:W], e2b0[:], srow[0:1, 1024:1024 + W],
                          start=True, stop=False)
                pl = sP.inc(PE.matmul(tl[:, 0:W], e2b1[:],
                                      srow[0:1, 1536:1536 + W],
                                      start=False, stop=True))
                VE.wait_ge(sP.h, pl)
                lrv = vchain(VE.tensor_copy(lrbuf.ap(0, W), tl[:, 0:W]))
                pp[6].consumed(sV, lrv)
                v1 = vchain(VE.tensor_tensor(
                    obq.ap(512 * hp, 512 * hp + W, p0=0, p1=64),
                    toA[0:64, 0:W], lrbuf.ap(0, W, p0=0, p1=64), op=ALU.mult))
                v2 = vchain(VE.tensor_tensor(
                    obq.ap(512 * hp, 512 * hp + W, p0=64, p1=128),
                    toB[0:64, 0:W], lrbuf.ap(0, W, p0=64, p1=128),
                    op=ALU.mult))
                pp[4].consumed(sV, v1)
                pp[5].consumed(sV, v2)
            ob_v = sV.v
            PE.wait_ge(sV.h, ob_v)
            for dc in range(NCH):
                t = pp[6 + (dc % 2)].acquire(PE)
                pv = None
                for hp in range(4):
                    mm = PE.matmul(t[:, 0:W],
                                   fwob.ap(1024 * hp + 128 * dc,
                                           1024 * hp + 128 * dc + 128),
                                   obq.ap(512 * hp, 512 * hp + W),
                                   start=(hp == 0), stop=(hp == 3))
                    if hp == 3:
                        pv = sP.inc(mm)
                VE.wait_ge(sP.h, pv)
                dst = UB(dc, qa, qb_)
                vv = vchain(VE.tensor_tensor(dst, t[:, 0:W], dst, op=ALU.add))
                if qg == 0:
                    vv = vchain(VE.tensor_scalar(UB(dc, 512, 640),
                                                 UB(dc, 512, 640), flag[:],
                                                 None, op0=ALU.mult))
                pp[6 + (dc % 2)].consumed(sV, vv)

        p.barrier(f"mlp_{tag}")
        dma(GP, w2blk.ap().rearrange("p (c m) -> p c m", m=1024),
            di[f"w2p{lidx}"].rearrange("(c p) m -> p c m", p=128),
            f"Lw2_{tag}")

        def ln2(g, slot):
            ta = 640 + 512 * g

            def get_in(c):
                off = 1 if c >= 4 else 0
                return UB(c, ta - off, ta + 512 - off)

            def put_out(c):
                return (z2roll[slot].ap(512 * c, 512 * (c + 1)),
                        lgc8[lidx].ap(c, c + 1))

            def lgrow(c):
                return lngbs[0:2, 128 * c:128 * (c + 1)]

            return ln_group(get_in, put_out, 512, ln_scr2,
                            (pp[0], pp[1], pp[6], pp[7]), gb=lgrow,
                            ready_v=sV.v)

        wblk_used = [0, 0]
        gel_cons = [0]

        def mlp(g, slot):
            ta = 640 + 512 * g
            z2 = z2roll[slot]

            def w1_load(ob):
                st_w = f"Lw1{ob % 2}_{tag}"
                if wblk_used[ob % 2]:
                    GP.wait_ge(sP.h, wblk_used[ob % 2])
                dma(GP, w1blk[ob % 2].ap().rearrange("p (c o) -> p c o", o=128),
                    di[f"w1p{lidx}"].rearrange("(c p) o -> p c o", p=128)
                    [:, :, 128 * ob:128 * (ob + 1)], st_w)
                return p.sem(st_w).v

            w1_ld = {0: w1_load(0), 1: w1_load(1)}
            for ob in range(44):
                st_w = f"Lw1{ob % 2}_{tag}"
                PE.wait_ge(p.sem(st_w).h, w1_ld[ob % 2])
                t = pp[2 + (ob % 2)].acquire(PE)
                pv = None
                for c in range(NCH):
                    mm = PE.matmul(t[:, 0:512],
                                   w1blk[ob % 2].ap(128 * c, 128 * (c + 1)),
                                   z2.ap(512 * c, 512 * (c + 1)),
                                   start=(c == 0), stop=(c == NCH - 1))
                    if c == NCH - 1:
                        pv = sP.inc(mm)
                wblk_used[ob % 2] = pv
                if ob + 2 < 44:
                    w1_ld[ob % 2] = w1_load(ob + 2)
                if ob < 22:
                    VE.wait_ge(sP.h, pv)
                    vv = sV.inc(VE.tensor_scalar(
                        abuf.ap(512 * ob, 512 * (ob + 1)), t[:, 0:512],
                        b1c[lidx].ap(ob, ob + 1), None, op0=ALU.add))
                    pp[2 + (ob % 2)].consumed(sV, vv)
                else:
                    hb = ob - 22
                    ACT.wait_ge(sP.h, pv)
                    ACT.wait_ge(sV.h, gel_cons[0])
                    aa = sA.inc(ACT.activation(gelscr.ap(0, 512), t[:, 0:512],
                                               AF.Gelu,
                                               bias=b1c[lidx].ap(ob, ob + 1)))
                    pp[2 + (ob % 2)].consumed(sA, aa)
                    VE.wait_ge(sA.h, aa)
                    gv = vchain(VE.tensor_tensor(
                        abuf.ap(512 * hb, 512 * (hb + 1)),
                        abuf.ap(512 * hb, 512 * (hb + 1)),
                        gelscr.ap(0, 512), op=ALU.mult))
                    gel_cons[0] = gv
            h_v = sV.v
            PE.wait_ge(sV.h, h_v)
            wait_stream(PE, f"Lw2_{tag}")
            for half in range(2):
                ts2 = [None] * 4
                pvs = [None] * 4
                for hc in range(22):
                    for di_ in range(4):
                        dc = 4 * half + di_
                        if hc == 0:
                            ts2[di_] = pp[2 + di_].acquire(PE)
                        mm = PE.matmul(ts2[di_][:, 0:512],
                                       w2blk.ap(1024 * hc + 128 * dc,
                                                1024 * hc + 128 * dc + 128),
                                       abuf.ap(512 * hc, 512 * (hc + 1)),
                                       start=(hc == 0), stop=(hc == 21))
                        if hc == 21:
                            pvs[di_] = sP.inc(mm)
                for di_ in range(4):
                    dc = 4 * half + di_
                    VE.wait_ge(sP.h, pvs[di_])
                    dst = UB(dc, ta, ta + 512)
                    vv = vchain(VE.scalar_tensor_tensor(
                        dst, ts2[di_][:, 0:512], b2c8[lidx].ap(dc, dc + 1),
                        dst, op0=ALU.add, op1=ALU.add))
                    pp[2 + di_].consumed(sV, vv)

        ln2(0, 0)
        ln2(1, 1)
        mlp(0, 0)
        ln2(2, 0)
        mlp(1, 1)
        ln2(3, 1)
        mlp(2, 0)
        mlp(3, 1)
        p.barrier(f"eol_{tag}")
        return sV.v

    def ag_exchange(ag_in, ag_out, tag):
        GP.wait_ge(sV.h, sV.v)
        for c in range(NCH):
            dma(GP, ag_in[128 * c:128 * (c + 1), :], UB(c, 2048, TEXT),
                f"GA_{tag}")
        wait_stream(GP, f"GA_{tag}")
        sc_ = p.sem(f"CC_{tag}")
        sc_.inc(GP.collective_compute(
            "AllGather", ALU.bypass, replica_groups=RG,
            ins=[ag_in.opt()], outs=[ag_out.opt()]), 1)
        GP.wait_ge(sc_.h, sc_.v)
        for c in range(NCH):
            dma(GP, UB(c, 0, HALO), ag_out[128 * c:128 * (c + 1), :],
                f"GH_{tag}")
        wait_stream(VE, f"GH_{tag}")
        hv = 0
        for c in range(NCH):
            hv = vchain(VE.tensor_scalar(UB(c, 0, HALO), UB(c, 0, HALO),
                                         flag[:], None, op0=ALU.mult))
        return hv

    def emit_ffn_stage(stage_tag, agi, ago):
        emit_ffn_layer(0, stage_tag)
        if stage_tag == "s2" and dbg("u1", (D, TOWN)) is not None:
            GP.wait_ge(sV.h, sV.v)
            for c in range(NCH):
                dma(GP, p.dbg_outs["u1"][128 * c:128 * (c + 1), :],
                    UB(c, HALO, TEXT), "GD")
            for e_ in (VE, GP, PE, ACT):
                wait_stream(e_, "GD")
        hv = ag_exchange(agi, ago, stage_tag)
        PE.wait_ge(sV.h, hv)
        ACT.wait_ge(sV.h, hv)
        emit_ffn_layer(1, stage_tag)

    emit_ffn_stage("s2", ag1_in, ag1_out)

    p.barrier("x2")
    a1.reset()
    xo2 = a1.alloc(16384, F32)

    def XO2(c, a, b):
        return xo2.ap(TOWN * c + a, TOWN * c + b)

    wait_stream(SY, "GS")
    for c in range(NCH):
        dma(SY, XO2(c, 0, TOWN), x_spill[128 * c:128 * (c + 1), :], "LS2")
    wait_stream(VE, "LS2")
    x2_v = 0
    for c in range(NCH):
        dst = xo2.ap(TOWN * c, TOWN * (c + 1))
        x2_v = vchain(VE.tensor_tensor(dst, dst, UB(c, HALO, TEXT),
                                       op=ALU.add))
    if dbg("x2", (D, TOWN)) is not None:
        GP.wait_ge(sV.h, x2_v)
        for c in range(NCH):
            dma(GP, p.dbg_outs["x2"][128 * c:128 * (c + 1), :],
                XO2(c, 0, TOWN), "GD")
        for e_ in (VE, GP, PE, ACT):
            wait_stream(e_, "GD")
    if upto == "x2":
        finish(p, out_x4, XO2, x2_v)
        return p

    a2.reset()
    kbb = a2.alloc(4 * S, BF16)
    vtb = a2.alloc(32 * 520, BF16)
    zrd = a2.alloc(8 * 512, BF16)
    qwn = a2.alloc(4 * 512, BF16)
    ost = a2.alloc(512, BF16)
    e0b2b = a2.alloc(512, F32)
    zown = zrd

    def KBB(hp, a, b):
        return kbb.ap(S * hp + a, S * hp + b)

    def VTB(t):
        return vtb.ap(520 * t, 520 * (t + 1))

    a3.reset()
    pib = [Seg(a3.t, 2048 * i, 8 * 512, BF16) for i in range(3)]
    pos = Seg(a3.t, 5120, 2 * 1024, BF16)

    for i, nm in enumerate(("PiQ", "PiK", "PiV")):
        dma(GP, pib[i].ap().rearrange("p (c m) -> p c m", m=512),
            di[nm].rearrange("(c p) m -> p c m", p=128), "LPi")

    for g in range(4):
        wait_stream(VE, "GZ")

        def get_in(c, g=g):
            return XO2(c, 512 * g, 512 * (g + 1))

        def put_out(c):
            return zown.ap(512 * c, 512 * (c + 1)), None

        zv = ln_group(get_in, put_out, 512, ln_scr2,
                      (pp[0], pp[1], pp[6], pp[7]), ready_v=x2_v)
        SY.wait_ge(sV.h, zv)
        for c in range(NCH):
            dma(SY, ag2z_in[128 * c:128 * (c + 1), 512 * g:512 * (g + 1)],
                zown.ap(512 * c, 512 * (c + 1)), "GZ")
    GP.wait_ge(sV.h, x2_v)
    for c in range(NCH):
        dma(GP, ag2x_in[128 * c:128 * (c + 1), :],
            XO2(c, TOWN - HALO, TOWN), "GX")
    wait_stream(GP, "GZ")
    wait_stream(GP, "GX")
    ccz = p.sem("CCz")
    ccz.inc(GP.collective_compute("AllGather", ALU.bypass, replica_groups=RG,
                                  ins=[ag2z_in.opt()], outs=[ag2z_out.opt()]), 1)
    ccz.inc(GP.collective_compute("AllGather", ALU.bypass, replica_groups=RG,
                                  ins=[ag2x_in.opt()], outs=[ag2x_out.opt()]), 1)
    GP.wait_ge(ccz.h, ccz.v)
    wait_stream(PE, "LPi")

    def zread_group(g, stream):
        sl = g // 4
        co = 512 * (g % 4)
        for c in range(NCH):
            dma(SY, zrd.ap(512 * c, 512 * (c + 1)),
                ag2z_out[1024 * sl + 128 * c:1024 * sl + 128 * (c + 1),
                         co:co + 512], stream)

    zr_pe = [0]
    for g in range(8):
        SY.wait_ge(sP.h, zr_pe[0])
        SY.wait_ge(ccz.h, 1)
        zread_group(g, "LZ")
        wait_stream(PE, "LZ")
        for hp in range(4):
            t = pp[hp % 2].acquire(PE)
            pv = None
            for c in range(NCH):
                mm = PE.matmul(t[:, 0:512],
                               pib[1].ap(512 * c + 128 * hp, 512 * c + 128 * (hp + 1)),
                               zrd.ap(512 * c, 512 * (c + 1)),
                               start=(c == 0), stop=(c == NCH - 1))
                if c == NCH - 1:
                    pv = sP.inc(mm)
            VE.wait_ge(sP.h, pv)
            vv = sV.inc(VE.tensor_copy(KBB(hp, 512 * g, 512 * (g + 1)),
                                       t[:, 0:512]))
            pp[hp % 2].consumed(sV, vv)
        for ts_ in range(4):
            tvi = 4 * g + ts_
            t = pp[2 + ts_ % 2].acquire(PE)
            pv = None
            for c in range(NCH):
                mm = PE.matmul(t[:, 0:512],
                               zrd.ap(512 * c + 128 * ts_, 512 * c + 128 * (ts_ + 1)),
                               pib[2].ap(512 * c, 512 * (c + 1)),
                               start=(c == 0), stop=(c == NCH - 1))
                if c == NCH - 1:
                    pv = sP.inc(mm)
            VE.wait_ge(sP.h, pv)
            vv = sV.inc(VE.tensor_copy(
                VTB(tvi).rearrange("p (h j) -> p h j", j=65)[:, :, 0:64],
                t[:, 0:512].rearrange("p (h j) -> p h j", j=64)))
            pp[2 + ts_ % 2].consumed(sV, vv)
        zr_pe[0] = sP.v
    vt1 = vchain(VE.memset(
        vtb.ap().rearrange("p (t h j) -> p t h j", h=8, j=65)[:, :, :, 64:65],
        1.0))
    PE.wait_ge(sV.h, vt1)

    pt_cons = [0, 0]
    e0_cons = [0]
    qw_pe = [0]
    wait_stream(SY, "LZ")
    SY.wait_ge(sP.h, zr_pe[0])
    for qw in range(8):
        SY.wait_ge(sP.h, qw_pe[0])
        zread_group(qw, "LZq")
        wait_stream(PE, "LZq")
        for hp in range(4):
            t = pp[7].acquire(PE)
            pv = None
            for c in range(NCH):
                mm = PE.matmul(t[:, 0:512],
                               pib[0].ap(512 * c + 128 * hp, 512 * c + 128 * (hp + 1)),
                               zrd.ap(512 * c, 512 * (c + 1)),
                               start=(c == 0), stop=(c == NCH - 1))
                if c == NCH - 1:
                    pv = sP.inc(mm)
            VE.wait_ge(sP.h, pv)
            vv = sV.inc(VE.tensor_copy(qwn.ap(512 * hp, 512 * (hp + 1)),
                                       t[:, 0:512]))
            pp[7].consumed(sV, vv)
        qw_pe[0] = sP.v
        PE.wait_ge(sV.h, sV.v)
        nk = 4 * (qw + 1)
        for hp in range(4):
            toA = pp[4].acquire(PE)
            toB = pp[5].acquire(PE)
            avs = [0, 0]
            pvk = {}
            sbk = {}

            def emit_qk(kk):
                ka = 128 * kk
                banks = (pp[(2 * kk) % 4], pp[(2 * kk + 1) % 4])
                sbk[kk] = banks
                pvs_ = []
                for hh in range(2):
                    tdst = banks[hh].acquire(PE)
                    first = True
                    if kk >= nk - 4:
                        PE.matmul(tdst[:, 0:512], identb[:],
                                  bmask(kk - (nk - 4))[:, 0:512],
                                  start=True, stop=False)
                        first = False
                    pvs_.append(sP.inc(PE.matmul(
                        tdst[:, 0:512],
                        KBB(hp, ka, ka + 128)[64 * hh:64 * (hh + 1), :],
                        qwn.ap(512 * hp, 512 * (hp + 1))[64 * hh:64 * (hh + 1), :],
                        start=first, stop=True)))
                pvk[kk] = pvs_

            def emit_tail(kk):
                banks = sbk[kk]
                for hh in range(2):
                    tdst = banks[hh].t
                    e0 = e0buf.ap(0, 512) if hh == 0 else e0b2b.ap(0, 512)
                    ACT.wait_ge(sP.h, pvk[kk][hh])
                    ACT.wait_ge(sV.h, e0_cons[0])
                    aa = sA.inc(ACT.activation(e0, tdst[:, 0:512],
                                               AF.Exp, scale=1.0 / 32.0))
                    ACT.wait_ge(sA.h, aa)
                    aa = sA.inc(ACT.activation(e0, e0, AF.Square))
                    banks[hh].consumed(sA, aa)
                    VE.wait_ge(sA.h, aa)
                    VE.wait_ge(sP.h, pt_cons[hh])
                    ptv = sV.inc(VE.tensor_tensor(
                        ptbuf.ap(512 * hh, 512 * hh + 512), e0, e0,
                        op=ALU.mult))
                    if hh == 1:
                        e0_cons[0] = ptv
                    PE.wait_ge(sV.h, ptv)
                    to = toA if hh == 0 else toB
                    h = 2 * hp + hh
                    av = sP.inc(PE.matmul(
                        to[0:65, 0:512],
                        VTB(kk)[:, 65 * h:65 * (h + 1)],
                        ptbuf.ap(512 * hh, 512 * hh + 512),
                        start=(kk == 0), stop=(kk == nk - 1)))
                    pt_cons[hh] = av
                    avs[hh] = av

            emit_qk(0)
            for kk in range(1, nk):
                emit_qk(kk)
                emit_tail(kk - 1)
            emit_tail(nk - 1)
            avA, avB = avs
            VE.wait_ge(sP.h, max(avA, avB))
            vchain(VE.tensor_copy(srow[0:1, 0:512], toA[64:65, 0:512]))
            lv = vchain(VE.tensor_copy(srow[0:1, 512:1024], toB[64:65, 0:512]))
            ACT.wait_ge(sV.h, lv)
            for seg0 in (0, 512):
                av = sA.inc(ACT.activation(srow[0:1, 1024 + seg0:1536 + seg0],
                                           srow[0:1, seg0:512 + seg0], AF.Ln))
                ACT.wait_ge(sA.h, av)
                av = sA.inc(ACT.activation(srow[0:1, 1024 + seg0:1536 + seg0],
                                           srow[0:1, 1024 + seg0:1536 + seg0],
                                           AF.Exp, scale=-1.0))
            PE.wait_ge(sA.h, av)
            tl = pp[6].acquire(PE)
            PE.matmul(tl[:, 0:512], e2b0[:], srow[0:1, 1024:1536],
                      start=True, stop=False)
            pl = sP.inc(PE.matmul(tl[:, 0:512], e2b1[:], srow[0:1, 1536:2048],
                                  start=False, stop=True))
            VE.wait_ge(sP.h, pl)
            lrv = vchain(VE.tensor_copy(lrbuf.ap(0, 512), tl[:, 0:512]))
            pp[6].consumed(sV, lrv)
            wait_stream(VE, "GO")
            vchain(VE.tensor_tensor(ost.ap(0, 512, p0=0, p1=64),
                                    toA[0:64, 0:512],
                                    lrbuf.ap(0, 512, p0=0, p1=64), op=ALU.mult))
            ov = vchain(VE.tensor_tensor(ost.ap(0, 512, p0=64, p1=128),
                                         toB[0:64, 0:512],
                                         lrbuf.ap(0, 512, p0=64, p1=128),
                                         op=ALU.mult))
            pp[4].consumed(sV, ov)
            pp[5].consumed(sV, ov)
            SY.wait_ge(sV.h, ov)
            dma(SY, ag3_in[128 * hp:128 * (hp + 1), 512 * qw:512 * (qw + 1)],
                ost.ap(0, 512), "GO")

    for nm_, seg_, w_ in (("Kbb", kbb, 4 * S), ("Vtb", vtb, 32 * 520)):
        if dbg(nm_, (128, w_), BF16) is not None:
            GP.wait_ge(sV.h, sV.v)
            dma(GP, p.dbg_outs[nm_], seg_.ap(0, w_), "GD")
            wait_stream(GP, "GD")
    if dbg("Ob", (512, S), BF16) is not None:
        wait_stream(GP, "GO")
        dma(GP, p.dbg_outs["Ob"], ag3_in, "GD")
        wait_stream(GP, "GD")
    wait_stream(GP, "GO")
    cc3 = p.sem("CC3")
    cc3.inc(GP.collective_compute("AllGather", ALU.bypass, replica_groups=RG,
                                  ins=[ag3_in.opt()], outs=[ag3_out.opt()]), 1)
    GP.wait_ge(cc3.h, cc3.v)

    GP.wait_ge(sV.h, sV.v)
    for c in range(NCH):
        dma(GP, xh.ap(640 * c, 640 * (c + 1)),
            ag2x_out[128 * c:128 * (c + 1), :], "LXH")
    wait_stream(VE, "LXH")
    for c in range(NCH):
        vchain(VE.tensor_scalar(xh.ap(640 * c, 640 * (c + 1)),
                                xh.ap(640 * c, 640 * (c + 1)),
                                flag[:], None, op0=ALU.mult))
    p.barrier("ob_comb")
    a2.reset()
    ocb = a2.alloc(8 * 2688, BF16)
    ourd = [a2.alloc(2 * 512, BF16), a2.alloc(2 * 512, BF16)]
    ot1 = a2.alloc(512, BF16)

    def OCB(ic, a, b):
        return ocb.ap(2688 * ic + a, 2688 * ic + b)

    oc_tiles = [(ic, ta, tb_) for ic in range(NCH)
                for (ta, tb_) in ((1408, 1920), (1920, 2048), (2048, 2560),
                                  (2560, 3072), (3072, 3584), (3584, 4096))]
    oc_used = [0, 0]
    oc_ld = [0, 0]

    def oc_load(i):
        ic, ta, tb_ = oc_tiles[i]
        sl = i % 2
        st = f"LOr{sl}"
        if oc_used[sl]:
            SY.wait_ge(sV.h, oc_used[sl])
        for rot in range(2):
            ca = (ta + 2048 * rot) % 4096
            dma(SY, ourd[sl].ap(512 * rot, 512 * rot + (tb_ - ta)),
                ag3_out[128 * ic:128 * (ic + 1), ca:ca + (tb_ - ta)], st)
        return p.sem(st).v

    oc_ld[0] = oc_load(0)
    oc_ld[1] = oc_load(1)
    ocv = 0
    for i, (ic, ta, tb_) in enumerate(oc_tiles):
        W = tb_ - ta
        sl = i % 2
        VE.wait_ge(p.sem(f"LOr{sl}").h, oc_ld[sl])
        vchain(VE.tensor_scalar(ot1.ap(0, W), ourd[sl].ap(0, W),
                                flag[:], None, op0=ALU.mult))
        ocv = vchain(VE.scalar_tensor_tensor(
            OCB(ic, ta - 1408, tb_ - 1408), ourd[sl].ap(512, 512 + W),
            flagn[:], ot1.ap(0, W), op0=ALU.mult, op1=ALU.add))
        oc_used[sl] = ocv
        if i + 2 < len(oc_tiles):
            oc_ld[sl] = oc_load(i + 2)
    pob = Seg(a3.t, 5120, 2 * 1024, BF16)
    ytiles = [(0, 512, "h"), (512, 640, "h"), (640, 1152, "o"),
              (1152, 1664, "o"), (1664, 2176, "o"), (2176, 2688, "o")]
    x3_v = 0
    po_used = [0, 0]
    for dc in range(NCH):
        st_p = f"LPo{dc % 2}"
        if po_used[dc % 2]:
            GP.wait_ge(sP.h, po_used[dc % 2])
        dma(GP, pob.ap(1024 * (dc % 2), 1024 * (dc % 2) + 1024)
            .rearrange("p (c m) -> p c m", m=128),
            di["Po"].rearrange("(c p) m -> p c m", p=128)
            [:, :, 128 * dc:128 * (dc + 1)], st_p)
        wait_stream(PE, st_p)
        PE.wait_ge(sV.h, ocv)
        for ti, (ta, tb_, kind) in enumerate(ytiles):
            W = tb_ - ta
            t = pp[ti % 4].acquire(PE)
            pv = None
            for ic in range(NCH):
                mm = PE.matmul(t[:, 0:W],
                               pob.ap(1024 * (dc % 2) + 128 * ic,
                                      1024 * (dc % 2) + 128 * (ic + 1)),
                               OCB(ic, ta, tb_),
                               start=(ic == 0), stop=(ic == NCH - 1))
                if ic == NCH - 1:
                    pv = sP.inc(mm)
            po_used[dc % 2] = pv
            VE.wait_ge(sP.h, pv)
            if kind == "h":
                dst = xh.ap(640 * dc + ta, 640 * dc + tb_)
                vchain(VE.tensor_tensor(dst, t[:, 0:W], dst, op=ALU.add))
                x3_v = vchain(VE.tensor_scalar(dst, dst, flag[:], None,
                                               op0=ALU.mult))
            else:
                dst = XO2(dc, ta - 640, tb_ - 640)
                x3_v = vchain(VE.tensor_tensor(dst, t[:, 0:W], dst, op=ALU.add))
            pp[ti % 4].consumed(sV, x3_v)

    if dbg("x3", (D, TOWN)) is not None:
        GP.wait_ge(sV.h, x3_v)
        for c in range(NCH):
            dma(GP, p.dbg_outs["x3"][128 * c:128 * (c + 1), :],
                XO2(c, 0, TOWN), "GD")
        for e_ in (VE, GP, PE, ACT):
            wait_stream(e_, "GD")
    if upto == "x3":
        finish(p, out_x4, XO2, x3_v)
        return p

    GP.wait_ge(sV.h, x3_v)
    for c in range(NCH):
        dma(GP, x_spill[128 * c:128 * (c + 1), :], XO2(c, 0, TOWN), "GS2")
    for e_ in (GP, SY, PE, VE, ACT):
        wait_stream(e_, "GS2")
    p.barrier("z0p")
    a2.reset()
    a2.off = NCH * UBW
    for c in range(NCH):
        vchain(VE.memset(ubt.ap(UBW * c, UBW * c + 1), 0.0))

    ln_tiles2 = [(0, 512), (512, 640), (640, 1152), (1152, 1664),
                 (1664, 2176), (2176, 2688)]
    z0p_v = 0
    for (ta, tb_) in ln_tiles2:
        W = tb_ - ta

        def get_in(c, ta=ta, tb_=tb_):
            if tb_ <= 640:
                return xh.ap(640 * c + ta, 640 * c + tb_)
            return XO2(c, ta - 640, tb_ - 640)

        def put_out(c, ta=ta, tb_=tb_):
            return UB(c, ta, tb_), None

        z0p_v = ln_group(get_in, put_out, W, ln_scr2,
                         (pp[4], pp[5], pp[6], pp[7]), ready_v=x3_v)
    p.barrier("s4")

    emit_ffn_stage("s4", ag4_in, ag4_out)

    p.barrier("x4")
    for c in range(NCH):
        dma(SY, XO2(c, 0, TOWN), x_spill[128 * c:128 * (c + 1), :], "LS4")
    wait_stream(VE, "LS4")
    x4_v = 0
    for c in range(NCH):
        dst = xo2.ap(TOWN * c, TOWN * (c + 1))
        x4_v = vchain(VE.tensor_tensor(dst, dst, UB(c, HALO, TEXT),
                                       op=ALU.add))
    finish(p, out_x4, XO2, x4_v)
    return p


def finish(p, out_x4, XO, ready_v):
    nc = p.nc
    sO = p.sem("O")
    nc.gpsimd.wait_ge(p.sem("V").h, ready_v)
    for c in range(NCH):
        sO.inc(nc.gpsimd.dma_start(out=out_x4[128 * c:128 * (c + 1), :],
                                   in_=XO(c, 0, TOWN)), 16)
    for e in (nc.gpsimd, nc.sync, nc.tensor, nc.vector, nc.scalar):
        e.wait_ge(sO.h, sO.v)
        if "GD" in p._sems:
            e.wait_ge(p.sem("GD").h, p.sem("GD").v)


def kernel(**inputs):
    in_maps = [_stage_core(inputs, c) for c in range(8)]
    p = build(upto=BUILD_UPTO, debug=DEBUG_OUT)
    res = run_bass_kernel_spmd(p.nc, in_maps, core_ids=list(range(8)))
    out = np.empty((B, S, D), dtype=np.float32)
    for c in range(8):
        b, half = c // 2, c % 2
        out[b, 2048 * half:2048 * half + 2048] = res.results[c]["o_x4"].T
    return out


BUILD_UPTO = "full"
DEBUG_OUT = ()


# version 05
# speedup vs baseline: 1.3619x; 1.0109x over previous
import contextlib
import numpy as np
import ml_dtypes

import concourse.bass as bass
import concourse.mybir as mybir
from concourse.bass_utils import run_bass_kernel_spmd

F32 = mybir.dt.float32
BF16 = mybir.dt.bfloat16
AF = mybir.ActivationFunctionType
ALU = mybir.AluOpType
AX = mybir.AxisListType

B, S, D = 4, 4096, 1024
NCH = 8
TOWN = 2048
HALO = 640
TEXT = HALO + TOWN
M = 128
CC = 128
DECAY, LR = 0.99, 0.01
DEC_C = float(DECAY ** CC)
NEG = -1.0e30
FH, FDH, WIN = 8, 64, 512
HID, HIDP = 2730, 2816
EPS = 1e-5

_f32 = np.float32
_bf16 = ml_dtypes.bfloat16


def _host_consts(half):
    c = {}
    i = np.arange(CC)
    dmT = np.where(i[:, None] < i[None, :],
                   LR * DECAY ** (-1.0 - i[:, None]), 0.0)
    c["dmaskT"] = (dmT * np.ones((CC, CC))).astype(_f32)
    c["decq512"] = np.broadcast_to(
        DECAY ** (np.arange(512) % 128), (128, 512)).astype(_f32)
    c["deck"] = (LR * DECAY ** (127.0 - i)).reshape(128, 1).astype(_f32)
    c["identity"] = np.eye(128, dtype=_f32)
    R = np.zeros((128, 128), dtype=_f32)
    for h0 in (0, 64):
        for k in range(32):
            R[h0 + 32 + k, h0 + k] = -1.0
            R[h0 + k, h0 + 32 + k] = 1.0
    c["Rperm"] = R
    oc2 = np.zeros((128, 2), dtype=_f32)
    oc2[:64, 0] = 1.0
    oc2[64:, 1] = 1.0
    c["ones2col"] = oc2
    e2 = np.zeros((2, 128), dtype=_f32)
    e2[0, :64] = 1.0
    e2[1, 64:] = 1.0
    c["e2"] = e2
    c["flag"] = np.full((128, 1), float(half), dtype=_f32)
    c["flagn"] = np.full((128, 1), 1.0 - float(half), dtype=_f32)
    c["epsc"] = np.full((128, 1), EPS, dtype=_f32)
    c["ones512"] = np.ones((1, 512), dtype=_f32)

    t0 = 2048 * half
    pos = np.arange(t0 - HALO, t0 + TOWN, dtype=np.float64)
    half_idx = np.arange(0, FDH, 2, dtype=np.float64)
    inv_freq = 1.0 / 10000.0 ** (half_idx / FDH)
    freqs = np.concatenate([pos[:, None] * inv_freq[None, :]] * 2, axis=1)
    base = (half_idx + 0.4 * FDH) / (1.4 * FDH)
    power = (pos - S // 2) / float(WIN // 2)
    scale = np.concatenate([base[None, :] ** power[:, None]] * 2, axis=1)
    cos, sin = np.cos(freqs), np.sin(freqs)
    for nm, tb in (("rotAq", cos * scale * 8.0), ("rotBq", sin * scale * 8.0),
                   ("rotAk", cos / scale), ("rotBk", sin / scale)):
        c[nm] = np.tile(tb.T.astype(_f32), (2, 1))

    kl = np.arange(128)[:, None]
    cf = np.arange(1408)[None, :]
    df = kl - cf + 384
    c["fmaskM"] = np.where((df > 0) | (df < -512), NEG, 0.0).astype(_f32)
    km = np.zeros((128, 640), dtype=_f32)
    if half == 0:
        km[:, 128:] = NEG
    c["khmask0"] = km
    return c


def _stage_core(inputs, core):
    b, half = core // 2, core % 2
    t0 = 2048 * half
    x = inputs["L0_attn_a_x"][b]
    d = dict(_host_consts(half))
    d["x_own"] = x[t0:t0 + TOWN].T
    d["x_pre"] = x[0:2048].T if half else np.zeros((D, 2048), dtype=_f32)
    d["x_halo"] = x[t0 - HALO:t0].T if half else np.zeros((D, HALO), dtype=_f32)
    d["wkT"] = inputs["L0_attn_a_Wk"].T
    d["wvT"] = inputs["L0_attn_a_Wv"].T
    d["wqT"] = inputs["L0_attn_a_Wq"].T
    d["woT"] = inputs["L0_attn_a_Wo"].T
    d["bk"] = inputs["L0_attn_a_bk"].reshape(128, 1)
    d["bq"] = inputs["L0_attn_a_bq"].reshape(128, 1)
    d["bkrow"] = inputs["L0_attn_a_bk"].reshape(1, 128)
    d["bvrow"] = inputs["L0_attn_a_bv"].reshape(1, 128)
    d["bo"] = inputs["L0_attn_a_bo"].reshape(D, 1)
    for l in range(2):
        d[f"wqkv{l}"] = inputs["L0_ffn_wqkv"][l]
        d[f"fwo{l}"] = inputs["L0_ffn_wo"][l]
        w1 = inputs["L0_ffn_w1"][l]
        w1p = np.zeros((D, 2 * HIDP), dtype=_f32)
        w1p[:, :HID] = w1[:, :HID]
        w1p[:, HIDP:HIDP + HID] = w1[:, HID:]
        d[f"w1p{l}"] = w1p
        b1 = inputs["L0_ffn_b1"][l]
        b1p = np.zeros((2 * HIDP, 1), dtype=_f32)
        b1p[:HID, 0] = b1[:HID]
        b1p[HIDP:HIDP + HID, 0] = b1[HID:]
        d[f"b1p{l}"] = b1p
        w2p = np.zeros((HIDP, D), dtype=_f32)
        w2p[:HID] = inputs["L0_ffn_w2"][l]
        d[f"w2p{l}"] = w2p
        d[f"b2_{l}"] = inputs["L0_ffn_b2"][l].reshape(D, 1)
        d[f"lng{l}"] = inputs["L0_ffn_ln_g"][l].reshape(D, 1)
        d[f"lnb{l}"] = inputs["L0_ffn_ln_b"][l].reshape(D, 1)
        d[f"lngb{l}"] = np.stack([inputs["L0_ffn_ln_g"][l],
                                  inputs["L0_ffn_ln_b"][l]])
        d[f"qs{l}"] = np.tile(inputs["L0_ffn_q_scale"][l], 2).reshape(128, 1)
        d[f"ks{l}"] = np.tile(inputs["L0_ffn_k_scale"][l], 2).reshape(128, 1)
    Pi = inputs["L0_attn_b_P_i"]
    myh = range(8 * half, 8 * half + 8)
    d["PiQ"] = np.concatenate([Pi[:, 192 * h:192 * h + 64] for h in myh], axis=1)
    d["PiK"] = np.concatenate([Pi[:, 192 * h + 64:192 * h + 128] for h in myh], axis=1)
    d["PiV"] = np.concatenate([Pi[:, 192 * h + 128:192 * h + 192] for h in myh], axis=1)
    d["Po"] = inputs["L0_attn_b_P_o"]
    return {k: np.ascontiguousarray(v, dtype=_f32) for k, v in d.items()}


class Sem:
    def __init__(self, handle):
        self.h = handle
        self.v = 0

    def inc(self, inst, n=1):
        inst.then_inc(self.h, n)
        self.v += n
        return self.v


class Seg:

    def __init__(self, t, off, cols, dtype):
        self.t, self.off, self.cols, self.dtype = t, off, cols, dtype

    def ap(self, a=0, b=None, p0=0, p1=128):
        b = self.cols if b is None else b
        if self.dtype == F32:
            return self.t[p0:p1, self.off + a:self.off + b]
        assert self.off % 1 == 0
        base = self.t[p0:p1, self.off:self.off + (self.cols + 1) // 2]
        return base.bitcast(BF16)[:, a:b]


class Arena:
    def __init__(self, p, name, cols):
        self.t = p.sb(name, [128, cols], F32)
        self.cols = cols
        self.off = 0
        self.name = name

    def alloc(self, cols, dtype=F32):
        fcols = cols if dtype == F32 else (cols + 1) // 2
        assert self.off + fcols <= self.cols, \
            f"arena {self.name} overflow: {self.off}+{fcols} > {self.cols}"
        s = Seg(self.t, self.off, cols, dtype)
        self.off += fcols
        return s

    def reset(self, off=0):
        self.off = off


class Prog:
    def __init__(self):
        self.nc = bass.Bass("TRN2", target_bir_lowering=False, debug=False)
        self.ctx = contextlib.ExitStack()
        self._sems = {}
        self.di = {}
        self.dbg_outs = {}
        self.debug = ()

    def sem(self, name):
        if name not in self._sems:
            self._sems[name] = Sem(self.ctx.enter_context(self.nc.semaphore(name)))
        return self._sems[name]

    def sb(self, name, shape, dtype):
        return self.ctx.enter_context(self.nc.sbuf_tensor(name, shape, dtype))

    def psum(self, name, shape, dtype=F32):
        return self.ctx.enter_context(self.nc.psum_tensor(name, shape, dtype))

    def barrier(self, tag):
        s = self.sem("bar")
        nc = self.nc
        engs = [nc.sync, nc.tensor, nc.vector, nc.scalar, nc.gpsimd]
        base = s.v
        for e in engs:
            s.inc(e.nop(), 1)
        for e in engs:
            e.wait_ge(s.h, base + 5)


class PsBank:

    def __init__(self, t):
        self.t = t
        self.free_at = None

    def acquire(self, eng):
        if self.free_at is not None:
            sem, v = self.free_at
            eng.wait_ge(sem.h, v)
            self.free_at = None
        return self.t

    def consumed(self, sem, v):
        self.free_at = (sem, v)


def build(upto="full", debug=()):
    p = Prog()
    p.debug = tuple(debug)
    nc = p.nc
    di = p.di

    def dram_in(name, shape, dtype=F32):
        di[name] = nc.dram_tensor(name, list(shape), dtype,
                                  kind="ExternalInput").ap()

    for nm, sh in [
        ("x_own", (D, TOWN)), ("x_pre", (D, 2048)), ("x_halo", (D, HALO)),
        ("flag", (128, 1)), ("flagn", (128, 1)),
        ("epsc", (128, 1)), ("ones512", (1, 512)),
        ("wkT", (D, 128)), ("wvT", (D, 128)), ("wqT", (D, 128)), ("woT", (128, D)),
        ("bk", (128, 1)), ("bq", (128, 1)), ("bkrow", (1, 128)), ("bvrow", (1, 128)),
        ("bo", (D, 1)),
        ("dmaskT", (128, 128)), ("decq512", (128, 512)), ("deck", (128, 1)),
        ("identity", (128, 128)), ("Rperm", (128, 128)),
        ("ones2col", (128, 2)), ("e2", (2, 128)),
        ("rotAq", (128, TEXT)), ("rotBq", (128, TEXT)),
        ("rotAk", (128, TEXT)), ("rotBk", (128, TEXT)),
        ("khmask0", (128, 640)),
        ("PiQ", (D, 512)), ("PiK", (D, 512)), ("PiV", (D, 512)), ("Po", (D, D)),
    ]:
        dram_in(nm, sh)
    dram_in("fmaskM", (128, 1408))
    for l in range(2):
        for nm, sh in [
            (f"wqkv{l}", (D, 1536)), (f"fwo{l}", (512, D)),
            (f"w1p{l}", (D, 2 * HIDP)), (f"b1p{l}", (2 * HIDP, 1)),
            (f"w2p{l}", (HIDP, D)), (f"b2_{l}", (D, 1)),
            (f"lng{l}", (D, 1)), (f"lnb{l}", (D, 1)),
            (f"lngb{l}", (2, D)),
            (f"qs{l}", (128, 1)), (f"ks{l}", (128, 1)),
        ]:
            dram_in(nm, sh)

    out_x4 = nc.dram_tensor("o_x4", [D, TOWN], F32, kind="ExternalOutput").ap()

    def dbg(name, shape, dtype=F32):
        if name in p.debug and name not in p.dbg_outs:
            p.dbg_outs[name] = nc.dram_tensor(
                "dbg_" + name, list(shape), dtype, kind="ExternalOutput").ap()
        return p.dbg_outs.get(name)

    x_spill = nc.dram_tensor("x_spill", [D, TOWN], F32, kind="Internal").ap()
    ag1_in = nc.dram_tensor("ag1_in", [D, HALO], BF16, kind="Internal").ap()
    ag1_out = nc.dram_tensor("ag1_out", [2 * D, HALO], BF16, kind="Internal").ap()
    ag2z_in = nc.dram_tensor("ag2z_in", [D, TOWN], BF16, kind="Internal").ap()
    ag2z_out = nc.dram_tensor("ag2z_out", [2 * D, TOWN], BF16, kind="Internal").ap()
    ag2x_in = nc.dram_tensor("ag2x_in", [D, HALO], BF16, kind="Internal").ap()
    ag2x_out = nc.dram_tensor("ag2x_out", [2 * D, HALO], BF16, kind="Internal").ap()
    ag3_in = nc.dram_tensor("ag3_in", [512, S], BF16, kind="Internal").ap()
    ag3_out = nc.dram_tensor("ag3_out", [1024, S], BF16, kind="Internal").ap()
    ag4_in = nc.dram_tensor("ag4_in", [D, HALO], BF16, kind="Internal").ap()
    ag4_out = nc.dram_tensor("ag4_out", [2 * D, HALO], BF16, kind="Internal").ap()
    RG = [[0, 1], [2, 3], [4, 5], [6, 7]]

    a1 = Arena(p, "a1", 19544)
    a2 = Arena(p, "a2", NCH * (TEXT + 1))
    a3 = Arena(p, "a3", 6656)
    sc = Arena(p, "sc", 768)
    flag = p.sb("flag_s", [128, 1], F32)
    flagn = p.sb("flagn_s", [128, 1], F32)
    epsc = p.sb("epsc_s", [128, 1], F32)
    ones1 = p.sb("ones1", [1, 128], F32)
    ones1b = p.sb("ones1b", [1, 128], BF16)
    onescol = p.sb("onescol", [128, 1], BF16)
    identb = p.sb("identb", [128, 128], BF16)
    srow = p.sb("srow", [2, 2048], F32)
    masksF = p.sb("masksF", [128, 1408], BF16)

    pp = [PsBank(p.psum(f"pp{i}", [128, 512])) for i in range(8)]

    block = p.ctx.enter_context(nc.Block())
    SY, PE, VE, ACT, GP = nc.sync, nc.tensor, nc.vector, nc.scalar, nc.gpsimd
    sP = p.sem("P")
    sV = p.sem("V")
    sA = p.sem("A")

    def dma(eng, out, in_, stream, **kw):
        sm = p.sem(stream)
        sm.inc(eng.dma_start(out=out, in_=in_, **kw), 16)
        return sm.v

    def wait_stream(eng, stream):
        sm = p.sem(stream)
        eng.wait_ge(sm.h, sm.v)

    def vchain(inst):
        vv = sV.inc(inst)
        VE.wait_ge(sV.h, vv)
        return vv

    xo = a1.alloc(16384, F32)

    def XO(c, a, b):
        return xo.ap(TOWN * c + a, TOWN * c + b)

    dma(SY, flag[:], di["flag"], "L1")
    dma(SY, flagn[:], di["flagn"], "L1")
    dma(SY, epsc[:], di["epsc"], "L1")
    dma(GP, identb[:], di["identity"], "G1")
    dma(GP, ones1b[:], di["ones512"][0:1, 0:128], "G1")
    for c in range(NCH):
        dma(SY, XO(c, 0, TOWN), di["x_own"][128 * c:128 * (c + 1), :], "L1")
    dma(GP, masksF[:], di["fmaskM"], "G1")

    def fmask(kk):
        return masksF[:, 896 - 128 * kk:1408 - 128 * kk]

    def bmask(j):
        return masksF[:, 384 - 128 * j:896 - 128 * j]
    dma(SY, srow[1:2, 1536:2048], di["ones512"], "L1")
    dma(SY, ones1[:], di["ones512"][0:1, 0:128], "L1")
    VE.memset(onescol[:], 1.0)

    def ln_group(get_in, put_out, W, scratch, banks, gb=None, ready_v=None):
        stA, stB, bcA, bcB = banks
        xbs, sqs, t1s = scratch["xb"], scratch["sq"], scratch["t1"]
        if ready_v is not None:
            ACT.wait_ge(sV.h, ready_v)
        tA = stA.acquire(PE)
        tB = stB.acquire(PE)
        slot_pe = scratch.setdefault("slot_pe", [0, 0])
        for c in range(NCH):
            xin = get_in(c)
            xb = xbs[c % 2]
            sq = sqs[c % 2]
            VE.wait_ge(sP.h, slot_pe[c % 2])
            ACT.wait_ge(sP.h, slot_pe[c % 2])
            vv = sV.inc(VE.tensor_copy(xb.ap(0, W), xin))
            aa = sA.inc(ACT.activation(sq.ap(0, W), xin, AF.Square))
            PE.wait_ge(sV.h, vv)
            PE.wait_ge(sA.h, aa)
            PE.matmul(tA[0:1, 0:W], onescol[:], xb.ap(0, W),
                      start=(c == 0), stop=(c == NCH - 1))
            pv = sP.inc(PE.matmul(tB[0:1, 0:W], onescol[:], sq.ap(0, W),
                                  start=(c == 0), stop=(c == NCH - 1)))
            slot_pe[c % 2] = pv
        pe_done = sP.v
        mu = srow[0:1, 0:0 + W]
        msq = srow[0:1, 512:512 + W]
        var = srow[0:1, 1024:1024 + W]
        sd = srow[0:1, 512:512 + W]
        rst = srow[0:1, 1024:1024 + W]
        nmb = srow[0:1, 1536:1536 + W]
        VE.wait_ge(sP.h, pe_done)
        vv = sV.inc(VE.tensor_scalar(mu, tA[0:1, 0:W], 1.0 / D, None,
                                     op0=ALU.mult))
        ACT.wait_ge(sV.h, vv)
        av = sA.inc(ACT.activation(msq, mu, AF.Square))
        VE.wait_ge(sA.h, av)
        vv = sV.inc(VE.scalar_tensor_tensor(
            var, tB[0:1, 0:W], 1.0 / D, msq,
            op0=ALU.mult, op1=ALU.subtract))
        stA.consumed(sV, vv)
        stB.consumed(sV, vv)
        ACT.wait_ge(sV.h, vv)
        av = sA.inc(ACT.activation(sd, var, AF.Ln, bias=epsc[0:1, :]))
        ACT.wait_ge(sA.h, av)
        av = sA.inc(ACT.activation(rst, sd, AF.Exp, scale=-0.5))
        VE.wait_ge(sA.h, av)
        vv = vchain(VE.scalar_tensor_tensor(nmb, mu, -1.0, rst,
                                            op0=ALU.mult, op1=ALU.mult))
        tbcA = bcA.acquire(PE)
        PE.wait_ge(sV.h, vv)
        if gb is not None:
            vv = vchain(VE.tensor_copy(scratch["lnrb"][0:1, 0:W], nmb))
            PE.wait_ge(sV.h, vv)
        pva = sP.inc(PE.matmul(tbcA[:, 0:W], ones1[:], rst,
                               start=True, stop=True))
        if gb is None:
            tbcB = bcB.acquire(PE)
            pvb = sP.inc(PE.matmul(tbcB[:, 0:W], ones1[:], nmb,
                                   start=True, stop=True))
            VE.wait_ge(sP.h, pvb)
        VE.wait_ge(sP.h, pva)
        last = None
        for c in range(NCH):
            xin = get_in(c)
            t1 = t1s.ap(0, W)
            if gb is not None:
                tbcB = bcB.acquire(PE)
                pvb = sP.inc(PE.matmul(tbcB[:, 0:W], gb(c),
                                       scratch["lnrb"][0:2, 0:W],
                                       start=True, stop=True))
            vchain(VE.tensor_tensor(t1, xin, tbcA[:, 0:W], op=ALU.mult))
            dst, lgcol = put_out(c)
            if gb is None:
                last = vchain(VE.scalar_tensor_tensor(
                    dst, t1, 1.0, tbcB[:, 0:W], op0=ALU.mult, op1=ALU.add))
            else:
                VE.wait_ge(sP.h, pvb)
                last = vchain(VE.scalar_tensor_tensor(
                    dst, t1, lgcol, tbcB[:, 0:W], op0=ALU.mult, op1=ALU.add))
                bcB.consumed(sV, last)
        bcA.consumed(sV, last)
        bcB.consumed(sV, last)
        return last

    za = a2.alloc(4096, BF16)
    kfm = a2.alloc(4096, BF16)
    qd0 = a2.alloc(3072, BF16)
    ktm = a2.alloc(4096, BF16)
    vtm = a2.alloc(4096, BF16)
    obuf = a2.alloc(2688, BF16)
    wob = a2.alloc(1024, BF16)
    awk = a2.alloc(512, F32)
    xpA = a2.alloc(4096, F32)
    xpB = a2.alloc(4096, F32)

    class _QD:
        def ap(self, a, b):
            return qd0.ap(a - 1024, b - 1024)
    qd = _QD()
    xh = a3.alloc(5120, F32)
    wkb = a3.alloc(1024, BF16)
    wvb = a3.alloc(1024, BF16)
    wqb = a3.alloc(1024, BF16)
    dmk = a1.alloc(128, F32)
    stf = a1.alloc(128, F32)
    stb = a1.alloc(128, BF16)
    wts = [a1.alloc(128, BF16), a1.alloc(128, BF16)]
    ktds = [a1.alloc(128, BF16), a1.alloc(128, BF16)]
    csm = a1.alloc(16, F32)
    bo8 = a1.alloc(8, F32)
    xb_s = [a1.alloc(512, BF16), a1.alloc(512, BF16)]
    sq_s = [a1.alloc(512, BF16), a1.alloc(512, BF16)]
    t1_s = a1.alloc(512, F32)

    brow_k = p.sb("brow_k", [1, 128], BF16)
    brow_v = p.sb("brow_v", [1, 128], BF16)

    for (dst, src) in ((wkb, "wkT"), (wvb, "wvT"), (wqb, "wqT")):
        dma(GP, dst.ap().rearrange("p (c m) -> p c m", m=128),
            di[src].rearrange("(c p) m -> p c m", p=128), "G1")
    dma(GP, wob.ap(), di["woT"], "G1")
    dma(GP, brow_k[:], di["bkrow"], "G1")
    dma(GP, brow_v[:], di["bvrow"], "G1")
    dma(SY, awk.ap(), di["decq512"], "L1")
    dma(SY, dmk.ap(), di["dmaskT"], "L1")
    dma(SY, csm.ap(0, 1), di["bk"], "L1")
    dma(SY, csm.ap(1, 2), di["bq"], "L1")
    dma(SY, csm.ap(2, 3), di["deck"], "L1")
    dma(SY, bo8.ap().rearrange("p (c o) -> p c o", o=1),
        di["bo"].rearrange("(c p) o -> p c o", p=128),
        allow_slow_non_contiguous=True, stream="L1")
    bk_col = lambda: csm.ap(0, 1)
    bq_col = lambda: csm.ap(1, 2)
    deck_col = lambda: csm.ap(2, 3)

    def load_xpre(g, seg, slot):
        for c in range(NCH):
            dma(SY, seg.ap(512 * c, 512 * (c + 1)),
                di["x_pre"][128 * c:128 * (c + 1), 512 * g:512 * (g + 1)],
                f"Lx{slot}")
        return p.sem(f"Lx{slot}").v

    ln_scr = {"xb": xb_s, "sq": sq_s, "t1": t1_s}

    xp_ready = [None, None]
    xp_ready[0] = load_xpre(0, xpA, 0)
    xp_ready[1] = load_xpre(1, xpB, 1)

    for e in (VE, PE, ACT):
        wait_stream(e, "L1")
        wait_stream(e, "G1")

    for g in range(8):
        own = g >= 4
        if not own:
            seg = (xpA, xpB)[g % 2]
            sm = p.sem(f"Lx{g % 2}")
            VE.wait_ge(sm.h, xp_ready[g % 2])
            ACT.wait_ge(sm.h, xp_ready[g % 2])
            get_in = lambda c, seg=seg: seg.ap(512 * c, 512 * c + 512)
        else:
            go = 512 * (g - 4)
            get_in = lambda c, go=go: XO(c, go, go + 512)

        def put_out(c):
            return za.ap(512 * c, 512 * c + 512), None

        lnv = ln_group(get_in, put_out, 512, ln_scr,
                       (pp[0], pp[1], pp[2], pp[3]))

        if not own and g + 2 < 4:
            SY.wait_ge(sV.h, lnv)
            xp_ready[g % 2] = load_xpre(g + 2, (xpA, xpB)[g % 2], g % 2)

        PE.wait_ge(sV.h, lnv)
        gs = 512 * g
        t = pp[4].acquire(PE)
        pv = None
        for c in range(NCH):
            mm = PE.matmul(t[:, 0:512], wkb.ap(128 * c, 128 * (c + 1)),
                           za.ap(512 * c, 512 * (c + 1)),
                           start=(c == 0), stop=(c == NCH - 1))
            if c == NCH - 1:
                pv = sP.inc(mm)
        VE.wait_ge(sP.h, pv)
        vv = sV.inc(VE.tensor_scalar(kfm.ap(gs, gs + 512), t[:, 0:512],
                                     bk_col(), None, op0=ALU.add))
        pp[4].consumed(sV, vv)
        if g >= 2:
            t = pp[5].acquire(PE)
            pv = None
            for c in range(NCH):
                mm = PE.matmul(t[:, 0:512], wqb.ap(128 * c, 128 * (c + 1)),
                               za.ap(512 * c, 512 * (c + 1)),
                               start=(c == 0), stop=(c == NCH - 1))
                if c == NCH - 1:
                    pv = sP.inc(mm)
            VE.wait_ge(sP.h, pv)
            vchain(VE.tensor_scalar(t1_s.ap(), t[:, 0:512], bq_col(), None,
                                    op0=ALU.add))
            vv = sV.inc(VE.tensor_tensor(qd.ap(gs, gs + 512), t1_s.ap(),
                                         awk.ap(), op=ALU.mult))
            pp[5].consumed(sV, vv)
        for srowb, wrow, dstseg in ((brow_k, wkb, ktm), (brow_v, wvb, vtm)):
            for ts in range(4):
                t = pp[6 if dstseg is ktm else 7].acquire(PE)
                sub = gs + 128 * ts
                for c in range(NCH):
                    PE.matmul(t[:, 0:128],
                              za.ap(512 * c + 128 * ts, 512 * c + 128 * (ts + 1)),
                              wrow.ap(128 * c, 128 * (c + 1)),
                              start=(c == 0), stop=False)
                pv = sP.inc(PE.matmul(t[:, 0:128], ones1b[:], srowb[:],
                                      start=False, stop=True))
                VE.wait_ge(sP.h, pv)
                vv = sV.inc(VE.tensor_copy(dstseg.ap(sub, sub + 128), t[:, 0:128]))
                (pp[6] if dstseg is ktm else pp[7]).consumed(sV, vv)

    vchain(VE.memset(stf.ap(), 0.0))
    vv = vchain(VE.tensor_copy(stb.ap(), stf.ap()))
    st_ready = vv
    for cch in range(32):
        cs = 128 * cch
        do_out = cch >= 11
        if do_out:
            t_at = pp[0].acquire(PE)
            pv = sP.inc(PE.matmul(t_at[:, 0:128], kfm.ap(cs, cs + 128),
                                  qd.ap(cs, cs + 128), start=True, stop=True))
            wt = wts[cch % 2]
            VE.wait_ge(sP.h, pv)
            vv = sV.inc(VE.tensor_tensor(wt.ap(), t_at[:, 0:128], dmk.ap(),
                                         op=ALU.mult))
            pp[0].consumed(sV, vv)
            wt_v = vv
        ktd = ktds[cch % 2]
        vv = sV.inc(VE.tensor_scalar(ktd.ap(), ktm.ap(cs, cs + 128),
                                     deck_col(), None, op0=ALU.mult))
        ktd_v = vv
        if do_out:
            t_o = pp[1].acquire(PE)
            PE.wait_ge(sV.h, max(st_ready, wt_v))
            PE.matmul(t_o[:, 0:128], stb.ap(), qd.ap(cs, cs + 128),
                      start=True, stop=False)
            pv_o = sP.inc(PE.matmul(t_o[:, 0:128], vtm.ap(cs, cs + 128), wt.ap(),
                                    start=False, stop=True))
        t_u = pp[2].acquire(PE)
        PE.wait_ge(sV.h, ktd_v)
        pv_u = sP.inc(PE.matmul(t_u[:, 0:128], ktd.ap(), vtm.ap(cs, cs + 128),
                                start=True, stop=True))
        VE.wait_ge(sP.h, pv_u)
        vchain(VE.scalar_tensor_tensor(stf.ap(), stf.ap(), DEC_C, t_u[:, 0:128],
                                       op0=ALU.mult, op1=ALU.add))
        vv = vchain(VE.tensor_copy(stb.ap(), stf.ap()))
        pp[2].consumed(sV, vv)
        st_ready = vv
        if cch == 15:
            vchain(VE.tensor_scalar(stf.ap(), stf.ap(), flag[:],
                                    None, op0=ALU.mult))
            st_ready = vchain(VE.tensor_copy(stb.ap(), stf.ap()))
        if do_out:
            VE.wait_ge(sP.h, pv_o)
            vv = sV.inc(VE.tensor_copy(obuf.ap(cs - 1408, cs - 1408 + 128),
                                       t_o[:, 0:128]))
            pp[1].consumed(sV, vv)
    obuf_v = sV.v

    for c in range(NCH):
        dma(SY, xh.ap(640 * c, 640 * (c + 1)),
            di["x_halo"][128 * c:128 * (c + 1), :], "LH")
    PE.wait_ge(sV.h, obuf_v)
    wait_stream(VE, "LH")
    tiles = [(0, 512, "h"), (512, 640, "h"),
             (640, 1152, "o"), (1152, 1664, "o"), (1664, 2176, "o"), (2176, 2688, "o")]
    for (ta, tb_, kind) in tiles:
        W = tb_ - ta
        for c in range(NCH):
            t = pp[c % 4].acquire(PE)
            pv = sP.inc(PE.matmul(t[:, 0:W], wob.ap(128 * c, 128 * (c + 1)),
                                  obuf.ap(ta, tb_), start=True, stop=True))
            VE.wait_ge(sP.h, pv)
            if kind == "h":
                dst = xh.ap(640 * c + ta, 640 * c + tb_)
                vchain(VE.scalar_tensor_tensor(dst, t[:, 0:W], bo8.ap(c, c + 1),
                                               dst, op0=ALU.add, op1=ALU.add))
                vv = vchain(VE.tensor_scalar(dst, dst, flag[:], None,
                                             op0=ALU.mult))
            else:
                dst = XO(c, ta - 640, tb_ - 640)
                vv = sV.inc(VE.scalar_tensor_tensor(
                    dst, t[:, 0:W], bo8.ap(c, c + 1), dst,
                    op0=ALU.add, op1=ALU.add))
            pp[c % 4].consumed(sV, vv)
    x1_v = sV.v

    for nm_, seg_, a_, w_ in (("za", za, 0, 4096), ("kfm", kfm, 0, 4096), ("qd", qd, 1024, 4096),
                              ("ktm", ktm, 0, 4096), ("vtm", vtm, 0, 4096),
                              ("obuf", obuf, 0, 2688)):
        if dbg(nm_, (128, w_ - a_), BF16) is not None:
            GP.wait_ge(sV.h, obuf_v)
            dma(GP, p.dbg_outs[nm_], seg_.ap(a_, w_), "GD")
    if dbg("x1", (D, TOWN)) is not None:
        GP.wait_ge(sV.h, x1_v)
        for c in range(NCH):
            dma(GP, p.dbg_outs["x1"][128 * c:128 * (c + 1), :],
                XO(c, 0, TOWN), "GD")
    if dbg("x1h", (D, HALO)) is not None:
        GP.wait_ge(sV.h, x1_v)
        for c in range(NCH):
            dma(GP, p.dbg_outs["x1h"][128 * c:128 * (c + 1), :],
                xh.ap(640 * c, 640 * (c + 1)), "GD")

    if upto == "x1":
        finish(p, out_x4, XO, x1_v)
        return p

    p.barrier("ub")
    a2.reset()
    UBW = TEXT + 1
    ubt = a2.alloc(NCH * UBW, F32)

    def UB(c, a, b):
        return ubt.ap(UBW * c + 1 + a, UBW * c + 1 + b)

    for c in range(NCH):
        VE.memset(ubt.ap(UBW * c, UBW * c + 1), 0.0)

    ln_tiles = [(0, 512), (512, 640), (640, 1152), (1152, 1664),
                (1664, 2176), (2176, 2688)]
    z0_v = 0
    for (ta, tb_) in ln_tiles:
        W = tb_ - ta

        def get_in(c, ta=ta, tb_=tb_):
            if tb_ <= 640:
                return xh.ap(640 * c + ta, 640 * c + tb_)
            return XO(c, ta - 640, tb_ - 640)

        def put_out(c, ta=ta, tb_=tb_):
            return UB(c, ta, tb_), None

        z0_v = ln_group(get_in, put_out, W, ln_scr,
                        (pp[4], pp[5], pp[6], pp[7]), ready_v=x1_v)

    if dbg("z0", (D, TEXT)) is not None:
        GP.wait_ge(sV.h, z0_v)
        for c in range(NCH):
            dma(GP, p.dbg_outs["z0"][128 * c:128 * (c + 1), :],
                UB(c, 0, TEXT), "GD")

    if upto == "z0":
        finish(p, out_x4, XO, x1_v)
        return p

    GP.wait_ge(sV.h, x1_v)
    for c in range(NCH):
        dma(GP, x_spill[128 * c:128 * (c + 1), :], XO(c, 0, TOWN), "GS")

    G3 = [(0, 512), (512, 1024), (1024, 1536), (1536, 2048),
          (2048, 2560), (2560, 2688)]
    G4 = [(512, 1024), (1024, 1536), (1536, 2048), (2048, 2560), (2560, 2688)]

    sc.reset()
    sqscr = sc.alloc(512, BF16)
    gelscr = sc.alloc(512, BF16)
    fsm = sc.alloc(64, F32)
    b1c = [sc.alloc(44, F32), sc.alloc(44, F32)]

    qs_col = lambda l: fsm.ap(2 * l, 2 * l + 1)
    ks_col = lambda l: fsm.ap(2 * l + 1, 2 * l + 2)
    b2_col = lambda l: fsm.ap(4 + l, 5 + l)

    for l in range(2):
        dma(SY, fsm.ap(2 * l, 2 * l + 1), di[f"qs{l}"], "L1")
        dma(SY, fsm.ap(2 * l + 1, 2 * l + 2), di[f"ks{l}"], "L1")
        dma(SY, fsm.ap(4 + l, 5 + l), di[f"b2_{l}"][0:128, :], "L1")
    b2c8 = [sc.alloc(8, F32), sc.alloc(8, F32)]
    lgc8 = [sc.alloc(8, F32), sc.alloc(8, F32)]
    assert sc.off <= sc.cols
    for l in range(2):
        dma(SY, b2c8[l].ap().rearrange("p (c o) -> p c o", o=1),
            di[f"b2_{l}"].rearrange("(c p) o -> p c o", p=128),
            allow_slow_non_contiguous=True, stream="L1")
        dma(SY, lgc8[l].ap().rearrange("p (c o) -> p c o", o=1),
            di[f"lng{l}"].rearrange("(c p) o -> p c o", p=128),
            allow_slow_non_contiguous=True, stream="L1")
        dma(SY, b1c[l].ap().rearrange("p (c o) -> p c o", o=1),
            di[f"b1p{l}"].rearrange("(c p) o -> p c o", p=128),
            allow_slow_non_contiguous=True, stream="L1")
    e2b0 = p.sb("e2b0", [1, 128], F32)
    e2b1 = p.sb("e2b1", [1, 128], F32)
    dma(SY, e2b0[:], di["e2"][0:1, :], "L1")
    dma(SY, e2b1[:], di["e2"][1:2, :], "L1")
    e2f = p.sb("e2f", [2, 128], F32)
    dma(SY, e2f[:], di["e2"], "L1")
    oc2b = p.sb("oc2b", [128, 2], BF16)
    dma(GP, oc2b[:], di["ones2col"], "G1")
    rpb = p.sb("rpb", [128, 128], BF16)
    dma(GP, rpb[:], di["Rperm"], "G1")
    khm0b = p.sb("khm0b", [128, 640], BF16)
    dma(GP, khm0b[:], di["khmask0"], "G1")

    for e in (GP, SY, PE, VE, ACT):
        wait_stream(e, "GS")
        wait_stream(e, "G1")
        wait_stream(e, "L1")
    p.barrier("ffn_setup")
    a1.reset()
    zcast = a1.alloc(4096, BF16)
    kbuf = a1.alloc(4 * TEXT, BF16)
    qbuf = a1.alloc(4 * (TEXT - 512), BF16)
    vtbuf = a1.alloc(21 * 520, BF16)
    e0buf = a1.alloc(512, F32)
    ptbuf = a1.alloc(1024, BF16)
    lrbuf = a1.alloc(512, BF16)
    obq = a1.alloc(4 * 512, BF16)
    A1_ATTN = a1.off
    a1.reset()
    abuf = a1.alloc(22 * 512, BF16)
    w2blk = a1.alloc(22 * 1024, BF16)
    a1.off = max(a1.off, A1_ATTN)
    LN2_BASE = 16896
    assert LN2_BASE >= a1.off - 2648
    xb_s2 = [Seg(a1.t, LN2_BASE + 0, 512, BF16),
             Seg(a1.t, LN2_BASE + 256, 512, BF16)]
    sq_s2 = [Seg(a1.t, LN2_BASE + 512, 512, BF16),
             Seg(a1.t, LN2_BASE + 768, 512, BF16)]
    t1_s2 = Seg(a1.t, LN2_BASE + 1024, 512, F32)
    ln_scr2 = {"xb": xb_s2, "sq": sq_s2, "t1": t1_s2}

    def KB(hp, a, b):
        return kbuf.ap(TEXT * hp + a, TEXT * hp + b)

    def QB(hp, a, b):
        return qbuf.ap((TEXT - 512) * hp + a - 512, (TEXT - 512) * hp + b - 512)

    def VT(t):
        return vtbuf.ap(520 * t, 520 * (t + 1))

    a3.reset()
    wqkvb = a3.alloc(8 * 1536, BF16)
    rots = a3.alloc(2 * 512, BF16)
    assert a3.off <= a3.cols
    fwob = Seg(a3.t, 0, 4 * 1024, BF16)
    z2roll = [Seg(a3.t, 0, 8 * 512, BF16), Seg(a3.t, 2048, 8 * 512, BF16)]
    w1blk = [Seg(a3.t, 4096, 8 * 128, BF16), Seg(a3.t, 4608, 8 * 128, BF16)]
    e0b2 = Seg(zcast.t, zcast.off + 1536, 512, F32)
    k1scr = Seg(e0buf.t, e0buf.off + 0, 512, BF16)
    k2scr = Seg(e0buf.t, e0buf.off + 256, 512, BF16)
    tascr = Seg(ptbuf.t, ptbuf.off + 0, 512, BF16)
    tbscr = Seg(ptbuf.t, ptbuf.off + 256, 512, BF16)

    lnrb = p.sb("lnrb", [2, 512], BF16)
    dma(GP, lnrb[1:2, :], di["ones512"], "G1")
    ln_scr2["lnrb"] = lnrb
    for e_ in (PE, VE, ACT, SY, GP):
        wait_stream(e_, "G1")
    lngbs = p.sb("lngbs", [2, D], BF16)

    G3 = [(0, 512), (512, 1024), (1024, 1536), (1536, 2048),
          (2048, 2560), (2560, 2688)]
    G4 = [(512, 1024), (1024, 1536), (1536, 2048), (2048, 2560), (2560, 2688)]

    def emit_ffn_layer(lidx, stage_tag):
        tag = f"{stage_tag}l{lidx}"
        dma(GP, wqkvb.ap().rearrange("p (c m) -> p c m", m=1536),
            di[f"wqkv{lidx}"].rearrange("(c p) m -> p c m", p=128),
            f"Lw_{tag}")
        dma(GP, lngbs[:], di[f"lngb{lidx}"], f"Lw_{tag}")
        wait_stream(PE, f"Lw_{tag}")

        rot_used = 0
        zc_pe = 0
        for g, (ta, tb_) in enumerate(G3):
            W = tb_ - ta
            st_r = f"Lr_{tag}"
            if rot_used:
                GP.wait_ge(sV.h, rot_used)
            wait_stream(GP, st_r)
            dma(GP, rots.ap(0, W), di["rotAk"][:, ta:tb_], st_r)
            dma(GP, rots.ap(512, 512 + W), di["rotBk"][:, ta:tb_], st_r)
            VE.wait_ge(sP.h, zc_pe)
            zcv = 0
            for c in range(NCH):
                off = 1 if c >= 4 else 0
                zcv = sV.inc(VE.tensor_copy(zcast.ap(512 * c, 512 * c + W),
                                            UB(c, ta - off, tb_ - off)))
            PE.wait_ge(sV.h, zcv)

            paths = [(0, hp) for hp in range(4)]
            if g > 0:
                paths += [(1, hp) for hp in range(4)]
            qtab_loaded = [False]
            prev_pr = [0]
            prev_pn2 = [0]

            def proj_emit(pi):
                isq, hp = paths[pi]
                bank = pp[hp % 4]
                t = bank.acquire(PE)
                cols = (0 if isq else 512) + 128 * hp
                pv = None
                for c in range(NCH):
                    mm = PE.matmul(t[:, 0:W],
                                   wqkvb.ap(1536 * c + cols,
                                            1536 * c + cols + 128),
                                   zcast.ap(512 * c, 512 * c + W),
                                   start=(c == 0), stop=(c == NCH - 1))
                    if c == NCH - 1:
                        pv = sP.inc(mm)
                return t, bank, pv

            def chain_emit(pi, t, bank, pv):
                isq, hp = paths[pi]
                if isq and not qtab_loaded[0]:
                    GP.wait_ge(sV.h, sV.v)
                    wait_stream(GP, st_r)
                    dma(GP, rots.ap(0, W), di["rotAq"][:, ta:tb_], st_r)
                    dma(GP, rots.ap(512, 512 + W), di["rotBq"][:, ta:tb_], st_r)
                    qtab_loaded[0] = True
                ACT.wait_ge(sP.h, pv)
                aa = sA.inc(ACT.activation(sqscr.ap(0, W), t[:, 0:W],
                                           AF.Square))
                PE.wait_ge(sA.h, aa)
                tn = pp[4].acquire(PE)
                pn = sP.inc(PE.matmul(tn[0:2, 0:W], oc2b[:], sqscr.ap(0, W),
                                      start=True, stop=True))
                VE.wait_ge(sP.h, pn)
                nv = vchain(VE.tensor_scalar(srow[0:2, 0:W], tn[0:2, 0:W],
                                             1e-24, None, op0=ALU.max))
                pp[4].consumed(sV, nv)
                ACT.wait_ge(sV.h, nv)
                av = sA.inc(ACT.activation(srow[0:2, 512:512 + W],
                                           srow[0:2, 0:W], AF.Ln))
                ACT.wait_ge(sA.h, av)
                ACT.wait_ge(sP.h, prev_pn2[0])
                av = sA.inc(ACT.activation(srow[0:2, 1024:1024 + W],
                                           srow[0:2, 512:512 + W],
                                           AF.Exp, scale=-0.5))
                PE.wait_ge(sA.h, av)
                tb2 = pp[5].acquire(PE)
                pn2 = sP.inc(PE.matmul(tb2[:, 0:W], e2f[:],
                                       srow[0:2, 1024:1024 + W],
                                       start=True, stop=True))
                prev_pn2[0] = pn2
                VE.wait_ge(sP.h, pn2)
                scol = qs_col(lidx) if isq else ks_col(lidx)
                vchain(VE.tensor_scalar(k1scr.ap(0, W), t[:, 0:W], scol,
                                        None, op0=ALU.mult))
                bank.consumed(sV, sV.v)
                VE.wait_ge(sP.h, prev_pr[0])
                k2v = vchain(VE.tensor_tensor(k2scr.ap(0, W),
                                              k1scr.ap(0, W),
                                              tb2[:, 0:W], op=ALU.mult))
                pp[5].consumed(sV, k2v)
                return isq, hp, k2v

            def tail_emit(isq, hp, k2v):
                PE.wait_ge(sV.h, k2v)
                tr = pp[6].acquire(PE)
                pr = sP.inc(PE.matmul(tr[:, 0:W], rpb[:], k2scr.ap(0, W),
                                      start=True, stop=True))
                prev_pr[0] = pr
                ra = rots.ap(0, W)
                rb = rots.ap(512, 512 + W)
                wait_stream(VE, st_r)
                vchain(VE.tensor_tensor(tascr.ap(0, W), k2scr.ap(0, W), ra,
                                        op=ALU.mult))
                VE.wait_ge(sP.h, pr)
                vchain(VE.tensor_tensor(tbscr.ap(0, W), tr[:, 0:W], rb,
                                        op=ALU.mult))
                pp[6].consumed(sV, sV.v)
                dst = QB(hp, ta, tb_) if isq else KB(hp, ta, tb_)
                vchain(VE.tensor_tensor(dst, tascr.ap(0, W),
                                        tbscr.ap(0, W), op=ALU.add))

            prev = None
            prevchain = None
            for pi in range(len(paths)):
                t_, b_, pv_ = proj_emit(pi)
                if prevchain is not None:
                    tail_emit(*prevchain)
                prevchain = chain_emit(pi, t_, b_, pv_)
            tail_emit(*prevchain)
            rot_used = sV.v
            for ts_ in range(W // 128):
                tvi = (ta + 128 * ts_) // 128
                t = pp[7].acquire(PE)
                pv = None
                for c in range(NCH):
                    mm = PE.matmul(t[:, 0:512],
                                   zcast.ap(512 * c + 128 * ts_,
                                            512 * c + 128 * (ts_ + 1)),
                                   wqkvb.ap(1536 * c + 1024, 1536 * c + 1536),
                                   start=(c == 0), stop=(c == NCH - 1))
                    if c == NCH - 1:
                        pv = sP.inc(mm)
                VE.wait_ge(sP.h, pv)
                vv = sV.inc(VE.tensor_copy(
                    VT(tvi).rearrange("p (h j) -> p h j", j=65)[:, :, 0:64],
                    t[:, 0:512].rearrange("p (h j) -> p h j", j=64)))
                pp[7].consumed(sV, vv)
            zc_pe = sP.v
        vt1 = vchain(VE.memset(
            vtbuf.ap().rearrange("p (t h j) -> p t h j", h=8, j=65)[:, :, :, 64:65],
            1.0))

        GP.wait_ge(sP.h, sP.v)
        dma(GP, fwob.ap().rearrange("p (c m) -> p c m", m=1024),
            di[f"fwo{lidx}"].rearrange("(c p) m -> p c m", p=128),
            f"Lwo_{tag}")
        wait_stream(PE, f"Lwo_{tag}")

        PE.wait_ge(sV.h, vt1)
        pt_cons = [0, 0]
        e0_cons = 0
        for qg, (qa, qb_) in enumerate(G4):
            W = qb_ - qa
            nk = (W + 512) // 128
            for hp in range(4):
                toA = pp[4].acquire(PE)
                toB = pp[5].acquire(PE)
                avs = [0, 0]
                pvk = {}
                sbk = {}

                def emit_qk(kk):
                    ka = qa - 512 + 128 * kk
                    banks = (pp[(2 * kk) % 4], pp[(2 * kk + 1) % 4])
                    sbk[kk] = banks
                    pvs_ = []
                    for hh in range(2):
                        tdst = banks[hh].acquire(PE)
                        PE.matmul(tdst[:, 0:W], identb[:], fmask(kk)[:, 0:W],
                                  start=True, stop=False)
                        if (qg == 0 and kk <= 3) or (qg == 1 and kk == 0):
                            PE.matmul(tdst[:, 0:W], identb[:],
                                      khm0b[:, 128:128 + W],
                                      start=False, stop=False)
                        if qg == 0 and kk == 4:
                            PE.matmul(tdst[:, 0:W], identb[:], khm0b[:, 0:W],
                                      start=False, stop=False)
                        pvs_.append(sP.inc(PE.matmul(
                            tdst[:, 0:W],
                            KB(hp, ka, ka + 128)[64 * hh:64 * (hh + 1), :],
                            QB(hp, qa, qb_)[64 * hh:64 * (hh + 1), :],
                            start=False, stop=True)))
                    pvk[kk] = pvs_

                def emit_tail(kk):
                    nonlocal e0_cons
                    banks = sbk[kk]
                    for hh in range(2):
                        tdst = banks[hh].t
                        e0 = e0buf.ap(0, W) if hh == 0 else e0b2.ap(0, W)
                        ACT.wait_ge(sP.h, pvk[kk][hh])
                        ACT.wait_ge(sV.h, e0_cons)
                        aa = sA.inc(ACT.activation(e0, tdst[:, 0:W],
                                                   AF.Exp, scale=0.25))
                        ACT.wait_ge(sA.h, aa)
                        aa = sA.inc(ACT.activation(e0, e0, AF.Square))
                        banks[hh].consumed(sA, aa)
                        VE.wait_ge(sA.h, aa)
                        VE.wait_ge(sP.h, pt_cons[hh])
                        ptv = sV.inc(VE.tensor_tensor(
                            ptbuf.ap(512 * hh, 512 * hh + W), e0, e0,
                            op=ALU.mult))
                        if hh == 1:
                            e0_cons = ptv
                        PE.wait_ge(sV.h, ptv)
                        to = toA if hh == 0 else toB
                        h = 2 * hp + hh
                        av = sP.inc(PE.matmul(
                            to[0:65, 0:W],
                            VT((qa - 512 + 128 * kk) // 128)[:, 65 * h:65 * (h + 1)],
                            ptbuf.ap(512 * hh, 512 * hh + W),
                            start=(kk == 0), stop=(kk == nk - 1)))
                        pt_cons[hh] = av
                        avs[hh] = av

                emit_qk(0)
                for kk in range(1, nk):
                    emit_qk(kk)
                    emit_tail(kk - 1)
                emit_tail(nk - 1)
                avA, avB = avs
                VE.wait_ge(sP.h, max(avA, avB))
                vchain(VE.tensor_copy(srow[0:1, 0:512][:, 0:W], toA[64:65, 0:W]))
                lv = vchain(VE.tensor_copy(srow[0:1, 512:512 + W],
                                           toB[64:65, 0:W]))
                ACT.wait_ge(sV.h, lv)
                for seg0 in (0, 512):
                    av = sA.inc(ACT.activation(
                        srow[0:1, 1024 + seg0:1024 + seg0 + W],
                        srow[0:1, seg0:seg0 + W], AF.Ln))
                    ACT.wait_ge(sA.h, av)
                    av = sA.inc(ACT.activation(
                        srow[0:1, 1024 + seg0:1024 + seg0 + W],
                        srow[0:1, 1024 + seg0:1024 + seg0 + W],
                        AF.Exp, scale=-1.0))
                PE.wait_ge(sA.h, av)
                tl = pp[6].acquire(PE)
                PE.matmul(tl[:, 0:W], e2b0[:], srow[0:1, 1024:1024 + W],
                          start=True, stop=False)
                pl = sP.inc(PE.matmul(tl[:, 0:W], e2b1[:],
                                      srow[0:1, 1536:1536 + W],
                                      start=False, stop=True))
                VE.wait_ge(sP.h, pl)
                lrv = vchain(VE.tensor_copy(lrbuf.ap(0, W), tl[:, 0:W]))
                pp[6].consumed(sV, lrv)
                v1 = vchain(VE.tensor_tensor(
                    obq.ap(512 * hp, 512 * hp + W, p0=0, p1=64),
                    toA[0:64, 0:W], lrbuf.ap(0, W, p0=0, p1=64), op=ALU.mult))
                v2 = vchain(VE.tensor_tensor(
                    obq.ap(512 * hp, 512 * hp + W, p0=64, p1=128),
                    toB[0:64, 0:W], lrbuf.ap(0, W, p0=64, p1=128),
                    op=ALU.mult))
                pp[4].consumed(sV, v1)
                pp[5].consumed(sV, v2)
            ob_v = sV.v
            PE.wait_ge(sV.h, ob_v)
            for dc in range(NCH):
                t = pp[6 + (dc % 2)].acquire(PE)
                pv = None
                for hp in range(4):
                    mm = PE.matmul(t[:, 0:W],
                                   fwob.ap(1024 * hp + 128 * dc,
                                           1024 * hp + 128 * dc + 128),
                                   obq.ap(512 * hp, 512 * hp + W),
                                   start=(hp == 0), stop=(hp == 3))
                    if hp == 3:
                        pv = sP.inc(mm)
                VE.wait_ge(sP.h, pv)
                dst = UB(dc, qa, qb_)
                vv = vchain(VE.tensor_tensor(dst, t[:, 0:W], dst, op=ALU.add))
                if qg == 0:
                    vv = vchain(VE.tensor_scalar(UB(dc, 512, 640),
                                                 UB(dc, 512, 640), flag[:],
                                                 None, op0=ALU.mult))
                pp[6 + (dc % 2)].consumed(sV, vv)

        p.barrier(f"mlp_{tag}")
        dma(GP, w2blk.ap().rearrange("p (c m) -> p c m", m=1024),
            di[f"w2p{lidx}"].rearrange("(c p) m -> p c m", p=128),
            f"Lw2_{tag}")

        def ln2(g, slot):
            ta = 640 + 512 * g

            def get_in(c):
                off = 1 if c >= 4 else 0
                return UB(c, ta - off, ta + 512 - off)

            def put_out(c):
                return (z2roll[slot].ap(512 * c, 512 * (c + 1)),
                        lgc8[lidx].ap(c, c + 1))

            def lgrow(c):
                return lngbs[0:2, 128 * c:128 * (c + 1)]

            return ln_group(get_in, put_out, 512, ln_scr2,
                            (pp[0], pp[1], pp[6], pp[7]), gb=lgrow,
                            ready_v=sV.v)

        wblk_used = [0, 0]
        gel_cons = [0]

        def mlp(g, slot):
            ta = 640 + 512 * g
            z2 = z2roll[slot]

            def w1_load(ob):
                st_w = f"Lw1{ob % 2}_{tag}"
                if wblk_used[ob % 2]:
                    GP.wait_ge(sP.h, wblk_used[ob % 2])
                dma(GP, w1blk[ob % 2].ap().rearrange("p (c o) -> p c o", o=128),
                    di[f"w1p{lidx}"].rearrange("(c p) o -> p c o", p=128)
                    [:, :, 128 * ob:128 * (ob + 1)], st_w)
                return p.sem(st_w).v

            w1_ld = {0: w1_load(0), 1: w1_load(1)}
            for ob in range(44):
                st_w = f"Lw1{ob % 2}_{tag}"
                PE.wait_ge(p.sem(st_w).h, w1_ld[ob % 2])
                t = pp[2 + (ob % 2)].acquire(PE)
                pv = None
                for c in range(NCH):
                    mm = PE.matmul(t[:, 0:512],
                                   w1blk[ob % 2].ap(128 * c, 128 * (c + 1)),
                                   z2.ap(512 * c, 512 * (c + 1)),
                                   start=(c == 0), stop=(c == NCH - 1))
                    if c == NCH - 1:
                        pv = sP.inc(mm)
                wblk_used[ob % 2] = pv
                if ob + 2 < 44:
                    w1_ld[ob % 2] = w1_load(ob + 2)
                if ob < 22:
                    VE.wait_ge(sP.h, pv)
                    vv = sV.inc(VE.tensor_scalar(
                        abuf.ap(512 * ob, 512 * (ob + 1)), t[:, 0:512],
                        b1c[lidx].ap(ob, ob + 1), None, op0=ALU.add))
                    pp[2 + (ob % 2)].consumed(sV, vv)
                else:
                    hb = ob - 22
                    ACT.wait_ge(sP.h, pv)
                    ACT.wait_ge(sV.h, gel_cons[0])
                    aa = sA.inc(ACT.activation(gelscr.ap(0, 512), t[:, 0:512],
                                               AF.Gelu,
                                               bias=b1c[lidx].ap(ob, ob + 1)))
                    pp[2 + (ob % 2)].consumed(sA, aa)
                    VE.wait_ge(sA.h, aa)
                    gv = vchain(VE.tensor_tensor(
                        abuf.ap(512 * hb, 512 * (hb + 1)),
                        abuf.ap(512 * hb, 512 * (hb + 1)),
                        gelscr.ap(0, 512), op=ALU.mult))
                    gel_cons[0] = gv
            h_v = sV.v
            PE.wait_ge(sV.h, h_v)
            wait_stream(PE, f"Lw2_{tag}")
            for half in range(2):
                ts2 = [None] * 4
                pvs = [None] * 4
                for hc in range(22):
                    for di_ in range(4):
                        dc = 4 * half + di_
                        if hc == 0:
                            ts2[di_] = pp[2 + di_].acquire(PE)
                        mm = PE.matmul(ts2[di_][:, 0:512],
                                       w2blk.ap(1024 * hc + 128 * dc,
                                                1024 * hc + 128 * dc + 128),
                                       abuf.ap(512 * hc, 512 * (hc + 1)),
                                       start=(hc == 0), stop=(hc == 21))
                        if hc == 21:
                            pvs[di_] = sP.inc(mm)
                for di_ in range(4):
                    dc = 4 * half + di_
                    VE.wait_ge(sP.h, pvs[di_])
                    dst = UB(dc, ta, ta + 512)
                    vv = vchain(VE.scalar_tensor_tensor(
                        dst, ts2[di_][:, 0:512], b2c8[lidx].ap(dc, dc + 1),
                        dst, op0=ALU.add, op1=ALU.add))
                    pp[2 + di_].consumed(sV, vv)

        ln2(0, 0)
        ln2(1, 1)
        mlp(0, 0)
        ln2(2, 0)
        mlp(1, 1)
        ln2(3, 1)
        mlp(2, 0)
        mlp(3, 1)
        p.barrier(f"eol_{tag}")
        return sV.v

    def ag_exchange(ag_in, ag_out, tag):
        GP.wait_ge(sV.h, sV.v)
        for c in range(NCH):
            dma(GP, ag_in[128 * c:128 * (c + 1), :], UB(c, 2048, TEXT),
                f"GA_{tag}")
        wait_stream(GP, f"GA_{tag}")
        sc_ = p.sem(f"CC_{tag}")
        sc_.inc(GP.collective_compute(
            "AllGather", ALU.bypass, replica_groups=RG,
            ins=[ag_in.opt()], outs=[ag_out.opt()]), 1)
        GP.wait_ge(sc_.h, sc_.v)
        for c in range(NCH):
            dma(GP, UB(c, 0, HALO), ag_out[128 * c:128 * (c + 1), :],
                f"GH_{tag}")
        wait_stream(VE, f"GH_{tag}")
        hv = 0
        for c in range(NCH):
            hv = vchain(VE.tensor_scalar(UB(c, 0, HALO), UB(c, 0, HALO),
                                         flag[:], None, op0=ALU.mult))
        return hv

    def emit_ffn_stage(stage_tag, agi, ago):
        emit_ffn_layer(0, stage_tag)
        if stage_tag == "s2" and dbg("u1", (D, TOWN)) is not None:
            GP.wait_ge(sV.h, sV.v)
            for c in range(NCH):
                dma(GP, p.dbg_outs["u1"][128 * c:128 * (c + 1), :],
                    UB(c, HALO, TEXT), "GD")
            for e_ in (VE, GP, PE, ACT):
                wait_stream(e_, "GD")
        hv = ag_exchange(agi, ago, stage_tag)
        PE.wait_ge(sV.h, hv)
        ACT.wait_ge(sV.h, hv)
        emit_ffn_layer(1, stage_tag)

    emit_ffn_stage("s2", ag1_in, ag1_out)

    p.barrier("x2")
    a1.reset()
    xo2 = a1.alloc(16384, F32)

    def XO2(c, a, b):
        return xo2.ap(TOWN * c + a, TOWN * c + b)

    wait_stream(SY, "GS")
    for c in range(NCH):
        dma(SY, XO2(c, 0, TOWN), x_spill[128 * c:128 * (c + 1), :], "LS2")
    wait_stream(VE, "LS2")
    x2_v = 0
    for c in range(NCH):
        dst = xo2.ap(TOWN * c, TOWN * (c + 1))
        x2_v = vchain(VE.tensor_tensor(dst, dst, UB(c, HALO, TEXT),
                                       op=ALU.add))
    if dbg("x2", (D, TOWN)) is not None:
        GP.wait_ge(sV.h, x2_v)
        for c in range(NCH):
            dma(GP, p.dbg_outs["x2"][128 * c:128 * (c + 1), :],
                XO2(c, 0, TOWN), "GD")
        for e_ in (VE, GP, PE, ACT):
            wait_stream(e_, "GD")
    if upto == "x2":
        finish(p, out_x4, XO2, x2_v)
        return p

    a2.reset()
    kbb = a2.alloc(4 * S, BF16)
    vtb = a2.alloc(32 * 520, BF16)
    zrd = a2.alloc(8 * 512, BF16)
    qwn = a2.alloc(4 * 512, BF16)
    ost = a2.alloc(512, BF16)
    e0b2b = a2.alloc(512, F32)
    zown = zrd

    def KBB(hp, a, b):
        return kbb.ap(S * hp + a, S * hp + b)

    def VTB(t):
        return vtb.ap(520 * t, 520 * (t + 1))

    a3.reset()
    pib = [Seg(a3.t, 2048 * i, 8 * 512, BF16) for i in range(3)]
    pos = Seg(a3.t, 5120, 2 * 1024, BF16)

    for i, nm in enumerate(("PiQ", "PiK", "PiV")):
        dma(GP, pib[i].ap().rearrange("p (c m) -> p c m", m=512),
            di[nm].rearrange("(c p) m -> p c m", p=128), "LPi")

    for g in range(4):
        wait_stream(VE, "GZ")

        def get_in(c, g=g):
            return XO2(c, 512 * g, 512 * (g + 1))

        def put_out(c):
            return zown.ap(512 * c, 512 * (c + 1)), None

        zv = ln_group(get_in, put_out, 512, ln_scr2,
                      (pp[0], pp[1], pp[6], pp[7]), ready_v=x2_v)
        SY.wait_ge(sV.h, zv)
        for c in range(NCH):
            dma(SY, ag2z_in[128 * c:128 * (c + 1), 512 * g:512 * (g + 1)],
                zown.ap(512 * c, 512 * (c + 1)), "GZ")
    GP.wait_ge(sV.h, x2_v)
    for c in range(NCH):
        dma(GP, ag2x_in[128 * c:128 * (c + 1), :],
            XO2(c, TOWN - HALO, TOWN), "GX")
    wait_stream(GP, "GZ")
    wait_stream(GP, "GX")
    ccz = p.sem("CCz")
    ccz.inc(GP.collective_compute("AllGather", ALU.bypass, replica_groups=RG,
                                  ins=[ag2z_in.opt()], outs=[ag2z_out.opt()]), 1)
    ccz.inc(GP.collective_compute("AllGather", ALU.bypass, replica_groups=RG,
                                  ins=[ag2x_in.opt()], outs=[ag2x_out.opt()]), 1)
    GP.wait_ge(ccz.h, ccz.v)
    wait_stream(PE, "LPi")

    def zread_group(g, stream):
        sl = g // 4
        co = 512 * (g % 4)
        for c in range(NCH):
            dma(SY, zrd.ap(512 * c, 512 * (c + 1)),
                ag2z_out[1024 * sl + 128 * c:1024 * sl + 128 * (c + 1),
                         co:co + 512], stream)

    zr_pe = [0]
    for g in range(8):
        SY.wait_ge(sP.h, zr_pe[0])
        SY.wait_ge(ccz.h, 1)
        zread_group(g, "LZ")
        wait_stream(PE, "LZ")
        for hp in range(4):
            t = pp[hp % 2].acquire(PE)
            pv = None
            for c in range(NCH):
                mm = PE.matmul(t[:, 0:512],
                               pib[1].ap(512 * c + 128 * hp, 512 * c + 128 * (hp + 1)),
                               zrd.ap(512 * c, 512 * (c + 1)),
                               start=(c == 0), stop=(c == NCH - 1))
                if c == NCH - 1:
                    pv = sP.inc(mm)
            VE.wait_ge(sP.h, pv)
            vv = sV.inc(VE.tensor_copy(KBB(hp, 512 * g, 512 * (g + 1)),
                                       t[:, 0:512]))
            pp[hp % 2].consumed(sV, vv)
        for ts_ in range(4):
            tvi = 4 * g + ts_
            t = pp[2 + ts_ % 2].acquire(PE)
            pv = None
            for c in range(NCH):
                mm = PE.matmul(t[:, 0:512],
                               zrd.ap(512 * c + 128 * ts_, 512 * c + 128 * (ts_ + 1)),
                               pib[2].ap(512 * c, 512 * (c + 1)),
                               start=(c == 0), stop=(c == NCH - 1))
                if c == NCH - 1:
                    pv = sP.inc(mm)
            VE.wait_ge(sP.h, pv)
            vv = sV.inc(VE.tensor_copy(
                VTB(tvi).rearrange("p (h j) -> p h j", j=65)[:, :, 0:64],
                t[:, 0:512].rearrange("p (h j) -> p h j", j=64)))
            pp[2 + ts_ % 2].consumed(sV, vv)
        zr_pe[0] = sP.v
    vt1 = vchain(VE.memset(
        vtb.ap().rearrange("p (t h j) -> p t h j", h=8, j=65)[:, :, :, 64:65],
        1.0))
    PE.wait_ge(sV.h, vt1)

    pt_cons = [0, 0]
    e0_cons = [0]
    qw_pe = [0]
    wait_stream(SY, "LZ")
    SY.wait_ge(sP.h, zr_pe[0])
    for qw in range(8):
        SY.wait_ge(sP.h, qw_pe[0])
        zread_group(qw, "LZq")
        wait_stream(PE, "LZq")
        for hp in range(4):
            t = pp[7].acquire(PE)
            pv = None
            for c in range(NCH):
                mm = PE.matmul(t[:, 0:512],
                               pib[0].ap(512 * c + 128 * hp, 512 * c + 128 * (hp + 1)),
                               zrd.ap(512 * c, 512 * (c + 1)),
                               start=(c == 0), stop=(c == NCH - 1))
                if c == NCH - 1:
                    pv = sP.inc(mm)
            VE.wait_ge(sP.h, pv)
            vv = sV.inc(VE.tensor_copy(qwn.ap(512 * hp, 512 * (hp + 1)),
                                       t[:, 0:512]))
            pp[7].consumed(sV, vv)
        qw_pe[0] = sP.v
        PE.wait_ge(sV.h, sV.v)
        nk = 4 * (qw + 1)
        for hp in range(4):
            toA = pp[4].acquire(PE)
            toB = pp[5].acquire(PE)
            avs = [0, 0]
            pvk = {}
            sbk = {}

            def emit_qk(kk):
                ka = 128 * kk
                banks = (pp[(2 * kk) % 4], pp[(2 * kk + 1) % 4])
                sbk[kk] = banks
                pvs_ = []
                for hh in range(2):
                    tdst = banks[hh].acquire(PE)
                    first = True
                    if kk >= nk - 4:
                        PE.matmul(tdst[:, 0:512], identb[:],
                                  bmask(kk - (nk - 4))[:, 0:512],
                                  start=True, stop=False)
                        first = False
                    pvs_.append(sP.inc(PE.matmul(
                        tdst[:, 0:512],
                        KBB(hp, ka, ka + 128)[64 * hh:64 * (hh + 1), :],
                        qwn.ap(512 * hp, 512 * (hp + 1))[64 * hh:64 * (hh + 1), :],
                        start=first, stop=True)))
                pvk[kk] = pvs_

            def emit_tail(kk):
                banks = sbk[kk]
                for hh in range(2):
                    tdst = banks[hh].t
                    e0 = e0buf.ap(0, 512) if hh == 0 else e0b2b.ap(0, 512)
                    ACT.wait_ge(sP.h, pvk[kk][hh])
                    ACT.wait_ge(sV.h, e0_cons[0])
                    aa = sA.inc(ACT.activation(e0, tdst[:, 0:512],
                                               AF.Exp, scale=1.0 / 32.0))
                    ACT.wait_ge(sA.h, aa)
                    aa = sA.inc(ACT.activation(e0, e0, AF.Square))
                    banks[hh].consumed(sA, aa)
                    VE.wait_ge(sA.h, aa)
                    VE.wait_ge(sP.h, pt_cons[hh])
                    ptv = sV.inc(VE.tensor_tensor(
                        ptbuf.ap(512 * hh, 512 * hh + 512), e0, e0,
                        op=ALU.mult))
                    if hh == 1:
                        e0_cons[0] = ptv
                    PE.wait_ge(sV.h, ptv)
                    to = toA if hh == 0 else toB
                    h = 2 * hp + hh
                    av = sP.inc(PE.matmul(
                        to[0:65, 0:512],
                        VTB(kk)[:, 65 * h:65 * (h + 1)],
                        ptbuf.ap(512 * hh, 512 * hh + 512),
                        start=(kk == 0), stop=(kk == nk - 1)))
                    pt_cons[hh] = av
                    avs[hh] = av

            emit_qk(0)
            for kk in range(1, nk):
                emit_qk(kk)
                emit_tail(kk - 1)
            emit_tail(nk - 1)
            avA, avB = avs
            VE.wait_ge(sP.h, max(avA, avB))
            vchain(VE.tensor_copy(srow[0:1, 0:512], toA[64:65, 0:512]))
            lv = vchain(VE.tensor_copy(srow[0:1, 512:1024], toB[64:65, 0:512]))
            ACT.wait_ge(sV.h, lv)
            for seg0 in (0, 512):
                av = sA.inc(ACT.activation(srow[0:1, 1024 + seg0:1536 + seg0],
                                           srow[0:1, seg0:512 + seg0], AF.Ln))
                ACT.wait_ge(sA.h, av)
                av = sA.inc(ACT.activation(srow[0:1, 1024 + seg0:1536 + seg0],
                                           srow[0:1, 1024 + seg0:1536 + seg0],
                                           AF.Exp, scale=-1.0))
            PE.wait_ge(sA.h, av)
            tl = pp[6].acquire(PE)
            PE.matmul(tl[:, 0:512], e2b0[:], srow[0:1, 1024:1536],
                      start=True, stop=False)
            pl = sP.inc(PE.matmul(tl[:, 0:512], e2b1[:], srow[0:1, 1536:2048],
                                  start=False, stop=True))
            VE.wait_ge(sP.h, pl)
            lrv = vchain(VE.tensor_copy(lrbuf.ap(0, 512), tl[:, 0:512]))
            pp[6].consumed(sV, lrv)
            wait_stream(VE, "GO")
            vchain(VE.tensor_tensor(ost.ap(0, 512, p0=0, p1=64),
                                    toA[0:64, 0:512],
                                    lrbuf.ap(0, 512, p0=0, p1=64), op=ALU.mult))
            ov = vchain(VE.tensor_tensor(ost.ap(0, 512, p0=64, p1=128),
                                         toB[0:64, 0:512],
                                         lrbuf.ap(0, 512, p0=64, p1=128),
                                         op=ALU.mult))
            pp[4].consumed(sV, ov)
            pp[5].consumed(sV, ov)
            SY.wait_ge(sV.h, ov)
            dma(SY, ag3_in[128 * hp:128 * (hp + 1), 512 * qw:512 * (qw + 1)],
                ost.ap(0, 512), "GO")

    for nm_, seg_, w_ in (("Kbb", kbb, 4 * S), ("Vtb", vtb, 32 * 520)):
        if dbg(nm_, (128, w_), BF16) is not None:
            GP.wait_ge(sV.h, sV.v)
            dma(GP, p.dbg_outs[nm_], seg_.ap(0, w_), "GD")
            wait_stream(GP, "GD")
    if dbg("Ob", (512, S), BF16) is not None:
        wait_stream(GP, "GO")
        dma(GP, p.dbg_outs["Ob"], ag3_in, "GD")
        wait_stream(GP, "GD")
    wait_stream(GP, "GO")
    cc3 = p.sem("CC3")
    cc3.inc(GP.collective_compute("AllGather", ALU.bypass, replica_groups=RG,
                                  ins=[ag3_in.opt()], outs=[ag3_out.opt()]), 1)
    GP.wait_ge(cc3.h, cc3.v)

    GP.wait_ge(sV.h, sV.v)
    for c in range(NCH):
        dma(GP, xh.ap(640 * c, 640 * (c + 1)),
            ag2x_out[128 * c:128 * (c + 1), :], "LXH")
    wait_stream(VE, "LXH")
    for c in range(NCH):
        vchain(VE.tensor_scalar(xh.ap(640 * c, 640 * (c + 1)),
                                xh.ap(640 * c, 640 * (c + 1)),
                                flag[:], None, op0=ALU.mult))
    p.barrier("ob_comb")
    a2.reset()
    ocb = a2.alloc(8 * 2688, BF16)
    ourd = [a2.alloc(2 * 512, BF16), a2.alloc(2 * 512, BF16)]
    ot1 = a2.alloc(512, BF16)

    def OCB(ic, a, b):
        return ocb.ap(2688 * ic + a, 2688 * ic + b)

    oc_tiles = [(ic, ta, tb_) for ic in range(NCH)
                for (ta, tb_) in ((1408, 1920), (1920, 2048), (2048, 2560),
                                  (2560, 3072), (3072, 3584), (3584, 4096))]
    oc_used = [0, 0]
    oc_ld = [0, 0]

    def oc_load(i):
        ic, ta, tb_ = oc_tiles[i]
        sl = i % 2
        st = f"LOr{sl}"
        if oc_used[sl]:
            SY.wait_ge(sV.h, oc_used[sl])
        for rot in range(2):
            ca = (ta + 2048 * rot) % 4096
            dma(SY, ourd[sl].ap(512 * rot, 512 * rot + (tb_ - ta)),
                ag3_out[128 * ic:128 * (ic + 1), ca:ca + (tb_ - ta)], st)
        return p.sem(st).v

    oc_ld[0] = oc_load(0)
    oc_ld[1] = oc_load(1)
    ocv = 0
    for i, (ic, ta, tb_) in enumerate(oc_tiles):
        W = tb_ - ta
        sl = i % 2
        VE.wait_ge(p.sem(f"LOr{sl}").h, oc_ld[sl])
        vchain(VE.tensor_scalar(ot1.ap(0, W), ourd[sl].ap(0, W),
                                flag[:], None, op0=ALU.mult))
        ocv = vchain(VE.scalar_tensor_tensor(
            OCB(ic, ta - 1408, tb_ - 1408), ourd[sl].ap(512, 512 + W),
            flagn[:], ot1.ap(0, W), op0=ALU.mult, op1=ALU.add))
        oc_used[sl] = ocv
        if i + 2 < len(oc_tiles):
            oc_ld[sl] = oc_load(i + 2)
    pob = Seg(a3.t, 5120, 2 * 1024, BF16)
    ytiles = [(0, 512, "h"), (512, 640, "h"), (640, 1152, "o"),
              (1152, 1664, "o"), (1664, 2176, "o"), (2176, 2688, "o")]
    x3_v = 0
    po_used = [0, 0]
    for dc in range(NCH):
        st_p = f"LPo{dc % 2}"
        if po_used[dc % 2]:
            GP.wait_ge(sP.h, po_used[dc % 2])
        dma(GP, pob.ap(1024 * (dc % 2), 1024 * (dc % 2) + 1024)
            .rearrange("p (c m) -> p c m", m=128),
            di["Po"].rearrange("(c p) m -> p c m", p=128)
            [:, :, 128 * dc:128 * (dc + 1)], st_p)
        wait_stream(PE, st_p)
        PE.wait_ge(sV.h, ocv)
        for ti, (ta, tb_, kind) in enumerate(ytiles):
            W = tb_ - ta
            t = pp[ti % 4].acquire(PE)
            pv = None
            for ic in range(NCH):
                mm = PE.matmul(t[:, 0:W],
                               pob.ap(1024 * (dc % 2) + 128 * ic,
                                      1024 * (dc % 2) + 128 * (ic + 1)),
                               OCB(ic, ta, tb_),
                               start=(ic == 0), stop=(ic == NCH - 1))
                if ic == NCH - 1:
                    pv = sP.inc(mm)
            po_used[dc % 2] = pv
            VE.wait_ge(sP.h, pv)
            if kind == "h":
                dst = xh.ap(640 * dc + ta, 640 * dc + tb_)
                vchain(VE.tensor_tensor(dst, t[:, 0:W], dst, op=ALU.add))
                x3_v = vchain(VE.tensor_scalar(dst, dst, flag[:], None,
                                               op0=ALU.mult))
            else:
                dst = XO2(dc, ta - 640, tb_ - 640)
                x3_v = vchain(VE.tensor_tensor(dst, t[:, 0:W], dst, op=ALU.add))
            pp[ti % 4].consumed(sV, x3_v)

    if dbg("x3", (D, TOWN)) is not None:
        GP.wait_ge(sV.h, x3_v)
        for c in range(NCH):
            dma(GP, p.dbg_outs["x3"][128 * c:128 * (c + 1), :],
                XO2(c, 0, TOWN), "GD")
        for e_ in (VE, GP, PE, ACT):
            wait_stream(e_, "GD")
    if upto == "x3":
        finish(p, out_x4, XO2, x3_v)
        return p

    GP.wait_ge(sV.h, x3_v)
    for c in range(NCH):
        dma(GP, x_spill[128 * c:128 * (c + 1), :], XO2(c, 0, TOWN), "GS2")
    for e_ in (GP, SY, PE, VE, ACT):
        wait_stream(e_, "GS2")
    p.barrier("z0p")
    a2.reset()
    a2.off = NCH * UBW
    for c in range(NCH):
        vchain(VE.memset(ubt.ap(UBW * c, UBW * c + 1), 0.0))

    ln_tiles2 = [(0, 512), (512, 640), (640, 1152), (1152, 1664),
                 (1664, 2176), (2176, 2688)]
    z0p_v = 0
    for (ta, tb_) in ln_tiles2:
        W = tb_ - ta

        def get_in(c, ta=ta, tb_=tb_):
            if tb_ <= 640:
                return xh.ap(640 * c + ta, 640 * c + tb_)
            return XO2(c, ta - 640, tb_ - 640)

        def put_out(c, ta=ta, tb_=tb_):
            return UB(c, ta, tb_), None

        z0p_v = ln_group(get_in, put_out, W, ln_scr2,
                         (pp[4], pp[5], pp[6], pp[7]), ready_v=x3_v)
    p.barrier("s4")

    emit_ffn_stage("s4", ag4_in, ag4_out)

    p.barrier("x4")
    for c in range(NCH):
        dma(SY, XO2(c, 0, TOWN), x_spill[128 * c:128 * (c + 1), :], "LS4")
    wait_stream(VE, "LS4")
    x4_v = 0
    for c in range(NCH):
        dst = xo2.ap(TOWN * c, TOWN * (c + 1))
        x4_v = vchain(VE.tensor_tensor(dst, dst, UB(c, HALO, TEXT),
                                       op=ALU.add))
    finish(p, out_x4, XO2, x4_v)
    return p


def finish(p, out_x4, XO, ready_v):
    nc = p.nc
    sO = p.sem("O")
    nc.gpsimd.wait_ge(p.sem("V").h, ready_v)
    for c in range(NCH):
        sO.inc(nc.gpsimd.dma_start(out=out_x4[128 * c:128 * (c + 1), :],
                                   in_=XO(c, 0, TOWN)), 16)
    for e in (nc.gpsimd, nc.sync, nc.tensor, nc.vector, nc.scalar):
        e.wait_ge(sO.h, sO.v)
        if "GD" in p._sems:
            e.wait_ge(p.sem("GD").h, p.sem("GD").v)


def kernel(**inputs):
    in_maps = [_stage_core(inputs, c) for c in range(8)]
    p = build(upto=BUILD_UPTO, debug=DEBUG_OUT)
    res = run_bass_kernel_spmd(p.nc, in_maps, core_ids=list(range(8)))
    out = np.empty((B, S, D), dtype=np.float32)
    for c in range(8):
        b, half = c // 2, c % 2
        out[b, 2048 * half:2048 * half + 2048] = res.results[c]["o_x4"].T
    return out


BUILD_UPTO = "full"
DEBUG_OUT = ()


# version 06
# speedup vs baseline: 1.4005x; 1.0283x over previous
import contextlib
import numpy as np
import ml_dtypes

import concourse.bass as bass
import concourse.mybir as mybir
from concourse.bass_utils import run_bass_kernel_spmd

F32 = mybir.dt.float32
BF16 = mybir.dt.bfloat16
AF = mybir.ActivationFunctionType
ALU = mybir.AluOpType
AX = mybir.AxisListType

B, S, D = 4, 4096, 1024
NCH = 8
TOWN = 2048
HALO = 640
TEXT = HALO + TOWN
M = 128
CC = 128
DECAY, LR = 0.99, 0.01
DEC_C = float(DECAY ** CC)
NEG = -1.0e30
FH, FDH, WIN = 8, 64, 512
HID, HIDP = 2730, 2816
EPS = 1e-5

_f32 = np.float32
_bf16 = ml_dtypes.bfloat16


def _host_consts(half):
    c = {}
    i = np.arange(CC)
    dmT = np.where(i[:, None] < i[None, :],
                   LR * DECAY ** (-1.0 - i[:, None]), 0.0)
    c["dmaskT"] = (dmT * np.ones((CC, CC))).astype(_f32)
    c["decq512"] = np.broadcast_to(
        DECAY ** (np.arange(512) % 128), (128, 512)).astype(_f32)
    c["deck"] = (LR * DECAY ** (127.0 - i)).reshape(128, 1).astype(_f32)
    c["identity"] = np.eye(128, dtype=_f32)
    R = np.zeros((128, 128), dtype=_f32)
    for h0 in (0, 64):
        for k in range(32):
            R[h0 + 32 + k, h0 + k] = -1.0
            R[h0 + k, h0 + 32 + k] = 1.0
    c["Rperm"] = R
    oc2 = np.zeros((128, 2), dtype=_f32)
    oc2[:64, 0] = 1.0
    oc2[64:, 1] = 1.0
    c["ones2col"] = oc2
    e2 = np.zeros((2, 128), dtype=_f32)
    e2[0, :64] = 1.0
    e2[1, 64:] = 1.0
    c["e2"] = e2
    c["flag"] = np.full((128, 1), float(half), dtype=_f32)
    c["flagn"] = np.full((128, 1), 1.0 - float(half), dtype=_f32)
    c["epsc"] = np.full((128, 1), EPS, dtype=_f32)
    c["ones512"] = np.ones((1, 512), dtype=_f32)

    t0 = 2048 * half
    pos = np.arange(t0 - HALO, t0 + TOWN, dtype=np.float64)
    half_idx = np.arange(0, FDH, 2, dtype=np.float64)
    inv_freq = 1.0 / 10000.0 ** (half_idx / FDH)
    freqs = np.concatenate([pos[:, None] * inv_freq[None, :]] * 2, axis=1)
    base = (half_idx + 0.4 * FDH) / (1.4 * FDH)
    power = (pos - S // 2) / float(WIN // 2)
    scale = np.concatenate([base[None, :] ** power[:, None]] * 2, axis=1)
    cos, sin = np.cos(freqs), np.sin(freqs)
    for nm, tb in (("rotAq", cos * scale * 8.0), ("rotBq", sin * scale * 8.0),
                   ("rotAk", cos / scale), ("rotBk", sin / scale)):
        c[nm] = np.tile(tb.T.astype(_f32), (2, 1))

    kl = np.arange(128)[:, None]
    cf = np.arange(1408)[None, :]
    df = kl - cf + 384
    c["fmaskM"] = np.where((df > 0) | (df < -512), NEG, 0.0).astype(_f32)
    km = np.zeros((128, 640), dtype=_f32)
    if half == 0:
        km[:, 128:] = NEG
    c["khmask0"] = km
    return c


def _stage_core(inputs, core):
    b, half = core // 2, core % 2
    t0 = 2048 * half
    x = inputs["L0_attn_a_x"][b]
    d = dict(_host_consts(half))
    d["x_own"] = x[t0:t0 + TOWN].T
    d["x_pre"] = x[0:2048].T if half else np.zeros((D, 2048), dtype=_f32)
    d["x_halo"] = x[t0 - HALO:t0].T if half else np.zeros((D, HALO), dtype=_f32)
    d["wkT"] = inputs["L0_attn_a_Wk"].T
    d["wvT"] = inputs["L0_attn_a_Wv"].T
    d["wqT"] = inputs["L0_attn_a_Wq"].T
    d["woT"] = inputs["L0_attn_a_Wo"].T
    d["bk"] = inputs["L0_attn_a_bk"].reshape(128, 1)
    d["bq"] = inputs["L0_attn_a_bq"].reshape(128, 1)
    d["bkrow"] = inputs["L0_attn_a_bk"].reshape(1, 128)
    d["bvrow"] = inputs["L0_attn_a_bv"].reshape(1, 128)
    d["bo"] = inputs["L0_attn_a_bo"].reshape(D, 1)
    for l in range(2):
        d[f"wqkv{l}"] = inputs["L0_ffn_wqkv"][l]
        d[f"fwo{l}"] = inputs["L0_ffn_wo"][l]
        w1 = inputs["L0_ffn_w1"][l]
        w1p = np.zeros((D, 2 * HIDP), dtype=_f32)
        w1p[:, :HID] = w1[:, :HID]
        w1p[:, HIDP:HIDP + HID] = w1[:, HID:]
        d[f"w1p{l}"] = w1p
        b1 = inputs["L0_ffn_b1"][l]
        b1p = np.zeros((2 * HIDP, 1), dtype=_f32)
        b1p[:HID, 0] = b1[:HID]
        b1p[HIDP:HIDP + HID, 0] = b1[HID:]
        d[f"b1p{l}"] = b1p
        w2p = np.zeros((HIDP, D), dtype=_f32)
        w2p[:HID] = inputs["L0_ffn_w2"][l]
        d[f"w2p{l}"] = w2p
        d[f"b2_{l}"] = inputs["L0_ffn_b2"][l].reshape(D, 1)
        d[f"lng{l}"] = inputs["L0_ffn_ln_g"][l].reshape(D, 1)
        d[f"lnb{l}"] = inputs["L0_ffn_ln_b"][l].reshape(D, 1)
        d[f"lngb{l}"] = np.stack([inputs["L0_ffn_ln_g"][l],
                                  inputs["L0_ffn_ln_b"][l]])
        d[f"qs{l}"] = np.tile(inputs["L0_ffn_q_scale"][l], 2).reshape(128, 1)
        d[f"ks{l}"] = np.tile(inputs["L0_ffn_k_scale"][l], 2).reshape(128, 1)
    Pi = inputs["L0_attn_b_P_i"]
    myh = range(8 * half, 8 * half + 8)
    d["PiQ"] = np.concatenate([Pi[:, 192 * h:192 * h + 64] for h in myh], axis=1)
    d["PiK"] = np.concatenate([Pi[:, 192 * h + 64:192 * h + 128] for h in myh], axis=1)
    d["PiV"] = np.concatenate([Pi[:, 192 * h + 128:192 * h + 192] for h in myh], axis=1)
    d["Po"] = inputs["L0_attn_b_P_o"]
    return {k: np.ascontiguousarray(v, dtype=_f32) for k, v in d.items()}


class Sem:
    def __init__(self, handle):
        self.h = handle
        self.v = 0

    def inc(self, inst, n=1):
        inst.then_inc(self.h, n)
        self.v += n
        return self.v


class Seg:

    def __init__(self, t, off, cols, dtype):
        self.t, self.off, self.cols, self.dtype = t, off, cols, dtype

    def ap(self, a=0, b=None, p0=0, p1=128):
        b = self.cols if b is None else b
        if self.dtype == F32:
            return self.t[p0:p1, self.off + a:self.off + b]
        assert self.off % 1 == 0
        base = self.t[p0:p1, self.off:self.off + (self.cols + 1) // 2]
        return base.bitcast(BF16)[:, a:b]


class Arena:
    def __init__(self, p, name, cols):
        self.t = p.sb(name, [128, cols], F32)
        self.cols = cols
        self.off = 0
        self.name = name

    def alloc(self, cols, dtype=F32):
        fcols = cols if dtype == F32 else (cols + 1) // 2
        assert self.off + fcols <= self.cols, \
            f"arena {self.name} overflow: {self.off}+{fcols} > {self.cols}"
        s = Seg(self.t, self.off, cols, dtype)
        self.off += fcols
        return s

    def reset(self, off=0):
        self.off = off


class Prog:
    def __init__(self):
        self.nc = bass.Bass("TRN2", target_bir_lowering=False, debug=False)
        self.ctx = contextlib.ExitStack()
        self._sems = {}
        self.di = {}
        self.dbg_outs = {}
        self.debug = ()

    def sem(self, name):
        if name not in self._sems:
            self._sems[name] = Sem(self.ctx.enter_context(self.nc.semaphore(name)))
        return self._sems[name]

    def sb(self, name, shape, dtype):
        return self.ctx.enter_context(self.nc.sbuf_tensor(name, shape, dtype))

    def psum(self, name, shape, dtype=F32):
        return self.ctx.enter_context(self.nc.psum_tensor(name, shape, dtype))

    def barrier(self, tag):
        s = self.sem("bar")
        nc = self.nc
        engs = [nc.sync, nc.tensor, nc.vector, nc.scalar, nc.gpsimd]
        base = s.v
        for e in engs:
            s.inc(e.nop(), 1)
        for e in engs:
            e.wait_ge(s.h, base + 5)


class PsBank:

    def __init__(self, t):
        self.t = t
        self.free_at = None

    def acquire(self, eng):
        if self.free_at is not None:
            sem, v = self.free_at
            eng.wait_ge(sem.h, v)
            self.free_at = None
        return self.t

    def consumed(self, sem, v):
        self.free_at = (sem, v)


def build(upto="full", debug=()):
    p = Prog()
    p.debug = tuple(debug)
    nc = p.nc
    di = p.di

    def dram_in(name, shape, dtype=F32):
        di[name] = nc.dram_tensor(name, list(shape), dtype,
                                  kind="ExternalInput").ap()

    for nm, sh in [
        ("x_own", (D, TOWN)), ("x_pre", (D, 2048)), ("x_halo", (D, HALO)),
        ("flag", (128, 1)), ("flagn", (128, 1)),
        ("epsc", (128, 1)), ("ones512", (1, 512)),
        ("wkT", (D, 128)), ("wvT", (D, 128)), ("wqT", (D, 128)), ("woT", (128, D)),
        ("bk", (128, 1)), ("bq", (128, 1)), ("bkrow", (1, 128)), ("bvrow", (1, 128)),
        ("bo", (D, 1)),
        ("dmaskT", (128, 128)), ("decq512", (128, 512)), ("deck", (128, 1)),
        ("identity", (128, 128)), ("Rperm", (128, 128)),
        ("ones2col", (128, 2)), ("e2", (2, 128)),
        ("rotAq", (128, TEXT)), ("rotBq", (128, TEXT)),
        ("rotAk", (128, TEXT)), ("rotBk", (128, TEXT)),
        ("khmask0", (128, 640)),
        ("PiQ", (D, 512)), ("PiK", (D, 512)), ("PiV", (D, 512)), ("Po", (D, D)),
    ]:
        dram_in(nm, sh)
    dram_in("fmaskM", (128, 1408))
    for l in range(2):
        for nm, sh in [
            (f"wqkv{l}", (D, 1536)), (f"fwo{l}", (512, D)),
            (f"w1p{l}", (D, 2 * HIDP)), (f"b1p{l}", (2 * HIDP, 1)),
            (f"w2p{l}", (HIDP, D)), (f"b2_{l}", (D, 1)),
            (f"lng{l}", (D, 1)), (f"lnb{l}", (D, 1)),
            (f"lngb{l}", (2, D)),
            (f"qs{l}", (128, 1)), (f"ks{l}", (128, 1)),
        ]:
            dram_in(nm, sh)

    out_x4 = nc.dram_tensor("o_x4", [D, TOWN], F32, kind="ExternalOutput").ap()

    def dbg(name, shape, dtype=F32):
        if name in p.debug and name not in p.dbg_outs:
            p.dbg_outs[name] = nc.dram_tensor(
                "dbg_" + name, list(shape), dtype, kind="ExternalOutput").ap()
        return p.dbg_outs.get(name)

    x_spill = nc.dram_tensor("x_spill", [D, TOWN], F32, kind="Internal").ap()
    ag1_in = nc.dram_tensor("ag1_in", [D, HALO], BF16, kind="Internal").ap()
    ag1_out = nc.dram_tensor("ag1_out", [2 * D, HALO], BF16, kind="Internal").ap()
    ag2z_in = nc.dram_tensor("ag2z_in", [D, TOWN], BF16, kind="Internal").ap()
    ag2z_out = nc.dram_tensor("ag2z_out", [2 * D, TOWN], BF16, kind="Internal").ap()
    ag2x_in = nc.dram_tensor("ag2x_in", [D, HALO], BF16, kind="Internal").ap()
    ag2x_out = nc.dram_tensor("ag2x_out", [2 * D, HALO], BF16, kind="Internal").ap()
    ag3_in = nc.dram_tensor("ag3_in", [512, S], BF16, kind="Internal").ap()
    ag3_out = nc.dram_tensor("ag3_out", [1024, S], BF16, kind="Internal").ap()
    ag4_in = nc.dram_tensor("ag4_in", [D, HALO], BF16, kind="Internal").ap()
    ag4_out = nc.dram_tensor("ag4_out", [2 * D, HALO], BF16, kind="Internal").ap()
    RG = [[0, 1], [2, 3], [4, 5], [6, 7]]

    a1 = Arena(p, "a1", 19544)
    a2 = Arena(p, "a2", NCH * (TEXT + 1))
    a3 = Arena(p, "a3", 6656)
    sc = Arena(p, "sc", 768)
    flag = p.sb("flag_s", [128, 1], F32)
    flagn = p.sb("flagn_s", [128, 1], F32)
    epsc = p.sb("epsc_s", [128, 1], F32)
    ones1 = p.sb("ones1", [1, 128], F32)
    ones1b = p.sb("ones1b", [1, 128], BF16)
    onescol = p.sb("onescol", [128, 1], BF16)
    identb = p.sb("identb", [128, 128], BF16)
    srow = p.sb("srow", [2, 2048], F32)
    masksF = p.sb("masksF", [128, 1408], BF16)

    pq = [p.psum("pq0", [128, 1024]), p.psum("pq1", [128, 1024])]
    pp = [PsBank(pq[0][:, 0:512]), PsBank(pq[0][:, 512:1024]),
          PsBank(pq[1][:, 0:512]), PsBank(pq[1][:, 512:1024])]
    pp += [PsBank(p.psum(f"pp{i}", [128, 512])) for i in range(4, 8)]

    block = p.ctx.enter_context(nc.Block())
    SY, PE, VE, ACT, GP = nc.sync, nc.tensor, nc.vector, nc.scalar, nc.gpsimd
    sP = p.sem("P")
    sV = p.sem("V")
    sA = p.sem("A")

    def dma(eng, out, in_, stream, **kw):
        sm = p.sem(stream)
        sm.inc(eng.dma_start(out=out, in_=in_, **kw), 16)
        return sm.v

    def wait_stream(eng, stream):
        sm = p.sem(stream)
        eng.wait_ge(sm.h, sm.v)

    def vchain(inst):
        vv = sV.inc(inst)
        VE.wait_ge(sV.h, vv)
        return vv

    xo = a1.alloc(16384, F32)

    def XO(c, a, b):
        return xo.ap(TOWN * c + a, TOWN * c + b)

    dma(SY, flag[:], di["flag"], "L1")
    dma(SY, flagn[:], di["flagn"], "L1")
    dma(SY, epsc[:], di["epsc"], "L1")
    dma(GP, identb[:], di["identity"], "G1")
    dma(GP, ones1b[:], di["ones512"][0:1, 0:128], "G1")
    for c in range(NCH):
        dma(SY, XO(c, 0, TOWN), di["x_own"][128 * c:128 * (c + 1), :], "L1")
    dma(GP, masksF[:], di["fmaskM"], "G1")

    def fmask(kk):
        return masksF[:, 896 - 128 * kk:1408 - 128 * kk]

    def bmask(j):
        return masksF[:, 384 - 128 * j:896 - 128 * j]
    dma(SY, srow[1:2, 1536:2048], di["ones512"], "L1")
    dma(SY, ones1[:], di["ones512"][0:1, 0:128], "L1")
    VE.memset(onescol[:], 1.0)

    def ln_group(get_in, put_out, W, scratch, banks, gb=None, ready_v=None):
        stA, stB, bcA, bcB = banks
        xbs, sqs, t1s = scratch["xb"], scratch["sq"], scratch["t1"]
        if ready_v is not None:
            ACT.wait_ge(sV.h, ready_v)
        tA = stA.acquire(PE)
        tB = stB.acquire(PE)
        slot_pe = scratch.setdefault("slot_pe", [0, 0])
        for c in range(NCH):
            xin = get_in(c)
            xb = xbs[c % 2]
            sq = sqs[c % 2]
            VE.wait_ge(sP.h, slot_pe[c % 2])
            ACT.wait_ge(sP.h, slot_pe[c % 2])
            vv = sV.inc(VE.tensor_copy(xb.ap(0, W), xin))
            aa = sA.inc(ACT.activation(sq.ap(0, W), xin, AF.Square))
            PE.wait_ge(sV.h, vv)
            PE.wait_ge(sA.h, aa)
            PE.matmul(tA[0:1, 0:W], onescol[:], xb.ap(0, W),
                      start=(c == 0), stop=(c == NCH - 1))
            pv = sP.inc(PE.matmul(tB[0:1, 0:W], onescol[:], sq.ap(0, W),
                                  start=(c == 0), stop=(c == NCH - 1)))
            slot_pe[c % 2] = pv
        pe_done = sP.v
        mu = srow[0:1, 0:0 + W]
        msq = srow[0:1, 512:512 + W]
        var = srow[0:1, 1024:1024 + W]
        sd = srow[0:1, 512:512 + W]
        rst = srow[0:1, 1024:1024 + W]
        nmb = srow[0:1, 1536:1536 + W]
        VE.wait_ge(sP.h, pe_done)
        vv = sV.inc(VE.tensor_scalar(mu, tA[0:1, 0:W], 1.0 / D, None,
                                     op0=ALU.mult))
        ACT.wait_ge(sV.h, vv)
        av = sA.inc(ACT.activation(msq, mu, AF.Square))
        VE.wait_ge(sA.h, av)
        vv = sV.inc(VE.scalar_tensor_tensor(
            var, tB[0:1, 0:W], 1.0 / D, msq,
            op0=ALU.mult, op1=ALU.subtract))
        stA.consumed(sV, vv)
        stB.consumed(sV, vv)
        ACT.wait_ge(sV.h, vv)
        av = sA.inc(ACT.activation(sd, var, AF.Ln, bias=epsc[0:1, :]))
        ACT.wait_ge(sA.h, av)
        av = sA.inc(ACT.activation(rst, sd, AF.Exp, scale=-0.5))
        VE.wait_ge(sA.h, av)
        vv = vchain(VE.scalar_tensor_tensor(nmb, mu, -1.0, rst,
                                            op0=ALU.mult, op1=ALU.mult))
        tbcA = bcA.acquire(PE)
        PE.wait_ge(sV.h, vv)
        if gb is not None:
            vv = vchain(VE.tensor_copy(scratch["lnrb"][0:1, 0:W], nmb))
            PE.wait_ge(sV.h, vv)
        pva = sP.inc(PE.matmul(tbcA[:, 0:W], ones1[:], rst,
                               start=True, stop=True))
        if gb is None:
            tbcB = bcB.acquire(PE)
            pvb = sP.inc(PE.matmul(tbcB[:, 0:W], ones1[:], nmb,
                                   start=True, stop=True))
            VE.wait_ge(sP.h, pvb)
        VE.wait_ge(sP.h, pva)
        last = None
        for c in range(NCH):
            xin = get_in(c)
            t1 = t1s.ap(0, W)
            if gb is not None:
                tbcB = bcB.acquire(PE)
                pvb = sP.inc(PE.matmul(tbcB[:, 0:W], gb(c),
                                       scratch["lnrb"][0:2, 0:W],
                                       start=True, stop=True))
            vchain(VE.tensor_tensor(t1, xin, tbcA[:, 0:W], op=ALU.mult))
            dst, lgcol = put_out(c)
            if gb is None:
                last = vchain(VE.scalar_tensor_tensor(
                    dst, t1, 1.0, tbcB[:, 0:W], op0=ALU.mult, op1=ALU.add))
            else:
                VE.wait_ge(sP.h, pvb)
                last = vchain(VE.scalar_tensor_tensor(
                    dst, t1, lgcol, tbcB[:, 0:W], op0=ALU.mult, op1=ALU.add))
                bcB.consumed(sV, last)
        bcA.consumed(sV, last)
        bcB.consumed(sV, last)
        return last

    za = a2.alloc(4096, BF16)
    kfm = a2.alloc(4096, BF16)
    qd0 = a2.alloc(3072, BF16)
    ktm = a2.alloc(4096, BF16)
    vtm = a2.alloc(4096, BF16)
    obuf = a2.alloc(2688, BF16)
    wob = a2.alloc(1024, BF16)
    awk = a2.alloc(512, F32)
    xpA = a2.alloc(4096, F32)
    xpB = a2.alloc(4096, F32)

    class _QD:
        def ap(self, a, b):
            return qd0.ap(a - 1024, b - 1024)
    qd = _QD()
    xh = a3.alloc(5120, F32)
    wkb = a3.alloc(1024, BF16)
    wvb = a3.alloc(1024, BF16)
    wqb = a3.alloc(1024, BF16)
    dmk = a1.alloc(128, F32)
    stf = a1.alloc(128, F32)
    stb = a1.alloc(128, BF16)
    wts = [a1.alloc(128, BF16), a1.alloc(128, BF16)]
    ktds = [a1.alloc(128, BF16), a1.alloc(128, BF16)]
    csm = a1.alloc(16, F32)
    bo8 = a1.alloc(8, F32)
    xb_s = [a1.alloc(512, BF16), a1.alloc(512, BF16)]
    sq_s = [a1.alloc(512, BF16), a1.alloc(512, BF16)]
    t1_s = a1.alloc(512, F32)

    brow_k = p.sb("brow_k", [1, 128], BF16)
    brow_v = p.sb("brow_v", [1, 128], BF16)

    for (dst, src) in ((wkb, "wkT"), (wvb, "wvT"), (wqb, "wqT")):
        dma(GP, dst.ap().rearrange("p (c m) -> p c m", m=128),
            di[src].rearrange("(c p) m -> p c m", p=128), "G1")
    dma(GP, wob.ap(), di["woT"], "G1")
    dma(GP, brow_k[:], di["bkrow"], "G1")
    dma(GP, brow_v[:], di["bvrow"], "G1")
    dma(SY, awk.ap(), di["decq512"], "L1")
    dma(SY, dmk.ap(), di["dmaskT"], "L1")
    dma(SY, csm.ap(0, 1), di["bk"], "L1")
    dma(SY, csm.ap(1, 2), di["bq"], "L1")
    dma(SY, csm.ap(2, 3), di["deck"], "L1")
    dma(SY, bo8.ap().rearrange("p (c o) -> p c o", o=1),
        di["bo"].rearrange("(c p) o -> p c o", p=128),
        allow_slow_non_contiguous=True, stream="L1")
    bk_col = lambda: csm.ap(0, 1)
    bq_col = lambda: csm.ap(1, 2)
    deck_col = lambda: csm.ap(2, 3)

    def load_xpre(g, seg, slot):
        for c in range(NCH):
            dma(SY, seg.ap(512 * c, 512 * (c + 1)),
                di["x_pre"][128 * c:128 * (c + 1), 512 * g:512 * (g + 1)],
                f"Lx{slot}")
        return p.sem(f"Lx{slot}").v

    ln_scr = {"xb": xb_s, "sq": sq_s, "t1": t1_s}

    xp_ready = [None, None]
    xp_ready[0] = load_xpre(0, xpA, 0)
    xp_ready[1] = load_xpre(1, xpB, 1)

    for e in (VE, PE, ACT):
        wait_stream(e, "L1")
        wait_stream(e, "G1")

    for g in range(8):
        own = g >= 4
        if not own:
            seg = (xpA, xpB)[g % 2]
            sm = p.sem(f"Lx{g % 2}")
            VE.wait_ge(sm.h, xp_ready[g % 2])
            ACT.wait_ge(sm.h, xp_ready[g % 2])
            get_in = lambda c, seg=seg: seg.ap(512 * c, 512 * c + 512)
        else:
            go = 512 * (g - 4)
            get_in = lambda c, go=go: XO(c, go, go + 512)

        def put_out(c):
            return za.ap(512 * c, 512 * c + 512), None

        lnv = ln_group(get_in, put_out, 512, ln_scr,
                       (pp[0], pp[1], pp[2], pp[3]))

        if not own and g + 2 < 4:
            SY.wait_ge(sV.h, lnv)
            xp_ready[g % 2] = load_xpre(g + 2, (xpA, xpB)[g % 2], g % 2)

        PE.wait_ge(sV.h, lnv)
        gs = 512 * g
        t = pp[4].acquire(PE)
        pv = None
        for c in range(NCH):
            mm = PE.matmul(t[:, 0:512], wkb.ap(128 * c, 128 * (c + 1)),
                           za.ap(512 * c, 512 * (c + 1)),
                           start=(c == 0), stop=(c == NCH - 1))
            if c == NCH - 1:
                pv = sP.inc(mm)
        VE.wait_ge(sP.h, pv)
        vv = sV.inc(VE.tensor_scalar(kfm.ap(gs, gs + 512), t[:, 0:512],
                                     bk_col(), None, op0=ALU.add))
        pp[4].consumed(sV, vv)
        if g >= 2:
            t = pp[5].acquire(PE)
            pv = None
            for c in range(NCH):
                mm = PE.matmul(t[:, 0:512], wqb.ap(128 * c, 128 * (c + 1)),
                               za.ap(512 * c, 512 * (c + 1)),
                               start=(c == 0), stop=(c == NCH - 1))
                if c == NCH - 1:
                    pv = sP.inc(mm)
            VE.wait_ge(sP.h, pv)
            vchain(VE.tensor_scalar(t1_s.ap(), t[:, 0:512], bq_col(), None,
                                    op0=ALU.add))
            vv = sV.inc(VE.tensor_tensor(qd.ap(gs, gs + 512), t1_s.ap(),
                                         awk.ap(), op=ALU.mult))
            pp[5].consumed(sV, vv)
        for srowb, wrow, dstseg in ((brow_k, wkb, ktm), (brow_v, wvb, vtm)):
            for ts in range(4):
                t = pp[6 if dstseg is ktm else 7].acquire(PE)
                sub = gs + 128 * ts
                for c in range(NCH):
                    PE.matmul(t[:, 0:128],
                              za.ap(512 * c + 128 * ts, 512 * c + 128 * (ts + 1)),
                              wrow.ap(128 * c, 128 * (c + 1)),
                              start=(c == 0), stop=False)
                pv = sP.inc(PE.matmul(t[:, 0:128], ones1b[:], srowb[:],
                                      start=False, stop=True))
                VE.wait_ge(sP.h, pv)
                vv = sV.inc(VE.tensor_copy(dstseg.ap(sub, sub + 128), t[:, 0:128]))
                (pp[6] if dstseg is ktm else pp[7]).consumed(sV, vv)

    vchain(VE.memset(stf.ap(), 0.0))
    vv = vchain(VE.tensor_copy(stb.ap(), stf.ap()))
    st_ready = vv
    for cch in range(32):
        cs = 128 * cch
        do_out = cch >= 11
        if do_out:
            t_at = pp[0].acquire(PE)
            pv = sP.inc(PE.matmul(t_at[:, 0:128], kfm.ap(cs, cs + 128),
                                  qd.ap(cs, cs + 128), start=True, stop=True))
            wt = wts[cch % 2]
            VE.wait_ge(sP.h, pv)
            vv = sV.inc(VE.tensor_tensor(wt.ap(), t_at[:, 0:128], dmk.ap(),
                                         op=ALU.mult))
            pp[0].consumed(sV, vv)
            wt_v = vv
        ktd = ktds[cch % 2]
        vv = sV.inc(VE.tensor_scalar(ktd.ap(), ktm.ap(cs, cs + 128),
                                     deck_col(), None, op0=ALU.mult))
        ktd_v = vv
        if do_out:
            t_o = pp[1].acquire(PE)
            PE.wait_ge(sV.h, max(st_ready, wt_v))
            PE.matmul(t_o[:, 0:128], stb.ap(), qd.ap(cs, cs + 128),
                      start=True, stop=False)
            pv_o = sP.inc(PE.matmul(t_o[:, 0:128], vtm.ap(cs, cs + 128), wt.ap(),
                                    start=False, stop=True))
        t_u = pp[2].acquire(PE)
        PE.wait_ge(sV.h, ktd_v)
        pv_u = sP.inc(PE.matmul(t_u[:, 0:128], ktd.ap(), vtm.ap(cs, cs + 128),
                                start=True, stop=True))
        VE.wait_ge(sP.h, pv_u)
        vchain(VE.scalar_tensor_tensor(stf.ap(), stf.ap(), DEC_C, t_u[:, 0:128],
                                       op0=ALU.mult, op1=ALU.add))
        vv = vchain(VE.tensor_copy(stb.ap(), stf.ap()))
        pp[2].consumed(sV, vv)
        st_ready = vv
        if cch == 15:
            vchain(VE.tensor_scalar(stf.ap(), stf.ap(), flag[:],
                                    None, op0=ALU.mult))
            st_ready = vchain(VE.tensor_copy(stb.ap(), stf.ap()))
        if do_out:
            VE.wait_ge(sP.h, pv_o)
            vv = sV.inc(VE.tensor_copy(obuf.ap(cs - 1408, cs - 1408 + 128),
                                       t_o[:, 0:128]))
            pp[1].consumed(sV, vv)
    obuf_v = sV.v

    for c in range(NCH):
        dma(SY, xh.ap(640 * c, 640 * (c + 1)),
            di["x_halo"][128 * c:128 * (c + 1), :], "LH")
    PE.wait_ge(sV.h, obuf_v)
    wait_stream(VE, "LH")
    tiles = [(0, 512, "h"), (512, 640, "h"),
             (640, 1152, "o"), (1152, 1664, "o"), (1664, 2176, "o"), (2176, 2688, "o")]
    for (ta, tb_, kind) in tiles:
        W = tb_ - ta
        for c in range(NCH):
            t = pp[c % 4].acquire(PE)
            pv = sP.inc(PE.matmul(t[:, 0:W], wob.ap(128 * c, 128 * (c + 1)),
                                  obuf.ap(ta, tb_), start=True, stop=True))
            VE.wait_ge(sP.h, pv)
            if kind == "h":
                dst = xh.ap(640 * c + ta, 640 * c + tb_)
                vchain(VE.scalar_tensor_tensor(dst, t[:, 0:W], bo8.ap(c, c + 1),
                                               dst, op0=ALU.add, op1=ALU.add))
                vv = vchain(VE.tensor_scalar(dst, dst, flag[:], None,
                                             op0=ALU.mult))
            else:
                dst = XO(c, ta - 640, tb_ - 640)
                vv = sV.inc(VE.scalar_tensor_tensor(
                    dst, t[:, 0:W], bo8.ap(c, c + 1), dst,
                    op0=ALU.add, op1=ALU.add))
            pp[c % 4].consumed(sV, vv)
    x1_v = sV.v

    for nm_, seg_, a_, w_ in (("za", za, 0, 4096), ("kfm", kfm, 0, 4096), ("qd", qd, 1024, 4096),
                              ("ktm", ktm, 0, 4096), ("vtm", vtm, 0, 4096),
                              ("obuf", obuf, 0, 2688)):
        if dbg(nm_, (128, w_ - a_), BF16) is not None:
            GP.wait_ge(sV.h, obuf_v)
            dma(GP, p.dbg_outs[nm_], seg_.ap(a_, w_), "GD")
    if dbg("x1", (D, TOWN)) is not None:
        GP.wait_ge(sV.h, x1_v)
        for c in range(NCH):
            dma(GP, p.dbg_outs["x1"][128 * c:128 * (c + 1), :],
                XO(c, 0, TOWN), "GD")
    if dbg("x1h", (D, HALO)) is not None:
        GP.wait_ge(sV.h, x1_v)
        for c in range(NCH):
            dma(GP, p.dbg_outs["x1h"][128 * c:128 * (c + 1), :],
                xh.ap(640 * c, 640 * (c + 1)), "GD")

    if upto == "x1":
        finish(p, out_x4, XO, x1_v)
        return p

    p.barrier("ub")
    a2.reset()
    UBW = TEXT + 1
    ubt = a2.alloc(NCH * UBW, F32)

    def UB(c, a, b):
        return ubt.ap(UBW * c + 1 + a, UBW * c + 1 + b)

    for c in range(NCH):
        VE.memset(ubt.ap(UBW * c, UBW * c + 1), 0.0)

    ln_tiles = [(0, 512), (512, 640), (640, 1152), (1152, 1664),
                (1664, 2176), (2176, 2688)]
    z0_v = 0
    for (ta, tb_) in ln_tiles:
        W = tb_ - ta

        def get_in(c, ta=ta, tb_=tb_):
            if tb_ <= 640:
                return xh.ap(640 * c + ta, 640 * c + tb_)
            return XO(c, ta - 640, tb_ - 640)

        def put_out(c, ta=ta, tb_=tb_):
            return UB(c, ta, tb_), None

        z0_v = ln_group(get_in, put_out, W, ln_scr,
                        (pp[4], pp[5], pp[6], pp[7]), ready_v=x1_v)

    if dbg("z0", (D, TEXT)) is not None:
        GP.wait_ge(sV.h, z0_v)
        for c in range(NCH):
            dma(GP, p.dbg_outs["z0"][128 * c:128 * (c + 1), :],
                UB(c, 0, TEXT), "GD")

    if upto == "z0":
        finish(p, out_x4, XO, x1_v)
        return p

    GP.wait_ge(sV.h, x1_v)
    for c in range(NCH):
        dma(GP, x_spill[128 * c:128 * (c + 1), :], XO(c, 0, TOWN), "GS")

    G3 = [(0, 512), (512, 1024), (1024, 1536), (1536, 2048),
          (2048, 2560), (2560, 2688)]
    G4 = [(512, 1024), (1024, 1536), (1536, 2048), (2048, 2560), (2560, 2688)]

    sc.reset()
    sqscr = sc.alloc(512, BF16)
    gelscr = sc.alloc(512, BF16)
    fsm = sc.alloc(64, F32)
    b1c = [sc.alloc(44, F32), sc.alloc(44, F32)]

    qs_col = lambda l: fsm.ap(2 * l, 2 * l + 1)
    ks_col = lambda l: fsm.ap(2 * l + 1, 2 * l + 2)
    b2_col = lambda l: fsm.ap(4 + l, 5 + l)

    for l in range(2):
        dma(SY, fsm.ap(2 * l, 2 * l + 1), di[f"qs{l}"], "L1")
        dma(SY, fsm.ap(2 * l + 1, 2 * l + 2), di[f"ks{l}"], "L1")
        dma(SY, fsm.ap(4 + l, 5 + l), di[f"b2_{l}"][0:128, :], "L1")
    b2c8 = [sc.alloc(8, F32), sc.alloc(8, F32)]
    lgc8 = [sc.alloc(8, F32), sc.alloc(8, F32)]
    assert sc.off <= sc.cols
    for l in range(2):
        dma(SY, b2c8[l].ap().rearrange("p (c o) -> p c o", o=1),
            di[f"b2_{l}"].rearrange("(c p) o -> p c o", p=128),
            allow_slow_non_contiguous=True, stream="L1")
        dma(SY, lgc8[l].ap().rearrange("p (c o) -> p c o", o=1),
            di[f"lng{l}"].rearrange("(c p) o -> p c o", p=128),
            allow_slow_non_contiguous=True, stream="L1")
        dma(SY, b1c[l].ap().rearrange("p (c o) -> p c o", o=1),
            di[f"b1p{l}"].rearrange("(c p) o -> p c o", p=128),
            allow_slow_non_contiguous=True, stream="L1")
    e2b0 = p.sb("e2b0", [1, 128], F32)
    e2b1 = p.sb("e2b1", [1, 128], F32)
    dma(SY, e2b0[:], di["e2"][0:1, :], "L1")
    dma(SY, e2b1[:], di["e2"][1:2, :], "L1")
    e2f = p.sb("e2f", [2, 128], F32)
    dma(SY, e2f[:], di["e2"], "L1")
    oc2b = p.sb("oc2b", [128, 2], BF16)
    dma(GP, oc2b[:], di["ones2col"], "G1")
    rpb = p.sb("rpb", [128, 128], BF16)
    dma(GP, rpb[:], di["Rperm"], "G1")
    khm0b = p.sb("khm0b", [128, 640], BF16)
    dma(GP, khm0b[:], di["khmask0"], "G1")

    for e in (GP, SY, PE, VE, ACT):
        wait_stream(e, "GS")
        wait_stream(e, "G1")
        wait_stream(e, "L1")
    p.barrier("ffn_setup")
    a1.reset()
    zcast = a1.alloc(4096, BF16)
    kbuf = a1.alloc(4 * TEXT, BF16)
    qbuf = a1.alloc(4 * (TEXT - 512), BF16)
    vtbuf = a1.alloc(21 * 520, BF16)
    e0buf = a1.alloc(512, F32)
    ptbuf = a1.alloc(1024, BF16)
    lrbuf = a1.alloc(512, BF16)
    obq = a1.alloc(4 * 512, BF16)
    A1_ATTN = a1.off
    a1.reset()
    abuf = a1.alloc(22 * 512, BF16)
    w2blk = a1.alloc(22 * 1024, BF16)
    a1.off = max(a1.off, A1_ATTN)
    LN2_BASE = 16896
    assert LN2_BASE >= a1.off - 2648
    xb_s2 = [Seg(a1.t, LN2_BASE + 0, 512, BF16),
             Seg(a1.t, LN2_BASE + 256, 512, BF16)]
    sq_s2 = [Seg(a1.t, LN2_BASE + 512, 512, BF16),
             Seg(a1.t, LN2_BASE + 768, 512, BF16)]
    t1_s2 = Seg(a1.t, LN2_BASE + 1024, 512, F32)
    ln_scr2 = {"xb": xb_s2, "sq": sq_s2, "t1": t1_s2}

    def KB(hp, a, b):
        return kbuf.ap(TEXT * hp + a, TEXT * hp + b)

    def QB(hp, a, b):
        return qbuf.ap((TEXT - 512) * hp + a - 512, (TEXT - 512) * hp + b - 512)

    def VT(t):
        return vtbuf.ap(520 * t, 520 * (t + 1))

    a3.reset()
    wqkvb = a3.alloc(8 * 1536, BF16)
    rots = a3.alloc(2 * 512, BF16)
    assert a3.off <= a3.cols
    fwob = Seg(a3.t, 0, 4 * 1024, BF16)
    z2roll = [Seg(a3.t, 0, 8 * 512, BF16), Seg(a3.t, 2048, 8 * 512, BF16)]
    w1blk = [Seg(a3.t, 4096, 8 * 128, BF16), Seg(a3.t, 4608, 8 * 128, BF16)]
    e0b2 = Seg(zcast.t, zcast.off + 1536, 512, F32)
    k1scr = Seg(e0buf.t, e0buf.off + 0, 512, BF16)
    k2scr = Seg(e0buf.t, e0buf.off + 256, 512, BF16)
    tascr = Seg(ptbuf.t, ptbuf.off + 0, 512, BF16)
    tbscr = Seg(ptbuf.t, ptbuf.off + 256, 512, BF16)

    lnrb = p.sb("lnrb", [2, 512], BF16)
    dma(GP, lnrb[1:2, :], di["ones512"], "G1")
    ln_scr2["lnrb"] = lnrb
    for e_ in (PE, VE, ACT, SY, GP):
        wait_stream(e_, "G1")
    lngbs = p.sb("lngbs", [2, D], BF16)

    G3 = [(0, 512), (512, 1024), (1024, 1536), (1536, 2048),
          (2048, 2560), (2560, 2688)]
    G4 = [(512, 1024), (1024, 1536), (1536, 2048), (2048, 2560), (2560, 2688)]

    def emit_ffn_layer(lidx, stage_tag):
        tag = f"{stage_tag}l{lidx}"
        dma(GP, wqkvb.ap().rearrange("p (c m) -> p c m", m=1536),
            di[f"wqkv{lidx}"].rearrange("(c p) m -> p c m", p=128),
            f"Lw_{tag}")
        dma(GP, lngbs[:], di[f"lngb{lidx}"], f"Lw_{tag}")
        wait_stream(PE, f"Lw_{tag}")

        rot_used = 0
        zc_pe = 0
        for g, (ta, tb_) in enumerate(G3):
            W = tb_ - ta
            st_r = f"Lr_{tag}"
            if rot_used:
                GP.wait_ge(sV.h, rot_used)
            wait_stream(GP, st_r)
            dma(GP, rots.ap(0, W), di["rotAk"][:, ta:tb_], st_r)
            dma(GP, rots.ap(512, 512 + W), di["rotBk"][:, ta:tb_], st_r)
            VE.wait_ge(sP.h, zc_pe)
            zcv = 0
            for c in range(NCH):
                off = 1 if c >= 4 else 0
                zcv = sV.inc(VE.tensor_copy(zcast.ap(512 * c, 512 * c + W),
                                            UB(c, ta - off, tb_ - off)))
            PE.wait_ge(sV.h, zcv)

            for isq in (0, 1):
                if isq:
                    GP.wait_ge(sV.h, sV.v)
                    wait_stream(GP, st_r)
                    dma(GP, rots.ap(0, W), di["rotAq"][:, ta:tb_], st_r)
                    dma(GP, rots.ap(512, 512 + W), di["rotBq"][:, ta:tb_], st_r)
                    if g == 0:
                        continue
                for hp in range(4):
                    bank = pp[hp % 4]
                    t = bank.acquire(PE)
                    cols = (0 if isq else 512) + 128 * hp
                    pv = None
                    for c in range(NCH):
                        mm = PE.matmul(t[:, 0:W],
                                       wqkvb.ap(1536 * c + cols,
                                                1536 * c + cols + 128),
                                       zcast.ap(512 * c, 512 * c + W),
                                       start=(c == 0), stop=(c == NCH - 1))
                        if c == NCH - 1:
                            pv = sP.inc(mm)
                    ACT.wait_ge(sP.h, pv)
                    aa = sA.inc(ACT.activation(sqscr.ap(0, W), t[:, 0:W],
                                               AF.Square))
                    PE.wait_ge(sA.h, aa)
                    tn = pp[4].acquire(PE)
                    pn = sP.inc(PE.matmul(tn[0:2, 0:W], oc2b[:], sqscr.ap(0, W),
                                          start=True, stop=True))
                    VE.wait_ge(sP.h, pn)
                    nv = vchain(VE.tensor_scalar(srow[0:2, 0:W], tn[0:2, 0:W],
                                                 1e-24, None, op0=ALU.max))
                    pp[4].consumed(sV, nv)
                    ACT.wait_ge(sV.h, nv)
                    av = sA.inc(ACT.activation(srow[0:2, 512:512 + W],
                                               srow[0:2, 0:W], AF.Ln))
                    ACT.wait_ge(sA.h, av)
                    av = sA.inc(ACT.activation(srow[0:2, 1024:1024 + W],
                                               srow[0:2, 512:512 + W],
                                               AF.Exp, scale=-0.5))
                    PE.wait_ge(sA.h, av)
                    tb2 = pp[5].acquire(PE)
                    pn2 = sP.inc(PE.matmul(tb2[:, 0:W], e2f[:],
                                           srow[0:2, 1024:1024 + W],
                                           start=True, stop=True))
                    VE.wait_ge(sP.h, pn2)
                    scol = qs_col(lidx) if isq else ks_col(lidx)
                    vchain(VE.tensor_scalar(k1scr.ap(0, W), t[:, 0:W], scol,
                                            None, op0=ALU.mult))
                    bank.consumed(sV, sV.v)
                    k2v = vchain(VE.tensor_tensor(k2scr.ap(0, W),
                                                  k1scr.ap(0, W),
                                                  tb2[:, 0:W], op=ALU.mult))
                    pp[5].consumed(sV, k2v)
                    PE.wait_ge(sV.h, k2v)
                    tr = pp[6].acquire(PE)
                    pr = sP.inc(PE.matmul(tr[:, 0:W], rpb[:], k2scr.ap(0, W),
                                          start=True, stop=True))
                    ra = rots.ap(0, W)
                    rb = rots.ap(512, 512 + W)
                    wait_stream(VE, st_r)
                    vchain(VE.tensor_tensor(tascr.ap(0, W), k2scr.ap(0, W), ra,
                                            op=ALU.mult))
                    VE.wait_ge(sP.h, pr)
                    vchain(VE.tensor_tensor(tbscr.ap(0, W), tr[:, 0:W], rb,
                                            op=ALU.mult))
                    pp[6].consumed(sV, sV.v)
                    dst = QB(hp, ta, tb_) if isq else KB(hp, ta, tb_)
                    vchain(VE.tensor_tensor(dst, tascr.ap(0, W),
                                            tbscr.ap(0, W), op=ALU.add))
            rot_used = sV.v
            for ts_ in range(W // 128):
                tvi = (ta + 128 * ts_) // 128
                t = pp[7].acquire(PE)
                pv = None
                for c in range(NCH):
                    mm = PE.matmul(t[:, 0:512],
                                   zcast.ap(512 * c + 128 * ts_,
                                            512 * c + 128 * (ts_ + 1)),
                                   wqkvb.ap(1536 * c + 1024, 1536 * c + 1536),
                                   start=(c == 0), stop=(c == NCH - 1))
                    if c == NCH - 1:
                        pv = sP.inc(mm)
                VE.wait_ge(sP.h, pv)
                vv = sV.inc(VE.tensor_copy(
                    VT(tvi).rearrange("p (h j) -> p h j", j=65)[:, :, 0:64],
                    t[:, 0:512].rearrange("p (h j) -> p h j", j=64)))
                pp[7].consumed(sV, vv)
            zc_pe = sP.v
        vt1 = vchain(VE.memset(
            vtbuf.ap().rearrange("p (t h j) -> p t h j", h=8, j=65)[:, :, :, 64:65],
            1.0))

        GP.wait_ge(sP.h, sP.v)
        dma(GP, fwob.ap().rearrange("p (c m) -> p c m", m=1024),
            di[f"fwo{lidx}"].rearrange("(c p) m -> p c m", p=128),
            f"Lwo_{tag}")
        wait_stream(PE, f"Lwo_{tag}")

        PE.wait_ge(sV.h, vt1)
        pt_cons = [0, 0]
        e0_cons = 0
        for qg, (qa, qb_) in enumerate(G4):
            W = qb_ - qa
            nk = (W + 512) // 128
            for hp in range(4):
                toA = pp[4].acquire(PE)
                toB = pp[5].acquire(PE)
                avs = [0, 0]
                pvk = {}
                sbk = {}

                def emit_qk(kk):
                    ka = qa - 512 + 128 * kk
                    banks = (pp[(2 * kk) % 4], pp[(2 * kk + 1) % 4])
                    sbk[kk] = banks
                    pvs_ = []
                    for hh in range(2):
                        tdst = banks[hh].acquire(PE)
                        PE.matmul(tdst[:, 0:W], identb[:], fmask(kk)[:, 0:W],
                                  start=True, stop=False)
                        if (qg == 0 and kk <= 3) or (qg == 1 and kk == 0):
                            PE.matmul(tdst[:, 0:W], identb[:],
                                      khm0b[:, 128:128 + W],
                                      start=False, stop=False)
                        if qg == 0 and kk == 4:
                            PE.matmul(tdst[:, 0:W], identb[:], khm0b[:, 0:W],
                                      start=False, stop=False)
                        pvs_.append(sP.inc(PE.matmul(
                            tdst[:, 0:W],
                            KB(hp, ka, ka + 128)[64 * hh:64 * (hh + 1), :],
                            QB(hp, qa, qb_)[64 * hh:64 * (hh + 1), :],
                            start=False, stop=True)))
                    pvk[kk] = pvs_

                def emit_tail(kk):
                    banks = sbk[kk]
                    ACT.wait_ge(sP.h, max(pvk[kk]))
                    ACT.wait_ge(sP.h, max(pt_cons))
                    if W == 512:
                        whole = pq[kk % 2][:, 0:1024]
                        aa = sA.inc(ACT.activation(ptbuf.ap(0, 1024), whole,
                                                   AF.Exp, scale=1.0))
                        banks[0].consumed(sA, aa)
                        banks[1].consumed(sA, aa)
                    else:
                        for hh in range(2):
                            aa = sA.inc(ACT.activation(
                                ptbuf.ap(512 * hh, 512 * hh + W),
                                banks[hh].t[:, 0:W], AF.Exp, scale=1.0))
                            banks[hh].consumed(sA, aa)
                    PE.wait_ge(sA.h, aa)
                    for hh in range(2):
                        to = toA if hh == 0 else toB
                        h = 2 * hp + hh
                        av = sP.inc(PE.matmul(
                            to[0:65, 0:W],
                            VT((qa - 512 + 128 * kk) // 128)[:, 65 * h:65 * (h + 1)],
                            ptbuf.ap(512 * hh, 512 * hh + W),
                            start=(kk == 0), stop=(kk == nk - 1)))
                        pt_cons[hh] = av
                        avs[hh] = av

                emit_qk(0)
                for kk in range(1, nk):
                    emit_qk(kk)
                    emit_tail(kk - 1)
                emit_tail(nk - 1)
                avA, avB = avs
                VE.wait_ge(sP.h, max(avA, avB))
                vchain(VE.tensor_copy(srow[0:1, 0:512][:, 0:W], toA[64:65, 0:W]))
                lv = vchain(VE.tensor_copy(srow[0:1, 512:512 + W],
                                           toB[64:65, 0:W]))
                ACT.wait_ge(sV.h, lv)
                for seg0 in (0, 512):
                    av = sA.inc(ACT.activation(
                        srow[0:1, 1024 + seg0:1024 + seg0 + W],
                        srow[0:1, seg0:seg0 + W], AF.Ln))
                    ACT.wait_ge(sA.h, av)
                    av = sA.inc(ACT.activation(
                        srow[0:1, 1024 + seg0:1024 + seg0 + W],
                        srow[0:1, 1024 + seg0:1024 + seg0 + W],
                        AF.Exp, scale=-1.0))
                PE.wait_ge(sA.h, av)
                tl = pp[6].acquire(PE)
                PE.matmul(tl[:, 0:W], e2b0[:], srow[0:1, 1024:1024 + W],
                          start=True, stop=False)
                pl = sP.inc(PE.matmul(tl[:, 0:W], e2b1[:],
                                      srow[0:1, 1536:1536 + W],
                                      start=False, stop=True))
                VE.wait_ge(sP.h, pl)
                lrv = vchain(VE.tensor_copy(lrbuf.ap(0, W), tl[:, 0:W]))
                pp[6].consumed(sV, lrv)
                v1 = vchain(VE.tensor_tensor(
                    obq.ap(512 * hp, 512 * hp + W, p0=0, p1=64),
                    toA[0:64, 0:W], lrbuf.ap(0, W, p0=0, p1=64), op=ALU.mult))
                v2 = vchain(VE.tensor_tensor(
                    obq.ap(512 * hp, 512 * hp + W, p0=64, p1=128),
                    toB[0:64, 0:W], lrbuf.ap(0, W, p0=64, p1=128),
                    op=ALU.mult))
                pp[4].consumed(sV, v1)
                pp[5].consumed(sV, v2)
            ob_v = sV.v
            PE.wait_ge(sV.h, ob_v)
            for dc in range(NCH):
                t = pp[6 + (dc % 2)].acquire(PE)
                pv = None
                for hp in range(4):
                    mm = PE.matmul(t[:, 0:W],
                                   fwob.ap(1024 * hp + 128 * dc,
                                           1024 * hp + 128 * dc + 128),
                                   obq.ap(512 * hp, 512 * hp + W),
                                   start=(hp == 0), stop=(hp == 3))
                    if hp == 3:
                        pv = sP.inc(mm)
                VE.wait_ge(sP.h, pv)
                dst = UB(dc, qa, qb_)
                vv = vchain(VE.tensor_tensor(dst, t[:, 0:W], dst, op=ALU.add))
                if qg == 0:
                    vv = vchain(VE.tensor_scalar(UB(dc, 512, 640),
                                                 UB(dc, 512, 640), flag[:],
                                                 None, op0=ALU.mult))
                pp[6 + (dc % 2)].consumed(sV, vv)

        p.barrier(f"mlp_{tag}")
        dma(GP, w2blk.ap().rearrange("p (c m) -> p c m", m=1024),
            di[f"w2p{lidx}"].rearrange("(c p) m -> p c m", p=128),
            f"Lw2_{tag}")

        def ln2(g, slot):
            ta = 640 + 512 * g

            def get_in(c):
                off = 1 if c >= 4 else 0
                return UB(c, ta - off, ta + 512 - off)

            def put_out(c):
                return (z2roll[slot].ap(512 * c, 512 * (c + 1)),
                        lgc8[lidx].ap(c, c + 1))

            def lgrow(c):
                return lngbs[0:2, 128 * c:128 * (c + 1)]

            return ln_group(get_in, put_out, 512, ln_scr2,
                            (pp[0], pp[1], pp[6], pp[7]), gb=lgrow,
                            ready_v=sV.v)

        wblk_used = [0, 0]
        gel_cons = [0]

        def mlp(g, slot):
            ta = 640 + 512 * g
            z2 = z2roll[slot]

            def w1_load(ob):
                st_w = f"Lw1{ob % 2}_{tag}"
                if wblk_used[ob % 2]:
                    GP.wait_ge(sP.h, wblk_used[ob % 2])
                dma(GP, w1blk[ob % 2].ap().rearrange("p (c o) -> p c o", o=128),
                    di[f"w1p{lidx}"].rearrange("(c p) o -> p c o", p=128)
                    [:, :, 128 * ob:128 * (ob + 1)], st_w)
                return p.sem(st_w).v

            w1_ld = {0: w1_load(0), 1: w1_load(1)}
            for ob in range(44):
                st_w = f"Lw1{ob % 2}_{tag}"
                PE.wait_ge(p.sem(st_w).h, w1_ld[ob % 2])
                t = pp[2 + (ob % 2)].acquire(PE)
                pv = None
                for c in range(NCH):
                    mm = PE.matmul(t[:, 0:512],
                                   w1blk[ob % 2].ap(128 * c, 128 * (c + 1)),
                                   z2.ap(512 * c, 512 * (c + 1)),
                                   start=(c == 0), stop=(c == NCH - 1))
                    if c == NCH - 1:
                        pv = sP.inc(mm)
                wblk_used[ob % 2] = pv
                if ob + 2 < 44:
                    w1_ld[ob % 2] = w1_load(ob + 2)
                if ob < 22:
                    VE.wait_ge(sP.h, pv)
                    vv = sV.inc(VE.tensor_scalar(
                        abuf.ap(512 * ob, 512 * (ob + 1)), t[:, 0:512],
                        b1c[lidx].ap(ob, ob + 1), None, op0=ALU.add))
                    pp[2 + (ob % 2)].consumed(sV, vv)
                else:
                    hb = ob - 22
                    ACT.wait_ge(sP.h, pv)
                    ACT.wait_ge(sV.h, gel_cons[0])
                    aa = sA.inc(ACT.activation(gelscr.ap(0, 512), t[:, 0:512],
                                               AF.Gelu,
                                               bias=b1c[lidx].ap(ob, ob + 1)))
                    pp[2 + (ob % 2)].consumed(sA, aa)
                    VE.wait_ge(sA.h, aa)
                    gv = vchain(VE.tensor_tensor(
                        abuf.ap(512 * hb, 512 * (hb + 1)),
                        abuf.ap(512 * hb, 512 * (hb + 1)),
                        gelscr.ap(0, 512), op=ALU.mult))
                    gel_cons[0] = gv
            h_v = sV.v
            PE.wait_ge(sV.h, h_v)
            wait_stream(PE, f"Lw2_{tag}")
            for half in range(2):
                ts2 = [None] * 4
                pvs = [None] * 4
                for hc in range(22):
                    for di_ in range(4):
                        dc = 4 * half + di_
                        if hc == 0:
                            ts2[di_] = pp[2 + di_].acquire(PE)
                        mm = PE.matmul(ts2[di_][:, 0:512],
                                       w2blk.ap(1024 * hc + 128 * dc,
                                                1024 * hc + 128 * dc + 128),
                                       abuf.ap(512 * hc, 512 * (hc + 1)),
                                       start=(hc == 0), stop=(hc == 21))
                        if hc == 21:
                            pvs[di_] = sP.inc(mm)
                for di_ in range(4):
                    dc = 4 * half + di_
                    VE.wait_ge(sP.h, pvs[di_])
                    dst = UB(dc, ta, ta + 512)
                    vv = vchain(VE.scalar_tensor_tensor(
                        dst, ts2[di_][:, 0:512], b2c8[lidx].ap(dc, dc + 1),
                        dst, op0=ALU.add, op1=ALU.add))
                    pp[2 + di_].consumed(sV, vv)

        ln2(0, 0)
        ln2(1, 1)
        mlp(0, 0)
        ln2(2, 0)
        mlp(1, 1)
        ln2(3, 1)
        mlp(2, 0)
        mlp(3, 1)
        p.barrier(f"eol_{tag}")
        return sV.v

    def ag_exchange(ag_in, ag_out, tag):
        GP.wait_ge(sV.h, sV.v)
        for c in range(NCH):
            dma(GP, ag_in[128 * c:128 * (c + 1), :], UB(c, 2048, TEXT),
                f"GA_{tag}")
        wait_stream(GP, f"GA_{tag}")
        sc_ = p.sem(f"CC_{tag}")
        sc_.inc(GP.collective_compute(
            "AllGather", ALU.bypass, replica_groups=RG,
            ins=[ag_in.opt()], outs=[ag_out.opt()]), 1)
        GP.wait_ge(sc_.h, sc_.v)
        for c in range(NCH):
            dma(GP, UB(c, 0, HALO), ag_out[128 * c:128 * (c + 1), :],
                f"GH_{tag}")
        wait_stream(VE, f"GH_{tag}")
        hv = 0
        for c in range(NCH):
            hv = vchain(VE.tensor_scalar(UB(c, 0, HALO), UB(c, 0, HALO),
                                         flag[:], None, op0=ALU.mult))
        return hv

    def emit_ffn_stage(stage_tag, agi, ago):
        emit_ffn_layer(0, stage_tag)
        if stage_tag == "s2" and dbg("u1", (D, TOWN)) is not None:
            GP.wait_ge(sV.h, sV.v)
            for c in range(NCH):
                dma(GP, p.dbg_outs["u1"][128 * c:128 * (c + 1), :],
                    UB(c, HALO, TEXT), "GD")
            for e_ in (VE, GP, PE, ACT):
                wait_stream(e_, "GD")
        hv = ag_exchange(agi, ago, stage_tag)
        PE.wait_ge(sV.h, hv)
        ACT.wait_ge(sV.h, hv)
        emit_ffn_layer(1, stage_tag)

    emit_ffn_stage("s2", ag1_in, ag1_out)

    p.barrier("x2")
    a1.reset()
    xo2 = a1.alloc(16384, F32)

    def XO2(c, a, b):
        return xo2.ap(TOWN * c + a, TOWN * c + b)

    wait_stream(SY, "GS")
    for c in range(NCH):
        dma(SY, XO2(c, 0, TOWN), x_spill[128 * c:128 * (c + 1), :], "LS2")
    wait_stream(VE, "LS2")
    x2_v = 0
    for c in range(NCH):
        dst = xo2.ap(TOWN * c, TOWN * (c + 1))
        x2_v = vchain(VE.tensor_tensor(dst, dst, UB(c, HALO, TEXT),
                                       op=ALU.add))
    if dbg("x2", (D, TOWN)) is not None:
        GP.wait_ge(sV.h, x2_v)
        for c in range(NCH):
            dma(GP, p.dbg_outs["x2"][128 * c:128 * (c + 1), :],
                XO2(c, 0, TOWN), "GD")
        for e_ in (VE, GP, PE, ACT):
            wait_stream(e_, "GD")
    if upto == "x2":
        finish(p, out_x4, XO2, x2_v)
        return p

    a2.reset()
    kbb = a2.alloc(4 * S, BF16)
    vtb = a2.alloc(32 * 520, BF16)
    zrd = a2.alloc(8 * 512, BF16)
    qwn = a2.alloc(4 * 512, BF16)
    ost = a2.alloc(512, BF16)
    e0big = a2.alloc(1024, F32)
    zown = zrd

    def KBB(hp, a, b):
        return kbb.ap(S * hp + a, S * hp + b)

    def VTB(t):
        return vtb.ap(520 * t, 520 * (t + 1))

    a3.reset()
    pib = [Seg(a3.t, 2048 * i, 8 * 512, BF16) for i in range(3)]
    pos = Seg(a3.t, 5120, 2 * 1024, BF16)

    for i, nm in enumerate(("PiQ", "PiK", "PiV")):
        dma(GP, pib[i].ap().rearrange("p (c m) -> p c m", m=512),
            di[nm].rearrange("(c p) m -> p c m", p=128), "LPi")

    for g in range(4):
        wait_stream(VE, "GZ")

        def get_in(c, g=g):
            return XO2(c, 512 * g, 512 * (g + 1))

        def put_out(c):
            return zown.ap(512 * c, 512 * (c + 1)), None

        zv = ln_group(get_in, put_out, 512, ln_scr2,
                      (pp[0], pp[1], pp[6], pp[7]), ready_v=x2_v)
        SY.wait_ge(sV.h, zv)
        for c in range(NCH):
            dma(SY, ag2z_in[128 * c:128 * (c + 1), 512 * g:512 * (g + 1)],
                zown.ap(512 * c, 512 * (c + 1)), "GZ")
    GP.wait_ge(sV.h, x2_v)
    for c in range(NCH):
        dma(GP, ag2x_in[128 * c:128 * (c + 1), :],
            XO2(c, TOWN - HALO, TOWN), "GX")
    wait_stream(GP, "GZ")
    wait_stream(GP, "GX")
    ccz = p.sem("CCz")
    ccz.inc(GP.collective_compute("AllGather", ALU.bypass, replica_groups=RG,
                                  ins=[ag2z_in.opt()], outs=[ag2z_out.opt()]), 1)
    ccz.inc(GP.collective_compute("AllGather", ALU.bypass, replica_groups=RG,
                                  ins=[ag2x_in.opt()], outs=[ag2x_out.opt()]), 1)
    GP.wait_ge(ccz.h, ccz.v)
    wait_stream(PE, "LPi")

    def zread_group(g, stream):
        sl = g // 4
        co = 512 * (g % 4)
        for c in range(NCH):
            dma(SY, zrd.ap(512 * c, 512 * (c + 1)),
                ag2z_out[1024 * sl + 128 * c:1024 * sl + 128 * (c + 1),
                         co:co + 512], stream)

    zr_pe = [0]
    for g in range(8):
        SY.wait_ge(sP.h, zr_pe[0])
        SY.wait_ge(ccz.h, 1)
        zread_group(g, "LZ")
        wait_stream(PE, "LZ")
        for hp in range(4):
            t = pp[hp % 2].acquire(PE)
            pv = None
            for c in range(NCH):
                mm = PE.matmul(t[:, 0:512],
                               pib[1].ap(512 * c + 128 * hp, 512 * c + 128 * (hp + 1)),
                               zrd.ap(512 * c, 512 * (c + 1)),
                               start=(c == 0), stop=(c == NCH - 1))
                if c == NCH - 1:
                    pv = sP.inc(mm)
            VE.wait_ge(sP.h, pv)
            vv = sV.inc(VE.tensor_copy(KBB(hp, 512 * g, 512 * (g + 1)),
                                       t[:, 0:512]))
            pp[hp % 2].consumed(sV, vv)
        for ts_ in range(4):
            tvi = 4 * g + ts_
            t = pp[2 + ts_ % 2].acquire(PE)
            pv = None
            for c in range(NCH):
                mm = PE.matmul(t[:, 0:512],
                               zrd.ap(512 * c + 128 * ts_, 512 * c + 128 * (ts_ + 1)),
                               pib[2].ap(512 * c, 512 * (c + 1)),
                               start=(c == 0), stop=(c == NCH - 1))
                if c == NCH - 1:
                    pv = sP.inc(mm)
            VE.wait_ge(sP.h, pv)
            vv = sV.inc(VE.tensor_copy(
                VTB(tvi).rearrange("p (h j) -> p h j", j=65)[:, :, 0:64],
                t[:, 0:512].rearrange("p (h j) -> p h j", j=64)))
            pp[2 + ts_ % 2].consumed(sV, vv)
        zr_pe[0] = sP.v
    vt1 = vchain(VE.memset(
        vtb.ap().rearrange("p (t h j) -> p t h j", h=8, j=65)[:, :, :, 64:65],
        1.0))
    PE.wait_ge(sV.h, vt1)

    pt_cons = [0, 0]
    e0_cons = [0]
    qw_pe = [0]
    wait_stream(SY, "LZ")
    SY.wait_ge(sP.h, zr_pe[0])
    for qw in range(8):
        SY.wait_ge(sP.h, qw_pe[0])
        zread_group(qw, "LZq")
        wait_stream(PE, "LZq")
        for hp in range(4):
            t = pp[7].acquire(PE)
            pv = None
            for c in range(NCH):
                mm = PE.matmul(t[:, 0:512],
                               pib[0].ap(512 * c + 128 * hp, 512 * c + 128 * (hp + 1)),
                               zrd.ap(512 * c, 512 * (c + 1)),
                               start=(c == 0), stop=(c == NCH - 1))
                if c == NCH - 1:
                    pv = sP.inc(mm)
            VE.wait_ge(sP.h, pv)
            vv = sV.inc(VE.tensor_copy(qwn.ap(512 * hp, 512 * (hp + 1)),
                                       t[:, 0:512]))
            pp[7].consumed(sV, vv)
        qw_pe[0] = sP.v
        PE.wait_ge(sV.h, sV.v)
        nk = 4 * (qw + 1)
        for hp in range(4):
            toA = pp[4].acquire(PE)
            toB = pp[5].acquire(PE)
            avs = [0, 0]
            pvk = {}
            sbk = {}

            def emit_qk(kk):
                ka = 128 * kk
                banks = (pp[(2 * kk) % 4], pp[(2 * kk + 1) % 4])
                sbk[kk] = banks
                pvs_ = []
                for hh in range(2):
                    tdst = banks[hh].acquire(PE)
                    first = True
                    if kk >= nk - 4:
                        PE.matmul(tdst[:, 0:512], identb[:],
                                  bmask(kk - (nk - 4))[:, 0:512],
                                  start=True, stop=False)
                        first = False
                    pvs_.append(sP.inc(PE.matmul(
                        tdst[:, 0:512],
                        KBB(hp, ka, ka + 128)[64 * hh:64 * (hh + 1), :],
                        qwn.ap(512 * hp, 512 * (hp + 1))[64 * hh:64 * (hh + 1), :],
                        start=first, stop=True)))
                pvk[kk] = pvs_

            def emit_tail(kk):
                banks = sbk[kk]
                whole = pq[kk % 2][:, 0:1024]
                ACT.wait_ge(sP.h, max(pvk[kk]))
                ACT.wait_ge(sV.h, e0_cons[0])
                aa = sA.inc(ACT.activation(e0big.ap(0, 1024), whole,
                                           AF.Exp, scale=1.0 / 32.0))
                ACT.wait_ge(sA.h, aa)
                aa = sA.inc(ACT.activation(e0big.ap(0, 1024), e0big.ap(0, 1024),
                                           AF.Square))
                banks[0].consumed(sA, aa)
                banks[1].consumed(sA, aa)
                VE.wait_ge(sA.h, aa)
                VE.wait_ge(sP.h, max(pt_cons))
                ptv = sV.inc(VE.tensor_tensor(ptbuf.ap(0, 1024),
                                              e0big.ap(0, 1024),
                                              e0big.ap(0, 1024), op=ALU.mult))
                e0_cons[0] = ptv
                PE.wait_ge(sV.h, ptv)
                for hh in range(2):
                    to = toA if hh == 0 else toB
                    h = 2 * hp + hh
                    av = sP.inc(PE.matmul(
                        to[0:65, 0:512],
                        VTB(kk)[:, 65 * h:65 * (h + 1)],
                        ptbuf.ap(512 * hh, 512 * hh + 512),
                        start=(kk == 0), stop=(kk == nk - 1)))
                    pt_cons[hh] = av
                    avs[hh] = av

            emit_qk(0)
            for kk in range(1, nk):
                emit_qk(kk)
                emit_tail(kk - 1)
            emit_tail(nk - 1)
            avA, avB = avs
            VE.wait_ge(sP.h, max(avA, avB))
            vchain(VE.tensor_copy(srow[0:1, 0:512], toA[64:65, 0:512]))
            lv = vchain(VE.tensor_copy(srow[0:1, 512:1024], toB[64:65, 0:512]))
            ACT.wait_ge(sV.h, lv)
            for seg0 in (0, 512):
                av = sA.inc(ACT.activation(srow[0:1, 1024 + seg0:1536 + seg0],
                                           srow[0:1, seg0:512 + seg0], AF.Ln))
                ACT.wait_ge(sA.h, av)
                av = sA.inc(ACT.activation(srow[0:1, 1024 + seg0:1536 + seg0],
                                           srow[0:1, 1024 + seg0:1536 + seg0],
                                           AF.Exp, scale=-1.0))
            PE.wait_ge(sA.h, av)
            tl = pp[6].acquire(PE)
            PE.matmul(tl[:, 0:512], e2b0[:], srow[0:1, 1024:1536],
                      start=True, stop=False)
            pl = sP.inc(PE.matmul(tl[:, 0:512], e2b1[:], srow[0:1, 1536:2048],
                                  start=False, stop=True))
            VE.wait_ge(sP.h, pl)
            lrv = vchain(VE.tensor_copy(lrbuf.ap(0, 512), tl[:, 0:512]))
            pp[6].consumed(sV, lrv)
            wait_stream(VE, "GO")
            vchain(VE.tensor_tensor(ost.ap(0, 512, p0=0, p1=64),
                                    toA[0:64, 0:512],
                                    lrbuf.ap(0, 512, p0=0, p1=64), op=ALU.mult))
            ov = vchain(VE.tensor_tensor(ost.ap(0, 512, p0=64, p1=128),
                                         toB[0:64, 0:512],
                                         lrbuf.ap(0, 512, p0=64, p1=128),
                                         op=ALU.mult))
            pp[4].consumed(sV, ov)
            pp[5].consumed(sV, ov)
            SY.wait_ge(sV.h, ov)
            dma(SY, ag3_in[128 * hp:128 * (hp + 1), 512 * qw:512 * (qw + 1)],
                ost.ap(0, 512), "GO")

    for nm_, seg_, w_ in (("Kbb", kbb, 4 * S), ("Vtb", vtb, 32 * 520)):
        if dbg(nm_, (128, w_), BF16) is not None:
            GP.wait_ge(sV.h, sV.v)
            dma(GP, p.dbg_outs[nm_], seg_.ap(0, w_), "GD")
            wait_stream(GP, "GD")
    if dbg("Ob", (512, S), BF16) is not None:
        wait_stream(GP, "GO")
        dma(GP, p.dbg_outs["Ob"], ag3_in, "GD")
        wait_stream(GP, "GD")
    wait_stream(GP, "GO")
    cc3 = p.sem("CC3")
    cc3.inc(GP.collective_compute("AllGather", ALU.bypass, replica_groups=RG,
                                  ins=[ag3_in.opt()], outs=[ag3_out.opt()]), 1)
    GP.wait_ge(cc3.h, cc3.v)

    GP.wait_ge(sV.h, sV.v)
    for c in range(NCH):
        dma(GP, xh.ap(640 * c, 640 * (c + 1)),
            ag2x_out[128 * c:128 * (c + 1), :], "LXH")
    wait_stream(VE, "LXH")
    for c in range(NCH):
        vchain(VE.tensor_scalar(xh.ap(640 * c, 640 * (c + 1)),
                                xh.ap(640 * c, 640 * (c + 1)),
                                flag[:], None, op0=ALU.mult))
    p.barrier("ob_comb")
    a2.reset()
    ocb = a2.alloc(8 * 2688, BF16)
    ourd = [a2.alloc(2 * 512, BF16), a2.alloc(2 * 512, BF16)]
    ot1 = a2.alloc(512, BF16)

    def OCB(ic, a, b):
        return ocb.ap(2688 * ic + a, 2688 * ic + b)

    oc_tiles = [(ic, ta, tb_) for ic in range(NCH)
                for (ta, tb_) in ((1408, 1920), (1920, 2048), (2048, 2560),
                                  (2560, 3072), (3072, 3584), (3584, 4096))]
    oc_used = [0, 0]
    oc_ld = [0, 0]

    def oc_load(i):
        ic, ta, tb_ = oc_tiles[i]
        sl = i % 2
        st = f"LOr{sl}"
        if oc_used[sl]:
            SY.wait_ge(sV.h, oc_used[sl])
        for rot in range(2):
            ca = (ta + 2048 * rot) % 4096
            dma(SY, ourd[sl].ap(512 * rot, 512 * rot + (tb_ - ta)),
                ag3_out[128 * ic:128 * (ic + 1), ca:ca + (tb_ - ta)], st)
        return p.sem(st).v

    oc_ld[0] = oc_load(0)
    oc_ld[1] = oc_load(1)
    ocv = 0
    for i, (ic, ta, tb_) in enumerate(oc_tiles):
        W = tb_ - ta
        sl = i % 2
        VE.wait_ge(p.sem(f"LOr{sl}").h, oc_ld[sl])
        vchain(VE.tensor_scalar(ot1.ap(0, W), ourd[sl].ap(0, W),
                                flag[:], None, op0=ALU.mult))
        ocv = vchain(VE.scalar_tensor_tensor(
            OCB(ic, ta - 1408, tb_ - 1408), ourd[sl].ap(512, 512 + W),
            flagn[:], ot1.ap(0, W), op0=ALU.mult, op1=ALU.add))
        oc_used[sl] = ocv
        if i + 2 < len(oc_tiles):
            oc_ld[sl] = oc_load(i + 2)
    pob = Seg(a3.t, 5120, 2 * 1024, BF16)
    ytiles = [(0, 512, "h"), (512, 640, "h"), (640, 1152, "o"),
              (1152, 1664, "o"), (1664, 2176, "o"), (2176, 2688, "o")]
    x3_v = 0
    po_used = [0, 0]
    for dc in range(NCH):
        st_p = f"LPo{dc % 2}"
        if po_used[dc % 2]:
            GP.wait_ge(sP.h, po_used[dc % 2])
        dma(GP, pob.ap(1024 * (dc % 2), 1024 * (dc % 2) + 1024)
            .rearrange("p (c m) -> p c m", m=128),
            di["Po"].rearrange("(c p) m -> p c m", p=128)
            [:, :, 128 * dc:128 * (dc + 1)], st_p)
        wait_stream(PE, st_p)
        PE.wait_ge(sV.h, ocv)
        for ti, (ta, tb_, kind) in enumerate(ytiles):
            W = tb_ - ta
            t = pp[ti % 4].acquire(PE)
            pv = None
            for ic in range(NCH):
                mm = PE.matmul(t[:, 0:W],
                               pob.ap(1024 * (dc % 2) + 128 * ic,
                                      1024 * (dc % 2) + 128 * (ic + 1)),
                               OCB(ic, ta, tb_),
                               start=(ic == 0), stop=(ic == NCH - 1))
                if ic == NCH - 1:
                    pv = sP.inc(mm)
            po_used[dc % 2] = pv
            VE.wait_ge(sP.h, pv)
            if kind == "h":
                dst = xh.ap(640 * dc + ta, 640 * dc + tb_)
                vchain(VE.tensor_tensor(dst, t[:, 0:W], dst, op=ALU.add))
                x3_v = vchain(VE.tensor_scalar(dst, dst, flag[:], None,
                                               op0=ALU.mult))
            else:
                dst = XO2(dc, ta - 640, tb_ - 640)
                x3_v = vchain(VE.tensor_tensor(dst, t[:, 0:W], dst, op=ALU.add))
            pp[ti % 4].consumed(sV, x3_v)

    if dbg("x3", (D, TOWN)) is not None:
        GP.wait_ge(sV.h, x3_v)
        for c in range(NCH):
            dma(GP, p.dbg_outs["x3"][128 * c:128 * (c + 1), :],
                XO2(c, 0, TOWN), "GD")
        for e_ in (VE, GP, PE, ACT):
            wait_stream(e_, "GD")
    if upto == "x3":
        finish(p, out_x4, XO2, x3_v)
        return p

    GP.wait_ge(sV.h, x3_v)
    for c in range(NCH):
        dma(GP, x_spill[128 * c:128 * (c + 1), :], XO2(c, 0, TOWN), "GS2")
    for e_ in (GP, SY, PE, VE, ACT):
        wait_stream(e_, "GS2")
    p.barrier("z0p")
    a2.reset()
    a2.off = NCH * UBW
    for c in range(NCH):
        vchain(VE.memset(ubt.ap(UBW * c, UBW * c + 1), 0.0))

    ln_tiles2 = [(0, 512), (512, 640), (640, 1152), (1152, 1664),
                 (1664, 2176), (2176, 2688)]
    z0p_v = 0
    for (ta, tb_) in ln_tiles2:
        W = tb_ - ta

        def get_in(c, ta=ta, tb_=tb_):
            if tb_ <= 640:
                return xh.ap(640 * c + ta, 640 * c + tb_)
            return XO2(c, ta - 640, tb_ - 640)

        def put_out(c, ta=ta, tb_=tb_):
            return UB(c, ta, tb_), None

        z0p_v = ln_group(get_in, put_out, W, ln_scr2,
                         (pp[4], pp[5], pp[6], pp[7]), ready_v=x3_v)
    p.barrier("s4")

    emit_ffn_stage("s4", ag4_in, ag4_out)

    p.barrier("x4")
    for c in range(NCH):
        dma(SY, XO2(c, 0, TOWN), x_spill[128 * c:128 * (c + 1), :], "LS4")
    wait_stream(VE, "LS4")
    x4_v = 0
    for c in range(NCH):
        dst = xo2.ap(TOWN * c, TOWN * (c + 1))
        x4_v = vchain(VE.tensor_tensor(dst, dst, UB(c, HALO, TEXT),
                                       op=ALU.add))
    finish(p, out_x4, XO2, x4_v)
    return p


def finish(p, out_x4, XO, ready_v):
    nc = p.nc
    sO = p.sem("O")
    nc.gpsimd.wait_ge(p.sem("V").h, ready_v)
    for c in range(NCH):
        sO.inc(nc.gpsimd.dma_start(out=out_x4[128 * c:128 * (c + 1), :],
                                   in_=XO(c, 0, TOWN)), 16)
    for e in (nc.gpsimd, nc.sync, nc.tensor, nc.vector, nc.scalar):
        e.wait_ge(sO.h, sO.v)
        if "GD" in p._sems:
            e.wait_ge(p.sem("GD").h, p.sem("GD").v)


def kernel(**inputs):
    in_maps = [_stage_core(inputs, c) for c in range(8)]
    p = build(upto=BUILD_UPTO, debug=DEBUG_OUT)
    res = run_bass_kernel_spmd(p.nc, in_maps, core_ids=list(range(8)))
    out = np.empty((B, S, D), dtype=np.float32)
    for c in range(8):
        b, half = c // 2, c % 2
        out[b, 2048 * half:2048 * half + 2048] = res.results[c]["o_x4"].T
    return out


BUILD_UPTO = "full"
DEBUG_OUT = ()


# version 07
# speedup vs baseline: 1.4114x; 1.0078x over previous
import contextlib
import numpy as np
import ml_dtypes

import concourse.bass as bass
import concourse.mybir as mybir
from concourse.bass_utils import run_bass_kernel_spmd

F32 = mybir.dt.float32
BF16 = mybir.dt.bfloat16
AF = mybir.ActivationFunctionType
ALU = mybir.AluOpType
AX = mybir.AxisListType

B, S, D = 4, 4096, 1024
NCH = 8
TOWN = 2048
HALO = 640
TEXT = HALO + TOWN
M = 128
CC = 128
DECAY, LR = 0.99, 0.01
DEC_C = float(DECAY ** CC)
NEG = -1.0e30
FH, FDH, WIN = 8, 64, 512
HID, HIDP = 2730, 2816
EPS = 1e-5

_f32 = np.float32
_bf16 = ml_dtypes.bfloat16


def _host_consts(half):
    c = {}
    i = np.arange(CC)
    dmT = np.where(i[:, None] < i[None, :],
                   LR * DECAY ** (-1.0 - i[:, None]), 0.0)
    c["dmaskT"] = (dmT * np.ones((CC, CC))).astype(_f32)
    c["decq512"] = np.broadcast_to(
        DECAY ** (np.arange(512) % 128), (128, 512)).astype(_f32)
    c["deck"] = (LR * DECAY ** (127.0 - i)).reshape(128, 1).astype(_f32)
    c["identity"] = np.eye(128, dtype=_f32)
    R = np.zeros((128, 128), dtype=_f32)
    for h0 in (0, 64):
        for k in range(32):
            R[h0 + 32 + k, h0 + k] = -1.0
            R[h0 + k, h0 + 32 + k] = 1.0
    c["Rperm"] = R
    oc2 = np.zeros((128, 2), dtype=_f32)
    oc2[:64, 0] = 1.0
    oc2[64:, 1] = 1.0
    c["ones2col"] = oc2
    e2 = np.zeros((2, 128), dtype=_f32)
    e2[0, :64] = 1.0
    e2[1, 64:] = 1.0
    c["e2"] = e2
    c["flag"] = np.full((128, 1), float(half), dtype=_f32)
    c["flagn"] = np.full((128, 1), 1.0 - float(half), dtype=_f32)
    c["epsc"] = np.full((128, 1), EPS, dtype=_f32)
    c["ones512"] = np.ones((1, 512), dtype=_f32)

    t0 = 2048 * half
    pos = np.arange(t0 - HALO, t0 + TOWN, dtype=np.float64)
    half_idx = np.arange(0, FDH, 2, dtype=np.float64)
    inv_freq = 1.0 / 10000.0 ** (half_idx / FDH)
    freqs = np.concatenate([pos[:, None] * inv_freq[None, :]] * 2, axis=1)
    base = (half_idx + 0.4 * FDH) / (1.4 * FDH)
    power = (pos - S // 2) / float(WIN // 2)
    scale = np.concatenate([base[None, :] ** power[:, None]] * 2, axis=1)
    cos, sin = np.cos(freqs), np.sin(freqs)
    for nm, tb in (("rotAq", cos * scale * 8.0), ("rotBq", sin * scale * 8.0),
                   ("rotAk", cos / scale), ("rotBk", sin / scale)):
        c[nm] = np.tile(tb.T.astype(_f32), (2, 1))

    kl = np.arange(128)[:, None]
    cf = np.arange(1408)[None, :]
    df = kl - cf + 384
    c["fmaskM"] = np.where((df > 0) | (df < -512), NEG, 0.0).astype(_f32)
    km = np.zeros((128, 640), dtype=_f32)
    if half == 0:
        km[:, 128:] = NEG
    c["khmask0"] = km
    return c


def _stage_core(inputs, core):
    b, half = core // 2, core % 2
    t0 = 2048 * half
    x = inputs["L0_attn_a_x"][b]
    d = dict(_host_consts(half))
    d["x_own"] = x[t0:t0 + TOWN].T
    d["x_pre"] = x[0:2048].T if half else np.zeros((D, 2048), dtype=_f32)
    d["x_halo"] = x[t0 - HALO:t0].T if half else np.zeros((D, HALO), dtype=_f32)
    d["wkT"] = inputs["L0_attn_a_Wk"].T
    d["wvT"] = inputs["L0_attn_a_Wv"].T
    d["wqT"] = inputs["L0_attn_a_Wq"].T
    d["woT"] = inputs["L0_attn_a_Wo"].T
    d["bk"] = inputs["L0_attn_a_bk"].reshape(128, 1)
    d["bq"] = inputs["L0_attn_a_bq"].reshape(128, 1)
    d["bkrow"] = inputs["L0_attn_a_bk"].reshape(1, 128)
    d["bvrow"] = inputs["L0_attn_a_bv"].reshape(1, 128)
    d["bo"] = inputs["L0_attn_a_bo"].reshape(D, 1)
    for l in range(2):
        d[f"wqkv{l}"] = inputs["L0_ffn_wqkv"][l]
        d[f"fwo{l}"] = inputs["L0_ffn_wo"][l]
        w1 = inputs["L0_ffn_w1"][l]
        w1p = np.zeros((D, 2 * HIDP), dtype=_f32)
        w1p[:, :HID] = w1[:, :HID]
        w1p[:, HIDP:HIDP + HID] = w1[:, HID:]
        d[f"w1p{l}"] = w1p
        b1 = inputs["L0_ffn_b1"][l]
        b1p = np.zeros((2 * HIDP, 1), dtype=_f32)
        b1p[:HID, 0] = b1[:HID]
        b1p[HIDP:HIDP + HID, 0] = b1[HID:]
        d[f"b1p{l}"] = b1p
        w2p = np.zeros((HIDP, D), dtype=_f32)
        w2p[:HID] = inputs["L0_ffn_w2"][l]
        d[f"w2p{l}"] = w2p
        d[f"b2_{l}"] = inputs["L0_ffn_b2"][l].reshape(D, 1)
        d[f"lng{l}"] = inputs["L0_ffn_ln_g"][l].reshape(D, 1)
        d[f"lnb{l}"] = inputs["L0_ffn_ln_b"][l].reshape(D, 1)
        d[f"lngb{l}"] = np.stack([inputs["L0_ffn_ln_g"][l],
                                  inputs["L0_ffn_ln_b"][l]])
        d[f"qs{l}"] = np.tile(inputs["L0_ffn_q_scale"][l], 2).reshape(128, 1)
        d[f"ks{l}"] = np.tile(inputs["L0_ffn_k_scale"][l], 2).reshape(128, 1)
    Pi = inputs["L0_attn_b_P_i"]
    myh = range(8 * half, 8 * half + 8)
    d["PiQ"] = np.concatenate([Pi[:, 192 * h:192 * h + 64] for h in myh], axis=1)
    d["PiK"] = np.concatenate([Pi[:, 192 * h + 64:192 * h + 128] for h in myh], axis=1)
    d["PiV"] = np.concatenate([Pi[:, 192 * h + 128:192 * h + 192] for h in myh], axis=1)
    d["Po"] = inputs["L0_attn_b_P_o"]
    return {k: np.ascontiguousarray(v, dtype=_f32) for k, v in d.items()}


class Sem:
    def __init__(self, handle):
        self.h = handle
        self.v = 0

    def inc(self, inst, n=1):
        inst.then_inc(self.h, n)
        self.v += n
        return self.v


class Seg:

    def __init__(self, t, off, cols, dtype):
        self.t, self.off, self.cols, self.dtype = t, off, cols, dtype

    def ap(self, a=0, b=None, p0=0, p1=128):
        b = self.cols if b is None else b
        if self.dtype == F32:
            return self.t[p0:p1, self.off + a:self.off + b]
        assert self.off % 1 == 0
        base = self.t[p0:p1, self.off:self.off + (self.cols + 1) // 2]
        return base.bitcast(BF16)[:, a:b]


class Arena:
    def __init__(self, p, name, cols):
        self.t = p.sb(name, [128, cols], F32)
        self.cols = cols
        self.off = 0
        self.name = name

    def alloc(self, cols, dtype=F32):
        fcols = cols if dtype == F32 else (cols + 1) // 2
        assert self.off + fcols <= self.cols, \
            f"arena {self.name} overflow: {self.off}+{fcols} > {self.cols}"
        s = Seg(self.t, self.off, cols, dtype)
        self.off += fcols
        return s

    def reset(self, off=0):
        self.off = off


class Prog:
    def __init__(self):
        self.nc = bass.Bass("TRN2", target_bir_lowering=False, debug=False)
        self.ctx = contextlib.ExitStack()
        self._sems = {}
        self.di = {}
        self.dbg_outs = {}
        self.debug = ()

    def sem(self, name):
        if name not in self._sems:
            self._sems[name] = Sem(self.ctx.enter_context(self.nc.semaphore(name)))
        return self._sems[name]

    def sb(self, name, shape, dtype):
        return self.ctx.enter_context(self.nc.sbuf_tensor(name, shape, dtype))

    def psum(self, name, shape, dtype=F32):
        return self.ctx.enter_context(self.nc.psum_tensor(name, shape, dtype))

    def barrier(self, tag):
        s = self.sem("bar")
        nc = self.nc
        engs = [nc.sync, nc.tensor, nc.vector, nc.scalar, nc.gpsimd]
        base = s.v
        for e in engs:
            s.inc(e.nop(), 1)
        for e in engs:
            e.wait_ge(s.h, base + 5)


class PsBank:

    def __init__(self, t):
        self.t = t
        self.free_at = None

    def acquire(self, eng):
        if self.free_at is not None:
            sem, v = self.free_at
            eng.wait_ge(sem.h, v)
            self.free_at = None
        return self.t

    def consumed(self, sem, v):
        self.free_at = (sem, v)


def build(upto="full", debug=()):
    p = Prog()
    p.debug = tuple(debug)
    nc = p.nc
    di = p.di

    def dram_in(name, shape, dtype=F32):
        di[name] = nc.dram_tensor(name, list(shape), dtype,
                                  kind="ExternalInput").ap()

    for nm, sh in [
        ("x_own", (D, TOWN)), ("x_pre", (D, 2048)), ("x_halo", (D, HALO)),
        ("flag", (128, 1)), ("flagn", (128, 1)),
        ("epsc", (128, 1)), ("ones512", (1, 512)),
        ("wkT", (D, 128)), ("wvT", (D, 128)), ("wqT", (D, 128)), ("woT", (128, D)),
        ("bk", (128, 1)), ("bq", (128, 1)), ("bkrow", (1, 128)), ("bvrow", (1, 128)),
        ("bo", (D, 1)),
        ("dmaskT", (128, 128)), ("decq512", (128, 512)), ("deck", (128, 1)),
        ("identity", (128, 128)), ("Rperm", (128, 128)),
        ("ones2col", (128, 2)), ("e2", (2, 128)),
        ("rotAq", (128, TEXT)), ("rotBq", (128, TEXT)),
        ("rotAk", (128, TEXT)), ("rotBk", (128, TEXT)),
        ("khmask0", (128, 640)),
        ("PiQ", (D, 512)), ("PiK", (D, 512)), ("PiV", (D, 512)), ("Po", (D, D)),
    ]:
        dram_in(nm, sh)
    dram_in("fmaskM", (128, 1408))
    for l in range(2):
        for nm, sh in [
            (f"wqkv{l}", (D, 1536)), (f"fwo{l}", (512, D)),
            (f"w1p{l}", (D, 2 * HIDP)), (f"b1p{l}", (2 * HIDP, 1)),
            (f"w2p{l}", (HIDP, D)), (f"b2_{l}", (D, 1)),
            (f"lng{l}", (D, 1)), (f"lnb{l}", (D, 1)),
            (f"lngb{l}", (2, D)),
            (f"qs{l}", (128, 1)), (f"ks{l}", (128, 1)),
        ]:
            dram_in(nm, sh)

    out_x4 = nc.dram_tensor("o_x4", [D, TOWN], F32, kind="ExternalOutput").ap()

    def dbg(name, shape, dtype=F32):
        if name in p.debug and name not in p.dbg_outs:
            p.dbg_outs[name] = nc.dram_tensor(
                "dbg_" + name, list(shape), dtype, kind="ExternalOutput").ap()
        return p.dbg_outs.get(name)

    x_spill = nc.dram_tensor("x_spill", [D, TOWN], F32, kind="Internal").ap()
    ag1_in = nc.dram_tensor("ag1_in", [D, HALO], BF16, kind="Internal").ap()
    ag1_out = nc.dram_tensor("ag1_out", [2 * D, HALO], BF16, kind="Internal").ap()
    ag2z_in = nc.dram_tensor("ag2z_in", [D, TOWN], BF16, kind="Internal").ap()
    ag2z_out = nc.dram_tensor("ag2z_out", [2 * D, TOWN], BF16, kind="Internal").ap()
    ag2x_in = nc.dram_tensor("ag2x_in", [D, HALO], BF16, kind="Internal").ap()
    ag2x_out = nc.dram_tensor("ag2x_out", [2 * D, HALO], BF16, kind="Internal").ap()
    ag3_in = nc.dram_tensor("ag3_in", [512, S], BF16, kind="Internal").ap()
    ag3_out = nc.dram_tensor("ag3_out", [1024, S], BF16, kind="Internal").ap()
    ag4_in = nc.dram_tensor("ag4_in", [D, HALO], BF16, kind="Internal").ap()
    ag4_out = nc.dram_tensor("ag4_out", [2 * D, HALO], BF16, kind="Internal").ap()
    RG = [[0, 1], [2, 3], [4, 5], [6, 7]]

    a1 = Arena(p, "a1", 19544)
    a2 = Arena(p, "a2", NCH * (TEXT + 1))
    a3 = Arena(p, "a3", 6656)
    sc = Arena(p, "sc", 768)
    flag = p.sb("flag_s", [128, 1], F32)
    flagn = p.sb("flagn_s", [128, 1], F32)
    epsc = p.sb("epsc_s", [128, 1], F32)
    ones1 = p.sb("ones1", [1, 128], F32)
    ones1b = p.sb("ones1b", [1, 128], BF16)
    onescol = p.sb("onescol", [128, 1], BF16)
    identb = p.sb("identb", [128, 128], BF16)
    srow = p.sb("srow", [2, 2048], F32)
    masksF = p.sb("masksF", [128, 1408], BF16)

    pq = [p.psum("pq0", [128, 1024]), p.psum("pq1", [128, 1024]),
          p.psum("pq2", [128, 1024])]
    pp = [PsBank(pq[0][:, 0:512]), PsBank(pq[0][:, 512:1024]),
          PsBank(pq[1][:, 0:512]), PsBank(pq[1][:, 512:1024]),
          PsBank(p.psum("pp4", [128, 512])), PsBank(p.psum("pp5", [128, 512])),
          PsBank(pq[2][:, 0:512]), PsBank(pq[2][:, 512:1024])]

    block = p.ctx.enter_context(nc.Block())
    SY, PE, VE, ACT, GP = nc.sync, nc.tensor, nc.vector, nc.scalar, nc.gpsimd
    sP = p.sem("P")
    sV = p.sem("V")
    sA = p.sem("A")

    def dma(eng, out, in_, stream, **kw):
        sm = p.sem(stream)
        sm.inc(eng.dma_start(out=out, in_=in_, **kw), 16)
        return sm.v

    def wait_stream(eng, stream):
        sm = p.sem(stream)
        eng.wait_ge(sm.h, sm.v)

    def vchain(inst):
        vv = sV.inc(inst)
        VE.wait_ge(sV.h, vv)
        return vv

    xo = a1.alloc(16384, F32)

    def XO(c, a, b):
        return xo.ap(TOWN * c + a, TOWN * c + b)

    dma(SY, flag[:], di["flag"], "L1")
    dma(SY, flagn[:], di["flagn"], "L1")
    dma(SY, epsc[:], di["epsc"], "L1")
    dma(GP, identb[:], di["identity"], "G1")
    dma(GP, ones1b[:], di["ones512"][0:1, 0:128], "G1")
    for c in range(NCH):
        dma(SY, XO(c, 0, TOWN), di["x_own"][128 * c:128 * (c + 1), :], "L1")
    dma(GP, masksF[:], di["fmaskM"], "G1")

    def fmask(kk):
        return masksF[:, 896 - 128 * kk:1408 - 128 * kk]

    def bmask(j):
        return masksF[:, 384 - 128 * j:896 - 128 * j]
    dma(SY, srow[1:2, 1536:2048], di["ones512"], "L1")
    dma(SY, ones1[:], di["ones512"][0:1, 0:128], "L1")
    VE.memset(onescol[:], 1.0)

    def ln_group(get_in, put_out, W, scratch, banks, gb=None, ready_v=None):
        stA, stB, bcA, bcB = banks
        xbs, sqs, t1s = scratch["xb"], scratch["sq"], scratch["t1"]
        if ready_v is not None:
            ACT.wait_ge(sV.h, ready_v)
        tA = stA.acquire(PE)
        tB = stB.acquire(PE)
        slot_pe = scratch.setdefault("slot_pe", [0, 0])
        for c in range(NCH):
            xin = get_in(c)
            xb = xbs[c % 2]
            sq = sqs[c % 2]
            VE.wait_ge(sP.h, slot_pe[c % 2])
            ACT.wait_ge(sP.h, slot_pe[c % 2])
            vv = sV.inc(VE.tensor_copy(xb.ap(0, W), xin))
            aa = sA.inc(ACT.activation(sq.ap(0, W), xin, AF.Square))
            PE.wait_ge(sV.h, vv)
            PE.wait_ge(sA.h, aa)
            PE.matmul(tA[0:1, 0:W], onescol[:], xb.ap(0, W),
                      start=(c == 0), stop=(c == NCH - 1))
            pv = sP.inc(PE.matmul(tB[0:1, 0:W], onescol[:], sq.ap(0, W),
                                  start=(c == 0), stop=(c == NCH - 1)))
            slot_pe[c % 2] = pv
        pe_done = sP.v
        mu = srow[0:1, 0:0 + W]
        msq = srow[0:1, 512:512 + W]
        var = srow[0:1, 1024:1024 + W]
        sd = srow[0:1, 512:512 + W]
        rst = srow[0:1, 1024:1024 + W]
        nmb = srow[0:1, 1536:1536 + W]
        VE.wait_ge(sP.h, pe_done)
        vv = sV.inc(VE.tensor_scalar(mu, tA[0:1, 0:W], 1.0 / D, None,
                                     op0=ALU.mult))
        ACT.wait_ge(sV.h, vv)
        av = sA.inc(ACT.activation(msq, mu, AF.Square))
        VE.wait_ge(sA.h, av)
        vv = sV.inc(VE.scalar_tensor_tensor(
            var, tB[0:1, 0:W], 1.0 / D, msq,
            op0=ALU.mult, op1=ALU.subtract))
        stA.consumed(sV, vv)
        stB.consumed(sV, vv)
        ACT.wait_ge(sV.h, vv)
        av = sA.inc(ACT.activation(sd, var, AF.Ln, bias=epsc[0:1, :]))
        ACT.wait_ge(sA.h, av)
        av = sA.inc(ACT.activation(rst, sd, AF.Exp, scale=-0.5))
        VE.wait_ge(sA.h, av)
        vv = vchain(VE.scalar_tensor_tensor(nmb, mu, -1.0, rst,
                                            op0=ALU.mult, op1=ALU.mult))
        tbcA = bcA.acquire(PE)
        PE.wait_ge(sV.h, vv)
        if gb is not None:
            vv = vchain(VE.tensor_copy(scratch["lnrb"][0:1, 0:W], nmb))
            PE.wait_ge(sV.h, vv)
        pva = sP.inc(PE.matmul(tbcA[:, 0:W], ones1[:], rst,
                               start=True, stop=True))
        if gb is None:
            tbcB = bcB.acquire(PE)
            pvb = sP.inc(PE.matmul(tbcB[:, 0:W], ones1[:], nmb,
                                   start=True, stop=True))
            VE.wait_ge(sP.h, pvb)
        VE.wait_ge(sP.h, pva)
        last = None
        for c in range(NCH):
            xin = get_in(c)
            t1 = t1s.ap(0, W)
            if gb is not None:
                tbcB = bcB.acquire(PE)
                pvb = sP.inc(PE.matmul(tbcB[:, 0:W], gb(c),
                                       scratch["lnrb"][0:2, 0:W],
                                       start=True, stop=True))
            vchain(VE.tensor_tensor(t1, xin, tbcA[:, 0:W], op=ALU.mult))
            dst, lgcol = put_out(c)
            if gb is None:
                last = vchain(VE.scalar_tensor_tensor(
                    dst, t1, 1.0, tbcB[:, 0:W], op0=ALU.mult, op1=ALU.add))
            else:
                VE.wait_ge(sP.h, pvb)
                last = vchain(VE.scalar_tensor_tensor(
                    dst, t1, lgcol, tbcB[:, 0:W], op0=ALU.mult, op1=ALU.add))
                bcB.consumed(sV, last)
        bcA.consumed(sV, last)
        bcB.consumed(sV, last)
        return last

    za = a2.alloc(4096, BF16)
    kfm = a2.alloc(4096, BF16)
    qd0 = a2.alloc(3072, BF16)
    ktm = a2.alloc(4096, BF16)
    vtm = a2.alloc(4096, BF16)
    obuf = a2.alloc(2688, BF16)
    wob = a2.alloc(1024, BF16)
    awk = a2.alloc(512, F32)
    xpA = a2.alloc(4096, F32)
    xpB = a2.alloc(4096, F32)

    class _QD:
        def ap(self, a, b):
            return qd0.ap(a - 1024, b - 1024)
    qd = _QD()
    xh = a3.alloc(5120, F32)
    wkb = a3.alloc(1024, BF16)
    wvb = a3.alloc(1024, BF16)
    wqb = a3.alloc(1024, BF16)
    dmk = a1.alloc(128, F32)
    stf = a1.alloc(128, F32)
    stb = a1.alloc(128, BF16)
    wts = [a1.alloc(128, BF16), a1.alloc(128, BF16)]
    ktds = [a1.alloc(128, BF16), a1.alloc(128, BF16)]
    csm = a1.alloc(16, F32)
    bo8 = a1.alloc(8, F32)
    xb_s = [a1.alloc(512, BF16), a1.alloc(512, BF16)]
    sq_s = [a1.alloc(512, BF16), a1.alloc(512, BF16)]
    t1_s = a1.alloc(512, F32)

    brow_k = p.sb("brow_k", [1, 128], BF16)
    brow_v = p.sb("brow_v", [1, 128], BF16)

    for (dst, src) in ((wkb, "wkT"), (wvb, "wvT"), (wqb, "wqT")):
        dma(GP, dst.ap().rearrange("p (c m) -> p c m", m=128),
            di[src].rearrange("(c p) m -> p c m", p=128), "G1")
    dma(GP, wob.ap(), di["woT"], "G1")
    dma(GP, brow_k[:], di["bkrow"], "G1")
    dma(GP, brow_v[:], di["bvrow"], "G1")
    dma(SY, awk.ap(), di["decq512"], "L1")
    dma(SY, dmk.ap(), di["dmaskT"], "L1")
    dma(SY, csm.ap(0, 1), di["bk"], "L1")
    dma(SY, csm.ap(1, 2), di["bq"], "L1")
    dma(SY, csm.ap(2, 3), di["deck"], "L1")
    dma(SY, bo8.ap().rearrange("p (c o) -> p c o", o=1),
        di["bo"].rearrange("(c p) o -> p c o", p=128),
        allow_slow_non_contiguous=True, stream="L1")
    bk_col = lambda: csm.ap(0, 1)
    bq_col = lambda: csm.ap(1, 2)
    deck_col = lambda: csm.ap(2, 3)

    def load_xpre(g, seg, slot):
        for c in range(NCH):
            dma(SY, seg.ap(512 * c, 512 * (c + 1)),
                di["x_pre"][128 * c:128 * (c + 1), 512 * g:512 * (g + 1)],
                f"Lx{slot}")
        return p.sem(f"Lx{slot}").v

    ln_scr = {"xb": xb_s, "sq": sq_s, "t1": t1_s}

    xp_ready = [None, None]
    xp_ready[0] = load_xpre(0, xpA, 0)
    xp_ready[1] = load_xpre(1, xpB, 1)

    for e in (VE, PE, ACT):
        wait_stream(e, "L1")
        wait_stream(e, "G1")

    for g in range(8):
        own = g >= 4
        if not own:
            seg = (xpA, xpB)[g % 2]
            sm = p.sem(f"Lx{g % 2}")
            VE.wait_ge(sm.h, xp_ready[g % 2])
            ACT.wait_ge(sm.h, xp_ready[g % 2])
            get_in = lambda c, seg=seg: seg.ap(512 * c, 512 * c + 512)
        else:
            go = 512 * (g - 4)
            get_in = lambda c, go=go: XO(c, go, go + 512)

        def put_out(c):
            return za.ap(512 * c, 512 * c + 512), None

        lnv = ln_group(get_in, put_out, 512, ln_scr,
                       (pp[0], pp[1], pp[2], pp[3]))

        if not own and g + 2 < 4:
            SY.wait_ge(sV.h, lnv)
            xp_ready[g % 2] = load_xpre(g + 2, (xpA, xpB)[g % 2], g % 2)

        PE.wait_ge(sV.h, lnv)
        gs = 512 * g
        t = pp[4].acquire(PE)
        pv = None
        for c in range(NCH):
            mm = PE.matmul(t[:, 0:512], wkb.ap(128 * c, 128 * (c + 1)),
                           za.ap(512 * c, 512 * (c + 1)),
                           start=(c == 0), stop=(c == NCH - 1))
            if c == NCH - 1:
                pv = sP.inc(mm)
        VE.wait_ge(sP.h, pv)
        vv = sV.inc(VE.tensor_scalar(kfm.ap(gs, gs + 512), t[:, 0:512],
                                     bk_col(), None, op0=ALU.add))
        pp[4].consumed(sV, vv)
        if g >= 2:
            t = pp[5].acquire(PE)
            pv = None
            for c in range(NCH):
                mm = PE.matmul(t[:, 0:512], wqb.ap(128 * c, 128 * (c + 1)),
                               za.ap(512 * c, 512 * (c + 1)),
                               start=(c == 0), stop=(c == NCH - 1))
                if c == NCH - 1:
                    pv = sP.inc(mm)
            VE.wait_ge(sP.h, pv)
            vchain(VE.tensor_scalar(t1_s.ap(), t[:, 0:512], bq_col(), None,
                                    op0=ALU.add))
            vv = sV.inc(VE.tensor_tensor(qd.ap(gs, gs + 512), t1_s.ap(),
                                         awk.ap(), op=ALU.mult))
            pp[5].consumed(sV, vv)
        for srowb, wrow, dstseg in ((brow_k, wkb, ktm), (brow_v, wvb, vtm)):
            for ts in range(4):
                t = pp[6 if dstseg is ktm else 7].acquire(PE)
                sub = gs + 128 * ts
                for c in range(NCH):
                    PE.matmul(t[:, 0:128],
                              za.ap(512 * c + 128 * ts, 512 * c + 128 * (ts + 1)),
                              wrow.ap(128 * c, 128 * (c + 1)),
                              start=(c == 0), stop=False)
                pv = sP.inc(PE.matmul(t[:, 0:128], ones1b[:], srowb[:],
                                      start=False, stop=True))
                VE.wait_ge(sP.h, pv)
                vv = sV.inc(VE.tensor_copy(dstseg.ap(sub, sub + 128), t[:, 0:128]))
                (pp[6] if dstseg is ktm else pp[7]).consumed(sV, vv)

    vchain(VE.memset(stf.ap(), 0.0))
    vv = vchain(VE.tensor_copy(stb.ap(), stf.ap()))
    st_ready = vv
    for cch in range(32):
        cs = 128 * cch
        do_out = cch >= 11
        if do_out:
            t_at = pp[0].acquire(PE)
            pv = sP.inc(PE.matmul(t_at[:, 0:128], kfm.ap(cs, cs + 128),
                                  qd.ap(cs, cs + 128), start=True, stop=True))
            wt = wts[cch % 2]
            VE.wait_ge(sP.h, pv)
            vv = sV.inc(VE.tensor_tensor(wt.ap(), t_at[:, 0:128], dmk.ap(),
                                         op=ALU.mult))
            pp[0].consumed(sV, vv)
            wt_v = vv
        ktd = ktds[cch % 2]
        vv = sV.inc(VE.tensor_scalar(ktd.ap(), ktm.ap(cs, cs + 128),
                                     deck_col(), None, op0=ALU.mult))
        ktd_v = vv
        if do_out:
            t_o = pp[1].acquire(PE)
            PE.wait_ge(sV.h, max(st_ready, wt_v))
            PE.matmul(t_o[:, 0:128], stb.ap(), qd.ap(cs, cs + 128),
                      start=True, stop=False)
            pv_o = sP.inc(PE.matmul(t_o[:, 0:128], vtm.ap(cs, cs + 128), wt.ap(),
                                    start=False, stop=True))
        t_u = pp[2].acquire(PE)
        PE.wait_ge(sV.h, ktd_v)
        pv_u = sP.inc(PE.matmul(t_u[:, 0:128], ktd.ap(), vtm.ap(cs, cs + 128),
                                start=True, stop=True))
        VE.wait_ge(sP.h, pv_u)
        vchain(VE.scalar_tensor_tensor(stf.ap(), stf.ap(), DEC_C, t_u[:, 0:128],
                                       op0=ALU.mult, op1=ALU.add))
        vv = vchain(VE.tensor_copy(stb.ap(), stf.ap()))
        pp[2].consumed(sV, vv)
        st_ready = vv
        if cch == 15:
            vchain(VE.tensor_scalar(stf.ap(), stf.ap(), flag[:],
                                    None, op0=ALU.mult))
            st_ready = vchain(VE.tensor_copy(stb.ap(), stf.ap()))
        if do_out:
            VE.wait_ge(sP.h, pv_o)
            vv = sV.inc(VE.tensor_copy(obuf.ap(cs - 1408, cs - 1408 + 128),
                                       t_o[:, 0:128]))
            pp[1].consumed(sV, vv)
    obuf_v = sV.v

    for c in range(NCH):
        dma(SY, xh.ap(640 * c, 640 * (c + 1)),
            di["x_halo"][128 * c:128 * (c + 1), :], "LH")
    PE.wait_ge(sV.h, obuf_v)
    wait_stream(VE, "LH")
    tiles = [(0, 512, "h"), (512, 640, "h"),
             (640, 1152, "o"), (1152, 1664, "o"), (1664, 2176, "o"), (2176, 2688, "o")]
    for (ta, tb_, kind) in tiles:
        W = tb_ - ta
        for c in range(NCH):
            t = pp[c % 4].acquire(PE)
            pv = sP.inc(PE.matmul(t[:, 0:W], wob.ap(128 * c, 128 * (c + 1)),
                                  obuf.ap(ta, tb_), start=True, stop=True))
            VE.wait_ge(sP.h, pv)
            if kind == "h":
                dst = xh.ap(640 * c + ta, 640 * c + tb_)
                vchain(VE.scalar_tensor_tensor(dst, t[:, 0:W], bo8.ap(c, c + 1),
                                               dst, op0=ALU.add, op1=ALU.add))
                vv = vchain(VE.tensor_scalar(dst, dst, flag[:], None,
                                             op0=ALU.mult))
            else:
                dst = XO(c, ta - 640, tb_ - 640)
                vv = sV.inc(VE.scalar_tensor_tensor(
                    dst, t[:, 0:W], bo8.ap(c, c + 1), dst,
                    op0=ALU.add, op1=ALU.add))
            pp[c % 4].consumed(sV, vv)
    x1_v = sV.v

    for nm_, seg_, a_, w_ in (("za", za, 0, 4096), ("kfm", kfm, 0, 4096), ("qd", qd, 1024, 4096),
                              ("ktm", ktm, 0, 4096), ("vtm", vtm, 0, 4096),
                              ("obuf", obuf, 0, 2688)):
        if dbg(nm_, (128, w_ - a_), BF16) is not None:
            GP.wait_ge(sV.h, obuf_v)
            dma(GP, p.dbg_outs[nm_], seg_.ap(a_, w_), "GD")
    if dbg("x1", (D, TOWN)) is not None:
        GP.wait_ge(sV.h, x1_v)
        for c in range(NCH):
            dma(GP, p.dbg_outs["x1"][128 * c:128 * (c + 1), :],
                XO(c, 0, TOWN), "GD")
    if dbg("x1h", (D, HALO)) is not None:
        GP.wait_ge(sV.h, x1_v)
        for c in range(NCH):
            dma(GP, p.dbg_outs["x1h"][128 * c:128 * (c + 1), :],
                xh.ap(640 * c, 640 * (c + 1)), "GD")

    if upto == "x1":
        finish(p, out_x4, XO, x1_v)
        return p

    p.barrier("ub")
    a2.reset()
    UBW = TEXT + 1
    ubt = a2.alloc(NCH * UBW, F32)

    def UB(c, a, b):
        return ubt.ap(UBW * c + 1 + a, UBW * c + 1 + b)

    for c in range(NCH):
        VE.memset(ubt.ap(UBW * c, UBW * c + 1), 0.0)

    ln_tiles = [(0, 512), (512, 640), (640, 1152), (1152, 1664),
                (1664, 2176), (2176, 2688)]
    z0_v = 0
    for (ta, tb_) in ln_tiles:
        W = tb_ - ta

        def get_in(c, ta=ta, tb_=tb_):
            if tb_ <= 640:
                return xh.ap(640 * c + ta, 640 * c + tb_)
            return XO(c, ta - 640, tb_ - 640)

        def put_out(c, ta=ta, tb_=tb_):
            return UB(c, ta, tb_), None

        z0_v = ln_group(get_in, put_out, W, ln_scr,
                        (pp[4], pp[5], pp[6], pp[7]), ready_v=x1_v)

    if dbg("z0", (D, TEXT)) is not None:
        GP.wait_ge(sV.h, z0_v)
        for c in range(NCH):
            dma(GP, p.dbg_outs["z0"][128 * c:128 * (c + 1), :],
                UB(c, 0, TEXT), "GD")

    if upto == "z0":
        finish(p, out_x4, XO, x1_v)
        return p

    GP.wait_ge(sV.h, x1_v)
    for c in range(NCH):
        dma(GP, x_spill[128 * c:128 * (c + 1), :], XO(c, 0, TOWN), "GS")

    G3 = [(0, 512), (512, 1024), (1024, 1536), (1536, 2048),
          (2048, 2560), (2560, 2688)]
    G4 = [(512, 1024), (1024, 1536), (1536, 2048), (2048, 2560), (2560, 2688)]

    sc.reset()
    sqscr = sc.alloc(512, BF16)
    gelscr = sc.alloc(512, BF16)
    fsm = sc.alloc(64, F32)
    b1c = [sc.alloc(44, F32), sc.alloc(44, F32)]

    qs_col = lambda l: fsm.ap(2 * l, 2 * l + 1)
    ks_col = lambda l: fsm.ap(2 * l + 1, 2 * l + 2)
    b2_col = lambda l: fsm.ap(4 + l, 5 + l)

    for l in range(2):
        dma(SY, fsm.ap(2 * l, 2 * l + 1), di[f"qs{l}"], "L1")
        dma(SY, fsm.ap(2 * l + 1, 2 * l + 2), di[f"ks{l}"], "L1")
        dma(SY, fsm.ap(4 + l, 5 + l), di[f"b2_{l}"][0:128, :], "L1")
    b2c8 = [sc.alloc(8, F32), sc.alloc(8, F32)]
    lgc8 = [sc.alloc(8, F32), sc.alloc(8, F32)]
    assert sc.off <= sc.cols
    for l in range(2):
        dma(SY, b2c8[l].ap().rearrange("p (c o) -> p c o", o=1),
            di[f"b2_{l}"].rearrange("(c p) o -> p c o", p=128),
            allow_slow_non_contiguous=True, stream="L1")
        dma(SY, lgc8[l].ap().rearrange("p (c o) -> p c o", o=1),
            di[f"lng{l}"].rearrange("(c p) o -> p c o", p=128),
            allow_slow_non_contiguous=True, stream="L1")
        dma(SY, b1c[l].ap().rearrange("p (c o) -> p c o", o=1),
            di[f"b1p{l}"].rearrange("(c p) o -> p c o", p=128),
            allow_slow_non_contiguous=True, stream="L1")
    e2b0 = p.sb("e2b0", [1, 128], F32)
    e2b1 = p.sb("e2b1", [1, 128], F32)
    dma(SY, e2b0[:], di["e2"][0:1, :], "L1")
    dma(SY, e2b1[:], di["e2"][1:2, :], "L1")
    e2f = p.sb("e2f", [2, 128], F32)
    dma(SY, e2f[:], di["e2"], "L1")
    oc2b = p.sb("oc2b", [128, 2], BF16)
    dma(GP, oc2b[:], di["ones2col"], "G1")
    rpb = p.sb("rpb", [128, 128], BF16)
    dma(GP, rpb[:], di["Rperm"], "G1")
    khm0b = p.sb("khm0b", [128, 640], BF16)
    dma(GP, khm0b[:], di["khmask0"], "G1")

    for e in (GP, SY, PE, VE, ACT):
        wait_stream(e, "GS")
        wait_stream(e, "G1")
        wait_stream(e, "L1")
    p.barrier("ffn_setup")
    a1.reset()
    zcast = a1.alloc(4096, BF16)
    kbuf = a1.alloc(4 * TEXT, BF16)
    qbuf = a1.alloc(4 * (TEXT - 512), BF16)
    vtbuf = a1.alloc(21 * 520, BF16)
    e0buf = a1.alloc(512, F32)
    ptbuf = a1.alloc(1024, BF16)
    lrbuf = a1.alloc(512, BF16)
    obq = a1.alloc(4 * 512, BF16)
    A1_ATTN = a1.off
    a1.reset()
    abuf = a1.alloc(22 * 512, BF16)
    w2blk = a1.alloc(22 * 1024, BF16)
    a1.off = max(a1.off, A1_ATTN)
    LN2_BASE = 16896
    assert LN2_BASE >= a1.off - 2648
    xb_s2 = [Seg(a1.t, LN2_BASE + 0, 512, BF16),
             Seg(a1.t, LN2_BASE + 256, 512, BF16)]
    sq_s2 = [Seg(a1.t, LN2_BASE + 512, 512, BF16),
             Seg(a1.t, LN2_BASE + 768, 512, BF16)]
    t1_s2 = Seg(a1.t, LN2_BASE + 1024, 512, F32)
    ln_scr2 = {"xb": xb_s2, "sq": sq_s2, "t1": t1_s2}

    def KB(hp, a, b):
        return kbuf.ap(TEXT * hp + a, TEXT * hp + b)

    def QB(hp, a, b):
        return qbuf.ap((TEXT - 512) * hp + a - 512, (TEXT - 512) * hp + b - 512)

    def VT(t):
        return vtbuf.ap(520 * t, 520 * (t + 1))

    a3.reset()
    wqkvb = a3.alloc(8 * 1536, BF16)
    rots = a3.alloc(2 * 512, BF16)
    assert a3.off <= a3.cols
    fwob = Seg(a3.t, 0, 4 * 1024, BF16)
    z2roll = [Seg(a3.t, 0, 8 * 512, BF16), Seg(a3.t, 2048, 8 * 512, BF16)]
    w1blk = [Seg(a3.t, 4096, 8 * 128, BF16), Seg(a3.t, 4608, 8 * 128, BF16)]
    e0b2 = Seg(zcast.t, zcast.off + 1536, 512, F32)
    k1scr = Seg(e0buf.t, e0buf.off + 0, 512, BF16)
    k2scr = Seg(e0buf.t, e0buf.off + 256, 512, BF16)
    tascr = Seg(ptbuf.t, ptbuf.off + 0, 512, BF16)
    tbscr = Seg(ptbuf.t, ptbuf.off + 256, 512, BF16)

    lnrb = p.sb("lnrb", [2, 512], BF16)
    dma(GP, lnrb[1:2, :], di["ones512"], "G1")
    ln_scr2["lnrb"] = lnrb
    for e_ in (PE, VE, ACT, SY, GP):
        wait_stream(e_, "G1")
    lngbs = p.sb("lngbs", [2, D], BF16)

    G3 = [(0, 512), (512, 1024), (1024, 1536), (1536, 2048),
          (2048, 2560), (2560, 2688)]
    G4 = [(512, 1024), (1024, 1536), (1536, 2048), (2048, 2560), (2560, 2688)]

    def emit_ffn_layer(lidx, stage_tag):
        tag = f"{stage_tag}l{lidx}"
        dma(GP, wqkvb.ap().rearrange("p (c m) -> p c m", m=1536),
            di[f"wqkv{lidx}"].rearrange("(c p) m -> p c m", p=128),
            f"Lw_{tag}")
        dma(GP, lngbs[:], di[f"lngb{lidx}"], f"Lw_{tag}")
        wait_stream(PE, f"Lw_{tag}")

        rot_used = 0
        zc_pe = 0
        for g, (ta, tb_) in enumerate(G3):
            W = tb_ - ta
            st_r = f"Lr_{tag}"
            if rot_used:
                GP.wait_ge(sV.h, rot_used)
            wait_stream(GP, st_r)
            dma(GP, rots.ap(0, W), di["rotAk"][:, ta:tb_], st_r)
            dma(GP, rots.ap(512, 512 + W), di["rotBk"][:, ta:tb_], st_r)
            VE.wait_ge(sP.h, zc_pe)
            zcv = 0
            for c in range(NCH):
                off = 1 if c >= 4 else 0
                zcv = sV.inc(VE.tensor_copy(zcast.ap(512 * c, 512 * c + W),
                                            UB(c, ta - off, tb_ - off)))
            PE.wait_ge(sV.h, zcv)

            for isq in (0, 1):
                if isq:
                    GP.wait_ge(sV.h, sV.v)
                    wait_stream(GP, st_r)
                    dma(GP, rots.ap(0, W), di["rotAq"][:, ta:tb_], st_r)
                    dma(GP, rots.ap(512, 512 + W), di["rotBq"][:, ta:tb_], st_r)
                    if g == 0:
                        continue
                for hp in range(4):
                    bank = pp[hp % 4]
                    t = bank.acquire(PE)
                    cols = (0 if isq else 512) + 128 * hp
                    pv = None
                    for c in range(NCH):
                        mm = PE.matmul(t[:, 0:W],
                                       wqkvb.ap(1536 * c + cols,
                                                1536 * c + cols + 128),
                                       zcast.ap(512 * c, 512 * c + W),
                                       start=(c == 0), stop=(c == NCH - 1))
                        if c == NCH - 1:
                            pv = sP.inc(mm)
                    ACT.wait_ge(sP.h, pv)
                    aa = sA.inc(ACT.activation(sqscr.ap(0, W), t[:, 0:W],
                                               AF.Square))
                    PE.wait_ge(sA.h, aa)
                    tn = pp[4].acquire(PE)
                    pn = sP.inc(PE.matmul(tn[0:2, 0:W], oc2b[:], sqscr.ap(0, W),
                                          start=True, stop=True))
                    VE.wait_ge(sP.h, pn)
                    nv = vchain(VE.tensor_scalar(srow[0:2, 0:W], tn[0:2, 0:W],
                                                 1e-24, None, op0=ALU.max))
                    pp[4].consumed(sV, nv)
                    ACT.wait_ge(sV.h, nv)
                    av = sA.inc(ACT.activation(srow[0:2, 512:512 + W],
                                               srow[0:2, 0:W], AF.Ln))
                    ACT.wait_ge(sA.h, av)
                    av = sA.inc(ACT.activation(srow[0:2, 1024:1024 + W],
                                               srow[0:2, 512:512 + W],
                                               AF.Exp, scale=-0.5))
                    PE.wait_ge(sA.h, av)
                    tb2 = pp[5].acquire(PE)
                    pn2 = sP.inc(PE.matmul(tb2[:, 0:W], e2f[:],
                                           srow[0:2, 1024:1024 + W],
                                           start=True, stop=True))
                    VE.wait_ge(sP.h, pn2)
                    scol = qs_col(lidx) if isq else ks_col(lidx)
                    vchain(VE.tensor_scalar(k1scr.ap(0, W), t[:, 0:W], scol,
                                            None, op0=ALU.mult))
                    bank.consumed(sV, sV.v)
                    k2v = vchain(VE.tensor_tensor(k2scr.ap(0, W),
                                                  k1scr.ap(0, W),
                                                  tb2[:, 0:W], op=ALU.mult))
                    pp[5].consumed(sV, k2v)
                    PE.wait_ge(sV.h, k2v)
                    tr = pp[6].acquire(PE)
                    pr = sP.inc(PE.matmul(tr[:, 0:W], rpb[:], k2scr.ap(0, W),
                                          start=True, stop=True))
                    ra = rots.ap(0, W)
                    rb = rots.ap(512, 512 + W)
                    wait_stream(VE, st_r)
                    vchain(VE.tensor_tensor(tascr.ap(0, W), k2scr.ap(0, W), ra,
                                            op=ALU.mult))
                    VE.wait_ge(sP.h, pr)
                    vchain(VE.tensor_tensor(tbscr.ap(0, W), tr[:, 0:W], rb,
                                            op=ALU.mult))
                    pp[6].consumed(sV, sV.v)
                    dst = QB(hp, ta, tb_) if isq else KB(hp, ta, tb_)
                    vchain(VE.tensor_tensor(dst, tascr.ap(0, W),
                                            tbscr.ap(0, W), op=ALU.add))
            rot_used = sV.v
            for ts_ in range(W // 128):
                tvi = (ta + 128 * ts_) // 128
                t = pp[7].acquire(PE)
                pv = None
                for c in range(NCH):
                    mm = PE.matmul(t[:, 0:512],
                                   zcast.ap(512 * c + 128 * ts_,
                                            512 * c + 128 * (ts_ + 1)),
                                   wqkvb.ap(1536 * c + 1024, 1536 * c + 1536),
                                   start=(c == 0), stop=(c == NCH - 1))
                    if c == NCH - 1:
                        pv = sP.inc(mm)
                VE.wait_ge(sP.h, pv)
                vv = sV.inc(VE.tensor_copy(
                    VT(tvi).rearrange("p (h j) -> p h j", j=65)[:, :, 0:64],
                    t[:, 0:512].rearrange("p (h j) -> p h j", j=64)))
                pp[7].consumed(sV, vv)
            zc_pe = sP.v
        vt1 = vchain(VE.memset(
            vtbuf.ap().rearrange("p (t h j) -> p t h j", h=8, j=65)[:, :, :, 64:65],
            1.0))

        GP.wait_ge(sP.h, sP.v)
        dma(GP, fwob.ap().rearrange("p (c m) -> p c m", m=1024),
            di[f"fwo{lidx}"].rearrange("(c p) m -> p c m", p=128),
            f"Lwo_{tag}")
        wait_stream(PE, f"Lwo_{tag}")

        PE.wait_ge(sV.h, vt1)
        pt_cons = [0, 0]
        e0_cons = 0
        for qg, (qa, qb_) in enumerate(G4):
            W = qb_ - qa
            nk = (W + 512) // 128
            for hp in range(4):
                toA = pp[4].acquire(PE)
                toB = pp[5].acquire(PE)
                avs = [0, 0]
                pvk = {}
                sbk = {}

                def emit_qk(kk):
                    ka = qa - 512 + 128 * kk
                    banks = (pp[(2 * kk) % 4], pp[(2 * kk + 1) % 4])
                    sbk[kk] = banks
                    pvs_ = []
                    for hh in range(2):
                        tdst = banks[hh].acquire(PE)
                        PE.matmul(tdst[:, 0:W], identb[:], fmask(kk)[:, 0:W],
                                  start=True, stop=False)
                        if (qg == 0 and kk <= 3) or (qg == 1 and kk == 0):
                            PE.matmul(tdst[:, 0:W], identb[:],
                                      khm0b[:, 128:128 + W],
                                      start=False, stop=False)
                        if qg == 0 and kk == 4:
                            PE.matmul(tdst[:, 0:W], identb[:], khm0b[:, 0:W],
                                      start=False, stop=False)
                        pvs_.append(sP.inc(PE.matmul(
                            tdst[:, 0:W],
                            KB(hp, ka, ka + 128)[64 * hh:64 * (hh + 1), :],
                            QB(hp, qa, qb_)[64 * hh:64 * (hh + 1), :],
                            start=False, stop=True)))
                    pvk[kk] = pvs_

                def emit_tail(kk):
                    banks = sbk[kk]
                    ACT.wait_ge(sP.h, max(pvk[kk]))
                    ACT.wait_ge(sP.h, max(pt_cons))
                    if W == 512:
                        whole = pq[kk % 2][:, 0:1024]
                        aa = sA.inc(ACT.activation(ptbuf.ap(0, 1024), whole,
                                                   AF.Exp, scale=1.0))
                        banks[0].consumed(sA, aa)
                        banks[1].consumed(sA, aa)
                    else:
                        for hh in range(2):
                            aa = sA.inc(ACT.activation(
                                ptbuf.ap(512 * hh, 512 * hh + W),
                                banks[hh].t[:, 0:W], AF.Exp, scale=1.0))
                            banks[hh].consumed(sA, aa)
                    PE.wait_ge(sA.h, aa)
                    for hh in range(2):
                        to = toA if hh == 0 else toB
                        h = 2 * hp + hh
                        av = sP.inc(PE.matmul(
                            to[0:65, 0:W],
                            VT((qa - 512 + 128 * kk) // 128)[:, 65 * h:65 * (h + 1)],
                            ptbuf.ap(512 * hh, 512 * hh + W),
                            start=(kk == 0), stop=(kk == nk - 1)))
                        pt_cons[hh] = av
                        avs[hh] = av

                emit_qk(0)
                for kk in range(1, nk):
                    emit_qk(kk)
                    emit_tail(kk - 1)
                emit_tail(nk - 1)
                avA, avB = avs
                VE.wait_ge(sP.h, max(avA, avB))
                vchain(VE.tensor_copy(srow[0:1, 0:512][:, 0:W], toA[64:65, 0:W]))
                lv = vchain(VE.tensor_copy(srow[0:1, 512:512 + W],
                                           toB[64:65, 0:W]))
                ACT.wait_ge(sV.h, lv)
                for seg0 in (0, 512):
                    av = sA.inc(ACT.activation(
                        srow[0:1, 1024 + seg0:1024 + seg0 + W],
                        srow[0:1, seg0:seg0 + W], AF.Ln))
                    ACT.wait_ge(sA.h, av)
                    av = sA.inc(ACT.activation(
                        srow[0:1, 1024 + seg0:1024 + seg0 + W],
                        srow[0:1, 1024 + seg0:1024 + seg0 + W],
                        AF.Exp, scale=-1.0))
                PE.wait_ge(sA.h, av)
                tl = pp[6].acquire(PE)
                PE.matmul(tl[:, 0:W], e2b0[:], srow[0:1, 1024:1024 + W],
                          start=True, stop=False)
                pl = sP.inc(PE.matmul(tl[:, 0:W], e2b1[:],
                                      srow[0:1, 1536:1536 + W],
                                      start=False, stop=True))
                VE.wait_ge(sP.h, pl)
                lrv = vchain(VE.tensor_copy(lrbuf.ap(0, W), tl[:, 0:W]))
                pp[6].consumed(sV, lrv)
                v1 = vchain(VE.tensor_tensor(
                    obq.ap(512 * hp, 512 * hp + W, p0=0, p1=64),
                    toA[0:64, 0:W], lrbuf.ap(0, W, p0=0, p1=64), op=ALU.mult))
                v2 = vchain(VE.tensor_tensor(
                    obq.ap(512 * hp, 512 * hp + W, p0=64, p1=128),
                    toB[0:64, 0:W], lrbuf.ap(0, W, p0=64, p1=128),
                    op=ALU.mult))
                pp[4].consumed(sV, v1)
                pp[5].consumed(sV, v2)
            ob_v = sV.v
            PE.wait_ge(sV.h, ob_v)
            for dc in range(NCH):
                t = pp[6 + (dc % 2)].acquire(PE)
                pv = None
                for hp in range(4):
                    mm = PE.matmul(t[:, 0:W],
                                   fwob.ap(1024 * hp + 128 * dc,
                                           1024 * hp + 128 * dc + 128),
                                   obq.ap(512 * hp, 512 * hp + W),
                                   start=(hp == 0), stop=(hp == 3))
                    if hp == 3:
                        pv = sP.inc(mm)
                VE.wait_ge(sP.h, pv)
                dst = UB(dc, qa, qb_)
                vv = vchain(VE.tensor_tensor(dst, t[:, 0:W], dst, op=ALU.add))
                if qg == 0:
                    vv = vchain(VE.tensor_scalar(UB(dc, 512, 640),
                                                 UB(dc, 512, 640), flag[:],
                                                 None, op0=ALU.mult))
                pp[6 + (dc % 2)].consumed(sV, vv)

        p.barrier(f"mlp_{tag}")
        dma(GP, w2blk.ap().rearrange("p (c m) -> p c m", m=1024),
            di[f"w2p{lidx}"].rearrange("(c p) m -> p c m", p=128),
            f"Lw2_{tag}")

        def ln2(g, slot):
            ta = 640 + 512 * g

            def get_in(c):
                off = 1 if c >= 4 else 0
                return UB(c, ta - off, ta + 512 - off)

            def put_out(c):
                return (z2roll[slot].ap(512 * c, 512 * (c + 1)),
                        lgc8[lidx].ap(c, c + 1))

            def lgrow(c):
                return lngbs[0:2, 128 * c:128 * (c + 1)]

            return ln_group(get_in, put_out, 512, ln_scr2,
                            (pp[0], pp[1], pp[6], pp[7]), gb=lgrow,
                            ready_v=sV.v)

        wblk_used = [0, 0]
        gel_cons = [0]

        def mlp(g, slot):
            ta = 640 + 512 * g
            z2 = z2roll[slot]

            def w1_load(ob):
                st_w = f"Lw1{ob % 2}_{tag}"
                if wblk_used[ob % 2]:
                    GP.wait_ge(sP.h, wblk_used[ob % 2])
                dma(GP, w1blk[ob % 2].ap().rearrange("p (c o) -> p c o", o=128),
                    di[f"w1p{lidx}"].rearrange("(c p) o -> p c o", p=128)
                    [:, :, 128 * ob:128 * (ob + 1)], st_w)
                return p.sem(st_w).v

            w1_ld = {0: w1_load(0), 1: w1_load(1)}
            for ob in range(44):
                st_w = f"Lw1{ob % 2}_{tag}"
                PE.wait_ge(p.sem(st_w).h, w1_ld[ob % 2])
                t = pp[2 + (ob % 4)].acquire(PE)
                pv = None
                for c in range(NCH):
                    mm = PE.matmul(t[:, 0:512],
                                   w1blk[ob % 2].ap(128 * c, 128 * (c + 1)),
                                   z2.ap(512 * c, 512 * (c + 1)),
                                   start=(c == 0), stop=(c == NCH - 1))
                    if c == NCH - 1:
                        pv = sP.inc(mm)
                wblk_used[ob % 2] = pv
                if ob + 2 < 44:
                    w1_ld[ob % 2] = w1_load(ob + 2)
                if ob < 22:
                    VE.wait_ge(sP.h, pv)
                    vv = sV.inc(VE.tensor_scalar(
                        abuf.ap(512 * ob, 512 * (ob + 1)), t[:, 0:512],
                        b1c[lidx].ap(ob, ob + 1), None, op0=ALU.add))
                    pp[2 + (ob % 4)].consumed(sV, vv)
                else:
                    hb = ob - 22
                    ACT.wait_ge(sP.h, pv)
                    ACT.wait_ge(sV.h, gel_cons[0])
                    aa = sA.inc(ACT.activation(gelscr.ap(0, 512), t[:, 0:512],
                                               AF.Gelu,
                                               bias=b1c[lidx].ap(ob, ob + 1)))
                    pp[2 + (ob % 4)].consumed(sA, aa)
                    VE.wait_ge(sA.h, aa)
                    gv = vchain(VE.tensor_tensor(
                        abuf.ap(512 * hb, 512 * (hb + 1)),
                        abuf.ap(512 * hb, 512 * (hb + 1)),
                        gelscr.ap(0, 512), op=ALU.mult))
                    gel_cons[0] = gv
            h_v = sV.v
            PE.wait_ge(sV.h, h_v)
            wait_stream(PE, f"Lw2_{tag}")
            for half in range(2):
                ts2 = [None] * 4
                pvs = [None] * 4
                for hc in range(22):
                    for di_ in range(4):
                        dc = 4 * half + di_
                        if hc == 0:
                            ts2[di_] = pp[2 + di_].acquire(PE)
                        mm = PE.matmul(ts2[di_][:, 0:512],
                                       w2blk.ap(1024 * hc + 128 * dc,
                                                1024 * hc + 128 * dc + 128),
                                       abuf.ap(512 * hc, 512 * (hc + 1)),
                                       start=(hc == 0), stop=(hc == 21))
                        if hc == 21:
                            pvs[di_] = sP.inc(mm)
                for di_ in range(4):
                    dc = 4 * half + di_
                    VE.wait_ge(sP.h, pvs[di_])
                    dst = UB(dc, ta, ta + 512)
                    vv = vchain(VE.scalar_tensor_tensor(
                        dst, ts2[di_][:, 0:512], b2c8[lidx].ap(dc, dc + 1),
                        dst, op0=ALU.add, op1=ALU.add))
                    pp[2 + di_].consumed(sV, vv)

        ln2(0, 0)
        ln2(1, 1)
        mlp(0, 0)
        ln2(2, 0)
        mlp(1, 1)
        ln2(3, 1)
        mlp(2, 0)
        mlp(3, 1)
        p.barrier(f"eol_{tag}")
        return sV.v

    def ag_exchange(ag_in, ag_out, tag):
        GP.wait_ge(sV.h, sV.v)
        for c in range(NCH):
            dma(GP, ag_in[128 * c:128 * (c + 1), :], UB(c, 2048, TEXT),
                f"GA_{tag}")
        wait_stream(GP, f"GA_{tag}")
        sc_ = p.sem(f"CC_{tag}")
        sc_.inc(GP.collective_compute(
            "AllGather", ALU.bypass, replica_groups=RG,
            ins=[ag_in.opt()], outs=[ag_out.opt()]), 1)
        GP.wait_ge(sc_.h, sc_.v)
        for c in range(NCH):
            dma(GP, UB(c, 0, HALO), ag_out[128 * c:128 * (c + 1), :],
                f"GH_{tag}")
        wait_stream(VE, f"GH_{tag}")
        hv = 0
        for c in range(NCH):
            hv = vchain(VE.tensor_scalar(UB(c, 0, HALO), UB(c, 0, HALO),
                                         flag[:], None, op0=ALU.mult))
        return hv

    def emit_ffn_stage(stage_tag, agi, ago):
        emit_ffn_layer(0, stage_tag)
        if stage_tag == "s2" and dbg("u1", (D, TOWN)) is not None:
            GP.wait_ge(sV.h, sV.v)
            for c in range(NCH):
                dma(GP, p.dbg_outs["u1"][128 * c:128 * (c + 1), :],
                    UB(c, HALO, TEXT), "GD")
            for e_ in (VE, GP, PE, ACT):
                wait_stream(e_, "GD")
        hv = ag_exchange(agi, ago, stage_tag)
        PE.wait_ge(sV.h, hv)
        ACT.wait_ge(sV.h, hv)
        emit_ffn_layer(1, stage_tag)

    emit_ffn_stage("s2", ag1_in, ag1_out)

    p.barrier("x2")
    a1.reset()
    xo2 = a1.alloc(16384, F32)

    def XO2(c, a, b):
        return xo2.ap(TOWN * c + a, TOWN * c + b)

    wait_stream(SY, "GS")
    for c in range(NCH):
        dma(SY, XO2(c, 0, TOWN), x_spill[128 * c:128 * (c + 1), :], "LS2")
    wait_stream(VE, "LS2")
    x2_v = 0
    for c in range(NCH):
        dst = xo2.ap(TOWN * c, TOWN * (c + 1))
        x2_v = vchain(VE.tensor_tensor(dst, dst, UB(c, HALO, TEXT),
                                       op=ALU.add))
    if dbg("x2", (D, TOWN)) is not None:
        GP.wait_ge(sV.h, x2_v)
        for c in range(NCH):
            dma(GP, p.dbg_outs["x2"][128 * c:128 * (c + 1), :],
                XO2(c, 0, TOWN), "GD")
        for e_ in (VE, GP, PE, ACT):
            wait_stream(e_, "GD")
    if upto == "x2":
        finish(p, out_x4, XO2, x2_v)
        return p

    a2.reset()
    kbb = a2.alloc(4 * S, BF16)
    vtb = a2.alloc(32 * 520, BF16)
    zrd = a2.alloc(8 * 512, BF16)
    qwn = a2.alloc(4 * 512, BF16)
    ost = a2.alloc(512, BF16)
    e0big = a2.alloc(1024, F32)
    zown = zrd

    def KBB(hp, a, b):
        return kbb.ap(S * hp + a, S * hp + b)

    def VTB(t):
        return vtb.ap(520 * t, 520 * (t + 1))

    a3.reset()
    pib = [Seg(a3.t, 2048 * i, 8 * 512, BF16) for i in range(3)]
    pos = Seg(a3.t, 5120, 2 * 1024, BF16)

    for i, nm in enumerate(("PiQ", "PiK", "PiV")):
        dma(GP, pib[i].ap().rearrange("p (c m) -> p c m", m=512),
            di[nm].rearrange("(c p) m -> p c m", p=128), "LPi")

    for g in range(4):
        wait_stream(VE, "GZ")

        def get_in(c, g=g):
            return XO2(c, 512 * g, 512 * (g + 1))

        def put_out(c):
            return zown.ap(512 * c, 512 * (c + 1)), None

        zv = ln_group(get_in, put_out, 512, ln_scr2,
                      (pp[0], pp[1], pp[6], pp[7]), ready_v=x2_v)
        SY.wait_ge(sV.h, zv)
        for c in range(NCH):
            dma(SY, ag2z_in[128 * c:128 * (c + 1), 512 * g:512 * (g + 1)],
                zown.ap(512 * c, 512 * (c + 1)), "GZ")
    GP.wait_ge(sV.h, x2_v)
    for c in range(NCH):
        dma(GP, ag2x_in[128 * c:128 * (c + 1), :],
            XO2(c, TOWN - HALO, TOWN), "GX")
    wait_stream(GP, "GZ")
    wait_stream(GP, "GX")
    ccz = p.sem("CCz")
    ccz.inc(GP.collective_compute("AllGather", ALU.bypass, replica_groups=RG,
                                  ins=[ag2z_in.opt()], outs=[ag2z_out.opt()]), 1)
    ccz.inc(GP.collective_compute("AllGather", ALU.bypass, replica_groups=RG,
                                  ins=[ag2x_in.opt()], outs=[ag2x_out.opt()]), 1)
    GP.wait_ge(ccz.h, ccz.v)
    wait_stream(PE, "LPi")

    def zread_group(g, stream):
        sl = g // 4
        co = 512 * (g % 4)
        for c in range(NCH):
            dma(SY, zrd.ap(512 * c, 512 * (c + 1)),
                ag2z_out[1024 * sl + 128 * c:1024 * sl + 128 * (c + 1),
                         co:co + 512], stream)

    zr_pe = [0]
    for g in range(8):
        SY.wait_ge(sP.h, zr_pe[0])
        SY.wait_ge(ccz.h, 1)
        zread_group(g, "LZ")
        wait_stream(PE, "LZ")
        for hp in range(4):
            t = pp[hp % 2].acquire(PE)
            pv = None
            for c in range(NCH):
                mm = PE.matmul(t[:, 0:512],
                               pib[1].ap(512 * c + 128 * hp, 512 * c + 128 * (hp + 1)),
                               zrd.ap(512 * c, 512 * (c + 1)),
                               start=(c == 0), stop=(c == NCH - 1))
                if c == NCH - 1:
                    pv = sP.inc(mm)
            VE.wait_ge(sP.h, pv)
            vv = sV.inc(VE.tensor_copy(KBB(hp, 512 * g, 512 * (g + 1)),
                                       t[:, 0:512]))
            pp[hp % 2].consumed(sV, vv)
        for ts_ in range(4):
            tvi = 4 * g + ts_
            t = pp[2 + ts_ % 2].acquire(PE)
            pv = None
            for c in range(NCH):
                mm = PE.matmul(t[:, 0:512],
                               zrd.ap(512 * c + 128 * ts_, 512 * c + 128 * (ts_ + 1)),
                               pib[2].ap(512 * c, 512 * (c + 1)),
                               start=(c == 0), stop=(c == NCH - 1))
                if c == NCH - 1:
                    pv = sP.inc(mm)
            VE.wait_ge(sP.h, pv)
            vv = sV.inc(VE.tensor_copy(
                VTB(tvi).rearrange("p (h j) -> p h j", j=65)[:, :, 0:64],
                t[:, 0:512].rearrange("p (h j) -> p h j", j=64)))
            pp[2 + ts_ % 2].consumed(sV, vv)
        zr_pe[0] = sP.v
    vt1 = vchain(VE.memset(
        vtb.ap().rearrange("p (t h j) -> p t h j", h=8, j=65)[:, :, :, 64:65],
        1.0))
    PE.wait_ge(sV.h, vt1)

    pt_cons = [0, 0]
    e0_cons = [0]
    qw_pe = [0]
    wait_stream(SY, "LZ")
    SY.wait_ge(sP.h, zr_pe[0])
    for qw in range(8):
        SY.wait_ge(sP.h, qw_pe[0])
        zread_group(qw, "LZq")
        wait_stream(PE, "LZq")
        for hp in range(4):
            t = pp[7].acquire(PE)
            pv = None
            for c in range(NCH):
                mm = PE.matmul(t[:, 0:512],
                               pib[0].ap(512 * c + 128 * hp, 512 * c + 128 * (hp + 1)),
                               zrd.ap(512 * c, 512 * (c + 1)),
                               start=(c == 0), stop=(c == NCH - 1))
                if c == NCH - 1:
                    pv = sP.inc(mm)
            VE.wait_ge(sP.h, pv)
            vv = sV.inc(VE.tensor_copy(qwn.ap(512 * hp, 512 * (hp + 1)),
                                       t[:, 0:512]))
            pp[7].consumed(sV, vv)
        qw_pe[0] = sP.v
        PE.wait_ge(sV.h, sV.v)
        nk = 4 * (qw + 1)
        for hp in range(4):
            toA = pp[4].acquire(PE)
            toB = pp[5].acquire(PE)
            avs = [0, 0]
            pvk = {}
            sbk = {}

            def emit_qk(kk):
                ka = 128 * kk
                banks = ((pp[0], pp[1]), (pp[2], pp[3]),
                         (pp[6], pp[7]))[kk % 3]
                sbk[kk] = banks
                pvs_ = []
                for hh in range(2):
                    tdst = banks[hh].acquire(PE)
                    first = True
                    if kk >= nk - 4:
                        PE.matmul(tdst[:, 0:512], identb[:],
                                  bmask(kk - (nk - 4))[:, 0:512],
                                  start=True, stop=False)
                        first = False
                    pvs_.append(sP.inc(PE.matmul(
                        tdst[:, 0:512],
                        KBB(hp, ka, ka + 128)[64 * hh:64 * (hh + 1), :],
                        qwn.ap(512 * hp, 512 * (hp + 1))[64 * hh:64 * (hh + 1), :],
                        start=first, stop=True)))
                pvk[kk] = pvs_

            def emit_tail(kk):
                banks = sbk[kk]
                whole = pq[kk % 3][:, 0:1024]
                ACT.wait_ge(sP.h, max(pvk[kk]))
                ACT.wait_ge(sV.h, e0_cons[0])
                aa = sA.inc(ACT.activation(e0big.ap(0, 1024), whole,
                                           AF.Exp, scale=1.0 / 32.0))
                ACT.wait_ge(sA.h, aa)
                aa = sA.inc(ACT.activation(e0big.ap(0, 1024), e0big.ap(0, 1024),
                                           AF.Square))
                banks[0].consumed(sA, aa)
                banks[1].consumed(sA, aa)
                VE.wait_ge(sA.h, aa)
                VE.wait_ge(sP.h, max(pt_cons))
                ptv = sV.inc(VE.tensor_tensor(ptbuf.ap(0, 1024),
                                              e0big.ap(0, 1024),
                                              e0big.ap(0, 1024), op=ALU.mult))
                e0_cons[0] = ptv
                PE.wait_ge(sV.h, ptv)
                for hh in range(2):
                    to = toA if hh == 0 else toB
                    h = 2 * hp + hh
                    av = sP.inc(PE.matmul(
                        to[0:65, 0:512],
                        VTB(kk)[:, 65 * h:65 * (h + 1)],
                        ptbuf.ap(512 * hh, 512 * hh + 512),
                        start=(kk == 0), stop=(kk == nk - 1)))
                    pt_cons[hh] = av
                    avs[hh] = av

            emit_qk(0)
            if nk > 1:
                emit_qk(1)
            for kk in range(2, nk):
                emit_qk(kk)
                emit_tail(kk - 2)
            if nk > 1:
                emit_tail(nk - 2)
            emit_tail(nk - 1)
            avA, avB = avs
            VE.wait_ge(sP.h, max(avA, avB))
            vchain(VE.tensor_copy(srow[0:1, 0:512], toA[64:65, 0:512]))
            lv = vchain(VE.tensor_copy(srow[0:1, 512:1024], toB[64:65, 0:512]))
            ACT.wait_ge(sV.h, lv)
            for seg0 in (0, 512):
                av = sA.inc(ACT.activation(srow[0:1, 1024 + seg0:1536 + seg0],
                                           srow[0:1, seg0:512 + seg0], AF.Ln))
                ACT.wait_ge(sA.h, av)
                av = sA.inc(ACT.activation(srow[0:1, 1024 + seg0:1536 + seg0],
                                           srow[0:1, 1024 + seg0:1536 + seg0],
                                           AF.Exp, scale=-1.0))
            PE.wait_ge(sA.h, av)
            tl = pp[6].acquire(PE)
            PE.matmul(tl[:, 0:512], e2b0[:], srow[0:1, 1024:1536],
                      start=True, stop=False)
            pl = sP.inc(PE.matmul(tl[:, 0:512], e2b1[:], srow[0:1, 1536:2048],
                                  start=False, stop=True))
            VE.wait_ge(sP.h, pl)
            lrv = vchain(VE.tensor_copy(lrbuf.ap(0, 512), tl[:, 0:512]))
            pp[6].consumed(sV, lrv)
            wait_stream(VE, "GO")
            vchain(VE.tensor_tensor(ost.ap(0, 512, p0=0, p1=64),
                                    toA[0:64, 0:512],
                                    lrbuf.ap(0, 512, p0=0, p1=64), op=ALU.mult))
            ov = vchain(VE.tensor_tensor(ost.ap(0, 512, p0=64, p1=128),
                                         toB[0:64, 0:512],
                                         lrbuf.ap(0, 512, p0=64, p1=128),
                                         op=ALU.mult))
            pp[4].consumed(sV, ov)
            pp[5].consumed(sV, ov)
            SY.wait_ge(sV.h, ov)
            dma(SY, ag3_in[128 * hp:128 * (hp + 1), 512 * qw:512 * (qw + 1)],
                ost.ap(0, 512), "GO")

    for nm_, seg_, w_ in (("Kbb", kbb, 4 * S), ("Vtb", vtb, 32 * 520)):
        if dbg(nm_, (128, w_), BF16) is not None:
            GP.wait_ge(sV.h, sV.v)
            dma(GP, p.dbg_outs[nm_], seg_.ap(0, w_), "GD")
            wait_stream(GP, "GD")
    if dbg("Ob", (512, S), BF16) is not None:
        wait_stream(GP, "GO")
        dma(GP, p.dbg_outs["Ob"], ag3_in, "GD")
        wait_stream(GP, "GD")
    wait_stream(GP, "GO")
    cc3 = p.sem("CC3")
    cc3.inc(GP.collective_compute("AllGather", ALU.bypass, replica_groups=RG,
                                  ins=[ag3_in.opt()], outs=[ag3_out.opt()]), 1)
    GP.wait_ge(cc3.h, cc3.v)

    GP.wait_ge(sV.h, sV.v)
    for c in range(NCH):
        dma(GP, xh.ap(640 * c, 640 * (c + 1)),
            ag2x_out[128 * c:128 * (c + 1), :], "LXH")
    wait_stream(VE, "LXH")
    for c in range(NCH):
        vchain(VE.tensor_scalar(xh.ap(640 * c, 640 * (c + 1)),
                                xh.ap(640 * c, 640 * (c + 1)),
                                flag[:], None, op0=ALU.mult))
    p.barrier("ob_comb")
    a2.reset()
    ocb = a2.alloc(8 * 2688, BF16)
    ourd = [a2.alloc(2 * 512, BF16), a2.alloc(2 * 512, BF16)]
    ot1 = a2.alloc(512, BF16)

    def OCB(ic, a, b):
        return ocb.ap(2688 * ic + a, 2688 * ic + b)

    oc_tiles = [(ic, ta, tb_) for ic in range(NCH)
                for (ta, tb_) in ((1408, 1920), (1920, 2048), (2048, 2560),
                                  (2560, 3072), (3072, 3584), (3584, 4096))]
    oc_used = [0, 0]
    oc_ld = [0, 0]

    def oc_load(i):
        ic, ta, tb_ = oc_tiles[i]
        sl = i % 2
        st = f"LOr{sl}"
        if oc_used[sl]:
            SY.wait_ge(sV.h, oc_used[sl])
        for rot in range(2):
            ca = (ta + 2048 * rot) % 4096
            dma(SY, ourd[sl].ap(512 * rot, 512 * rot + (tb_ - ta)),
                ag3_out[128 * ic:128 * (ic + 1), ca:ca + (tb_ - ta)], st)
        return p.sem(st).v

    oc_ld[0] = oc_load(0)
    oc_ld[1] = oc_load(1)
    ocv = 0
    for i, (ic, ta, tb_) in enumerate(oc_tiles):
        W = tb_ - ta
        sl = i % 2
        VE.wait_ge(p.sem(f"LOr{sl}").h, oc_ld[sl])
        vchain(VE.tensor_scalar(ot1.ap(0, W), ourd[sl].ap(0, W),
                                flag[:], None, op0=ALU.mult))
        ocv = vchain(VE.scalar_tensor_tensor(
            OCB(ic, ta - 1408, tb_ - 1408), ourd[sl].ap(512, 512 + W),
            flagn[:], ot1.ap(0, W), op0=ALU.mult, op1=ALU.add))
        oc_used[sl] = ocv
        if i + 2 < len(oc_tiles):
            oc_ld[sl] = oc_load(i + 2)
    pob = Seg(a3.t, 5120, 2 * 1024, BF16)
    ytiles = [(0, 512, "h"), (512, 640, "h"), (640, 1152, "o"),
              (1152, 1664, "o"), (1664, 2176, "o"), (2176, 2688, "o")]
    x3_v = 0
    po_used = [0, 0]
    for dc in range(NCH):
        st_p = f"LPo{dc % 2}"
        if po_used[dc % 2]:
            GP.wait_ge(sP.h, po_used[dc % 2])
        dma(GP, pob.ap(1024 * (dc % 2), 1024 * (dc % 2) + 1024)
            .rearrange("p (c m) -> p c m", m=128),
            di["Po"].rearrange("(c p) m -> p c m", p=128)
            [:, :, 128 * dc:128 * (dc + 1)], st_p)
        wait_stream(PE, st_p)
        PE.wait_ge(sV.h, ocv)
        for ti, (ta, tb_, kind) in enumerate(ytiles):
            W = tb_ - ta
            t = pp[ti % 4].acquire(PE)
            pv = None
            for ic in range(NCH):
                mm = PE.matmul(t[:, 0:W],
                               pob.ap(1024 * (dc % 2) + 128 * ic,
                                      1024 * (dc % 2) + 128 * (ic + 1)),
                               OCB(ic, ta, tb_),
                               start=(ic == 0), stop=(ic == NCH - 1))
                if ic == NCH - 1:
                    pv = sP.inc(mm)
            po_used[dc % 2] = pv
            VE.wait_ge(sP.h, pv)
            if kind == "h":
                dst = xh.ap(640 * dc + ta, 640 * dc + tb_)
                vchain(VE.tensor_tensor(dst, t[:, 0:W], dst, op=ALU.add))
                x3_v = vchain(VE.tensor_scalar(dst, dst, flag[:], None,
                                               op0=ALU.mult))
            else:
                dst = XO2(dc, ta - 640, tb_ - 640)
                x3_v = vchain(VE.tensor_tensor(dst, t[:, 0:W], dst, op=ALU.add))
            pp[ti % 4].consumed(sV, x3_v)

    if dbg("x3", (D, TOWN)) is not None:
        GP.wait_ge(sV.h, x3_v)
        for c in range(NCH):
            dma(GP, p.dbg_outs["x3"][128 * c:128 * (c + 1), :],
                XO2(c, 0, TOWN), "GD")
        for e_ in (VE, GP, PE, ACT):
            wait_stream(e_, "GD")
    if upto == "x3":
        finish(p, out_x4, XO2, x3_v)
        return p

    GP.wait_ge(sV.h, x3_v)
    for c in range(NCH):
        dma(GP, x_spill[128 * c:128 * (c + 1), :], XO2(c, 0, TOWN), "GS2")
    for e_ in (GP, SY, PE, VE, ACT):
        wait_stream(e_, "GS2")
    p.barrier("z0p")
    a2.reset()
    a2.off = NCH * UBW
    for c in range(NCH):
        vchain(VE.memset(ubt.ap(UBW * c, UBW * c + 1), 0.0))

    ln_tiles2 = [(0, 512), (512, 640), (640, 1152), (1152, 1664),
                 (1664, 2176), (2176, 2688)]
    z0p_v = 0
    for (ta, tb_) in ln_tiles2:
        W = tb_ - ta

        def get_in(c, ta=ta, tb_=tb_):
            if tb_ <= 640:
                return xh.ap(640 * c + ta, 640 * c + tb_)
            return XO2(c, ta - 640, tb_ - 640)

        def put_out(c, ta=ta, tb_=tb_):
            return UB(c, ta, tb_), None

        z0p_v = ln_group(get_in, put_out, W, ln_scr2,
                         (pp[4], pp[5], pp[6], pp[7]), ready_v=x3_v)
    p.barrier("s4")

    emit_ffn_stage("s4", ag4_in, ag4_out)

    p.barrier("x4")
    for c in range(NCH):
        dma(SY, XO2(c, 0, TOWN), x_spill[128 * c:128 * (c + 1), :], "LS4")
    wait_stream(VE, "LS4")
    x4_v = 0
    for c in range(NCH):
        dst = xo2.ap(TOWN * c, TOWN * (c + 1))
        x4_v = vchain(VE.tensor_tensor(dst, dst, UB(c, HALO, TEXT),
                                       op=ALU.add))
    finish(p, out_x4, XO2, x4_v)
    return p


def finish(p, out_x4, XO, ready_v):
    nc = p.nc
    sO = p.sem("O")
    nc.gpsimd.wait_ge(p.sem("V").h, ready_v)
    for c in range(NCH):
        sO.inc(nc.gpsimd.dma_start(out=out_x4[128 * c:128 * (c + 1), :],
                                   in_=XO(c, 0, TOWN)), 16)
    for e in (nc.gpsimd, nc.sync, nc.tensor, nc.vector, nc.scalar):
        e.wait_ge(sO.h, sO.v)
        if "GD" in p._sems:
            e.wait_ge(p.sem("GD").h, p.sem("GD").v)


def kernel(**inputs):
    in_maps = [_stage_core(inputs, c) for c in range(8)]
    p = build(upto=BUILD_UPTO, debug=DEBUG_OUT)
    res = run_bass_kernel_spmd(p.nc, in_maps, core_ids=list(range(8)))
    out = np.empty((B, S, D), dtype=np.float32)
    for c in range(8):
        b, half = c // 2, c % 2
        out[b, 2048 * half:2048 * half + 2048] = res.results[c]["o_x4"].T
    return out


BUILD_UPTO = "full"
DEBUG_OUT = ()
